# version 03
# baseline (speedup 1.0000x reference)
import math
from contextlib import ExitStack

import numpy as np
import ml_dtypes
import concourse.bass as bass
import concourse.mybir as mybir
from concourse.bass_utils import run_bass_kernel_spmd

F32 = mybir.dt.float32
BF16 = mybir.dt.bfloat16
FP16 = mybir.dt.float16
FP8 = mybir.dt.float8e4
AF = mybir.ActivationFunctionType
ALU = mybir.AluOpType
AX = mybir.AxisListType.X

NCORE = 8
D = 1024
S = 2048
NT = D // 128
STL = S // 128
NH = 16
DK = 64
MAGIC = 12582912.0
NEGBIG = -1.0e30
LN_EPS = 1e-5

_bf = ml_dtypes.bfloat16


class Prog:
    NQ = 8

    def __init__(self, nc):
        self.nc = nc
        self.stream = {e: [] for e in "tvsyg"}
        self.semnames = ["t", "v", "s"] + \
            ["y%d" % i for i in range(self.NQ)] + \
            ["g%d" % i for i in range(self.NQ)]
        self.cnt = {sn: 0 for sn in self.semnames}
        self.dmaidx = {"y": 0, "g": 0}
        self.lastw = {}
        self.readers = {}
        self.hw = {e: {} for e in "tvsyg"}

    def emit(self, eng, fn, reads=(), writes=(), tag=""):
        need = {}
        if eng in "yg":
            slot = self.dmaidx[eng] % self.NQ
            self.dmaidx[eng] += 1
            sn = eng + str(slot)
            if self.cnt[sn]:
                need[sn] = self.cnt[sn]
        else:
            sn = eng
        for b in reads:
            w = self.lastw.get(b)
            if w is not None and not (eng == "t" and w[0] == "t"):
                need[w[0]] = max(need.get(w[0], 0), w[1])
        for b in writes:
            w = self.lastw.get(b)
            if w is not None and not (eng == "t" and w[0] == "t"):
                need[w[0]] = max(need.get(w[0], 0), w[1])
            for s2, c2 in self.readers.get(b, {}).items():
                if not (eng == "t" and s2 == "t"):
                    need[s2] = max(need.get(s2, 0), c2)
        for s2 in sorted(need):
            c2 = need[s2]
            if self.hw[eng].get(s2, 0) >= c2:
                continue
            self.stream[eng].append(("wait", s2, c2))
            self.hw[eng][s2] = c2
        inc = 16 if eng in "yg" else 1
        self.cnt[sn] += inc
        c = self.cnt[sn]
        self.stream[eng].append(("op", fn, sn))
        for b in writes:
            self.lastw[b] = (sn, c)
            self.readers[b] = {}
        for b in reads:
            self.readers.setdefault(b, {})[sn] = c

    def finalize(self, ctx):
        nc = self.nc
        sems = {sn: ctx.enter_context(nc.semaphore("sem_" + sn))
                for sn in self.semnames}
        block = ctx.enter_context(nc.Block())
        streams, cnt = self.stream, self.cnt

        def replay(e):
            inc = 16 if e in "yg" else 1

            def body(eng):
                for item in streams[e]:
                    if item[0] == "wait":
                        eng.wait_ge(sems[item[1]], item[2])
                    else:
                        item[1](eng).then_inc(sems[item[2]], inc)
                if e == "g":
                    for sn in self.semnames:
                        if cnt[sn]:
                            eng.wait_ge(sems[sn], cnt[sn])
            return body

        block.tensor(replay("t"))
        block.vector(replay("v"))
        block.scalar(replay("s"))
        block.sync(replay("y"))
        block.gpsimd(replay("g"))


class Mem:
    def __init__(self, nc, ctx):
        self.nc = nc
        self.ctx = ctx
        self.t = {}
        self.dram = {}

    def sb(self, name, shape, dtype):
        if name not in self.t:
            self.t[name] = self.ctx.enter_context(
                self.nc.sbuf_tensor(name, shape, dtype))
        return self.t[name]

    def ps(self, name, shape, dtype=F32):
        if name not in self.t:
            self.t[name] = self.ctx.enter_context(
                self.nc.psum_tensor(name, shape, dtype))
        return self.t[name]

    def dr(self, name, shape, dtype=F32):
        if name not in self.dram:
            self.dram[name] = self.nc.dram_tensor(name, shape, dtype,
                                                  kind="Internal")
        return self.dram[name].ap()

    def tiles(self, base, n, cols=S, dtype=BF16):
        return [self.sb("%s%d" % (base, i), [128, cols], dtype)
                for i in range(n)]


def keys(base, n):
    return ["%s%d" % (base, i) for i in range(n)]



def emit_load_wchunk(P, dst, dst_key, src_ap, src_keys, kt, c0, cols):
    def fn(eng):
        src = src_ap[:, c0:c0 + cols]
        src3 = src.rearrange("(kc p) c -> p kc c", p=128)
        return eng.dma_start(out=dst[:, 0:kt * cols], in_=src3)
    P.emit("g", fn, reads=list(src_keys), writes=[dst_key], tag="wload")


def emit_proj_fmajor(P, M, name, wt_ap, Kdim, Mout, rhs_tiles, rhs_keys, evac,
                     wchunk=512, wdtype=BF16, wt_keys=(), pbanks=("P4", "P5")):
    kt = Kdim // 128
    assert kt * wchunk <= 4096 and wdtype == BF16
    wb = [M.sb("WPA", [128, 4096], BF16), M.sb("WPB", [128, 4096], BF16)]
    wk = ["WPA", "WPB"]
    nchunk = (Mout + wchunk - 1) // wchunk
    emit_load_wchunk(P, wb[0], wk[0], wt_ap, wt_keys, kt, 0, min(wchunk, Mout))
    pidx = 0
    for ci in range(nchunk):
        par = ci % 2
        if ci + 1 < nchunk:
            c0n = (ci + 1) * wchunk
            emit_load_wchunk(P, wb[1 - par], wk[1 - par], wt_ap, wt_keys, kt,
                             c0n, min(wchunk, Mout - c0n))
        ntile = min(wchunk, Mout - ci * wchunk) // 128
        for mt in range(ntile):
            for sc in range(4):
                pn = pbanks[pidx % len(pbanks)]
                pidx += 1
                psum = M.ps(pn, [128, 512])
                for kc in range(kt):
                    def fn(eng, kc=kc, mt=mt, sc=sc, par=par, psum=psum):
                        lhsT = wb[par][:, kc * wchunk + mt * 128:
                                       kc * wchunk + (mt + 1) * 128]
                        rhs = rhs_tiles[kc][:, sc * 512:(sc + 1) * 512]
                        return eng.matmul(psum[:], lhsT, rhs,
                                          start=(kc == 0), stop=(kc == kt - 1))
                    P.emit("t", fn, reads=[wk[par], rhs_keys[kc]], writes=[pn],
                           tag=name + "mm")
                evac(ci * (wchunk // 128) + mt, sc, psum, pn)


def emit_bcast_row(P, M, name, src_row_ap, src_key, npart, dst_ap, dst_key,
                   ncols):
    dr = M.dr("dsc_" + name, [1, ncols], F32)
    dk = "dsc_" + name
    P.emit("y", lambda eng: eng.dma_start(out=dr[:, 0:ncols], in_=src_row_ap),
           reads=[src_key], writes=[dk], tag="bco")
    P.emit("g", lambda eng: eng.dma_start(
        out=dst_ap, in_=dr[0:1, 0:ncols].partition_broadcast(npart)),
        reads=[dk], writes=[dst_key], tag="bci")


def sq_views(M):
    t32 = [M.sb("T32A", [128, S], F32), M.sb("T32B", [128, S], F32)]
    return ([t32[0].bitcast(BF16)[:, 0:S], t32[1].bitcast(BF16)[:, 0:S]],
            ["T32A", "T32B"])


def c_area(M):
    ct = [M.sb("C%d" % i, [128, 2080], BF16) for i in range(NT)]
    return ct


def va_ap(ct, kt):
    return ct[kt // 2][:, (kt % 2) * 1040:(kt % 2) * 1040 + 1040]


def fq8_views(M):
    ut = M.tiles("U", NT)
    aps = []
    ks = []
    for i in range(NT):
        host = ut[2 + i // 2]
        aps.append(host.bitcast(FP8)[:, (i % 2) * 2048:(i % 2) * 2048 + 2048])
        ks.append("U%d" % (2 + i // 2))
    return aps, ks


def h8_views(M):
    ut = M.tiles("U", NT)
    b0 = M.sb("B0", [128, S], BF16)
    b1 = M.sb("B1", [128, S], BF16)
    hosts = [(ut[6], "U6"), (ut[7], "U7"), (b0, "B0"), (b1, "B1")]
    aps, ks = [], []
    for i in range(32):
        h, k = hosts[i // 8]
        aps.append(h.bitcast(FP8)[:, (i % 8) * 512:(i % 8) * 512 + 512])
        ks.append(k)
    return aps, ks


def emit_ln(P, M, x_tiles, x_keys, eps=LN_EPS, out_tiles=None, out_keys=None,
            fq8=False, f32out=None):
    ones = M.sb("ONESC", [128, 1], BF16)
    p6 = M.ps("P6", [128, 512])
    pb = M.ps("PB", [128, 2048])
    r16a = M.sb("R16A", [16, S], F32)
    r16b = M.sb("R16B", [16, S], F32)
    bc1 = M.sb("BCA", [128, S], BF16)
    bc2 = M.sb("BCB", [128, S], BF16)
    t32 = [M.sb("T32A", [128, S], F32), M.sb("T32B", [128, S], F32)]
    sq, sqk = sq_views(M)
    if out_tiles is None:
        out_tiles, out_keys = x_tiles, x_keys
    if fq8:
        q8, q8k = fq8_views(M)
    for sc in range(4):
        for i in range(NT):
            def fn(eng, i=i, sc=sc):
                return eng.matmul(p6[0:1, 0:512], ones[:, 0:1],
                                  x_tiles[i][:, sc * 512:(sc + 1) * 512],
                                  start=(i == 0), stop=(i == NT - 1))
            P.emit("t", fn, reads=["ONESC", x_keys[i]], writes=["P6"],
                   tag="ln_sum")
        P.emit("v", lambda eng, sc=sc: eng.tensor_scalar(
            r16a[0:1, sc * 512:(sc + 1) * 512], p6[0:1, 0:512],
            1.0 / D, None, ALU.mult),
            reads=["P6"], writes=["R16A"], tag="ln_mean")
    for i in range(NT):
        P.emit("s", lambda eng, i=i: eng.activation(
            sq[i % 2], x_tiles[i][:], AF.Square),
            reads=[x_keys[i]], writes=[sqk[i % 2]], tag="ln_sq")
        for sc in range(4):
            def fn(eng, i=i, sc=sc):
                return eng.matmul(pb[0:1, sc * 512:(sc + 1) * 512], ones[:, 0:1],
                                  sq[i % 2][:, sc * 512:(sc + 1) * 512],
                                  start=(i == 0), stop=(i == NT - 1))
            P.emit("t", fn, reads=["ONESC", sqk[i % 2]],
                   writes=["PB%d" % sc], tag="ln_ssq")
    P.emit("v", lambda eng: eng.tensor_scalar(
        r16b[0:1, :], pb[0:1, 0:2048], 1.0 / D, None, ALU.mult),
        reads=pbkeys(), writes=["R16B"], tag="ln_msq")
    P.emit("v", lambda eng: eng.tensor_mul(
        t32[0][0:1, 0:2048], r16a[0:1, :], r16a[0:1, :]),
        reads=["R16A"], writes=["T32A"], tag="ln_m2")
    P.emit("v", lambda eng: eng.scalar_tensor_tensor(
        r16b[0:1, :], r16b[0:1, :], float(eps), t32[0][0:1, 0:2048],
        ALU.add, ALU.subtract),
        reads=["R16B", "T32A"], writes=["R16B"], tag="ln_var")
    P.emit("v", lambda eng: eng.reciprocal(r16b[0:1, :], r16b[0:1, :]),
           reads=["R16B"], writes=["R16B"], tag="ln_rec")
    P.emit("s", lambda eng: eng.activation(r16b[0:1, :], r16b[0:1, :], AF.Sqrt),
           reads=["R16B"], writes=["R16B"], tag="ln_rstd")
    emit_bcast_row(P, M, "lnm", r16a[0:1, :], "R16A", 128, bc1[:], "BCA", S)
    emit_bcast_row(P, M, "lnr", r16b[0:1, :], "R16B", 128, bc2[:], "BCB", S)
    for i in range(NT):
        tk = "T32%s" % ("A" if i % 2 == 0 else "B")
        tt = t32[i % 2]
        P.emit("v", lambda eng, i=i, tt=tt: eng.tensor_sub(
            tt[:], x_tiles[i][:], bc1[:]),
            reads=[x_keys[i], "BCA"], writes=[tk], tag="ln_a1")
        if fq8:
            P.emit("v", lambda eng, i=i, tt=tt: eng.tensor_mul(
                q8[i], tt[:], bc2[:]),
                reads=[tk, "BCB"], writes=[q8k[i]], tag="ln_a8")
        if f32out is not None:
            P.emit("v", lambda eng, i=i, tt=tt: eng.tensor_mul(
                tt[:], tt[:], bc2[:]),
                reads=[tk, "BCB"], writes=[tk], tag="ln_a32")
            P.emit("g", lambda eng, i=i, tt=tt: eng.dma_start(
                out=f32out[i * 128:(i + 1) * 128, :], in_=tt[:]),
                reads=[tk], writes=["__out__"], tag="ln_oy")
            P.emit("v", lambda eng, i=i, tt=tt: eng.tensor_copy(
                out_tiles[i][:], tt[:]),
                reads=[tk], writes=[out_keys[i]], tag="ln_a2c")
        else:
            P.emit("v", lambda eng, i=i, tt=tt: eng.tensor_mul(
                out_tiles[i][:], tt[:], bc2[:]),
                reads=[tk, "BCB"], writes=[out_keys[i]], tag="ln_a2")


def pbkeys(c0=0, c1=2048):
    return ["PB%d" % i for i in range(c0 // 512, (c1 + 511) // 512)]



def emit_attention(P, M, name, proj_qk, kstart_fn, masked_diag):
    a0, a1 = M.sb("A0", [128, S], BF16), M.sb("A1", [128, S], BF16)
    sq, sqk = sq_views(M)
    sqf0 = M.sb("T32A", [128, S], F32)
    idn = M.sb("IDN", [128, 128], BF16)
    mtri = M.sb("MTRI", [128, 512], BF16)
    negb = M.sb("NEGB", [128, 16], F32)
    r16a = M.sb("R16A", [16, S], F32)
    r16b = M.sb("R16B", [16, S], F32)
    bc1 = M.sb("A2", [128, S], BF16)
    bo = M.sb("BO16", [128, 16], BF16)
    pb = M.ps("PB", [128, 2048])
    p4, p5 = M.ps("P4", [128, 512]), M.ps("P5", [128, 512])
    pq = M.ps("P6", [128, 512])
    ctiles = c_area(M)
    utiles = M.tiles("U", NT)
    pbuf = [M.sb("PF0", [128, 512], BF16), M.sb("PF1", [128, 512], BF16)]

    for pair in range(NH // 2):
        proj_qk(pair)
        for t, (src, skey) in enumerate([(a0, "A0"), (a1, "A1")]):
            dstrow = r16a if t == 0 else r16b
            dkey = "R16A" if t == 0 else "R16B"
            P.emit("s", lambda eng, src=src: eng.activation(
                sq[0], src[:], AF.Square),
                reads=[skey], writes=[sqk[0]], tag="bn_sq")
            for sc in range(4):
                def fn(eng, sc=sc, pair=pair):
                    return eng.matmul(pq[0:2, 0:512], bo[:, 2 * pair:2 * pair + 2],
                                      sq[0][:, sc * 512:(sc + 1) * 512],
                                      start=True, stop=True)
                P.emit("t", fn, reads=["BO16", "T32A"], writes=["P6"], tag="bn_mm")
                P.emit("v", lambda eng, sc=sc, dstrow=dstrow: eng.tensor_copy(
                    dstrow[0:2, sc * 512:(sc + 1) * 512], pq[0:2, 0:512]),
                    reads=["P6"], writes=[dkey], tag="bn_cp")
        P.emit("v", lambda eng: eng.reduce_max(
            sqf0[0:2, 0:1], r16a[0:2, :], axis=AX),
            reads=["R16A"], writes=["T32A"], tag="bn_rmax")
        P.emit("v", lambda eng: eng.reduce_max(
            sqf0[0:2, 1:2], r16b[0:2, :], axis=AX),
            reads=["R16B"], writes=["T32A"], tag="bn_rmax2")
        P.emit("v", lambda eng: eng.tensor_mul(
            sqf0[0:2, 2:3], sqf0[0:2, 0:1],
            sqf0[0:2, 1:2]),
            reads=["T32A"], writes=["T32A"], tag="bn_mul")
        P.emit("s", lambda eng: eng.activation(
            sqf0[0:2, 2:3], sqf0[0:2, 2:3], AF.Sqrt),
            reads=["T32A"], writes=["T32A"], tag="bn_sqrt")
        P.emit("v", lambda eng: eng.tensor_scalar(
            sqf0[0:2, 2:3], sqf0[0:2, 2:3],
            -1.0, -1.0, ALU.add, ALU.mult),
            reads=["T32A"], writes=["T32A"], tag="bn_neg")
        dnb = M.dr("dnb", [1, 16], F32)
        P.emit("y", lambda eng, pair=pair: eng.dma_start(
            out=dnb[0:1, 2 * pair:2 * pair + 2],
            in_=sqf0[0:2, 2:3]),
            reads=["T32A"], writes=["dnb"], tag="bn_out")
        P.emit("g", lambda eng, pair=pair: eng.dma_start(
            out=negb[:, 2 * pair:2 * pair + 2],
            in_=dnb[0:1, 2 * pair:2 * pair + 2].partition_broadcast(128)),
            reads=["dnb"], writes=["NEGB"], tag="bn_in")
        for hh in range(2):
            h = 2 * pair + hh
            qrow = a0[64 * hh:64 * (hh + 1), :]
            krow = a1[64 * hh:64 * (hh + 1), :]
            sidx = 0
            for kt in range(STL):
                q0 = kstart_fn(kt)
                bounds = []
                c = q0
                while c < S:
                    e = min(S, (c // 512 + 1) * 512)
                    bounds.append((c, e))
                    c = e
                for (c0, c1) in bounds:
                    w = c1 - c0
                    pS = [p4, p5][sidx % 2]
                    pSk = ["P4", "P5"][sidx % 2]
                    sidx += 1
                    first = (c0 == q0 and masked_diag)
                    if first:
                        def fnm(eng, c0=c0, w=w, pS=pS):
                            return eng.matmul(pS[:, 0:w], idn[:],
                                              mtri[:, 0:w], start=True,
                                              stop=False)
                        P.emit("t", fnm, reads=["IDN", "MTRI"], writes=[pSk],
                               tag="attn_mask")
                    def fns(eng, kt=kt, c0=c0, w=w, pS=pS, krow=krow,
                            qrow=qrow, first=first):
                        return eng.matmul(
                            pS[:, 0:w], krow[:, kt * 128:(kt + 1) * 128],
                            qrow[:, c0:c0 + w],
                            start=not first, stop=True)
                    P.emit("t", fns, reads=["A1", "A0"], writes=[pSk],
                           tag="attn_sc")
                    pf = pbuf[sidx % 2]
                    pfk = "PF%d" % (sidx % 2)
                    def fne(eng, w=w, pS=pS, pf=pf, h=h):
                        return eng.activation(pf[:, 0:w], pS[:, 0:w], AF.Exp,
                                              bias=negb[:, h:h + 1])
                    P.emit("s", fne, reads=[pSk, "NEGB"], writes=[pfk],
                           tag="attn_exp")
                    kt_last = (c1 - 1) // 128
                    def fnv(eng, kt=kt, c0=c0, w=w, pf=pf, h=h,
                            kt_last=kt_last):
                        va = va_ap(ctiles, kt)
                        return eng.matmul(
                            pb[0:65, c0:c0 + w],
                            va[:, 65 * h:65 * h + 65], pf[:, 0:w],
                            start=(kt == 0), stop=(kt == kt_last))
                    P.emit("t", fnv, reads=["C%d" % (kt // 2), pfk],
                           writes=pbkeys(c0, c1), tag="attn_av")
            P.emit("s", lambda eng: eng.activation(
                r16a[0:1, :], pb[64:65, 0:2048], AF.Copy),
                reads=pbkeys(), writes=["R16A"], tag="attn_sum")
            P.emit("v", lambda eng: eng.reciprocal(r16a[0:1, :], r16a[0:1, :]),
                   reads=["R16A"], writes=["R16A"], tag="attn_rec")
            emit_bcast_row(P, M, name + "rs", r16a[0:1, :], "R16A", 64,
                           bc1[0:64, :], "A2", S)
            def fnev(eng, hh=hh, h=h):
                return eng.tensor_mul(
                    utiles[h // 2][64 * hh:64 * (hh + 1), :],
                    pb[0:64, 0:2048], bc1[0:64, :])
            P.emit("v", fnev, reads=pbkeys() + ["A2"],
                   writes=["U%d" % (h // 2)], tag="attn_ev")



def emit_stage_l0_attn(P, M, W):
    nc = P.nc
    xt = M.tiles("X", NT)
    xk = keys("X", NT)
    ut = M.tiles("U", NT)
    a0, a1, a2 = M.sb("A0", [128, S], BF16), M.sb("A1", [128, S], BF16), \
        M.sb("A2", [128, S], BF16)
    ck = M.sb("BCA", [128, S], BF16)[:, 0:S]
    sk = M.sb("BCB", [128, S], BF16)[:, 0:S]
    ctiles = c_area(M)
    p4 = M.ps("P4", [128, 512])
    p5 = M.ps("P5", [128, 512])

    wvb = M.sb("WPA", [128, 4096], BF16)
    emit_load_wchunk(P, wvb, "WPA", W["L0_wvT"][0], [W["L0_wvT"][1]], NT, 0, 512)
    wvb2 = M.sb("WPB", [128, 4096], BF16)
    emit_load_wchunk(P, wvb2, "WPB", W["L0_wvT"][0], [W["L0_wvT"][1]], NT,
                     512, 512)
    for kt in range(STL):
        P.emit("v", lambda eng, kt=kt: eng.memset(
            va_ap(ctiles, kt).rearrange("p (h c) -> p h c", c=65)[:, :, 64:65],
            1.0), reads=[], writes=["C%d" % (kt // 2)], tag="vones")
    for st in range(STL):
        for ec in range(2):
            pn = ["P4", "P5"][(st * 2 + ec) % 2]
            psum = [p4, p5][(st * 2 + ec) % 2]
            wsrc, wkey = (wvb, "WPA") if ec == 0 else (wvb2, "WPB")
            for kc in range(NT):
                def fn(eng, st=st, ec=ec, kc=kc, psum=psum, wsrc=wsrc):
                    return eng.matmul(
                        psum[:], xt[kc][:, st * 128:(st + 1) * 128],
                        wsrc[:, kc * 512:(kc + 1) * 512],
                        start=(kc == 0), stop=(kc == NT - 1))
                P.emit("t", fn, reads=[xk[kc], wkey], writes=[pn], tag="l0v")
            def fne(eng, st=st, ec=ec, psum=psum):
                dst = va_ap(ctiles, st).rearrange("p (h c) -> p h c", c=65)[
                    :, 8 * ec:8 * ec + 8, 0:64]
                src = psum[:].rearrange("p (h c) -> p h c", c=64)
                return eng.activation(dst, src, AF.Copy)
            P.emit("s", fne, reads=[pn], writes=["C%d" % (st // 2)],
                   tag="l0v_ev")

    P.emit("g", lambda eng: eng.dma_start(out=ck, in_=W["ropeC"][0]),
           reads=[W["ropeC"][1]], writes=["BCA"], tag="ropec")
    P.emit("g", lambda eng: eng.dma_start(out=sk, in_=W["ropeS"][0]),
           reads=[W["ropeS"][1]], writes=["BCB"], tag="ropes")

    qwb = M.sb("WPA", [128, 4096], BF16)[:, 0:1024]
    kwb = M.sb("WPA", [128, 4096], BF16)[:, 1024:2048]

    def proj_qk(pair):
        emit_load_wchunk(P, qwb, "WPA", W["L0_wqT"][0], [W["L0_wqT"][1]], NT,
                         pair * 128, 128)
        emit_load_wchunk(P, kwb, "WPA", W["L0_wkT"][0], [W["L0_wkT"][1]], NT,
                         pair * 128, 128)
        for (wb, wkey, dst, dkey, scl) in [
                (qwb, "WPA", a0, "A0", 0.125), (kwb, "WPA", a1, "A1", 1.0)]:
            for sc in range(4):
                pn = "P4" if dkey == "A0" else "P5"
                psum = p4 if dkey == "A0" else p5
                for kc in range(NT):
                    def fn(eng, kc=kc, sc=sc, psum=psum, wb=wb):
                        return eng.matmul(
                            psum[:], wb[:, kc * 128:(kc + 1) * 128],
                            xt[kc][:, sc * 512:(sc + 1) * 512],
                            start=(kc == 0), stop=(kc == NT - 1))
                    P.emit("t", fn, reads=[wkey, xk[kc]], writes=[pn],
                           tag="l0qk")
                P.emit("s", lambda eng, sc=sc, psum=psum, dst=dst, scl=scl:
                       eng.activation(dst[:, sc * 512:(sc + 1) * 512],
                                      psum[:], AF.Copy, scale=scl),
                       reads=[pn], writes=[dkey], tag="l0qk_ev")
            P.emit("v", lambda eng, dst=dst: eng.stream_shuffle(
                a2[:], dst[:], [i ^ 1 for i in range(32)]),
                reads=[dkey], writes=["A2"], tag="rope_sh")
            P.emit("v", lambda eng, dst=dst: eng.tensor_mul(dst[:], dst[:], ck),
                   reads=[dkey, "BCA"], writes=[dkey], tag="rope_c")
            P.emit("v", lambda eng: eng.tensor_mul(a2[:], a2[:], sk),
                   reads=["A2", "BCB"], writes=["A2"], tag="rope_s")
            P.emit("v", lambda eng, dst=dst: eng.tensor_add(dst[:], dst[:], a2[:]),
                   reads=[dkey, "A2"], writes=[dkey], tag="rope_a")

    emit_attention(P, M, "l0", proj_qk, lambda kt: kt * 128, True)

    def evac(mt, sc, psum, pn):
        def fn(eng, mt=mt, sc=sc, psum=psum):
            return eng.tensor_add(xt[mt][:, sc * 512:(sc + 1) * 512],
                                  xt[mt][:, sc * 512:(sc + 1) * 512],
                                  psum[:])
        P.emit("v", fn, reads=[pn, xk[mt]], writes=[xk[mt]], tag="l0wo_ev")
    emit_proj_fmajor(P, M, "l0wo", W["L0_woT"][0], D, D,
                     M.tiles("U", NT), keys("U", NT), evac,
                     wt_keys=[W["L0_woT"][1]])


def emit_stage_l0_ffn(P, M, W):
    nc = P.nc
    xt = M.tiles("X", NT)
    xk = keys("X", NT)
    xq8, xq8k = fq8_views(M)
    h8, h8k = h8_views(M)
    ut = M.tiles("U", NT)
    h16 = ut[0].bitcast(FP16)[:, 0:512]
    p4, p5 = M.ps("P4", [128, 512]), M.ps("P5", [128, 512])
    b1q = M.sb("B1Q", [128, 32], FP8)
    b2q = M.sb("B2Q", [128, 8], FP8)
    P.emit("g", lambda eng: eng.dma_start(out=b1q[:], in_=W["L0_b1"][0]),
           reads=[W["L0_b1"][1]], writes=["B1Q"], tag="b1")
    P.emit("g", lambda eng: eng.dma_start(out=b2q[:], in_=W["L0_b2"][0]),
           reads=[W["L0_b2"][1]], writes=["B2Q"], tag="b2")
    b1f = M.sb("B1F", [128, 32], F32)
    b2f = M.sb("B2F", [128, 8], F32)
    P.emit("v", lambda eng: eng.tensor_copy(b1f[:], b1q[:]),
           reads=["B1Q"], writes=["B1F"], tag="b1f")
    P.emit("v", lambda eng: eng.tensor_copy(b2f[:], b2q[:]),
           reads=["B2Q"], writes=["B2F"], tag="b2f")

    wpa = M.sb("WPA", [128, 4096], BF16)
    wpb = M.sb("WPB", [128, 4096], BF16)
    w1b = [wpa.bitcast(FP8)[:, 0:4096], wpb.bitcast(FP8)[:, 0:4096]]
    w1k = ["WPA", "WPB"]
    w2b = w1b
    w2k = w1k
    dm = M.sb("DMHB", [128, 1024], FP16)

    for sc in range(4):
        emit_load_wchunk(P, w1b[0], w1k[0], W["L0_w1T"][0], [W["L0_w1T"][1]],
                         NT, 0, 512)
        for mc in range(8):
            par = mc % 2
            if mc + 1 < 8:
                emit_load_wchunk(P, w1b[1 - par], w1k[1 - par], W["L0_w1T"][0],
                                 [W["L0_w1T"][1]], NT, (mc + 1) * 512, 512)
            for mt in range(4):
                mg = mc * 4 + mt
                pn = ["P4", "P5"][mg % 2]
                psum = [p4, p5][mg % 2]
                for kc in range(NT):
                    def fn(eng, kc=kc, mt=mt, sc=sc, par=par, psum=psum):
                        return eng.matmul(
                            psum[:], w1b[par][:, kc * 512 + mt * 128:
                                              kc * 512 + (mt + 1) * 128],
                            xq8[kc][:, sc * 512:(sc + 1) * 512],
                            start=(kc == 0), stop=(kc == NT - 1))
                    P.emit("t", fn, reads=[w1k[par], xq8k[kc]], writes=[pn],
                           tag="l0f_mm1")
                def fe1(eng, mg=mg, psum=psum):
                    return eng.activation(h16, psum[:], AF.Relu,
                                          bias=b1f[:, mg:mg + 1])
                P.emit("s", fe1, reads=[pn, "B1F"], writes=["U0"], tag="l0f_h16")
                def fe2(eng, mg=mg):
                    return eng.activation(h8[mg], h16, AF.Copy)
                P.emit("s", fe2, reads=["U0"], writes=[h8k[mg]], tag="l0f_h8")
        emit_load_wchunk(P, w2b[0], w2k[0], W["L0_w2T"][0], [W["L0_w2T"][1]],
                         32, 0, 128)
        for dt in range(NT):
            par = dt % 2
            if dt + 1 < NT:
                emit_load_wchunk(P, w2b[1 - par], w2k[1 - par], W["L0_w2T"][0],
                                 [W["L0_w2T"][1]], 32, (dt + 1) * 128, 128)
            pn = ["P4", "P5"][dt % 2]
            psum = [p4, p5][dt % 2]
            for kc in range(32):
                def fn(eng, kc=kc, dt=dt, sc=sc, par=par, psum=psum):
                    return eng.matmul(
                        psum[:], w2b[par][:, kc * 128:(kc + 1) * 128],
                        h8[kc][:, 0:512],
                        start=(kc == 0), stop=(kc == 31))
                P.emit("t", fn, reads=[w2k[par], h8k[kc]], writes=[pn],
                       tag="l0f_mm2")
            def fm(eng, dt=dt, sc=sc):
                src = W["L0_dmask"][0][dt * 128:(dt + 1) * 128,
                                       sc * 512:(sc + 1) * 512]
                return eng.dma_start(out=dm[:, 0:512], in_=src)
            P.emit("g", fm, reads=[W["L0_dmask"][1]], writes=["DMHB"],
                   tag="l0f_dm")
            def fe(eng, dt=dt, psum=psum):
                return eng.tensor_scalar(h16, psum[:], b2f[:, dt:dt + 1],
                                         None, ALU.add)
            P.emit("v", fe, reads=[pn, "B2F"], writes=["U0"], tag="l0f_y16")
            def fmu(eng, dt=dt, sc=sc):
                return eng.tensor_mul(ut[1].bitcast(FP16)[:, 0:512], h16,
                                      dm[:, 0:512])
            P.emit("v", fmu, reads=["U0", "DMHB"], writes=["U1"], tag="l0f_msk")
            def fad(eng, dt=dt, sc=sc):
                return eng.tensor_add(
                    xt[dt][:, sc * 512:(sc + 1) * 512],
                    xt[dt][:, sc * 512:(sc + 1) * 512],
                    ut[1].bitcast(FP16)[:, 0:512])
            P.emit("v", fad, reads=["U1", xk[dt]], writes=[xk[dt]],
                   tag="l0f_res")



STAGE_EMITTERS = {}


_DECLARED = {}


def _declare_weight_inputs(nc, stage, host_w):
    W = {}
    decl = _DECLARED.setdefault(id(nc), {})
    for name, arr in host_w[stage].items():
        if name in decl:
            W[name] = decl[name]
            continue
        dt = {np.dtype(np.float32): F32, np.dtype(np.float16): FP16,
              np.dtype(_bf): BF16}[arr.dtype]
        h = nc.dram_tensor("i_" + name, list(arr.shape), dt,
                           kind="ExternalInput")
        W[name] = (h.ap(), "i_" + name)
        decl[name] = W[name]
    return W


def emit_consts(P, M, W):
    ones = M.sb("ONESC", [128, 1], BF16)
    P.emit("v", lambda eng: eng.memset(ones[:], 1.0), reads=(), writes=["ONESC"])
    for cname, sname, shape in [("cIDN", "IDN", [128, 128]),
                                ("cMTRI", "MTRI", [128, 512]),
                                ("cBO16", "BO16", [128, 16])]:
        if cname in W:
            t = M.sb(sname, shape, BF16)
            P.emit("g", lambda eng, t=t, cname=cname: eng.dma_start(
                out=t[:], in_=W[cname][0]),
                reads=[W[cname][1]], writes=[sname], tag="const")


def emit_x_in(P, M, W, fq8):
    xt = M.tiles("X", NT)
    for i in range(NT):
        P.emit("g", lambda eng, i=i: eng.dma_start(
            out=xt[i][:], in_=W["x_in"][0][i * 128:(i + 1) * 128, :]),
            reads=[W["x_in"][1]], writes=["X%d" % i], tag="xin")
    if fq8:
        q8, q8k = fq8_views(M)
        for i in range(NT):
            P.emit("g", lambda eng, i=i: eng.dma_start(
                out=q8[i], in_=W["x_in"][0][i * 128:(i + 1) * 128, :]),
                reads=[W["x_in"][1]], writes=[q8k[i]], tag="xin8")


def build_program(stages, host_w, first, last):
    nc = bass.Bass("TRN2", target_bir_lowering=False)
    ctx = ExitStack()
    M = Mem(nc, ctx)
    P = Prog(nc)
    Wall = {}
    for st in stages:
        Wall.update(_declare_weight_inputs(nc, st, host_w))
    Wall.update(_declare_weight_inputs(nc, "const", host_w))
    if 1 in stages:
        hm = nc.dram_tensor("i_L0_dmask", [D, S], FP16, kind="ExternalInput")
        Wall["L0_dmask"] = (hm.ap(), "i_L0_dmask")
    h = nc.dram_tensor("i_x_in", [D, S], F32, kind="ExternalInput")
    Wall["x_in"] = (h.ap(), "i_x_in")
    ho = nc.dram_tensor("o_y", [D, S], F32, kind="ExternalOutput")
    NEED_FQ8 = {1}
    eps_of = {i: (1e-10 if False else LN_EPS) for i in range(8)}
    with ctx:
        emit_consts(P, M, Wall)
        emit_x_in(P, M, Wall, stages[0] in NEED_FQ8)
        xt = M.tiles("X", NT)
        xk = keys("X", NT)
        for si, st in enumerate(stages):
            STAGE_EMITTERS[st](P, M, Wall)
            nxt = stages[si + 1] if si + 1 < len(stages) else None
            lastln = (si == len(stages) - 1)
            emit_ln(P, M, xt, xk, eps=eps_of[st],
                    fq8=(nxt in NEED_FQ8),
                    f32out=(ho.ap() if lastln else None))
        P.finalize(ctx)
    return nc


STAGE_EMITTERS[0] = emit_stage_l0_attn
STAGE_EMITTERS[1] = emit_stage_l0_ffn



def _ct(a):
    return np.ascontiguousarray(a)


def make_host_weights(inp):
    w = {}
    w[0] = {
        "L0_wqT": _ct(inp["L0_attn_wq"].T),
        "L0_wkT": _ct(inp["L0_attn_wk"].T),
        "L0_wvT": _ct(inp["L0_attn_wv"].T),
        "L0_woT": _ct(inp["L0_attn_wo"].T),
        "ropeC": _rope_table(cos=True),
        "ropeS": _rope_table(cos=False),
    }
    w[2] = {
        "L1_wattnT": _ct(inp["L1_attn_w_attn"].T),
        "L1_bq": _ct(inp["L1_attn_b_attn"][0:1024].reshape(8, 128).T),
        "L1_bk": _ct(inp["L1_attn_b_attn"][1024:2048].reshape(8, 128).T),
        "L1_bv": _ct(inp["L1_attn_b_attn"][2048:3072].reshape(1, 1024)),
        "L1_wprojT": _ct(inp["L1_attn_w_proj"].T),
        "L1_bp": _ct(inp["L1_attn_b_proj"].reshape(8, 128).T),
        "L1_wfcT": _ct(inp["L1_attn_w_fc"].T),
        "L1_bfc": _ct(inp["L1_attn_b_fc"].reshape(32, 128).T),
        "L1_wmlpT": _ct(inp["L1_attn_w_mlp_proj"].T),
        "L1_bm": _ct(inp["L1_attn_b_mlp_proj"].reshape(8, 128).T),
    }
    w[3] = {
        "L1f_w1T": _ct(inp["L1_ffn_w1"].T),
        "L1f_w2T": _ct(inp["L1_ffn_w2"].T),
        "L1f_cw": _ct(inp["L1_ffn_conv_w"].reshape(32, 128, 3)
                      .transpose(1, 0, 2).reshape(128, 96)),
        "L1f_cb": _ct(inp["L1_ffn_conv_b"].reshape(32, 128).T),
    }
    w[4] = {
        "L2_WqT": _ct(inp["L2_attn_Wq"].T),
        "L2_WkT": _ct(inp["L2_attn_Wk"].T),
        "L2_WvT": _ct(inp["L2_attn_Wv"].T),
        "L2_WoT": _ct(inp["L2_attn_Wo"].T),
        "L2_bq": _ct(inp["L2_attn_bq"].reshape(8, 128).T),
        "L2_bk": _ct(inp["L2_attn_bk"].reshape(8, 128).T),
        "L2_bv": _ct(inp["L2_attn_bv"].reshape(1, 1024)),
        "L2_bo": _ct(inp["L2_attn_bo"].reshape(8, 128).T),
    }
    w[5] = {
        "L2f_WrT": _ct(inp["L2_ffn_W_r"].T),
        "L2f_WkT": _ct(inp["L2_ffn_W_k"].T),
        "L2f_WvT": _ct(inp["L2_ffn_W_v"].T),
        "L2_mu": _ct(np.concatenate([
            inp["L2_ffn_mu_r"].reshape(8, 128).T,
            inp["L2_ffn_mu_k"].reshape(8, 128).T], axis=1)),
    }
    w[6] = {
        "L3_wq": _ct(inp["L3_attn_wq"]),
        "L3_wd": _ct(inp["L3_attn_w_down"]),
        "L3_wkup": _ct(inp["L3_attn_wk_up"]),
        "L3_wvup": _ct(inp["L3_attn_wv_up"]),
        "L3_wo": _ct(inp["L3_attn_wo"]),
        "ropeC": _rope_table(cos=True),
        "ropeS": _rope_table(cos=False),
    }
    w[7] = {
        "L3f_wgT": _ct(inp["L3_ffn_w_gate"].T),
        "L3f_wuT": _ct(inp["L3_ffn_w_up"].T),
        "L3f_wdT": _ct(inp["L3_ffn_w_down"].T),
    }
    w[1] = {
        "L0_w1T": _ct(inp["L0_ffn_w1"].T),
        "L0_w2T": _ct(inp["L0_ffn_w2"].T),
        "L0_b1": _ct(inp["L0_ffn_b1"].reshape(32, 128).T),
        "L0_b2": _ct(inp["L0_ffn_b2"].reshape(8, 128).T),
    }
    idn = np.eye(128, dtype=_bf)
    mtri = np.zeros((128, 512), dtype=np.float32)
    for r in range(128):
        mtri[r, :r] = NEGBIG
    bo = np.zeros((128, 16), dtype=np.float32)
    for c in range(16):
        bo[(c % 2) * 64:(c % 2) * 64 + 64, c] = 1.0
    w["const"] = {"cIDN": idn, "cMTRI": mtri.astype(_bf), "cBO16": bo.astype(_bf)}
    return w


def _rope_table(cos):
    pos = np.arange(S, dtype=np.float64)
    j = np.arange(32, dtype=np.float64)
    inv = 1.0 / (10000.0 ** (2.0 * j / 64.0))
    ang = np.outer(pos, inv)
    tab = np.zeros((128, S), dtype=np.float64)
    for r in range(128):
        jj = (r % 64) // 2
        if cos:
            tab[r] = np.cos(ang[:, jj])
        else:
            tab[r] = np.sin(ang[:, jj]) * (-1.0 if r % 2 == 0 else 1.0)
    return tab.astype(_bf)


def _threefry2x32(key, x0, x1):
    def rotl(x, d):
        return ((x << np.uint32(d)) | (x >> np.uint32(32 - d))).astype(np.uint32)
    ks0, ks1 = np.uint32(key[0]), np.uint32(key[1])
    ks2 = np.uint32(0x1BD11BDA) ^ ks0 ^ ks1
    rot = [(13, 15, 26, 6), (17, 29, 16, 24)]
    x0 = (x0 + ks0).astype(np.uint32)
    x1 = (x1 + ks1).astype(np.uint32)
    ks = [(ks1, ks2), (ks2, ks0), (ks0, ks1), (ks1, ks2), (ks2, ks0)]
    for r in range(5):
        for d in rot[r % 2]:
            x0 = (x0 + x1).astype(np.uint32)
            x1 = rotl(x1, d) ^ x0
        x0 = (x0 + ks[r][0]).astype(np.uint32)
        x1 = (x1 + ks[r][1] + np.uint32(r + 1)).astype(np.uint32)
    return x0, x1


def make_dropout_mask():
    n = NCORE * S * D
    i = np.arange(n, dtype=np.uint64)
    hi = (i >> np.uint64(32)).astype(np.uint32)
    lo = (i & np.uint64(0xFFFFFFFF)).astype(np.uint32)
    b0, b1 = _threefry2x32((np.uint32(0), np.uint32(42)), hi, lo)
    bits = b0 ^ b1
    fl = ((bits >> np.uint32(9)) | np.uint32(0x3F800000)).view(np.float32) - 1.0
    keep = (fl < np.float32(0.8)).reshape(NCORE, S, D)
    return np.where(keep, np.float16(1.25), np.float16(0.0))



_STAGE_GROUPS = [[0], [1], [2], [3], [4], [5], [6], [7]]


def _run(inputs, groups):
    host_w = make_host_weights(inputs)
    dmask = make_dropout_mask()
    x = np.asarray(inputs["L0_attn_x"])
    xT = [np.ascontiguousarray(x[b].T).astype(np.float32) for b in range(NCORE)]
    state = None
    for gi, stages in enumerate(groups):
        first = (gi == 0)
        last = (gi == len(groups) - 1)
        nc = build_program(stages, host_w, first, last)
        in_maps = []
        for b in range(NCORE):
            m = {}
            for st in stages + ["const"]:
                for name, arr in host_w[st].items():
                    m["i_" + name] = arr
            if 1 in stages:
                m["i_L0_dmask"] = np.ascontiguousarray(dmask[b].T)
            m["i_x_in"] = xT[b] if first else state[b]
            in_maps.append(m)
        res = run_bass_kernel_spmd(nc, in_maps, core_ids=list(range(NCORE)))
        if last:
            return np.stack([res.results[b]["o_y"].T for b in range(NCORE)])
        state = [res.results[b]["o_y"] for b in range(NCORE)]
    raise RuntimeError("no last group")


def kernel(**inputs):
    return _run(inputs, _STAGE_GROUPS)



GELU_K2 = math.sqrt(2.0 / math.pi) * 2.0
GELU_K1 = GELU_K2 * 0.044715


def emit_stage_l1_attn(P, M, W):
    xt = M.tiles("X", NT)
    xk = keys("X", NT)
    bt = M.tiles("B", NT)
    bk = keys("B", NT)
    ut = M.tiles("U", NT)
    at, ak = ut, keys("U", NT)
    a0, a1 = M.sb("A0", [128, S], BF16), M.sb("A1", [128, S], BF16)
    ctiles = c_area(M)
    p4, p5 = M.ps("P4", [128, 512]), M.ps("P5", [128, 512])
    bqc = M.sb("BQC", [128, 8], F32)
    bkc = M.sb("BKC", [128, 8], F32)
    bvb = M.sb("BVB", [128, 1024], F32)
    bpc = M.sb("BPC", [128, 8], F32)
    bfcc = M.sb("BFCC", [128, 32], F32)
    bmc = M.sb("BMC", [128, 8], F32)
    for nm, t_, k_ in [("L1_bq", bqc, "BQC"), ("L1_bk", bkc, "BKC"),
                       ("L1_bp", bpc, "BPC"), ("L1_bfc", bfcc, "BFCC"),
                       ("L1_bm", bmc, "BMC")]:
        P.emit("g", lambda eng, t_=t_, nm=nm: eng.dma_start(
            out=t_[:], in_=W[nm][0]), reads=[W[nm][1]], writes=[k_], tag="bia")
    P.emit("g", lambda eng: eng.dma_start(
        out=bvb[:], in_=W["L1_bv"][0][0:1, :].partition_broadcast(128)),
        reads=[W["L1_bv"][1]], writes=["BVB"], tag="bvb")

    emit_ln(P, M, xt, xk, eps=1e-10, out_tiles=bt, out_keys=bk)

    wvb = M.sb("WPA", [128, 4096], BF16)
    wvb2 = M.sb("WPB", [128, 4096], BF16)
    emit_load_wchunk(P, wvb, "WPA", W["L1_wattnT"][0], [W["L1_wattnT"][1]],
                     NT, 2048, 512)
    emit_load_wchunk(P, wvb2, "WPB", W["L1_wattnT"][0], [W["L1_wattnT"][1]],
                     NT, 2560, 512)
    for kt in range(STL):
        P.emit("v", lambda eng, kt=kt: eng.memset(
            va_ap(ctiles, kt).rearrange("p (h c) -> p h c", c=65)[:, :, 64:65],
            1.0), reads=[], writes=["C%d" % (kt // 2)], tag="vones")
    for st in range(STL):
        for ec in range(2):
            pn = ["P4", "P5"][(st * 2 + ec) % 2]
            psum = [p4, p5][(st * 2 + ec) % 2]
            wsrc, wkey = (wvb, "WPA") if ec == 0 else (wvb2, "WPB")
            for kc in range(NT):
                def fn(eng, st=st, kc=kc, psum=psum, wsrc=wsrc):
                    return eng.matmul(
                        psum[:], bt[kc][:, st * 128:(st + 1) * 128],
                        wsrc[:, kc * 512:(kc + 1) * 512],
                        start=(kc == 0), stop=(kc == NT - 1))
                P.emit("t", fn, reads=[bk[kc], wkey], writes=[pn], tag="l1v")
            def fne(eng, st=st, ec=ec, psum=psum, pn=pn):
                dst = va_ap(ctiles, st).rearrange("p (h c) -> p h c", c=65)[
                    :, 8 * ec:8 * ec + 8, 0:64]
                src = psum[:].rearrange("p (h c) -> p h c", c=64)
                bsl = bvb[:, ec * 512:(ec + 1) * 512].rearrange(
                    "p (h c) -> p h c", c=64)
                return eng.tensor_add(dst, src, bsl)
            P.emit("v", fne, reads=[pn, "BVB"], writes=["C%d" % (st // 2)],
                   tag="l1v_ev")

    qwb = M.sb("WPA", [128, 4096], BF16)[:, 0:1024]
    kwb = M.sb("WPA", [128, 4096], BF16)[:, 1024:2048]

    def proj_qk(pair):
        emit_load_wchunk(P, qwb, "WPA", W["L1_wattnT"][0],
                         [W["L1_wattnT"][1]], NT, pair * 128, 128)
        emit_load_wchunk(P, kwb, "WPA", W["L1_wattnT"][0],
                         [W["L1_wattnT"][1]], NT, 1024 + pair * 128, 128)
        for (wb, wkey, dst, dkey, scl, bcol, bckey) in [
                (qwb, "WPA", a0, "A0", 0.125, bqc, "BQC"),
                (kwb, "WPA", a1, "A1", 1.0, bkc, "BKC")]:
            for sc in range(4):
                pn = "P4" if dkey == "A0" else "P5"
                psum = p4 if dkey == "A0" else p5
                for kc in range(NT):
                    def fn(eng, kc=kc, sc=sc, psum=psum, wb=wb):
                        return eng.matmul(
                            psum[:], wb[:, kc * 128:(kc + 1) * 128],
                            bt[kc][:, sc * 512:(sc + 1) * 512],
                            start=(kc == 0), stop=(kc == NT - 1))
                    P.emit("t", fn, reads=[wkey, bk[kc]], writes=[pn],
                           tag="l1qk")
                def fev(eng, sc=sc, psum=psum, dst=dst, scl=scl, bcol=bcol,
                        pair=pair):
                    return eng.tensor_scalar(
                        dst[:, sc * 512:(sc + 1) * 512], psum[:],
                        bcol[:, pair:pair + 1], scl, ALU.add, ALU.mult)
                P.emit("v", fev, reads=[pn, bckey], writes=[dkey],
                       tag="l1qk_ev")

    emit_attention(P, M, "l1", proj_qk, lambda kt: kt * 128, True)

    def evac(mt, sc, psum, pn):
        def fn(eng, mt=mt, sc=sc, psum=psum):
            return eng.scalar_tensor_tensor(
                bt[mt][:, sc * 512:(sc + 1) * 512], psum[:],
                bpc[:, mt:mt + 1], xt[mt][:, sc * 512:(sc + 1) * 512],
                ALU.add, ALU.add)
        P.emit("v", fn, reads=[pn, "BPC", xk[mt]], writes=[bk[mt]],
               tag="l1pr_ev")
    emit_proj_fmajor(P, M, "l1pr", W["L1_wprojT"][0], D, D,
                     ut, keys("U", NT), evac, wt_keys=[W["L1_wprojT"][1]])

    emit_ln(P, M, bt, bk, eps=1e-10, out_tiles=at, out_keys=ak)

    gsc = M.sb("DMHB", [128, 1024], FP16).bitcast(BF16)
    g0 = gsc[:, 0:512]
    g1 = gsc[:, 512:1024]
    wfb = [M.sb("WPA", [128, 4096], BF16), M.sb("WPB", [128, 4096], BF16)]
    wfk = ["WPA", "WPB"]
    hml = [(ctiles[i // 4], "C%d" % (i // 4), (i % 4) * 512) for i in range(32)]
    for sc in range(4):
        for mc in range(8):
            par = mc % 2
            emit_load_wchunk(P, wfb[par], wfk[par], W["L1_wfcT"][0],
                             [W["L1_wfcT"][1]], NT, mc * 512, 512)
            for mt in range(4):
                mg = mc * 4 + mt
                pn = ["P4", "P5"][mg % 2]
                psum = [p4, p5][mg % 2]
                for kc in range(NT):
                    def fn(eng, kc=kc, mt=mt, sc=sc, par=par, psum=psum):
                        return eng.matmul(
                            psum[:], wfb[par][:, kc * 512 + mt * 128:
                                              kc * 512 + (mt + 1) * 128],
                            at[kc][:, sc * 512:(sc + 1) * 512],
                            start=(kc == 0), stop=(kc == NT - 1))
                    P.emit("t", fn, reads=[wfk[par], ak[kc]], writes=[pn],
                           tag="l1fc")
                P.emit("v", lambda eng, mg=mg, psum=psum: eng.tensor_scalar(
                    g0, psum[:], bfcc[:, mg:mg + 1], None, ALU.add),
                    reads=[pn, "BFCC"], writes=["DMHB"], tag="gel0")
                P.emit("s", lambda eng: eng.activation(g1, g0, AF.Square),
                       reads=["DMHB"], writes=["DMHB"], tag="gel1")
                P.emit("v", lambda eng: eng.tensor_scalar(
                    g1, g1, GELU_K1, GELU_K2, ALU.mult, ALU.add),
                    reads=["DMHB"], writes=["DMHB"], tag="gel2")
                P.emit("v", lambda eng: eng.tensor_mul(g1, g0, g1),
                       reads=["DMHB"], writes=["DMHB"], tag="gel3")
                P.emit("s", lambda eng: eng.activation(g1, g1, AF.Sigmoid),
                       reads=["DMHB"], writes=["DMHB"], tag="gel4")
                hm, hk, hoff = hml[mg]
                P.emit("v", lambda eng, hm=hm, hoff=hoff: eng.tensor_mul(
                    hm[:, hoff:hoff + 512], g0, g1),
                    reads=["DMHB"], writes=[hk], tag="gel5")
        w2b = wfb
        w2k = wfk
        emit_load_wchunk(P, w2b[0], w2k[0], W["L1_wmlpT"][0],
                         [W["L1_wmlpT"][1]], 32, 0, 128)
        for dt in range(NT):
            par = dt % 2
            if dt + 1 < NT:
                emit_load_wchunk(P, w2b[1 - par], w2k[1 - par],
                                 W["L1_wmlpT"][0], [W["L1_wmlpT"][1]], 32,
                                 (dt + 1) * 128, 128)
            pn = ["P4", "P5"][dt % 2]
            psum = [p4, p5][dt % 2]
            for kc in range(32):
                hm, hk, hoff = hml[kc]
                def fn(eng, kc=kc, dt=dt, par=par, psum=psum, hm=hm,
                       hoff=hoff):
                    return eng.matmul(
                        psum[:], w2b[par][:, kc * 128:(kc + 1) * 128],
                        hm[:, hoff:hoff + 512],
                        start=(kc == 0), stop=(kc == 31))
                P.emit("t", fn, reads=[w2k[par], hk], writes=[pn],
                       tag="l1mlp")
            def fe(eng, dt=dt, sc=sc, psum=psum):
                return eng.scalar_tensor_tensor(
                    bt[dt][:, sc * 512:(sc + 1) * 512], psum[:],
                    bmc[:, dt:dt + 1], bt[dt][:, sc * 512:(sc + 1) * 512],
                    ALU.add, ALU.add)
            P.emit("v", fe, reads=[pn, "BMC", bk[dt]], writes=[bk[dt]],
                   tag="l1mlp_ev")
    for i in range(NT):
        P.emit("v", lambda eng, i=i: eng.tensor_add(xt[i][:], xt[i][:],
                                                    bt[i][:]),
               reads=[xk[i], bk[i]], writes=[xk[i]], tag="l1out")


STAGE_EMITTERS[2] = emit_stage_l1_attn



def emit_weight_quant(P, M, nm, wt_ap, wt_key, Kdim, Mout, stash_ap, stash_key):
    kt = Kdim // 128
    wf = [M.sb("T32A", [128, S], F32), M.sb("T32B", [128, S], F32)]
    wfk = ["T32A", "T32B"]
    sm = M.sb("SM1", [128, 64], F32)
    onesf = M.sb("ONESF", [128, 1], F32)
    bca = M.sb("BCA", [128, S], BF16)
    p6 = M.ps("P6", [128, 512])
    P.emit("v", lambda eng: eng.memset(onesf[:], 1.0), (), ["ONESF"])
    P.emit("v", lambda eng: eng.memset(sm[:, 0:2], 0.0), (), ["SM1"])
    cw = 2048 // kt
    nch = Mout // cw
    for ci in range(nch):
        par = ci % 2
        emit_load_wchunk(P, wf[par], wfk[par], wt_ap, [wt_key], kt, ci * cw, cw)
        for kc in range(kt):
            P.emit("v", lambda eng, kc=kc, par=par: eng.reduce_sum(
                sm[:, 1:2], wf[par][:, kc * cw:(kc + 1) * cw], axis=AX,
                apply_absolute_value=True),
                reads=[wfk[par]], writes=["SM1"], tag=nm + "wqr")
            P.emit("v", lambda eng: eng.tensor_add(sm[:, 0:1], sm[:, 0:1],
                                                   sm[:, 1:2]),
                   reads=["SM1"], writes=["SM1"], tag=nm + "wqa")
    P.emit("t", lambda eng: eng.matmul(p6[0:1, 0:1], onesf[:, 0:1],
                                       sm[:, 0:1], start=True, stop=True),
           reads=["ONESF", "SM1"], writes=["P6"], tag=nm + "wqs")
    P.emit("v", lambda eng: eng.tensor_scalar(
        sm[0:1, 2:3], p6[0:1, 0:1], 1.0 / (Kdim * Mout), 1e-5,
        ALU.mult, ALU.max), reads=["P6"], writes=["SM1"], tag=nm + "wqm")
    P.emit("v", lambda eng: eng.reciprocal(sm[0:1, 3:4], sm[0:1, 2:3]),
           reads=["SM1"], writes=["SM1"], tag=nm + "wqi")
    emit_bcast_row(P, M, nm + "sw", sm[0:1, 3:4], "SM1", 128, sm[:, 4:5],
                   "SM1", 1)
    for ci in range(nch):
        par = ci % 2
        emit_load_wchunk(P, wf[par], wfk[par], wt_ap, [wt_key], kt, ci * cw, cw)
        P.emit("v", lambda eng, par=par: eng.tensor_scalar(
            wf[par][:, 0:2048], wf[par][:, 0:2048], sm[:, 4:5], None, ALU.mult),
            reads=[wfk[par], "SM1"], writes=[wfk[par]], tag=nm + "wq1")
        P.emit("v", lambda eng, par=par: eng.tensor_scalar(
            wf[par][:, 0:2048], wf[par][:, 0:2048], MAGIC, MAGIC,
            ALU.add, ALU.subtract),
            reads=[wfk[par]], writes=[wfk[par]], tag=nm + "wq2")
        P.emit("v", lambda eng, par=par: eng.tensor_scalar(
            bca[:, 0:2048], wf[par][:, 0:2048], -1.0, 1.0, ALU.max, ALU.min),
            reads=[wfk[par]], writes=["BCA"], tag=nm + "wq3")
        def fo(eng, ci=ci):
            dst = stash_ap[:, ci * cw:(ci + 1) * cw].rearrange(
                "(kc p) c -> p kc c", p=128)
            return eng.dma_start(out=dst, in_=bca[:, 0:2048].rearrange(
                "p (kc c) -> p kc c", kc=kt))
        P.emit("y", fo, reads=["BCA"], writes=[stash_key], tag=nm + "wqo")
    return sm


def emit_act_quant(P, M, nm, width, col0, mx_ready, mw_col_ap, int_round_cb,
                   sclx_name="T32A"):
    mxt = M.sb("U3", [128, S], BF16)
    idn = M.sb("IDN", [128, 128], BF16)
    pt = M.ps("PT", [128, 1024], BF16)
    sm2 = M.sb("SM2", [128, 64], F32)
    spb = M.sb("SPB", [128, 64], BF16)
    sclx = M.sb(sclx_name, [128, S], F32)
    bca = M.sb("BCA", [128, S], BF16)
    nb = width // 128
    for b in range(nb):
        P.emit("t", lambda eng, b=b: eng.transpose(
            pt[:, 0:128], mxt[:, col0 + b * 128:col0 + (b + 1) * 128], idn[:]),
            reads=["U3", "IDN"], writes=["PT"], tag=nm + "aqt")
        P.emit("v", lambda eng, b=b: eng.reduce_max(
            sm2[:, b:b + 1], pt[:, 0:128], axis=AX),
            reads=["PT"], writes=["SM2"], tag=nm + "aqm")
    P.emit("v", lambda eng: eng.tensor_scalar(
        sm2[:, 16:16 + nb], sm2[:, 0:nb], 1e-5, None, ALU.max),
        reads=["SM2"], writes=["SM2"], tag=nm + "aqc")
    P.emit("v", lambda eng: eng.reciprocal(sm2[:, 32:32 + nb],
                                           sm2[:, 16:16 + nb]),
           reads=["SM2"], writes=["SM2"], tag=nm + "aqr")
    P.emit("v", lambda eng: eng.tensor_scalar(
        spb[:, 0:nb], sm2[:, 32:32 + nb], 127.0, None, ALU.mult),
        reads=["SM2"], writes=["SPB"], tag=nm + "aqs")
    P.emit("v", lambda eng: eng.reciprocal(sm2[:, 32:32 + nb], spb[:, 0:nb]),
           reads=["SPB"], writes=["SM2"], tag=nm + "aqi")
    if mw_col_ap is not None:
        emit_bcast_row(P, M, nm + "mw", mw_col_ap, "SM1", 128, sm2[:, 48:49],
                       "SM2", 1)
        P.emit("v", lambda eng: eng.tensor_scalar(
            sm2[:, 32:32 + nb], sm2[:, 32:32 + nb], sm2[:, 48:49], None,
            ALU.mult), reads=["SM2"], writes=["SM2"], tag=nm + "aqmw")
    P.emit("v", lambda eng: eng.tensor_copy(spb[:, 16:16 + nb],
                                            sm2[:, 32:32 + nb]),
           reads=["SM2"], writes=["SPB"], tag=nm + "aqh")
    P.emit("v", lambda eng: eng.tensor_sub(
        sm2[:, 48:48 + nb], sm2[:, 32:32 + nb], spb[:, 16:16 + nb]),
        reads=["SM2", "SPB"], writes=["SM2"], tag=nm + "aql0")
    P.emit("v", lambda eng: eng.tensor_copy(spb[:, 32:32 + nb],
                                            sm2[:, 48:48 + nb]),
           reads=["SM2"], writes=["SPB"], tag=nm + "aql")
    P.emit("t", lambda eng: eng.transpose(pt[0:48, 0:128], spb[:, 0:48],
                                          idn[:]),
           reads=["SPB", "IDN"], writes=["PT"], tag=nm + "aqt2")
    P.emit("v", lambda eng: eng.tensor_copy(bca[0:48, 0:128],
                                            pt[0:48, 0:128]),
           reads=["PT"], writes=["BCA"], tag=nm + "aqe")
    drq = M.dr("dq_" + nm, [3, S], BF16)
    def fo(eng):
        dst = drq.rearrange("t c -> (t c)")[0:48 * 128]
        return eng.dma_start(out=dst, in_=bca[0:48, 0:128])
    P.emit("y", fo, reads=["BCA"], writes=["dq_" + nm], tag=nm + "aqo")
    P.emit("g", lambda eng: eng.dma_start(
        out=mxt[:, col0:col0 + width],
        in_=drq[0:1, 0:width].partition_broadcast(128)),
        reads=["dq_" + nm], writes=["U3"], tag=nm + "aqb1")
    P.emit("g", lambda eng: eng.dma_start(
        out=sclx[:, col0:col0 + width],
        in_=drq[1:2, 0:width].partition_broadcast(128)),
        reads=["dq_" + nm], writes=[sclx_name], tag=nm + "aqb2")
    P.emit("g", lambda eng: eng.dma_start(
        out=bca[:, col0:col0 + width],
        in_=drq[2:3, 0:width].partition_broadcast(128)),
        reads=["dq_" + nm], writes=["BCA"], tag=nm + "aqb3")
    P.emit("v", lambda eng: eng.tensor_add(
        sclx[:, col0:col0 + width], sclx[:, col0:col0 + width],
        bca[:, col0:col0 + width]),
        reads=[sclx_name, "BCA"], writes=[sclx_name], tag=nm + "aqlo")



def emit_stage_l1_ffn(P, M, W):
    xt = M.tiles("X", NT)
    xk = keys("X", NT)
    ct = c_area(M)
    ai = [ct[i][:, 0:2048] for i in range(NT)]
    aik = keys("C", NT)
    ut = M.tiles("U", NT)
    mxt = M.sb("U3", [128, S], BF16)
    sclx = M.sb("T32A", [128, S], F32)
    hh = M.sb("U4", [128, S], BF16)
    cws = M.sb("CWS", [128, 96], F32)
    cbc = M.sb("CBC", [128, 32], F32)
    p4, p5 = M.ps("P4", [128, 512]), M.ps("P5", [128, 512])
    w1q = M.dr("L1W1Q", [D, 4096], BF16)
    w2q = M.dr("L1W2Q", [4096, D], BF16)
    sild = M.dr("L1SILD", [4096, S], BF16)
    wpa = M.sb("WPA", [128, 4096], BF16)
    wpb = M.sb("WPB", [128, 4096], BF16)

    sm = emit_weight_quant(P, M, "l1w1", W["L1f_w1T"][0], W["L1f_w1T"][1],
                           D, 4096, w1q, "L1W1Q")
    mw1 = sm[0:1, 2:3]
    P.emit("g", lambda eng: eng.dma_start(out=cbc[:], in_=W["L1f_cb"][0]),
           reads=[W["L1f_cb"][1]], writes=["CBC"], tag="cb")
    P.emit("g", lambda eng: eng.dma_start(out=cws[:], in_=W["L1f_cw"][0]),
           reads=[W["L1f_cw"][1]], writes=["CWS"], tag="cw")
    emit_bcast_row(P, M, "l1mw1", mw1, "SM1", 128, sclx[:, 0:1], "T32A", 1)
    P.emit("v", lambda eng: eng.tensor_scalar(
        cws[:], cws[:], sclx[:, 0:1], None, ALU.mult),
        reads=["CWS", "T32A"], writes=["CWS"], tag="cwm")
    sm2q = emit_weight_quant(P, M, "l1w2", W["L1f_w2T"][0], W["L1f_w2T"][1],
                             4096, D, w2q, "L1W2Q")
    mw2 = sm2q[0:1, 2:3]

    absx = M.sb("U5", [128, S], BF16)
    for i in range(NT):
        P.emit("s", lambda eng, i=i: eng.activation(
            absx[:], xt[i][:], AF.Abs),
            reads=[xk[i]], writes=["U5"], tag="aqxa")
        if i == 0:
            P.emit("v", lambda eng: eng.tensor_copy(mxt[:], absx[:]),
                   reads=["U5"], writes=["U3"], tag="aqx0")
        else:
            P.emit("v", lambda eng: eng.tensor_max(mxt[:], mxt[:], absx[:]),
                   reads=["U3", "U5"], writes=["U3"], tag="aqx")
    emit_act_quant(P, M, "l1x", S, 0, True, None, None)
    for i in range(NT):
        P.emit("v", lambda eng, i=i: eng.tensor_mul(ai[i][:], xt[i][:], mxt[:]),
               reads=[xk[i], "U3"], writes=[aik[i]], tag="aqxm")
        P.emit("v", lambda eng, i=i: eng.tensor_scalar(
            ai[i][:], ai[i][:], MAGIC, MAGIC, ALU.add, ALU.subtract),
            reads=[aik[i]], writes=[aik[i]], tag="aqxr")

    wb = [wpa, wpb]
    wbk = ["WPA", "WPB"]
    def loadw1(mt, par):
        def fn(eng, mt=mt):
            src = w1q[:, mt * 128:(mt + 1) * 128].rearrange(
                "(kc p) c -> p kc c", p=128)
            return eng.dma_start(out=wb[par][:, 0:1024], in_=src)
        P.emit("g", fn, reads=["L1W1Q"], writes=[wbk[par]], tag="w1l")
    loadw1(0, 0)
    for mt in range(32):
        par = mt % 2
        if mt + 1 < 32:
            loadw1(mt + 1, 1 - par)
        for sc in range(4):
            pn = ["P4", "P5"][sc % 2]
            psum = [p4, p5][sc % 2]
            for kc in range(NT):
                def fn(eng, kc=kc, sc=sc, par=par, psum=psum):
                    return eng.matmul(
                        psum[:], wb[par][:, kc * 128:(kc + 1) * 128],
                        ai[kc][:, sc * 512:(sc + 1) * 512],
                        start=(kc == 0), stop=(kc == NT - 1))
                P.emit("t", fn, reads=[wbk[par], aik[kc]], writes=[pn],
                       tag="l1fmm1")
            P.emit("v", lambda eng, sc=sc, psum=psum: eng.tensor_mul(
                hh[:, sc * 512:(sc + 1) * 512], psum[:],
                sclx[:, sc * 512:(sc + 1) * 512]),
                reads=[pn, "T32A"], writes=["U4"], tag="l1fev1")
        P.emit("v", lambda eng, mt=mt: eng.tensor_scalar(
            ut[0][:], hh[:, 0:2048], cws[:, 3 * mt + 1:3 * mt + 2], None,
            ALU.mult), reads=["U4", "CWS"], writes=["U0"], tag="cv1")
        P.emit("v", lambda eng, mt=mt: eng.scalar_tensor_tensor(
            ut[0][:, 1:2048], hh[:, 0:2047], cws[:, 3 * mt:3 * mt + 1],
            ut[0][:, 1:2048],
            ALU.mult, ALU.add), reads=["U4", "CWS", "U0"], writes=["U0"],
            tag="cv0")
        P.emit("v", lambda eng, mt=mt: eng.scalar_tensor_tensor(
            ut[0][:, 0:2047], hh[:, 1:2048], cws[:, 3 * mt + 2:3 * mt + 3],
            ut[0][:, 0:2047],
            ALU.mult, ALU.add), reads=["U4", "CWS", "U0"], writes=["U0"],
            tag="cv2")
        P.emit("s", lambda eng, mt=mt: eng.activation(
            ut[1][:], ut[0][:], AF.Sigmoid, bias=cbc[:, mt:mt + 1]),
            reads=["U0", "CBC"], writes=["U1"], tag="sil1")
        P.emit("v", lambda eng, mt=mt: eng.scalar_tensor_tensor(
            ut[2][:], ut[0][:], cbc[:, mt:mt + 1], ut[1][:],
            ALU.add, ALU.mult), reads=["U0", "CBC", "U1"], writes=["U2"],
            tag="sil2")
        P.emit("s", lambda eng: eng.activation(absx[:], ut[2][:], AF.Abs),
               reads=["U2"], writes=["U5"], tag="aqha")
        if mt == 0:
            P.emit("v", lambda eng: eng.tensor_copy(mxt[:], absx[:]),
                   reads=["U5"], writes=["U3"], tag="aqh0")
        else:
            P.emit("v", lambda eng: eng.tensor_max(mxt[:], mxt[:], absx[:]),
                   reads=["U3", "U5"], writes=["U3"], tag="aqh")
        P.emit("y", lambda eng, mt=mt: eng.dma_start(
            out=sild[mt * 128:(mt + 1) * 128, :], in_=ut[2][:]),
            reads=["U2"], writes=["L1SILD"], tag="silo")

    emit_act_quant(P, M, "l1h", S, 0, True, mw2, None)

    w2b = [wpa, wpb]
    w2bk = ["WPA", "WPB"]
    for sc in range(4):
        for i in range(NT):
            def fl(eng, i=i, sc=sc):
                src = sild[:, sc * 512:(sc + 1) * 512].rearrange(
                    "(j q p) c -> p j q c", p=128, q=NT)[:, :, i, :]
                return eng.dma_start(
                    out=ai[i].rearrange("p (j c) -> p j c", j=4), in_=src)
            P.emit("g", fl, reads=["L1SILD"], writes=[aik[i]], tag="sili")
            for j in range(4):
                P.emit("v", lambda eng, i=i, j=j, sc=sc: eng.tensor_mul(
                    ai[i][:, j * 512:(j + 1) * 512],
                    ai[i][:, j * 512:(j + 1) * 512],
                    mxt[:, sc * 512:(sc + 1) * 512]),
                    reads=[aik[i], "U3"], writes=[aik[i]], tag="hqm")
            P.emit("v", lambda eng, i=i: eng.tensor_scalar(
                ai[i][:], ai[i][:], MAGIC, MAGIC, ALU.add, ALU.subtract),
                reads=[aik[i]], writes=[aik[i]], tag="hqr")
        def loadw2(dt, par):
            def fn(eng, dt=dt):
                src = w2q[:, dt * 128:(dt + 1) * 128].rearrange(
                    "(kc p) c -> p kc c", p=128)
                return eng.dma_start(out=w2b[par][:, 0:4096], in_=src)
            P.emit("g", fn, reads=["L1W2Q"], writes=[w2bk[par]], tag="w2l")
        loadw2(0, 0)
        for dt in range(NT):
            par = dt % 2
            if dt + 1 < NT:
                loadw2(dt + 1, 1 - par)
            pn = ["P4", "P5"][dt % 2]
            psum = [p4, p5][dt % 2]
            for kc in range(32):
                def fn(eng, kc=kc, par=par, psum=psum):
                    return eng.matmul(
                        psum[:], w2b[par][:, kc * 128:(kc + 1) * 128],
                        ai[kc % NT][:, (kc // NT) * 512:(kc // NT) * 512 + 512],
                        start=(kc == 0), stop=(kc == 31))
                P.emit("t", fn, reads=[w2bk[par], aik[kc % NT]], writes=[pn],
                       tag="l1fmm2")
            P.emit("v", lambda eng, dt=dt, sc=sc, psum=psum: eng.tensor_mul(
                ut[0][:, 0:512], psum[:], sclx[:, sc * 512:(sc + 1) * 512]),
                reads=[pn, "T32A"], writes=["U0"], tag="l1fev2a")
            P.emit("v", lambda eng, dt=dt, sc=sc: eng.tensor_add(
                xt[dt][:, sc * 512:(sc + 1) * 512],
                xt[dt][:, sc * 512:(sc + 1) * 512], ut[0][:, 0:512]),
                reads=["U0", xk[dt]], writes=[xk[dt]], tag="l1fev2b")


STAGE_EMITTERS[3] = emit_stage_l1_ffn



L2EPS = 1e-6
L2TAP = None


def emit_stage_l2_attn(P, M, W):
    xt = M.tiles("X", NT)
    xk = keys("X", NT)
    ut = M.tiles("U", NT)
    bt = M.tiles("B", 3)
    a0, a1 = M.sb("A0", [128, S], BF16), M.sb("A1", [128, S], BF16)
    ct = c_area(M)
    sm2 = M.sb("SM2", [128, 64], F32)
    ksem = M.sb("KSEM", [128, 4], BF16)
    onesc = M.sb("ONESC", [128, 1], BF16)
    bca = M.sb("BCA", [128, S], BF16)
    bcb = M.sb("BCB", [128, S], BF16)
    r16a = M.sb("R16A", [16, S], F32)
    r16b = M.sb("R16B", [16, S], F32)
    bqc = M.sb("BQC", [128, 8], F32)
    bkc = M.sb("BKC", [128, 8], F32)
    boc = M.sb("BPC", [128, 8], F32)
    bvb = M.sb("BVB", [128, 1024], F32)
    kvs = M.sb("KVS", [128, 128], BF16)
    wpa = M.sb("WPA", [128, 4096], BF16)
    p4, p5 = M.ps("P4", [128, 512]), M.ps("P5", [128, 512])
    p6 = M.ps("P6", [128, 512])
    pb = M.ps("PB", [128, 2048])
    drl2 = M.dr("drl2", [2, S], F32)

    for nm, t_, k_ in [("L2_bq", bqc, "BQC"), ("L2_bk", bkc, "BKC"),
                       ("L2_bo", boc, "BPC")]:
        P.emit("g", lambda eng, t_=t_, nm=nm: eng.dma_start(
            out=t_[:], in_=W[nm][0]), reads=[W[nm][1]], writes=[k_], tag="bia")
    P.emit("g", lambda eng: eng.dma_start(
        out=bvb[:], in_=W["L2_bv"][0][0:1, :].partition_broadcast(128)),
        reads=[W["L2_bv"][1]], writes=["BVB"], tag="bvb")

    def rowexp(nm, src_rows, src_key, dst, dst_key):
        P.emit("y", lambda eng: eng.dma_start(out=drl2[:, :], in_=src_rows),
               reads=[src_key], writes=["drl2"], tag=nm + "xo")
        P.emit("g", lambda eng: eng.dma_start(
            out=dst[0:64, :], in_=drl2[0:1, :].partition_broadcast(64)),
            reads=["drl2"], writes=[dst_key], tag=nm + "xi0")
        P.emit("g", lambda eng: eng.dma_start(
            out=dst[64:128, :], in_=drl2[1:2, :].partition_broadcast(64)),
            reads=["drl2"], writes=[dst_key], tag=nm + "xi1")

    def rowmm(nm, msk_col_f32, rhs, rhs_key, eps_corr_col, out_rows, out_key):
        P.emit("v", lambda eng: eng.memset(ksem[:, 0:2], 0.0), (), ["KSEM"])
        P.emit("v", lambda eng: eng.tensor_copy(ksem[0:64, 0:1],
                                                msk_col_f32[0:64, :]),
               reads=["SM2"], writes=["KSEM"], tag=nm + "m0")
        P.emit("v", lambda eng: eng.tensor_copy(ksem[64:128, 1:2],
                                                msk_col_f32[64:128, :]),
               reads=["SM2"], writes=["KSEM"], tag=nm + "m1")
        for sc in range(4):
            P.emit("t", lambda eng, sc=sc: eng.matmul(
                p6[0:2, 0:512], ksem[:, 0:2],
                rhs[:, sc * 512:(sc + 1) * 512], start=True, stop=True),
                reads=["KSEM", rhs_key], writes=["P6"], tag=nm + "mm")
            P.emit("v", lambda eng, sc=sc: eng.tensor_copy(
                out_rows[0:2, sc * 512:(sc + 1) * 512], p6[0:2, 0:512]),
                reads=["P6"], writes=[out_key], tag=nm + "cp")
        P.emit("t", lambda eng: eng.matmul(
            p6[0:2, 0:1], ksem[:, 0:2], onesc[:, 0:1], start=True, stop=True),
            reads=["KSEM", "ONESC"], writes=["P6"], tag=nm + "cs")
        P.emit("v", lambda eng: eng.tensor_scalar(
            sm2[0:2, 20:21], p6[0:2, 0:1], L2EPS, None, ALU.mult),
            reads=["P6"], writes=["SM2"], tag=nm + "cse")
        P.emit("v", lambda eng: eng.tensor_scalar(
            out_rows[0:2, :], out_rows[0:2, :], sm2[0:2, 20:21], None,
            ALU.add), reads=[out_key, "SM2"], writes=[out_key], tag=nm + "ca")

    for pair in range(8):
        fc = pair * 128
        for wi, wnm in enumerate(["L2_WqT", "L2_WkT", "L2_WvT"]):
            def fl(eng, wi=wi, wnm=wnm, fc=fc):
                src = W[wnm][0][:, fc:fc + 128].rearrange(
                    "(kc p) c -> p kc c", p=128)
                return eng.dma_start(out=wpa[:, wi * 1024:(wi + 1) * 1024]
                                     .rearrange("p (kc c) -> p kc c", kc=8),
                                     in_=src)
            P.emit("g", fl, reads=[W[wnm][1]], writes=["WPA"], tag="l2wl")
        for wi, (dst, dkey, bcol, bckey) in enumerate(
                [(a0, "A0", bqc, "BQC"), (a1, "A1", bkc, "BKC")]):
            for sc in range(4):
                psum, pn = (p4, "P4") if wi == 0 else (p5, "P5")
                for kc in range(NT):
                    def fn(eng, wi=wi, kc=kc, sc=sc, psum=psum):
                        return eng.matmul(
                            psum[:],
                            wpa[:, wi * 1024 + kc * 128:wi * 1024 + (kc + 1) * 128],
                            xt[kc][:, sc * 512:(sc + 1) * 512],
                            start=(kc == 0), stop=(kc == NT - 1))
                    P.emit("t", fn, reads=["WPA", xk[kc]], writes=[pn],
                           tag="l2qk")
                P.emit("s", lambda eng, sc=sc, psum=psum, dst=dst, bcol=bcol,
                       pair=pair: eng.activation(
                           dst[:, sc * 512:(sc + 1) * 512], psum[:],
                           AF.Sigmoid, bias=bcol[:, pair:pair + 1]),
                       reads=[pn, bckey], writes=[dkey], tag="l2qk_ev")
        for st in range(STL):
            psum, pn = (p4, "P4") if st % 2 == 0 else (p5, "P5")
            for kc in range(NT):
                def fn(eng, st=st, kc=kc, psum=psum):
                    return eng.matmul(
                        psum[:, 0:128], xt[kc][:, st * 128:(st + 1) * 128],
                        wpa[:, 2048 + kc * 128:2048 + (kc + 1) * 128],
                        start=(kc == 0), stop=(kc == NT - 1))
                P.emit("t", fn, reads=[xk[kc], "WPA"], writes=[pn], tag="l2v")
            def fne(eng, st=st, psum=psum, fc=fc):
                return eng.tensor_add(
                    ct[2 + st // 8][:, (st % 8) * 128:(st % 8) * 128 + 128],
                    psum[:, 0:128], bvb[:, fc:fc + 128])
            P.emit("v", fne, reads=[pn, "BVB"],
                   writes=["C%d" % (2 + st // 8)], tag="l2v_ev")
        P.emit("v", lambda eng: eng.reduce_sum(sm2[:, 0:1], a1[:], axis=AX),
               reads=["A1"], writes=["SM2"], tag="l2ks")
        P.emit("v", lambda eng: eng.tensor_scalar(
            sm2[:, 1:2], sm2[:, 0:1], L2EPS, None, ALU.add),
            reads=["SM2"], writes=["SM2"], tag="l2kse")
        P.emit("v", lambda eng: eng.reduce_sum(sm2[:, 2:3], a0[:], axis=AX),
               reads=["A0"], writes=["SM2"], tag="l2qs")
        P.emit("v", lambda eng: eng.tensor_scalar(
            sm2[:, 3:4], sm2[:, 2:3], L2EPS, None, ALU.add),
            reads=["SM2"], writes=["SM2"], tag="l2qse")
        rowmm("l2si", sm2[:, 1:2], a0, "A0", None, r16a, "R16A")
        P.emit("v", lambda eng: eng.reciprocal(r16a[0:2, :], r16a[0:2, :]),
               reads=["R16A"], writes=["R16A"], tag="l2sir")
        rowmm("l2so", sm2[:, 3:4], a1, "A1", None, r16b, "R16B")
        P.emit("v", lambda eng: eng.reciprocal(r16b[0:2, :], r16b[0:2, :]),
               reads=["R16B"], writes=["R16B"], tag="l2sor")
        rowexp("l2so", r16b[0:2, :], "R16B", bca, "BCA")
        P.emit("v", lambda eng: eng.tensor_mul(bt[0][:], a1[:], bca[:]),
               reads=["A1", "BCA"], writes=["B0"], tag="l2kso")
        P.emit("v", lambda eng: eng.reduce_sum(sm2[:, 5:6], bt[0][:], axis=AX),
               reads=["B0"], writes=["SM2"], tag="l2ksos")
        P.emit("v", lambda eng: eng.tensor_scalar(
            sm2[:, 6:7], sm2[:, 5:6], L2EPS, None, ALU.add),
            reads=["SM2"], writes=["SM2"], tag="l2ksose")
        rowexp("l2si", r16a[0:2, :], "R16A", bcb, "BCB")
        P.emit("v", lambda eng: eng.tensor_mul(bt[1][:], a0[:], bcb[:]),
               reads=["A0", "BCB"], writes=["B1"], tag="l2qsi")
        P.emit("v", lambda eng: eng.reduce_sum(sm2[:, 7:8], bt[1][:], axis=AX),
               reads=["B1"], writes=["SM2"], tag="l2qsis")
        P.emit("v", lambda eng: eng.tensor_scalar(
            sm2[:, 8:9], sm2[:, 7:8], L2EPS, None, ALU.add),
            reads=["SM2"], writes=["SM2"], tag="l2qsise")
        rowmm("l2ck", sm2[:, 6:7], a0, "A0", None, r16b, "R16B")
        P.emit("s", lambda eng: eng.activation(r16b[0:2, :], r16b[0:2, :],
                                               AF.Sigmoid),
               reads=["R16B"], writes=["R16B"], tag="l2sa")
        P.emit("v", lambda eng: eng.tensor_mul(r16a[0:2, :], r16a[0:2, :],
                                               r16b[0:2, :]),
               reads=["R16A", "R16B"], writes=["R16A"], tag="l2sisa")
        rowmm("l2cs", sm2[:, 8:9], a1, "A1", None, r16b, "R16B")
        P.emit("v", lambda eng: eng.tensor_scalar(
            r16b[0:2, :], r16b[0:2, :], -1.0, 1.0, ALU.max, ALU.min),
            reads=["R16B"], writes=["R16B"], tag="l2clip")
        P.emit("s", lambda eng: eng.activation(r16b[0:2, :], r16b[0:2, :],
                                               AF.Exp),
               reads=["R16B"], writes=["R16B"], tag="l2exp")
        P.emit("v", lambda eng: eng.reduce_sum(sm2[0:2, 9:10], r16b[0:2, :],
                                               axis=AX),
               reads=["R16B"], writes=["SM2"], tag="l2ssum")
        P.emit("v", lambda eng: eng.reciprocal(sm2[0:2, 10:11],
                                               sm2[0:2, 9:10]),
               reads=["SM2"], writes=["SM2"], tag="l2srec")
        P.emit("v", lambda eng: eng.tensor_scalar(
            sm2[0:2, 10:11], sm2[0:2, 10:11], float(S), None, ALU.mult),
            reads=["SM2"], writes=["SM2"], tag="l2s2048")
        P.emit("v", lambda eng: eng.tensor_scalar(
            r16b[0:2, :], r16b[0:2, :], sm2[0:2, 10:11], None, ALU.mult),
            reads=["R16B", "SM2"], writes=["R16B"], tag="l2sc")
        if L2TAP is not None and pair == 0:
            P.emit("g", lambda eng: eng.dma_start(
                out=L2TAP["sisa"], in_=r16a[0:2, :]),
                reads=["R16A"], writes=["__t__"], tag="tap")
            P.emit("g", lambda eng: eng.dma_start(
                out=L2TAP["sc"], in_=r16b[0:2, :]),
                reads=["R16B"], writes=["__t__"], tag="tap")
            P.emit("g", lambda eng: eng.dma_start(
                out=L2TAP["q"], in_=a0[:]),
                reads=["A0"], writes=["__t__"], tag="tap")
            P.emit("g", lambda eng: eng.dma_start(
                out=L2TAP["k"], in_=a1[:]),
                reads=["A1"], writes=["__t__"], tag="tap")
            P.emit("g", lambda eng: eng.dma_start(
                out=L2TAP["v"], in_=ct[2][:, 0:1024]),
                reads=["C2"], writes=["__t__"], tag="tap")
        rowexp("l2sc", r16b[0:2, :], "R16B", bca, "BCA")
        P.emit("v", lambda eng: eng.tensor_mul(bt[0][:], a1[:], bca[:]),
               reads=["A1", "BCA"], writes=["B0"], tag="l2ksc")
        for st in range(STL):
            def ft(eng, st=st):
                return eng.dma_start(
                    out=ct[st // 8][:, (st % 8) * 128:(st % 8) * 128 + 128],
                    in_=bt[0][:, st * 128:(st + 1) * 128], transpose=True)
            P.emit("y", ft, reads=["B0"], writes=["C%d" % (st // 8)],
                   tag="l2tr")
        for hh in range(2):
            for st in range(STL):
                def fkv(eng, hh=hh, st=st):
                    lhsT = ct[st // 8][:, (st % 8) * 128 + hh * 64:
                                       (st % 8) * 128 + hh * 64 + 64]
                    rhs = ct[2 + st // 8][:, (st % 8) * 128 + hh * 64:
                                          (st % 8) * 128 + hh * 64 + 64]
                    if hh == 0:
                        return eng.matmul(p6[0:64, 0:64], lhsT, rhs,
                                          start=(st == 0), stop=(st == 15))
                    return eng.matmul(p6[64:128, 0:64], lhsT, rhs,
                                      start=(st == 0), stop=(st == 15),
                                      tile_position=(0, 64))
                P.emit("t", fkv, reads=["C%d" % (st // 8),
                                        "C%d" % (2 + st // 8)],
                       writes=["P6"], tag="l2kv")
        P.emit("v", lambda eng: eng.tensor_copy(kvs[0:64, 0:64],
                                                p6[0:64, 0:64]),
               reads=["P6"], writes=["KVS"], tag="l2kve0")
        P.emit("v", lambda eng: eng.tensor_copy(kvs[64:128, 64:128],
                                                p6[64:128, 0:64]),
               reads=["P6"], writes=["KVS"], tag="l2kve1")
        rowexp("l2sisa", r16a[0:2, :], "R16A", bcb, "BCB")
        P.emit("v", lambda eng: eng.tensor_mul(bt[1][:], a0[:], bcb[:]),
               reads=["A0", "BCB"], writes=["B1"], tag="l2qx")
        for hh in range(2):
            for sc in range(4):
                def fo(eng, hh=hh, sc=sc):
                    if hh == 0:
                        return eng.matmul(
                            pb[0:64, sc * 512:(sc + 1) * 512],
                            kvs[0:64, 0:64],
                            bt[1][0:64, sc * 512:(sc + 1) * 512],
                            start=True, stop=True)
                    return eng.matmul(
                        pb[64:128, sc * 512:(sc + 1) * 512],
                        kvs[64:128, 64:128],
                        bt[1][64:128, sc * 512:(sc + 1) * 512],
                        start=True, stop=True, tile_position=(64, 64))
                P.emit("t", fo, reads=["KVS", "B1"],
                       writes=["PB%d" % sc], tag="l2out")
        P.emit("v", lambda eng, pair=pair: eng.tensor_copy(
            ut[pair][:], pb[0:128, 0:2048]),
            reads=pbkeys(), writes=["U%d" % pair], tag="l2oev")
        if L2TAP is not None and pair == 0:
            P.emit("g", lambda eng: eng.dma_start(
                out=L2TAP["kv"], in_=kvs[:]),
                reads=["KVS"], writes=["__t__"], tag="tap")
            P.emit("g", lambda eng: eng.dma_start(
                out=L2TAP["out"], in_=ut[0][:]),
                reads=["U0"], writes=["__t__"], tag="tap")

    def evac(mt, sc, psum, pn):
        def fn(eng, mt=mt, sc=sc, psum=psum):
            return eng.scalar_tensor_tensor(
                xt[mt][:, sc * 512:(sc + 1) * 512], psum[:],
                boc[:, mt:mt + 1], xt[mt][:, sc * 512:(sc + 1) * 512],
                ALU.add, ALU.add)
        P.emit("v", fn, reads=[pn, "BPC", xk[mt]], writes=[xk[mt]],
               tag="l2wo_ev")
    emit_proj_fmajor(P, M, "l2wo", W["L2_WoT"][0], D, D,
                     ut, keys("U", NT), evac, wt_keys=[W["L2_WoT"][1]])


STAGE_EMITTERS[4] = emit_stage_l2_attn



def emit_stage_l2_ffn(P, M, W):
    xt = M.tiles("X", NT)
    xk = keys("X", NT)
    ut = M.tiles("U", NT)
    at = [M.sb("A%d" % i, [128, S], BF16) for i in range(3)]
    bt = M.tiles("B", NT)
    bk = keys("B", NT)
    ct = c_area(M)
    mur = M.sb("MUR", [128, 16], F32)
    p4, p5 = M.ps("P4", [128, 512]), M.ps("P5", [128, 512])

    P.emit("g", lambda eng: eng.dma_start(out=mur[:], in_=W["L2_mu"][0]),
           reads=[W["L2_mu"][1]], writes=["MUR"], tag="mu")
    sm2 = M.sb("SM2", [128, 64], F32)
    P.emit("v", lambda eng: eng.tensor_scalar(
        sm2[:, 0:16], mur[:], -1.0, 1.0, ALU.mult, ALU.add),
        reads=["MUR"], writes=["SM2"], tag="mu1")

    def mix(dst_tiles, dkeys, mucol0, nm):
        for i in range(NT):
            P.emit("v", lambda eng, i=i: eng.tensor_scalar(
                dst_tiles[i][:], xt[i][:], mur[:, mucol0 + i:mucol0 + i + 1],
                None, ALU.mult),
                reads=[xk[i], "MUR"], writes=[dkeys[i]], tag=nm + "a")
            P.emit("v", lambda eng, i=i: eng.scalar_tensor_tensor(
                dst_tiles[i][:, 1:2048], xt[i][:, 0:2047],
                sm2[:, mucol0 + i:mucol0 + i + 1],
                dst_tiles[i][:, 1:2048], ALU.mult, ALU.add),
                reads=[xk[i], "SM2", dkeys[i]], writes=[dkeys[i]],
                tag=nm + "b")

    mix(bt, bk, 0, "mxr")
    def evr(mt, sc, psum, pn):
        P.emit("s", lambda eng, mt=mt, sc=sc, psum=psum: eng.activation(
            ut[mt][:, sc * 512:(sc + 1) * 512], psum[:], AF.Sigmoid),
            reads=[pn], writes=["U%d" % mt], tag="l2r_ev")
    emit_proj_fmajor(P, M, "l2r", W["L2f_WrT"][0], D, D, bt, bk, evr,
                     wt_keys=[W["L2f_WrT"][1]])
    mix(bt, bk, 8, "mxk")
    def evk(mt, sc, psum, pn):
        P.emit("s", lambda eng, mt=mt, sc=sc, psum=psum: eng.activation(
            at[0][:, 0:512], psum[:], AF.Relu),
            reads=[pn], writes=["A0"], tag="l2k_ev1")
        P.emit("v", lambda eng, mt=mt, sc=sc: eng.tensor_mul(
            ct[mt][:, sc * 512:(sc + 1) * 512], at[0][:, 0:512],
            at[0][:, 0:512]),
            reads=["A0"], writes=["C%d" % mt], tag="l2k_ev2")
    emit_proj_fmajor(P, M, "l2k", W["L2f_WkT"][0], D, D, bt, bk, evk,
                     wt_keys=[W["L2f_WkT"][1]])
    def evv(mt, sc, psum, pn):
        P.emit("v", lambda eng, mt=mt, sc=sc, psum=psum: eng.tensor_mul(
            at[0][:, 0:512], psum[:], ut[mt][:, sc * 512:(sc + 1) * 512]),
            reads=[pn, "U%d" % mt], writes=["A0"], tag="l2v_ev1")
        P.emit("v", lambda eng, mt=mt, sc=sc: eng.tensor_add(
            xt[mt][:, sc * 512:(sc + 1) * 512],
            xt[mt][:, sc * 512:(sc + 1) * 512], at[0][:, 0:512]),
            reads=["A0", xk[mt]], writes=[xk[mt]], tag="l2v_ev2")
    emit_proj_fmajor(P, M, "l2v", W["L2f_WvT"][0], D, D,
                     [ct[i][:, 0:2048] for i in range(NT)], keys("C", NT),
                     evv, wt_keys=[W["L2f_WvT"][1]])


STAGE_EMITTERS[5] = emit_stage_l2_ffn



def emit_stage_l3_attn(P, M, W):
    xt = M.tiles("X", NT)
    xk = keys("X", NT)
    bt = M.tiles("B", 4)
    lk = keys("B", 4)
    a0, a1, a2 = M.sb("A0", [128, S], BF16), M.sb("A1", [128, S], BF16), \
        M.sb("A2", [128, S], BF16)
    ck = M.sb("BCA", [128, S], BF16)[:, 0:S]
    sk = M.sb("BCB", [128, S], BF16)[:, 0:S]
    ctiles = c_area(M)
    p4, p5 = M.ps("P4", [128, 512]), M.ps("P5", [128, 512])
    wpa = M.sb("WPA", [128, 4096], BF16)
    wpb = M.sb("WPB", [128, 4096], BF16)

    P.emit("g", lambda eng: eng.dma_start(out=ck, in_=W["ropeC"][0]),
           reads=[W["ropeC"][1]], writes=["BCA"], tag="ropec")
    P.emit("g", lambda eng: eng.dma_start(out=sk, in_=W["ropeS"][0]),
           reads=[W["ropeS"][1]], writes=["BCB"], tag="ropes")

    emit_load_wchunk(P, wpa, "WPA", W["L3_wd"][0], [W["L3_wd"][1]], NT, 0, 512)
    for mt in range(4):
        for sc in range(4):
            pn = ["P4", "P5"][(mt * 4 + sc) % 2]
            psum = [p4, p5][(mt * 4 + sc) % 2]
            for kc in range(NT):
                def fn(eng, kc=kc, mt=mt, sc=sc, psum=psum):
                    return eng.matmul(
                        psum[:], wpa[:, kc * 512 + mt * 128:
                                     kc * 512 + (mt + 1) * 128],
                        xt[kc][:, sc * 512:(sc + 1) * 512],
                        start=(kc == 0), stop=(kc == NT - 1))
                P.emit("t", fn, reads=["WPA", xk[kc]], writes=[pn],
                       tag="l3lat")
            P.emit("s", lambda eng, mt=mt, sc=sc, psum=psum: eng.activation(
                bt[mt][:, sc * 512:(sc + 1) * 512], psum[:], AF.Copy),
                reads=[pn], writes=[lk[mt]], tag="l3lat_ev")

    emit_load_wchunk(P, wpa, "WPA", W["L3_wvup"][0], [W["L3_wvup"][1]], 4, 0,
                     512)
    emit_load_wchunk(P, wpb, "WPB", W["L3_wvup"][0], [W["L3_wvup"][1]], 4, 512,
                     512)
    for kt in range(STL):
        P.emit("v", lambda eng, kt=kt: eng.memset(
            va_ap(ctiles, kt).rearrange("p (h c) -> p h c", c=65)[:, :, 64:65],
            1.0), reads=[], writes=["C%d" % (kt // 2)], tag="vones")
    for st in range(STL):
        for ec in range(2):
            pn = ["P4", "P5"][(st * 2 + ec) % 2]
            psum = [p4, p5][(st * 2 + ec) % 2]
            wsrc, wkey = (wpa, "WPA") if ec == 0 else (wpb, "WPB")
            for kc in range(4):
                def fn(eng, st=st, kc=kc, psum=psum, wsrc=wsrc):
                    return eng.matmul(
                        psum[:], bt[kc][:, st * 128:(st + 1) * 128],
                        wsrc[:, kc * 512:(kc + 1) * 512],
                        start=(kc == 0), stop=(kc == 3))
                P.emit("t", fn, reads=[lk[kc], wkey], writes=[pn], tag="l3v")
            def fne(eng, st=st, ec=ec, psum=psum):
                dst = va_ap(ctiles, st).rearrange("p (h c) -> p h c", c=65)[
                    :, 8 * ec:8 * ec + 8, 0:64]
                src = psum[:].rearrange("p (h c) -> p h c", c=64)
                return eng.activation(dst, src, AF.Copy)
            P.emit("s", fne, reads=[pn], writes=["C%d" % (st // 2)],
                   tag="l3v_ev")

    qwb = M.sb("WPA", [128, 4096], BF16)[:, 0:1024]
    kwb = M.sb("WPA", [128, 4096], BF16)[:, 1024:1536]

    def proj_qk(pair):
        emit_load_wchunk(P, qwb, "WPA", W["L3_wq"][0], [W["L3_wq"][1]], NT,
                         pair * 128, 128)
        emit_load_wchunk(P, kwb, "WPA", W["L3_wkup"][0], [W["L3_wkup"][1]], 4,
                         pair * 128, 128)
        for sc in range(4):
            psum = p4
            for kc in range(NT):
                def fn(eng, kc=kc, sc=sc, psum=psum):
                    return eng.matmul(
                        psum[:], qwb[:, kc * 128:(kc + 1) * 128],
                        xt[kc][:, sc * 512:(sc + 1) * 512],
                        start=(kc == 0), stop=(kc == NT - 1))
                P.emit("t", fn, reads=["WPA", xk[kc]], writes=["P4"],
                       tag="l3q")
            P.emit("s", lambda eng, sc=sc, psum=psum: eng.activation(
                a0[:, sc * 512:(sc + 1) * 512], psum[:], AF.Copy, scale=0.125),
                reads=["P4"], writes=["A0"], tag="l3q_ev")
        for sc in range(4):
            psum = p5
            for kc in range(4):
                def fn(eng, kc=kc, sc=sc, psum=psum):
                    return eng.matmul(
                        psum[:], kwb[:, kc * 128:(kc + 1) * 128],
                        bt[kc][:, sc * 512:(sc + 1) * 512],
                        start=(kc == 0), stop=(kc == 3))
                P.emit("t", fn, reads=["WPA", lk[kc]], writes=["P5"],
                       tag="l3k")
            P.emit("s", lambda eng, sc=sc, psum=psum: eng.activation(
                a1[:, sc * 512:(sc + 1) * 512], psum[:], AF.Copy),
                reads=["P5"], writes=["A1"], tag="l3k_ev")
        for (dst, dkey) in [(a0, "A0"), (a1, "A1")]:
            P.emit("v", lambda eng, dst=dst: eng.stream_shuffle(
                a2[:], dst[:], [i ^ 1 for i in range(32)]),
                reads=[dkey], writes=["A2"], tag="rope_sh")
            P.emit("v", lambda eng, dst=dst: eng.tensor_mul(dst[:], dst[:], ck),
                   reads=[dkey, "BCA"], writes=[dkey], tag="rope_c")
            P.emit("v", lambda eng: eng.tensor_mul(a2[:], a2[:], sk),
                   reads=["A2", "BCB"], writes=["A2"], tag="rope_s")
            P.emit("v", lambda eng, dst=dst: eng.tensor_add(dst[:], dst[:],
                                                            a2[:]),
                   reads=[dkey, "A2"], writes=[dkey], tag="rope_a")

    emit_attention(P, M, "l3", proj_qk, lambda kt: 256 * (kt // 2), False)

    def evac(mt, sc, psum, pn):
        def fn(eng, mt=mt, sc=sc, psum=psum):
            return eng.tensor_add(xt[mt][:, sc * 512:(sc + 1) * 512],
                                  xt[mt][:, sc * 512:(sc + 1) * 512],
                                  psum[:])
        P.emit("v", fn, reads=[pn, xk[mt]], writes=[xk[mt]], tag="l3wo_ev")
    emit_proj_fmajor(P, M, "l3wo", W["L3_wo"][0], D, D,
                     M.tiles("U", NT), keys("U", NT), evac,
                     wt_keys=[W["L3_wo"][1]])


STAGE_EMITTERS[6] = emit_stage_l3_attn



def emit_stage_l3_ffn(P, M, W):
    xt = M.tiles("X", NT)
    xk = keys("X", NT)
    ut = M.tiles("U", NT)
    ct = c_area(M)
    ci = [ct[i][:, 0:2048] for i in range(NT)]
    cik = keys("C", NT)
    mxt = M.sb("U3", [128, S], BF16)
    sclx = M.sb("T32A", [128, S], F32)
    sclx2 = M.sb("T32B", [128, S], F32)
    sm1 = M.sb("SM1", [128, 64], F32)
    p4, p5 = M.ps("P4", [128, 512]), M.ps("P5", [128, 512])
    wpa = M.sb("WPA", [128, 4096], BF16)
    wpb = M.sb("WPB", [128, 4096], BF16)
    wgq = M.dr("L3WGQ", [D, 4096], BF16)
    wuq = M.dr("L3WUQ", [D, 4096], BF16)
    wdq = M.dr("L3WDQ", [4096, D], BF16)

    emit_weight_quant(P, M, "l3wg", W["L3f_wgT"][0], W["L3f_wgT"][1], D, 4096,
                      wgq, "L3WGQ")
    P.emit("v", lambda eng: eng.tensor_copy(sm1[0:1, 8:9], sm1[0:1, 2:3]),
           reads=["SM1"], writes=["SM1"], tag="mwg")
    emit_weight_quant(P, M, "l3wu", W["L3f_wuT"][0], W["L3f_wuT"][1], D, 4096,
                      wuq, "L3WUQ")
    P.emit("v", lambda eng: eng.tensor_copy(sm1[0:1, 9:10], sm1[0:1, 2:3]),
           reads=["SM1"], writes=["SM1"], tag="mwu")
    emit_weight_quant(P, M, "l3wd", W["L3f_wdT"][0], W["L3f_wdT"][1], 4096, D,
                      wdq, "L3WDQ")
    P.emit("v", lambda eng: eng.tensor_copy(sm1[0:1, 10:11], sm1[0:1, 2:3]),
           reads=["SM1"], writes=["SM1"], tag="mwd")
    P.emit("v", lambda eng: eng.tensor_mul(sm1[0:1, 12:13], sm1[0:1, 8:9],
                                           sm1[0:1, 9:10]),
           reads=["SM1"], writes=["SM1"], tag="mwc1")
    P.emit("v", lambda eng: eng.tensor_mul(sm1[0:1, 12:13], sm1[0:1, 12:13],
                                           sm1[0:1, 10:11]),
           reads=["SM1"], writes=["SM1"], tag="mwc2")
    emit_bcast_row(P, M, "l3mwg", sm1[0:1, 8:9], "SM1", 128, sm1[:, 13:14],
                   "SM1", 1)

    absx = M.sb("U5", [128, S], BF16)
    for i in range(NT):
        P.emit("s", lambda eng, i=i: eng.activation(absx[:], xt[i][:], AF.Abs),
               reads=[xk[i]], writes=["U5"], tag="l3aqa")
        if i == 0:
            P.emit("v", lambda eng: eng.tensor_copy(mxt[:], absx[:]),
                   reads=["U5"], writes=["U3"], tag="l3aq0")
        else:
            P.emit("v", lambda eng: eng.tensor_max(mxt[:], mxt[:], absx[:]),
                   reads=["U3", "U5"], writes=["U3"], tag="l3aq")
    emit_act_quant(P, M, "l3x", S, 0, True, None, None)
    for i in range(NT):
        P.emit("v", lambda eng, i=i: eng.tensor_mul(ci[i], xt[i][:], mxt[:]),
               reads=[xk[i], "U3"], writes=[cik[i]], tag="l3xm")
        P.emit("v", lambda eng, i=i: eng.tensor_scalar(
            ci[i], ci[i], MAGIC, MAGIC, ALU.add, ALU.subtract),
            reads=[cik[i]], writes=[cik[i]], tag="l3xr")

    for sc in range(4):
        for mt in range(32):
            def flg(eng, mt=mt):
                src = wgq[:, mt * 128:(mt + 1) * 128].rearrange(
                    "(kc p) c -> p kc c", p=128)
                return eng.dma_start(out=wpa[:, 0:1024], in_=src)
            P.emit("g", flg, reads=["L3WGQ"], writes=["WPA"], tag="wgl")
            def flu(eng, mt=mt):
                src = wuq[:, mt * 128:(mt + 1) * 128].rearrange(
                    "(kc p) c -> p kc c", p=128)
                return eng.dma_start(out=wpb[:, 0:1024], in_=src)
            P.emit("g", flu, reads=["L3WUQ"], writes=["WPB"], tag="wul")
            for kc in range(NT):
                P.emit("t", lambda eng, kc=kc, sc=sc: eng.matmul(
                    p4[:], wpa[:, kc * 128:(kc + 1) * 128],
                    ci[kc][:, sc * 512:(sc + 1) * 512],
                    start=(kc == 0), stop=(kc == NT - 1)),
                    reads=["WPA", cik[kc]], writes=["P4"], tag="l3g")
            for kc in range(NT):
                P.emit("t", lambda eng, kc=kc, sc=sc: eng.matmul(
                    p5[:], wpb[:, kc * 128:(kc + 1) * 128],
                    ci[kc][:, sc * 512:(sc + 1) * 512],
                    start=(kc == 0), stop=(kc == NT - 1)),
                    reads=["WPB", cik[kc]], writes=["P5"], tag="l3u")
            P.emit("v", lambda eng, sc=sc: eng.tensor_mul(
                ut[0][:, 0:512], p4[:], sclx[:, sc * 512:(sc + 1) * 512]),
                reads=["P4", "T32A"], writes=["U0"], tag="l3gev")
            P.emit("v", lambda eng, sc=sc: eng.tensor_mul(
                ut[1][:, 0:512], p5[:], sclx[:, sc * 512:(sc + 1) * 512]),
                reads=["P5", "T32A"], writes=["U1"], tag="l3uev")
            P.emit("s", lambda eng: eng.activation(
                ut[2][:, 0:512], ut[0][:, 0:512], AF.Sigmoid,
                scale=sm1[:, 13:14]),
                reads=["U0", "SM1"], writes=["U2"], tag="l3sg")
            P.emit("v", lambda eng: eng.tensor_mul(
                ut[0][:, 0:512], ut[0][:, 0:512], ut[2][:, 0:512]),
                reads=["U0", "U2"], writes=["U0"], tag="l3gg")
            P.emit("v", lambda eng, mt=mt: eng.tensor_mul(
                ct[mt % 8][:, (mt // 8) * 512:(mt // 8) * 512 + 512]
                if False else ut[4][:, 0:512], ut[0][:, 0:512],
                ut[1][:, 0:512]),
                reads=["U0", "U1"], writes=["U4"], tag="l3gu")
            P.emit("s", lambda eng: eng.activation(
                absx[:, 0:512], ut[4][:, 0:512], AF.Abs),
                reads=["U4"], writes=["U5"], tag="l3ga")
            if mt == 0:
                P.emit("v", lambda eng, sc=sc: eng.tensor_copy(
                    mxt[:, sc * 512:(sc + 1) * 512], absx[:, 0:512]),
                    reads=["U5"], writes=["U3"], tag="l3gm0")
            else:
                P.emit("v", lambda eng, sc=sc: eng.tensor_max(
                    mxt[:, sc * 512:(sc + 1) * 512],
                    mxt[:, sc * 512:(sc + 1) * 512], absx[:, 0:512]),
                    reads=["U3", "U5"], writes=["U3"], tag="l3gm")
            bts = M.sb("B%d" % (mt % 8), [128, S], BF16)
            P.emit("v", lambda eng, mt=mt, bts=bts: eng.tensor_copy(
                bts[:, (mt // 8) * 512:(mt // 8) * 512 + 512],
                ut[4][:, 0:512]),
                reads=["U4"], writes=["B%d" % (mt % 8)], tag="l3gst")
        emit_act_quant(P, M, "l3gu%d" % sc, 512, sc * 512, True,
                       sm1[0:1, 12:13], None, sclx_name="T32B")
        for i in range(NT):
            bts = M.sb("B%d" % i, [128, S], BF16)
            for j in range(4):
                P.emit("v", lambda eng, i=i, j=j, sc=sc, bts=bts:
                       eng.tensor_mul(
                           bts[:, j * 512:(j + 1) * 512],
                           bts[:, j * 512:(j + 1) * 512],
                           mxt[:, sc * 512:(sc + 1) * 512]),
                       reads=["B%d" % i, "U3"], writes=["B%d" % i],
                       tag="l3gqm")
            P.emit("v", lambda eng, i=i, bts=bts: eng.tensor_scalar(
                bts[:], bts[:], MAGIC, MAGIC, ALU.add, ALU.subtract),
                reads=["B%d" % i], writes=["B%d" % i], tag="l3gqr")
        def loadwd(dt, par):
            wb = [wpa, wpb][par]
            def fn(eng, dt=dt, wb=wb):
                src = wdq[:, dt * 128:(dt + 1) * 128].rearrange(
                    "(kc p) c -> p kc c", p=128)
                return eng.dma_start(out=wb[:, 0:4096], in_=src)
            P.emit("g", fn, reads=["L3WDQ"], writes=[["WPA", "WPB"][par]],
                   tag="wdl")
        loadwd(0, 0)
        for dt in range(NT):
            par = dt % 2
            if dt + 1 < NT:
                loadwd(dt + 1, 1 - par)
            wb = [wpa, wpb][par]
            wk = ["WPA", "WPB"][par]
            pn = ["P4", "P5"][dt % 2]
            psum = [p4, p5][dt % 2]
            for kc in range(32):
                bts = M.sb("B%d" % (kc % 8), [128, S], BF16)
                def fn(eng, kc=kc, wb=wb, psum=psum, bts=bts):
                    return eng.matmul(
                        psum[:], wb[:, kc * 128:(kc + 1) * 128],
                        bts[:, (kc // 8) * 512:(kc // 8) * 512 + 512],
                        start=(kc == 0), stop=(kc == 31))
                P.emit("t", fn, reads=[wk, "B%d" % (kc % 8)], writes=[pn],
                       tag="l3dmm")
            P.emit("v", lambda eng, sc=sc, psum=psum: eng.tensor_mul(
                ut[0][:, 0:512], psum[:], sclx2[:, sc * 512:(sc + 1) * 512]),
                reads=[pn, "T32B"], writes=["U0"], tag="l3dev1")
            P.emit("v", lambda eng, dt=dt, sc=sc: eng.tensor_add(
                xt[dt][:, sc * 512:(sc + 1) * 512],
                xt[dt][:, sc * 512:(sc + 1) * 512], ut[0][:, 0:512]),
                reads=["U0", xk[dt]], writes=[xk[dt]], tag="l3dev2")


STAGE_EMITTERS[7] = emit_stage_l3_ffn


# version 04
# speedup vs baseline: 1.0350x; 1.0350x over previous
import math
from contextlib import ExitStack

import numpy as np
import ml_dtypes
import concourse.bass as bass
import concourse.mybir as mybir
from concourse.bass_utils import run_bass_kernel_spmd

F32 = mybir.dt.float32
BF16 = mybir.dt.bfloat16
FP16 = mybir.dt.float16
FP8 = mybir.dt.float8e4
AF = mybir.ActivationFunctionType
ALU = mybir.AluOpType
AX = mybir.AxisListType.X

NCORE = 8
D = 1024
S = 2048
NT = D // 128
STL = S // 128
NH = 16
DK = 64
MAGIC = 12582912.0
NEGBIG = -1.0e30
LN_EPS = 1e-5

_bf = ml_dtypes.bfloat16


class Prog:
    NQ = 8

    def __init__(self, nc):
        self.nc = nc
        self.stream = {e: [] for e in "tvsyg"}
        self.semnames = ["t", "v", "s"] + \
            ["y%d" % i for i in range(self.NQ)] + \
            ["g%d" % i for i in range(self.NQ)]
        self.cnt = {sn: 0 for sn in self.semnames}
        self.dmaidx = {"y": 0, "g": 0}
        self.lastw = {}
        self.readers = {}
        self.hw = {e: {} for e in "tvsyg"}

    def emit(self, eng, fn, reads=(), writes=(), tag=""):
        need = {}
        if eng in "yg":
            slot = self.dmaidx[eng] % self.NQ
            self.dmaidx[eng] += 1
            sn = eng + str(slot)
            if self.cnt[sn]:
                need[sn] = self.cnt[sn]
        else:
            sn = eng
        for b in reads:
            w = self.lastw.get(b)
            if w is not None and not (eng == "t" and w[0] == "t"):
                need[w[0]] = max(need.get(w[0], 0), w[1])
        for b in writes:
            w = self.lastw.get(b)
            if w is not None and not (eng == "t" and w[0] == "t"):
                need[w[0]] = max(need.get(w[0], 0), w[1])
            for s2, c2 in self.readers.get(b, {}).items():
                if not (eng == "t" and s2 == "t"):
                    need[s2] = max(need.get(s2, 0), c2)
        for s2 in sorted(need):
            c2 = need[s2]
            if self.hw[eng].get(s2, 0) >= c2:
                continue
            self.stream[eng].append(("wait", s2, c2))
            self.hw[eng][s2] = c2
        inc = 16 if eng in "yg" else 1
        self.cnt[sn] += inc
        c = self.cnt[sn]
        self.stream[eng].append(("op", fn, sn))
        for b in writes:
            self.lastw[b] = (sn, c)
            self.readers[b] = {}
        for b in reads:
            self.readers.setdefault(b, {})[sn] = c

    def finalize(self, ctx):
        nc = self.nc
        sems = {sn: ctx.enter_context(nc.semaphore("sem_" + sn))
                for sn in self.semnames}
        block = ctx.enter_context(nc.Block())
        streams, cnt = self.stream, self.cnt

        def replay(e):
            inc = 16 if e in "yg" else 1

            def body(eng):
                for item in streams[e]:
                    if item[0] == "wait":
                        eng.wait_ge(sems[item[1]], item[2])
                    else:
                        item[1](eng).then_inc(sems[item[2]], inc)
                if e == "g":
                    for sn in self.semnames:
                        if cnt[sn]:
                            eng.wait_ge(sems[sn], cnt[sn])
            return body

        block.tensor(replay("t"))
        block.vector(replay("v"))
        block.scalar(replay("s"))
        block.sync(replay("y"))
        block.gpsimd(replay("g"))


class Mem:
    def __init__(self, nc, ctx):
        self.nc = nc
        self.ctx = ctx
        self.t = {}
        self.dram = {}

    def sb(self, name, shape, dtype):
        if name not in self.t:
            self.t[name] = self.ctx.enter_context(
                self.nc.sbuf_tensor(name, shape, dtype))
        return self.t[name]

    def ps(self, name, shape, dtype=F32):
        if name not in self.t:
            self.t[name] = self.ctx.enter_context(
                self.nc.psum_tensor(name, shape, dtype))
        return self.t[name]

    def dr(self, name, shape, dtype=F32):
        if name not in self.dram:
            self.dram[name] = self.nc.dram_tensor(name, shape, dtype,
                                                  kind="Internal")
        return self.dram[name].ap()

    def tiles(self, base, n, cols=S, dtype=BF16):
        return [self.sb("%s%d" % (base, i), [128, cols], dtype)
                for i in range(n)]


def keys(base, n):
    return ["%s%d" % (base, i) for i in range(n)]



def emit_load_wchunk(P, dst, dst_key, src_ap, src_keys, kt, c0, cols):
    def fn(eng):
        src = src_ap[:, c0:c0 + cols]
        src3 = src.rearrange("(kc p) c -> p kc c", p=128)
        return eng.dma_start(out=dst[:, 0:kt * cols], in_=src3)
    P.emit("g", fn, reads=list(src_keys), writes=[dst_key], tag="wload")


def emit_proj_fmajor(P, M, name, wt_ap, Kdim, Mout, rhs_tiles, rhs_keys, evac,
                     wchunk=512, wdtype=BF16, wt_keys=(), pbanks=("P4", "P5")):
    kt = Kdim // 128
    assert kt * wchunk <= 4096 and wdtype == BF16
    wb = [M.sb("WPA", [128, 4096], BF16), M.sb("WPB", [128, 4096], BF16)]
    wk = ["WPA", "WPB"]
    nchunk = (Mout + wchunk - 1) // wchunk
    emit_load_wchunk(P, wb[0], wk[0], wt_ap, wt_keys, kt, 0, min(wchunk, Mout))
    pidx = 0
    for ci in range(nchunk):
        par = ci % 2
        if ci + 1 < nchunk:
            c0n = (ci + 1) * wchunk
            emit_load_wchunk(P, wb[1 - par], wk[1 - par], wt_ap, wt_keys, kt,
                             c0n, min(wchunk, Mout - c0n))
        ntile = min(wchunk, Mout - ci * wchunk) // 128
        for mt in range(ntile):
            for sc in range(4):
                pn = pbanks[pidx % len(pbanks)]
                pidx += 1
                psum = M.ps(pn, [128, 512])
                for kc in range(kt):
                    def fn(eng, kc=kc, mt=mt, sc=sc, par=par, psum=psum):
                        lhsT = wb[par][:, kc * wchunk + mt * 128:
                                       kc * wchunk + (mt + 1) * 128]
                        rhs = rhs_tiles[kc][:, sc * 512:(sc + 1) * 512]
                        return eng.matmul(psum[:], lhsT, rhs,
                                          start=(kc == 0), stop=(kc == kt - 1))
                    P.emit("t", fn, reads=[wk[par], rhs_keys[kc]], writes=[pn],
                           tag=name + "mm")
                evac(ci * (wchunk // 128) + mt, sc, psum, pn)


def emit_bcast_row(P, M, name, src_row_ap, src_key, npart, dst_ap, dst_key,
                   ncols):
    dr = M.dr("dsc_" + name, [1, ncols], F32)
    dk = "dsc_" + name
    P.emit("y", lambda eng: eng.dma_start(out=dr[:, 0:ncols], in_=src_row_ap),
           reads=[src_key], writes=[dk], tag="bco")
    P.emit("g", lambda eng: eng.dma_start(
        out=dst_ap, in_=dr[0:1, 0:ncols].partition_broadcast(npart)),
        reads=[dk], writes=[dst_key], tag="bci")


def sq_views(M):
    t32 = [M.sb("T32A", [128, S], F32), M.sb("T32B", [128, S], F32)]
    return ([t32[0].bitcast(BF16)[:, 0:S], t32[1].bitcast(BF16)[:, 0:S]],
            ["T32A", "T32B"])


def c_area(M):
    ct = [M.sb("C%d" % i, [128, 2080], BF16) for i in range(NT)]
    return ct


def va_ap(ct, kt):
    return ct[kt // 2][:, (kt % 2) * 1040:(kt % 2) * 1040 + 1040]


def fq8_views(M):
    ut = M.tiles("U", NT)
    aps = []
    ks = []
    for i in range(NT):
        host = ut[2 + i // 2]
        aps.append(host.bitcast(FP8)[:, (i % 2) * 2048:(i % 2) * 2048 + 2048])
        ks.append("U%d" % (2 + i // 2))
    return aps, ks


def h8_views(M):
    ut = M.tiles("U", NT)
    b0 = M.sb("B0", [128, S], BF16)
    b1 = M.sb("B1", [128, S], BF16)
    hosts = [(ut[6], "U6"), (ut[7], "U7"), (b0, "B0"), (b1, "B1")]
    aps, ks = [], []
    for i in range(32):
        h, k = hosts[i // 8]
        aps.append(h.bitcast(FP8)[:, (i % 8) * 512:(i % 8) * 512 + 512])
        ks.append(k)
    return aps, ks


def emit_ln(P, M, x_tiles, x_keys, eps=LN_EPS, out_tiles=None, out_keys=None,
            fq8=False, f32out=None):
    ones = M.sb("ONESC", [128, 1], BF16)
    p6 = M.ps("P6", [128, 512])
    pb = M.ps("PB", [128, 2048])
    r16a = M.sb("R16A", [16, S], F32)
    r16b = M.sb("R16B", [16, S], F32)
    bc1 = M.sb("BCA", [128, S], BF16)
    bc2 = M.sb("BCB", [128, S], BF16)
    t32 = [M.sb("T32A", [128, S], F32), M.sb("T32B", [128, S], F32)]
    sq, sqk = sq_views(M)
    if out_tiles is None:
        out_tiles, out_keys = x_tiles, x_keys
    if fq8:
        q8, q8k = fq8_views(M)
    for sc in range(4):
        for i in range(NT):
            def fn(eng, i=i, sc=sc):
                return eng.matmul(p6[0:1, 0:512], ones[:, 0:1],
                                  x_tiles[i][:, sc * 512:(sc + 1) * 512],
                                  start=(i == 0), stop=(i == NT - 1))
            P.emit("t", fn, reads=["ONESC", x_keys[i]], writes=["P6"],
                   tag="ln_sum")
        P.emit("v", lambda eng, sc=sc: eng.tensor_scalar(
            r16a[0:1, sc * 512:(sc + 1) * 512], p6[0:1, 0:512],
            1.0 / D, None, ALU.mult),
            reads=["P6"], writes=["R16A"], tag="ln_mean")
    for i in range(NT):
        P.emit("s", lambda eng, i=i: eng.activation(
            sq[i % 2], x_tiles[i][:], AF.Square),
            reads=[x_keys[i]], writes=[sqk[i % 2]], tag="ln_sq")
        for sc in range(4):
            def fn(eng, i=i, sc=sc):
                return eng.matmul(pb[0:1, sc * 512:(sc + 1) * 512], ones[:, 0:1],
                                  sq[i % 2][:, sc * 512:(sc + 1) * 512],
                                  start=(i == 0), stop=(i == NT - 1))
            P.emit("t", fn, reads=["ONESC", sqk[i % 2]],
                   writes=["PB%d" % sc], tag="ln_ssq")
    P.emit("v", lambda eng: eng.tensor_scalar(
        r16b[0:1, :], pb[0:1, 0:2048], 1.0 / D, None, ALU.mult),
        reads=pbkeys(), writes=["R16B"], tag="ln_msq")
    P.emit("v", lambda eng: eng.tensor_mul(
        t32[0][0:1, 0:2048], r16a[0:1, :], r16a[0:1, :]),
        reads=["R16A"], writes=["T32A"], tag="ln_m2")
    P.emit("v", lambda eng: eng.scalar_tensor_tensor(
        r16b[0:1, :], r16b[0:1, :], float(eps), t32[0][0:1, 0:2048],
        ALU.add, ALU.subtract),
        reads=["R16B", "T32A"], writes=["R16B"], tag="ln_var")
    P.emit("v", lambda eng: eng.reciprocal(r16b[0:1, :], r16b[0:1, :]),
           reads=["R16B"], writes=["R16B"], tag="ln_rec")
    P.emit("s", lambda eng: eng.activation(r16b[0:1, :], r16b[0:1, :], AF.Sqrt),
           reads=["R16B"], writes=["R16B"], tag="ln_rstd")
    emit_bcast_row(P, M, "lnm", r16a[0:1, :], "R16A", 128, bc1[:], "BCA", S)
    emit_bcast_row(P, M, "lnr", r16b[0:1, :], "R16B", 128, bc2[:], "BCB", S)
    for i in range(NT):
        tk = "T32%s" % ("A" if i % 2 == 0 else "B")
        tt = t32[i % 2]
        P.emit("v", lambda eng, i=i, tt=tt: eng.tensor_sub(
            tt[:], x_tiles[i][:], bc1[:]),
            reads=[x_keys[i], "BCA"], writes=[tk], tag="ln_a1")
        if fq8:
            P.emit("v", lambda eng, i=i, tt=tt: eng.tensor_mul(
                q8[i], tt[:], bc2[:]),
                reads=[tk, "BCB"], writes=[q8k[i]], tag="ln_a8")
        if f32out is not None:
            P.emit("v", lambda eng, i=i, tt=tt: eng.tensor_mul(
                tt[:], tt[:], bc2[:]),
                reads=[tk, "BCB"], writes=[tk], tag="ln_a32")
            P.emit("g", lambda eng, i=i, tt=tt: eng.dma_start(
                out=f32out[i * 128:(i + 1) * 128, :], in_=tt[:]),
                reads=[tk], writes=["__out__"], tag="ln_oy")
            P.emit("v", lambda eng, i=i, tt=tt: eng.tensor_copy(
                out_tiles[i][:], tt[:]),
                reads=[tk], writes=[out_keys[i]], tag="ln_a2c")
        else:
            P.emit("v", lambda eng, i=i, tt=tt: eng.tensor_mul(
                out_tiles[i][:], tt[:], bc2[:]),
                reads=[tk, "BCB"], writes=[out_keys[i]], tag="ln_a2")


def pbkeys(c0=0, c1=2048):
    return ["PB%d" % i for i in range(c0 // 512, (c1 + 511) // 512)]



def emit_attention(P, M, name, proj_qk, kstart_fn, masked_diag):
    a0, a1 = M.sb("A0", [128, S], BF16), M.sb("A1", [128, S], BF16)
    sq, sqk = sq_views(M)
    sqf0 = M.sb("T32A", [128, S], F32)
    idn = M.sb("IDN", [128, 128], BF16)
    mtri = M.sb("MTRI", [128, 512], BF16)
    negb = M.sb("NEGB", [128, 16], F32)
    r16a = M.sb("R16A", [16, S], F32)
    r16b = M.sb("R16B", [16, S], F32)
    bc1 = M.sb("A2", [128, S], BF16)
    bo = M.sb("BO16", [128, 16], BF16)
    pb = M.ps("PB", [128, 2048])
    p4, p5 = M.ps("P4", [128, 512]), M.ps("P5", [128, 512])
    pq = M.ps("P6", [128, 512])
    ctiles = c_area(M)
    utiles = M.tiles("U", NT)
    pbuf = [M.sb("PF0", [128, 512], BF16), M.sb("PF1", [128, 512], BF16)]

    for pair in range(NH // 2):
        proj_qk(pair)
        for t, (src, skey) in enumerate([(a0, "A0"), (a1, "A1")]):
            dstrow = r16a if t == 0 else r16b
            dkey = "R16A" if t == 0 else "R16B"
            P.emit("s", lambda eng, src=src: eng.activation(
                sq[0], src[:], AF.Square),
                reads=[skey], writes=[sqk[0]], tag="bn_sq")
            for sc in range(4):
                def fn(eng, sc=sc, pair=pair):
                    return eng.matmul(pq[0:2, 0:512], bo[:, 2 * pair:2 * pair + 2],
                                      sq[0][:, sc * 512:(sc + 1) * 512],
                                      start=True, stop=True)
                P.emit("t", fn, reads=["BO16", "T32A"], writes=["P6"], tag="bn_mm")
                P.emit("v", lambda eng, sc=sc, dstrow=dstrow: eng.tensor_copy(
                    dstrow[0:2, sc * 512:(sc + 1) * 512], pq[0:2, 0:512]),
                    reads=["P6"], writes=[dkey], tag="bn_cp")
        P.emit("v", lambda eng: eng.reduce_max(
            sqf0[0:2, 0:1], r16a[0:2, :], axis=AX),
            reads=["R16A"], writes=["T32A"], tag="bn_rmax")
        P.emit("v", lambda eng: eng.reduce_max(
            sqf0[0:2, 1:2], r16b[0:2, :], axis=AX),
            reads=["R16B"], writes=["T32A"], tag="bn_rmax2")
        P.emit("v", lambda eng: eng.tensor_mul(
            sqf0[0:2, 2:3], sqf0[0:2, 0:1],
            sqf0[0:2, 1:2]),
            reads=["T32A"], writes=["T32A"], tag="bn_mul")
        P.emit("s", lambda eng: eng.activation(
            sqf0[0:2, 2:3], sqf0[0:2, 2:3], AF.Sqrt),
            reads=["T32A"], writes=["T32A"], tag="bn_sqrt")
        P.emit("v", lambda eng: eng.tensor_scalar(
            sqf0[0:2, 2:3], sqf0[0:2, 2:3],
            -1.0, -1.0, ALU.add, ALU.mult),
            reads=["T32A"], writes=["T32A"], tag="bn_neg")
        dnb = M.dr("dnb", [1, 16], F32)
        P.emit("y", lambda eng, pair=pair: eng.dma_start(
            out=dnb[0:1, 2 * pair:2 * pair + 2],
            in_=sqf0[0:2, 2:3]),
            reads=["T32A"], writes=["dnb"], tag="bn_out")
        P.emit("g", lambda eng, pair=pair: eng.dma_start(
            out=negb[:, 2 * pair:2 * pair + 2],
            in_=dnb[0:1, 2 * pair:2 * pair + 2].partition_broadcast(128)),
            reads=["dnb"], writes=["NEGB"], tag="bn_in")
        for hh in range(2):
            h = 2 * pair + hh
            qrow = a0[64 * hh:64 * (hh + 1), :]
            krow = a1[64 * hh:64 * (hh + 1), :]
            sidx = 0
            for kt in range(STL):
                q0 = kstart_fn(kt)
                bounds = []
                c = q0
                while c < S:
                    e = min(S, (c // 512 + 1) * 512)
                    bounds.append((c, e))
                    c = e
                for (c0, c1) in bounds:
                    w = c1 - c0
                    pS = [p4, p5][sidx % 2]
                    pSk = ["P4", "P5"][sidx % 2]
                    sidx += 1
                    first = (c0 == q0 and masked_diag)
                    if first:
                        def fnm(eng, c0=c0, w=w, pS=pS):
                            return eng.matmul(pS[:, 0:w], idn[:],
                                              mtri[:, 0:w], start=True,
                                              stop=False)
                        P.emit("t", fnm, reads=["IDN", "MTRI"], writes=[pSk],
                               tag="attn_mask")
                    def fns(eng, kt=kt, c0=c0, w=w, pS=pS, krow=krow,
                            qrow=qrow, first=first):
                        return eng.matmul(
                            pS[:, 0:w], krow[:, kt * 128:(kt + 1) * 128],
                            qrow[:, c0:c0 + w],
                            start=not first, stop=True)
                    P.emit("t", fns, reads=["A1", "A0"], writes=[pSk],
                           tag="attn_sc")
                    pf = pbuf[sidx % 2]
                    pfk = "PF%d" % (sidx % 2)
                    def fne(eng, w=w, pS=pS, pf=pf, h=h):
                        return eng.activation(pf[:, 0:w], pS[:, 0:w], AF.Exp,
                                              bias=negb[:, h:h + 1])
                    P.emit("s", fne, reads=[pSk, "NEGB"], writes=[pfk],
                           tag="attn_exp")
                    kt_last = (c1 - 1) // 128
                    def fnv(eng, kt=kt, c0=c0, w=w, pf=pf, h=h,
                            kt_last=kt_last):
                        va = va_ap(ctiles, kt)
                        return eng.matmul(
                            pb[0:65, c0:c0 + w],
                            va[:, 65 * h:65 * h + 65], pf[:, 0:w],
                            start=(kt == 0), stop=(kt == kt_last))
                    P.emit("t", fnv, reads=["C%d" % (kt // 2), pfk],
                           writes=pbkeys(c0, c1), tag="attn_av")
            P.emit("s", lambda eng: eng.activation(
                r16a[0:1, :], pb[64:65, 0:2048], AF.Copy),
                reads=pbkeys(), writes=["R16A"], tag="attn_sum")
            P.emit("v", lambda eng: eng.reciprocal(r16a[0:1, :], r16a[0:1, :]),
                   reads=["R16A"], writes=["R16A"], tag="attn_rec")
            emit_bcast_row(P, M, name + "rs", r16a[0:1, :], "R16A", 64,
                           bc1[0:64, :], "A2", S)
            def fnev(eng, hh=hh, h=h):
                return eng.tensor_mul(
                    utiles[h // 2][64 * hh:64 * (hh + 1), :],
                    pb[0:64, 0:2048], bc1[0:64, :])
            P.emit("v", fnev, reads=pbkeys() + ["A2"],
                   writes=["U%d" % (h // 2)], tag="attn_ev")



def emit_stage_l0_attn(P, M, W):
    nc = P.nc
    xt = M.tiles("X", NT)
    xk = keys("X", NT)
    ut = M.tiles("U", NT)
    a0, a1, a2 = M.sb("A0", [128, S], BF16), M.sb("A1", [128, S], BF16), \
        M.sb("A2", [128, S], BF16)
    ck = M.sb("BCA", [128, S], BF16)[:, 0:S]
    sk = M.sb("BCB", [128, S], BF16)[:, 0:S]
    ctiles = c_area(M)
    p4 = M.ps("P4", [128, 512])
    p5 = M.ps("P5", [128, 512])

    wvb = M.sb("WPA", [128, 4096], BF16)
    emit_load_wchunk(P, wvb, "WPA", W["L0_wvT"][0], [W["L0_wvT"][1]], NT, 0, 512)
    wvb2 = M.sb("WPB", [128, 4096], BF16)
    emit_load_wchunk(P, wvb2, "WPB", W["L0_wvT"][0], [W["L0_wvT"][1]], NT,
                     512, 512)
    for kt in range(STL):
        P.emit("v", lambda eng, kt=kt: eng.memset(
            va_ap(ctiles, kt).rearrange("p (h c) -> p h c", c=65)[:, :, 64:65],
            1.0), reads=[], writes=["C%d" % (kt // 2)], tag="vones")
    for st in range(STL):
        for ec in range(2):
            pn = ["P4", "P5"][(st * 2 + ec) % 2]
            psum = [p4, p5][(st * 2 + ec) % 2]
            wsrc, wkey = (wvb, "WPA") if ec == 0 else (wvb2, "WPB")
            for kc in range(NT):
                def fn(eng, st=st, ec=ec, kc=kc, psum=psum, wsrc=wsrc):
                    return eng.matmul(
                        psum[:], xt[kc][:, st * 128:(st + 1) * 128],
                        wsrc[:, kc * 512:(kc + 1) * 512],
                        start=(kc == 0), stop=(kc == NT - 1))
                P.emit("t", fn, reads=[xk[kc], wkey], writes=[pn], tag="l0v")
            def fne(eng, st=st, ec=ec, psum=psum):
                dst = va_ap(ctiles, st).rearrange("p (h c) -> p h c", c=65)[
                    :, 8 * ec:8 * ec + 8, 0:64]
                src = psum[:].rearrange("p (h c) -> p h c", c=64)
                return eng.activation(dst, src, AF.Copy)
            P.emit("s", fne, reads=[pn], writes=["C%d" % (st // 2)],
                   tag="l0v_ev")

    P.emit("g", lambda eng: eng.dma_start(out=ck, in_=W["ropeC"][0]),
           reads=[W["ropeC"][1]], writes=["BCA"], tag="ropec")
    P.emit("g", lambda eng: eng.dma_start(out=sk, in_=W["ropeS"][0]),
           reads=[W["ropeS"][1]], writes=["BCB"], tag="ropes")

    qwb = M.sb("WPA", [128, 4096], BF16)[:, 0:1024]
    kwb = M.sb("WPA", [128, 4096], BF16)[:, 1024:2048]

    def proj_qk(pair):
        emit_load_wchunk(P, qwb, "WPA", W["L0_wqT"][0], [W["L0_wqT"][1]], NT,
                         pair * 128, 128)
        emit_load_wchunk(P, kwb, "WPA", W["L0_wkT"][0], [W["L0_wkT"][1]], NT,
                         pair * 128, 128)
        for (wb, wkey, dst, dkey, scl) in [
                (qwb, "WPA", a0, "A0", 0.125), (kwb, "WPA", a1, "A1", 1.0)]:
            for sc in range(4):
                pn = "P4" if dkey == "A0" else "P5"
                psum = p4 if dkey == "A0" else p5
                for kc in range(NT):
                    def fn(eng, kc=kc, sc=sc, psum=psum, wb=wb):
                        return eng.matmul(
                            psum[:], wb[:, kc * 128:(kc + 1) * 128],
                            xt[kc][:, sc * 512:(sc + 1) * 512],
                            start=(kc == 0), stop=(kc == NT - 1))
                    P.emit("t", fn, reads=[wkey, xk[kc]], writes=[pn],
                           tag="l0qk")
                P.emit("s", lambda eng, sc=sc, psum=psum, dst=dst, scl=scl:
                       eng.activation(dst[:, sc * 512:(sc + 1) * 512],
                                      psum[:], AF.Copy, scale=scl),
                       reads=[pn], writes=[dkey], tag="l0qk_ev")
            P.emit("v", lambda eng, dst=dst: eng.stream_shuffle(
                a2[:], dst[:], [i ^ 1 for i in range(32)]),
                reads=[dkey], writes=["A2"], tag="rope_sh")
            P.emit("v", lambda eng, dst=dst: eng.tensor_mul(dst[:], dst[:], ck),
                   reads=[dkey, "BCA"], writes=[dkey], tag="rope_c")
            P.emit("v", lambda eng: eng.tensor_mul(a2[:], a2[:], sk),
                   reads=["A2", "BCB"], writes=["A2"], tag="rope_s")
            P.emit("v", lambda eng, dst=dst: eng.tensor_add(dst[:], dst[:], a2[:]),
                   reads=[dkey, "A2"], writes=[dkey], tag="rope_a")

    emit_attention(P, M, "l0", proj_qk, lambda kt: kt * 128, True)

    def evac(mt, sc, psum, pn):
        def fn(eng, mt=mt, sc=sc, psum=psum):
            return eng.tensor_add(xt[mt][:, sc * 512:(sc + 1) * 512],
                                  xt[mt][:, sc * 512:(sc + 1) * 512],
                                  psum[:])
        P.emit("v", fn, reads=[pn, xk[mt]], writes=[xk[mt]], tag="l0wo_ev")
    emit_proj_fmajor(P, M, "l0wo", W["L0_woT"][0], D, D,
                     M.tiles("U", NT), keys("U", NT), evac,
                     wt_keys=[W["L0_woT"][1]])


def emit_stage_l0_ffn(P, M, W):
    nc = P.nc
    xt = M.tiles("X", NT)
    xk = keys("X", NT)
    xq8, xq8k = fq8_views(M)
    h8, h8k = h8_views(M)
    ut = M.tiles("U", NT)
    h16 = ut[0].bitcast(FP16)[:, 0:512]
    p4, p5 = M.ps("P4", [128, 512]), M.ps("P5", [128, 512])
    b1q = M.sb("B1Q", [128, 32], FP8)
    b2q = M.sb("B2Q", [128, 8], FP8)
    P.emit("g", lambda eng: eng.dma_start(out=b1q[:], in_=W["L0_b1"][0]),
           reads=[W["L0_b1"][1]], writes=["B1Q"], tag="b1")
    P.emit("g", lambda eng: eng.dma_start(out=b2q[:], in_=W["L0_b2"][0]),
           reads=[W["L0_b2"][1]], writes=["B2Q"], tag="b2")
    b1f = M.sb("B1F", [128, 32], F32)
    b2f = M.sb("B2F", [128, 8], F32)
    P.emit("v", lambda eng: eng.tensor_copy(b1f[:], b1q[:]),
           reads=["B1Q"], writes=["B1F"], tag="b1f")
    P.emit("v", lambda eng: eng.tensor_copy(b2f[:], b2q[:]),
           reads=["B2Q"], writes=["B2F"], tag="b2f")

    wpa = M.sb("WPA", [128, 4096], BF16)
    wpb = M.sb("WPB", [128, 4096], BF16)
    w1b = [wpa.bitcast(FP8)[:, 0:4096], wpb.bitcast(FP8)[:, 0:4096]]
    w1k = ["WPA", "WPB"]
    w2b = w1b
    w2k = w1k
    dm = M.sb("DMHB", [128, 1024], FP16)

    for sc in range(4):
        emit_load_wchunk(P, w1b[0], w1k[0], W["L0_w1T"][0], [W["L0_w1T"][1]],
                         NT, 0, 512)
        for mc in range(8):
            par = mc % 2
            if mc + 1 < 8:
                emit_load_wchunk(P, w1b[1 - par], w1k[1 - par], W["L0_w1T"][0],
                                 [W["L0_w1T"][1]], NT, (mc + 1) * 512, 512)
            for mt in range(4):
                mg = mc * 4 + mt
                pn = ["P4", "P5"][mg % 2]
                psum = [p4, p5][mg % 2]
                for kc in range(NT):
                    def fn(eng, kc=kc, mt=mt, sc=sc, par=par, psum=psum):
                        return eng.matmul(
                            psum[:], w1b[par][:, kc * 512 + mt * 128:
                                              kc * 512 + (mt + 1) * 128],
                            xq8[kc][:, sc * 512:(sc + 1) * 512],
                            start=(kc == 0), stop=(kc == NT - 1))
                    P.emit("t", fn, reads=[w1k[par], xq8k[kc]], writes=[pn],
                           tag="l0f_mm1")
                def fe1(eng, mg=mg, psum=psum):
                    return eng.activation(h16, psum[:], AF.Relu,
                                          bias=b1f[:, mg:mg + 1])
                P.emit("s", fe1, reads=[pn, "B1F"], writes=["U0"], tag="l0f_h16")
                def fe2(eng, mg=mg):
                    return eng.activation(h8[mg], h16, AF.Copy)
                P.emit("s", fe2, reads=["U0"], writes=[h8k[mg]], tag="l0f_h8")
        emit_load_wchunk(P, w2b[0], w2k[0], W["L0_w2T"][0], [W["L0_w2T"][1]],
                         32, 0, 128)
        for dt in range(NT):
            par = dt % 2
            if dt + 1 < NT:
                emit_load_wchunk(P, w2b[1 - par], w2k[1 - par], W["L0_w2T"][0],
                                 [W["L0_w2T"][1]], 32, (dt + 1) * 128, 128)
            pn = ["P4", "P5"][dt % 2]
            psum = [p4, p5][dt % 2]
            for kc in range(32):
                def fn(eng, kc=kc, dt=dt, sc=sc, par=par, psum=psum):
                    return eng.matmul(
                        psum[:], w2b[par][:, kc * 128:(kc + 1) * 128],
                        h8[kc][:, 0:512],
                        start=(kc == 0), stop=(kc == 31))
                P.emit("t", fn, reads=[w2k[par], h8k[kc]], writes=[pn],
                       tag="l0f_mm2")
            def fm(eng, dt=dt, sc=sc):
                src = W["L0_dmask"][0][dt * 128:(dt + 1) * 128,
                                       sc * 512:(sc + 1) * 512]
                return eng.dma_start(out=dm[:, 0:512], in_=src)
            P.emit("g", fm, reads=[W["L0_dmask"][1]], writes=["DMHB"],
                   tag="l0f_dm")
            def fe(eng, dt=dt, psum=psum):
                return eng.tensor_scalar(h16, psum[:], b2f[:, dt:dt + 1],
                                         None, ALU.add)
            P.emit("v", fe, reads=[pn, "B2F"], writes=["U0"], tag="l0f_y16")
            def fmu(eng, dt=dt, sc=sc):
                return eng.tensor_mul(ut[1].bitcast(FP16)[:, 0:512], h16,
                                      dm[:, 0:512])
            P.emit("v", fmu, reads=["U0", "DMHB"], writes=["U1"], tag="l0f_msk")
            def fad(eng, dt=dt, sc=sc):
                return eng.tensor_add(
                    xt[dt][:, sc * 512:(sc + 1) * 512],
                    xt[dt][:, sc * 512:(sc + 1) * 512],
                    ut[1].bitcast(FP16)[:, 0:512])
            P.emit("v", fad, reads=["U1", xk[dt]], writes=[xk[dt]],
                   tag="l0f_res")



STAGE_EMITTERS = {}


_DECLARED = {}


def _declare_weight_inputs(nc, stage, host_w):
    W = {}
    decl = _DECLARED.setdefault(id(nc), {})
    for name, arr in host_w[stage].items():
        if name in decl:
            W[name] = decl[name]
            continue
        dt = {np.dtype(np.float32): F32, np.dtype(np.float16): FP16,
              np.dtype(_bf): BF16}[arr.dtype]
        h = nc.dram_tensor("i_" + name, list(arr.shape), dt,
                           kind="ExternalInput")
        W[name] = (h.ap(), "i_" + name)
        decl[name] = W[name]
    return W


def emit_consts(P, M, W):
    ones = M.sb("ONESC", [128, 1], BF16)
    P.emit("v", lambda eng: eng.memset(ones[:], 1.0), reads=(), writes=["ONESC"])
    for cname, sname, shape in [("cIDN", "IDN", [128, 128]),
                                ("cMTRI", "MTRI", [128, 512]),
                                ("cBO16", "BO16", [128, 16])]:
        if cname in W:
            t = M.sb(sname, shape, BF16)
            P.emit("g", lambda eng, t=t, cname=cname: eng.dma_start(
                out=t[:], in_=W[cname][0]),
                reads=[W[cname][1]], writes=[sname], tag="const")


def emit_x_in(P, M, W, fq8):
    xt = M.tiles("X", NT)
    for i in range(NT):
        P.emit("g", lambda eng, i=i: eng.dma_start(
            out=xt[i][:], in_=W["x_in"][0][i * 128:(i + 1) * 128, :]),
            reads=[W["x_in"][1]], writes=["X%d" % i], tag="xin")
    if fq8:
        q8, q8k = fq8_views(M)
        for i in range(NT):
            P.emit("g", lambda eng, i=i: eng.dma_start(
                out=q8[i], in_=W["x_in"][0][i * 128:(i + 1) * 128, :]),
                reads=[W["x_in"][1]], writes=[q8k[i]], tag="xin8")


def build_program(stages, host_w, first, last):
    nc = bass.Bass("TRN2", target_bir_lowering=False)
    ctx = ExitStack()
    M = Mem(nc, ctx)
    P = Prog(nc)
    Wall = {}
    for st in stages:
        Wall.update(_declare_weight_inputs(nc, st, host_w))
    Wall.update(_declare_weight_inputs(nc, "const", host_w))
    if 1 in stages:
        hm = nc.dram_tensor("i_L0_dmask", [D, S], FP16, kind="ExternalInput")
        Wall["L0_dmask"] = (hm.ap(), "i_L0_dmask")
    h = nc.dram_tensor("i_x_in", [D, S], F32, kind="ExternalInput")
    Wall["x_in"] = (h.ap(), "i_x_in")
    ho = nc.dram_tensor("o_y", [D, S], F32, kind="ExternalOutput")
    NEED_FQ8 = {1}
    eps_of = {i: (1e-10 if False else LN_EPS) for i in range(8)}
    with ctx:
        emit_consts(P, M, Wall)
        emit_x_in(P, M, Wall, stages[0] in NEED_FQ8)
        xt = M.tiles("X", NT)
        xk = keys("X", NT)
        for si, st in enumerate(stages):
            STAGE_EMITTERS[st](P, M, Wall)
            nxt = stages[si + 1] if si + 1 < len(stages) else None
            lastln = (si == len(stages) - 1)
            emit_ln(P, M, xt, xk, eps=eps_of[st],
                    fq8=(nxt in NEED_FQ8),
                    f32out=(ho.ap() if lastln else None))
        P.finalize(ctx)
    return nc


STAGE_EMITTERS[0] = emit_stage_l0_attn
STAGE_EMITTERS[1] = emit_stage_l0_ffn



def _ct(a):
    return np.ascontiguousarray(a)


def make_host_weights(inp):
    w = {}
    w[0] = {
        "L0_wqT": _ct(inp["L0_attn_wq"].T),
        "L0_wkT": _ct(inp["L0_attn_wk"].T),
        "L0_wvT": _ct(inp["L0_attn_wv"].T),
        "L0_woT": _ct(inp["L0_attn_wo"].T),
        "ropeC": _rope_table(cos=True),
        "ropeS": _rope_table(cos=False),
    }
    w[2] = {
        "L1_wattnT": _ct(inp["L1_attn_w_attn"].T),
        "L1_bq": _ct(inp["L1_attn_b_attn"][0:1024].reshape(8, 128).T),
        "L1_bk": _ct(inp["L1_attn_b_attn"][1024:2048].reshape(8, 128).T),
        "L1_bv": _ct(inp["L1_attn_b_attn"][2048:3072].reshape(1, 1024)),
        "L1_wprojT": _ct(inp["L1_attn_w_proj"].T),
        "L1_bp": _ct(inp["L1_attn_b_proj"].reshape(8, 128).T),
        "L1_wfcT": _ct(inp["L1_attn_w_fc"].T),
        "L1_bfc": _ct(inp["L1_attn_b_fc"].reshape(32, 128).T),
        "L1_wmlpT": _ct(inp["L1_attn_w_mlp_proj"].T),
        "L1_bm": _ct(inp["L1_attn_b_mlp_proj"].reshape(8, 128).T),
    }
    w[3] = {
        "L1f_w1T": _ct(inp["L1_ffn_w1"].T),
        "L1f_w2T": _ct(inp["L1_ffn_w2"].T),
        "L1f_cw": _ct(inp["L1_ffn_conv_w"].reshape(32, 128, 3)
                      .transpose(1, 0, 2).reshape(128, 96)),
        "L1f_cb": _ct(inp["L1_ffn_conv_b"].reshape(32, 128).T),
    }
    w[4] = {
        "L2_WqT": _ct(inp["L2_attn_Wq"].T),
        "L2_WkT": _ct(inp["L2_attn_Wk"].T),
        "L2_WvT": _ct(inp["L2_attn_Wv"].T),
        "L2_WoT": _ct(inp["L2_attn_Wo"].T),
        "L2_bq": _ct(inp["L2_attn_bq"].reshape(8, 128).T),
        "L2_bk": _ct(inp["L2_attn_bk"].reshape(8, 128).T),
        "L2_bv": _ct(inp["L2_attn_bv"].reshape(1, 1024)),
        "L2_bo": _ct(inp["L2_attn_bo"].reshape(8, 128).T),
    }
    w[5] = {
        "L2f_WrT": _ct(inp["L2_ffn_W_r"].T),
        "L2f_WkT": _ct(inp["L2_ffn_W_k"].T),
        "L2f_WvT": _ct(inp["L2_ffn_W_v"].T),
        "L2_mu": _ct(np.concatenate([
            inp["L2_ffn_mu_r"].reshape(8, 128).T,
            inp["L2_ffn_mu_k"].reshape(8, 128).T], axis=1)),
    }
    w[6] = {
        "L3_wq": _ct(inp["L3_attn_wq"]),
        "L3_wd": _ct(inp["L3_attn_w_down"]),
        "L3_wkup": _ct(inp["L3_attn_wk_up"]),
        "L3_wvup": _ct(inp["L3_attn_wv_up"]),
        "L3_wo": _ct(inp["L3_attn_wo"]),
        "ropeC": _rope_table(cos=True),
        "ropeS": _rope_table(cos=False),
    }
    w[7] = {
        "L3f_wgT": _ct(inp["L3_ffn_w_gate"].T),
        "L3f_wuT": _ct(inp["L3_ffn_w_up"].T),
        "L3f_wdT": _ct(inp["L3_ffn_w_down"].T),
    }
    w[1] = {
        "L0_w1T": _ct(inp["L0_ffn_w1"].T),
        "L0_w2T": _ct(inp["L0_ffn_w2"].T),
        "L0_b1": _ct(inp["L0_ffn_b1"].reshape(32, 128).T),
        "L0_b2": _ct(inp["L0_ffn_b2"].reshape(8, 128).T),
    }
    idn = np.eye(128, dtype=_bf)
    mtri = np.zeros((128, 512), dtype=np.float32)
    for r in range(128):
        mtri[r, :r] = NEGBIG
    bo = np.zeros((128, 16), dtype=np.float32)
    for c in range(16):
        bo[(c % 2) * 64:(c % 2) * 64 + 64, c] = 1.0
    w["const"] = {"cIDN": idn, "cMTRI": mtri.astype(_bf), "cBO16": bo.astype(_bf)}
    return w


def _rope_table(cos):
    pos = np.arange(S, dtype=np.float64)
    j = np.arange(32, dtype=np.float64)
    inv = 1.0 / (10000.0 ** (2.0 * j / 64.0))
    ang = np.outer(pos, inv)
    tab = np.zeros((128, S), dtype=np.float64)
    for r in range(128):
        jj = (r % 64) // 2
        if cos:
            tab[r] = np.cos(ang[:, jj])
        else:
            tab[r] = np.sin(ang[:, jj]) * (-1.0 if r % 2 == 0 else 1.0)
    return tab.astype(_bf)


def _threefry2x32(key, x0, x1):
    def rotl(x, d):
        return ((x << np.uint32(d)) | (x >> np.uint32(32 - d))).astype(np.uint32)
    ks0, ks1 = np.uint32(key[0]), np.uint32(key[1])
    ks2 = np.uint32(0x1BD11BDA) ^ ks0 ^ ks1
    rot = [(13, 15, 26, 6), (17, 29, 16, 24)]
    x0 = (x0 + ks0).astype(np.uint32)
    x1 = (x1 + ks1).astype(np.uint32)
    ks = [(ks1, ks2), (ks2, ks0), (ks0, ks1), (ks1, ks2), (ks2, ks0)]
    for r in range(5):
        for d in rot[r % 2]:
            x0 = (x0 + x1).astype(np.uint32)
            x1 = rotl(x1, d) ^ x0
        x0 = (x0 + ks[r][0]).astype(np.uint32)
        x1 = (x1 + ks[r][1] + np.uint32(r + 1)).astype(np.uint32)
    return x0, x1


def make_dropout_mask():
    n = NCORE * S * D
    i = np.arange(n, dtype=np.uint64)
    hi = (i >> np.uint64(32)).astype(np.uint32)
    lo = (i & np.uint64(0xFFFFFFFF)).astype(np.uint32)
    b0, b1 = _threefry2x32((np.uint32(0), np.uint32(42)), hi, lo)
    bits = b0 ^ b1
    fl = ((bits >> np.uint32(9)) | np.uint32(0x3F800000)).view(np.float32) - 1.0
    keep = (fl < np.float32(0.8)).reshape(NCORE, S, D)
    return np.where(keep, np.float16(1.25), np.float16(0.0))



_STAGE_GROUPS = [list(range(8))]


def _run(inputs, groups):
    host_w = make_host_weights(inputs)
    dmask = make_dropout_mask()
    x = np.asarray(inputs["L0_attn_x"])
    xT = [np.ascontiguousarray(x[b].T).astype(np.float32) for b in range(NCORE)]
    state = None
    for gi, stages in enumerate(groups):
        first = (gi == 0)
        last = (gi == len(groups) - 1)
        nc = build_program(stages, host_w, first, last)
        in_maps = []
        for b in range(NCORE):
            m = {}
            for st in stages + ["const"]:
                for name, arr in host_w[st].items():
                    m["i_" + name] = arr
            if 1 in stages:
                m["i_L0_dmask"] = np.ascontiguousarray(dmask[b].T)
            m["i_x_in"] = xT[b] if first else state[b]
            in_maps.append(m)
        res = run_bass_kernel_spmd(nc, in_maps, core_ids=list(range(NCORE)))
        if last:
            return np.stack([res.results[b]["o_y"].T for b in range(NCORE)])
        state = [res.results[b]["o_y"] for b in range(NCORE)]
    raise RuntimeError("no last group")


def kernel(**inputs):
    return _run(inputs, _STAGE_GROUPS)



GELU_K2 = math.sqrt(2.0 / math.pi) * 2.0
GELU_K1 = GELU_K2 * 0.044715


def emit_stage_l1_attn(P, M, W):
    xt = M.tiles("X", NT)
    xk = keys("X", NT)
    bt = M.tiles("B", NT)
    bk = keys("B", NT)
    ut = M.tiles("U", NT)
    at, ak = ut, keys("U", NT)
    a0, a1 = M.sb("A0", [128, S], BF16), M.sb("A1", [128, S], BF16)
    ctiles = c_area(M)
    p4, p5 = M.ps("P4", [128, 512]), M.ps("P5", [128, 512])
    bqc = M.sb("BQC", [128, 8], F32)
    bkc = M.sb("BKC", [128, 8], F32)
    bvb = M.sb("BVB", [128, 1024], F32)
    bpc = M.sb("BPC", [128, 8], F32)
    bfcc = M.sb("BFCC", [128, 32], F32)
    bmc = M.sb("BMC", [128, 8], F32)
    for nm, t_, k_ in [("L1_bq", bqc, "BQC"), ("L1_bk", bkc, "BKC"),
                       ("L1_bp", bpc, "BPC"), ("L1_bfc", bfcc, "BFCC"),
                       ("L1_bm", bmc, "BMC")]:
        P.emit("g", lambda eng, t_=t_, nm=nm: eng.dma_start(
            out=t_[:], in_=W[nm][0]), reads=[W[nm][1]], writes=[k_], tag="bia")
    P.emit("g", lambda eng: eng.dma_start(
        out=bvb[:], in_=W["L1_bv"][0][0:1, :].partition_broadcast(128)),
        reads=[W["L1_bv"][1]], writes=["BVB"], tag="bvb")

    emit_ln(P, M, xt, xk, eps=1e-10, out_tiles=bt, out_keys=bk)

    wvb = M.sb("WPA", [128, 4096], BF16)
    wvb2 = M.sb("WPB", [128, 4096], BF16)
    emit_load_wchunk(P, wvb, "WPA", W["L1_wattnT"][0], [W["L1_wattnT"][1]],
                     NT, 2048, 512)
    emit_load_wchunk(P, wvb2, "WPB", W["L1_wattnT"][0], [W["L1_wattnT"][1]],
                     NT, 2560, 512)
    for kt in range(STL):
        P.emit("v", lambda eng, kt=kt: eng.memset(
            va_ap(ctiles, kt).rearrange("p (h c) -> p h c", c=65)[:, :, 64:65],
            1.0), reads=[], writes=["C%d" % (kt // 2)], tag="vones")
    for st in range(STL):
        for ec in range(2):
            pn = ["P4", "P5"][(st * 2 + ec) % 2]
            psum = [p4, p5][(st * 2 + ec) % 2]
            wsrc, wkey = (wvb, "WPA") if ec == 0 else (wvb2, "WPB")
            for kc in range(NT):
                def fn(eng, st=st, kc=kc, psum=psum, wsrc=wsrc):
                    return eng.matmul(
                        psum[:], bt[kc][:, st * 128:(st + 1) * 128],
                        wsrc[:, kc * 512:(kc + 1) * 512],
                        start=(kc == 0), stop=(kc == NT - 1))
                P.emit("t", fn, reads=[bk[kc], wkey], writes=[pn], tag="l1v")
            def fne(eng, st=st, ec=ec, psum=psum, pn=pn):
                dst = va_ap(ctiles, st).rearrange("p (h c) -> p h c", c=65)[
                    :, 8 * ec:8 * ec + 8, 0:64]
                src = psum[:].rearrange("p (h c) -> p h c", c=64)
                bsl = bvb[:, ec * 512:(ec + 1) * 512].rearrange(
                    "p (h c) -> p h c", c=64)
                return eng.tensor_add(dst, src, bsl)
            P.emit("v", fne, reads=[pn, "BVB"], writes=["C%d" % (st // 2)],
                   tag="l1v_ev")

    qwb = M.sb("WPA", [128, 4096], BF16)[:, 0:1024]
    kwb = M.sb("WPA", [128, 4096], BF16)[:, 1024:2048]

    def proj_qk(pair):
        emit_load_wchunk(P, qwb, "WPA", W["L1_wattnT"][0],
                         [W["L1_wattnT"][1]], NT, pair * 128, 128)
        emit_load_wchunk(P, kwb, "WPA", W["L1_wattnT"][0],
                         [W["L1_wattnT"][1]], NT, 1024 + pair * 128, 128)
        for (wb, wkey, dst, dkey, scl, bcol, bckey) in [
                (qwb, "WPA", a0, "A0", 0.125, bqc, "BQC"),
                (kwb, "WPA", a1, "A1", 1.0, bkc, "BKC")]:
            for sc in range(4):
                pn = "P4" if dkey == "A0" else "P5"
                psum = p4 if dkey == "A0" else p5
                for kc in range(NT):
                    def fn(eng, kc=kc, sc=sc, psum=psum, wb=wb):
                        return eng.matmul(
                            psum[:], wb[:, kc * 128:(kc + 1) * 128],
                            bt[kc][:, sc * 512:(sc + 1) * 512],
                            start=(kc == 0), stop=(kc == NT - 1))
                    P.emit("t", fn, reads=[wkey, bk[kc]], writes=[pn],
                           tag="l1qk")
                def fev(eng, sc=sc, psum=psum, dst=dst, scl=scl, bcol=bcol,
                        pair=pair):
                    return eng.tensor_scalar(
                        dst[:, sc * 512:(sc + 1) * 512], psum[:],
                        bcol[:, pair:pair + 1], scl, ALU.add, ALU.mult)
                P.emit("v", fev, reads=[pn, bckey], writes=[dkey],
                       tag="l1qk_ev")

    emit_attention(P, M, "l1", proj_qk, lambda kt: kt * 128, True)

    def evac(mt, sc, psum, pn):
        def fn(eng, mt=mt, sc=sc, psum=psum):
            return eng.scalar_tensor_tensor(
                bt[mt][:, sc * 512:(sc + 1) * 512], psum[:],
                bpc[:, mt:mt + 1], xt[mt][:, sc * 512:(sc + 1) * 512],
                ALU.add, ALU.add)
        P.emit("v", fn, reads=[pn, "BPC", xk[mt]], writes=[bk[mt]],
               tag="l1pr_ev")
    emit_proj_fmajor(P, M, "l1pr", W["L1_wprojT"][0], D, D,
                     ut, keys("U", NT), evac, wt_keys=[W["L1_wprojT"][1]])

    emit_ln(P, M, bt, bk, eps=1e-10, out_tiles=at, out_keys=ak)

    gsc = M.sb("DMHB", [128, 1024], FP16).bitcast(BF16)
    g0 = gsc[:, 0:512]
    g1 = gsc[:, 512:1024]
    wfb = [M.sb("WPA", [128, 4096], BF16), M.sb("WPB", [128, 4096], BF16)]
    wfk = ["WPA", "WPB"]
    hml = [(ctiles[i // 4], "C%d" % (i // 4), (i % 4) * 512) for i in range(32)]
    for sc in range(4):
        for mc in range(8):
            par = mc % 2
            emit_load_wchunk(P, wfb[par], wfk[par], W["L1_wfcT"][0],
                             [W["L1_wfcT"][1]], NT, mc * 512, 512)
            for mt in range(4):
                mg = mc * 4 + mt
                pn = ["P4", "P5"][mg % 2]
                psum = [p4, p5][mg % 2]
                for kc in range(NT):
                    def fn(eng, kc=kc, mt=mt, sc=sc, par=par, psum=psum):
                        return eng.matmul(
                            psum[:], wfb[par][:, kc * 512 + mt * 128:
                                              kc * 512 + (mt + 1) * 128],
                            at[kc][:, sc * 512:(sc + 1) * 512],
                            start=(kc == 0), stop=(kc == NT - 1))
                    P.emit("t", fn, reads=[wfk[par], ak[kc]], writes=[pn],
                           tag="l1fc")
                P.emit("v", lambda eng, mg=mg, psum=psum: eng.tensor_scalar(
                    g0, psum[:], bfcc[:, mg:mg + 1], None, ALU.add),
                    reads=[pn, "BFCC"], writes=["DMHB"], tag="gel0")
                P.emit("s", lambda eng: eng.activation(g1, g0, AF.Square),
                       reads=["DMHB"], writes=["DMHB"], tag="gel1")
                P.emit("v", lambda eng: eng.tensor_scalar(
                    g1, g1, GELU_K1, GELU_K2, ALU.mult, ALU.add),
                    reads=["DMHB"], writes=["DMHB"], tag="gel2")
                P.emit("v", lambda eng: eng.tensor_mul(g1, g0, g1),
                       reads=["DMHB"], writes=["DMHB"], tag="gel3")
                P.emit("s", lambda eng: eng.activation(g1, g1, AF.Sigmoid),
                       reads=["DMHB"], writes=["DMHB"], tag="gel4")
                hm, hk, hoff = hml[mg]
                P.emit("v", lambda eng, hm=hm, hoff=hoff: eng.tensor_mul(
                    hm[:, hoff:hoff + 512], g0, g1),
                    reads=["DMHB"], writes=[hk], tag="gel5")
        w2b = wfb
        w2k = wfk
        emit_load_wchunk(P, w2b[0], w2k[0], W["L1_wmlpT"][0],
                         [W["L1_wmlpT"][1]], 32, 0, 128)
        for dt in range(NT):
            par = dt % 2
            if dt + 1 < NT:
                emit_load_wchunk(P, w2b[1 - par], w2k[1 - par],
                                 W["L1_wmlpT"][0], [W["L1_wmlpT"][1]], 32,
                                 (dt + 1) * 128, 128)
            pn = ["P4", "P5"][dt % 2]
            psum = [p4, p5][dt % 2]
            for kc in range(32):
                hm, hk, hoff = hml[kc]
                def fn(eng, kc=kc, dt=dt, par=par, psum=psum, hm=hm,
                       hoff=hoff):
                    return eng.matmul(
                        psum[:], w2b[par][:, kc * 128:(kc + 1) * 128],
                        hm[:, hoff:hoff + 512],
                        start=(kc == 0), stop=(kc == 31))
                P.emit("t", fn, reads=[w2k[par], hk], writes=[pn],
                       tag="l1mlp")
            def fe(eng, dt=dt, sc=sc, psum=psum):
                return eng.scalar_tensor_tensor(
                    bt[dt][:, sc * 512:(sc + 1) * 512], psum[:],
                    bmc[:, dt:dt + 1], bt[dt][:, sc * 512:(sc + 1) * 512],
                    ALU.add, ALU.add)
            P.emit("v", fe, reads=[pn, "BMC", bk[dt]], writes=[bk[dt]],
                   tag="l1mlp_ev")
    for i in range(NT):
        P.emit("v", lambda eng, i=i: eng.tensor_add(xt[i][:], xt[i][:],
                                                    bt[i][:]),
               reads=[xk[i], bk[i]], writes=[xk[i]], tag="l1out")


STAGE_EMITTERS[2] = emit_stage_l1_attn



def emit_weight_quant(P, M, nm, wt_ap, wt_key, Kdim, Mout, stash_ap, stash_key):
    kt = Kdim // 128
    wf = [M.sb("T32A", [128, S], F32), M.sb("T32B", [128, S], F32)]
    wfk = ["T32A", "T32B"]
    sm = M.sb("SM1", [128, 64], F32)
    onesf = M.sb("ONESF", [128, 1], F32)
    bca = M.sb("BCA", [128, S], BF16)
    p6 = M.ps("P6", [128, 512])
    P.emit("v", lambda eng: eng.memset(onesf[:], 1.0), (), ["ONESF"])
    P.emit("v", lambda eng: eng.memset(sm[:, 0:2], 0.0), (), ["SM1"])
    cw = 2048 // kt
    nch = Mout // cw
    for ci in range(nch):
        par = ci % 2
        emit_load_wchunk(P, wf[par], wfk[par], wt_ap, [wt_key], kt, ci * cw, cw)
        for kc in range(kt):
            P.emit("v", lambda eng, kc=kc, par=par: eng.reduce_sum(
                sm[:, 1:2], wf[par][:, kc * cw:(kc + 1) * cw], axis=AX,
                apply_absolute_value=True),
                reads=[wfk[par]], writes=["SM1"], tag=nm + "wqr")
            P.emit("v", lambda eng: eng.tensor_add(sm[:, 0:1], sm[:, 0:1],
                                                   sm[:, 1:2]),
                   reads=["SM1"], writes=["SM1"], tag=nm + "wqa")
    P.emit("t", lambda eng: eng.matmul(p6[0:1, 0:1], onesf[:, 0:1],
                                       sm[:, 0:1], start=True, stop=True),
           reads=["ONESF", "SM1"], writes=["P6"], tag=nm + "wqs")
    P.emit("v", lambda eng: eng.tensor_scalar(
        sm[0:1, 2:3], p6[0:1, 0:1], 1.0 / (Kdim * Mout), 1e-5,
        ALU.mult, ALU.max), reads=["P6"], writes=["SM1"], tag=nm + "wqm")
    P.emit("v", lambda eng: eng.reciprocal(sm[0:1, 3:4], sm[0:1, 2:3]),
           reads=["SM1"], writes=["SM1"], tag=nm + "wqi")
    emit_bcast_row(P, M, nm + "sw", sm[0:1, 3:4], "SM1", 128, sm[:, 4:5],
                   "SM1", 1)
    for ci in range(nch):
        par = ci % 2
        emit_load_wchunk(P, wf[par], wfk[par], wt_ap, [wt_key], kt, ci * cw, cw)
        P.emit("v", lambda eng, par=par: eng.tensor_scalar(
            wf[par][:, 0:2048], wf[par][:, 0:2048], sm[:, 4:5], None, ALU.mult),
            reads=[wfk[par], "SM1"], writes=[wfk[par]], tag=nm + "wq1")
        P.emit("v", lambda eng, par=par: eng.tensor_scalar(
            wf[par][:, 0:2048], wf[par][:, 0:2048], MAGIC, MAGIC,
            ALU.add, ALU.subtract),
            reads=[wfk[par]], writes=[wfk[par]], tag=nm + "wq2")
        P.emit("v", lambda eng, par=par: eng.tensor_scalar(
            bca[:, 0:2048], wf[par][:, 0:2048], -1.0, 1.0, ALU.max, ALU.min),
            reads=[wfk[par]], writes=["BCA"], tag=nm + "wq3")
        def fo(eng, ci=ci):
            dst = stash_ap[:, ci * cw:(ci + 1) * cw].rearrange(
                "(kc p) c -> p kc c", p=128)
            return eng.dma_start(out=dst, in_=bca[:, 0:2048].rearrange(
                "p (kc c) -> p kc c", kc=kt))
        P.emit("y", fo, reads=["BCA"], writes=[stash_key], tag=nm + "wqo")
    return sm


def emit_act_quant(P, M, nm, width, col0, mx_ready, mw_col_ap, int_round_cb,
                   sclx_name="T32A"):
    mxt = M.sb("U3", [128, S], BF16)
    idn = M.sb("IDN", [128, 128], BF16)
    pt = M.ps("PT", [128, 1024], BF16)
    sm2 = M.sb("SM2", [128, 64], F32)
    spb = M.sb("SPB", [128, 64], BF16)
    sclx = M.sb(sclx_name, [128, S], F32)
    bca = M.sb("BCA", [128, S], BF16)
    nb = width // 128
    for b in range(nb):
        P.emit("t", lambda eng, b=b: eng.transpose(
            pt[:, 0:128], mxt[:, col0 + b * 128:col0 + (b + 1) * 128], idn[:]),
            reads=["U3", "IDN"], writes=["PT"], tag=nm + "aqt")
        P.emit("v", lambda eng, b=b: eng.reduce_max(
            sm2[:, b:b + 1], pt[:, 0:128], axis=AX),
            reads=["PT"], writes=["SM2"], tag=nm + "aqm")
    P.emit("v", lambda eng: eng.tensor_scalar(
        sm2[:, 16:16 + nb], sm2[:, 0:nb], 1e-5, None, ALU.max),
        reads=["SM2"], writes=["SM2"], tag=nm + "aqc")
    P.emit("v", lambda eng: eng.reciprocal(sm2[:, 32:32 + nb],
                                           sm2[:, 16:16 + nb]),
           reads=["SM2"], writes=["SM2"], tag=nm + "aqr")
    P.emit("v", lambda eng: eng.tensor_scalar(
        spb[:, 0:nb], sm2[:, 32:32 + nb], 127.0, None, ALU.mult),
        reads=["SM2"], writes=["SPB"], tag=nm + "aqs")
    P.emit("v", lambda eng: eng.reciprocal(sm2[:, 32:32 + nb], spb[:, 0:nb]),
           reads=["SPB"], writes=["SM2"], tag=nm + "aqi")
    if mw_col_ap is not None:
        emit_bcast_row(P, M, nm + "mw", mw_col_ap, "SM1", 128, sm2[:, 48:49],
                       "SM2", 1)
        P.emit("v", lambda eng: eng.tensor_scalar(
            sm2[:, 32:32 + nb], sm2[:, 32:32 + nb], sm2[:, 48:49], None,
            ALU.mult), reads=["SM2"], writes=["SM2"], tag=nm + "aqmw")
    P.emit("v", lambda eng: eng.tensor_copy(spb[:, 16:16 + nb],
                                            sm2[:, 32:32 + nb]),
           reads=["SM2"], writes=["SPB"], tag=nm + "aqh")
    P.emit("v", lambda eng: eng.tensor_sub(
        sm2[:, 48:48 + nb], sm2[:, 32:32 + nb], spb[:, 16:16 + nb]),
        reads=["SM2", "SPB"], writes=["SM2"], tag=nm + "aql0")
    P.emit("v", lambda eng: eng.tensor_copy(spb[:, 32:32 + nb],
                                            sm2[:, 48:48 + nb]),
           reads=["SM2"], writes=["SPB"], tag=nm + "aql")
    P.emit("t", lambda eng: eng.transpose(pt[0:48, 0:128], spb[:, 0:48],
                                          idn[:]),
           reads=["SPB", "IDN"], writes=["PT"], tag=nm + "aqt2")
    P.emit("v", lambda eng: eng.tensor_copy(bca[0:48, 0:128],
                                            pt[0:48, 0:128]),
           reads=["PT"], writes=["BCA"], tag=nm + "aqe")
    drq = M.dr("dq_" + nm, [3, S], BF16)
    def fo(eng):
        dst = drq.rearrange("t c -> (t c)")[0:48 * 128]
        return eng.dma_start(out=dst, in_=bca[0:48, 0:128])
    P.emit("y", fo, reads=["BCA"], writes=["dq_" + nm], tag=nm + "aqo")
    P.emit("g", lambda eng: eng.dma_start(
        out=mxt[:, col0:col0 + width],
        in_=drq[0:1, 0:width].partition_broadcast(128)),
        reads=["dq_" + nm], writes=["U3"], tag=nm + "aqb1")
    P.emit("g", lambda eng: eng.dma_start(
        out=sclx[:, col0:col0 + width],
        in_=drq[1:2, 0:width].partition_broadcast(128)),
        reads=["dq_" + nm], writes=[sclx_name], tag=nm + "aqb2")
    P.emit("g", lambda eng: eng.dma_start(
        out=bca[:, col0:col0 + width],
        in_=drq[2:3, 0:width].partition_broadcast(128)),
        reads=["dq_" + nm], writes=["BCA"], tag=nm + "aqb3")
    P.emit("v", lambda eng: eng.tensor_add(
        sclx[:, col0:col0 + width], sclx[:, col0:col0 + width],
        bca[:, col0:col0 + width]),
        reads=[sclx_name, "BCA"], writes=[sclx_name], tag=nm + "aqlo")



def emit_stage_l1_ffn(P, M, W):
    xt = M.tiles("X", NT)
    xk = keys("X", NT)
    ct = c_area(M)
    ai = [ct[i][:, 0:2048] for i in range(NT)]
    aik = keys("C", NT)
    ut = M.tiles("U", NT)
    mxt = M.sb("U3", [128, S], BF16)
    sclx = M.sb("T32A", [128, S], F32)
    hh = M.sb("U4", [128, S], BF16)
    cws = M.sb("CWS", [128, 96], F32)
    cbc = M.sb("CBC", [128, 32], F32)
    p4, p5 = M.ps("P4", [128, 512]), M.ps("P5", [128, 512])
    w1q = M.dr("L1W1Q", [D, 4096], BF16)
    w2q = M.dr("L1W2Q", [4096, D], BF16)
    sild = M.dr("L1SILD", [4096, S], BF16)
    wpa = M.sb("WPA", [128, 4096], BF16)
    wpb = M.sb("WPB", [128, 4096], BF16)

    sm = emit_weight_quant(P, M, "l1w1", W["L1f_w1T"][0], W["L1f_w1T"][1],
                           D, 4096, w1q, "L1W1Q")
    mw1 = sm[0:1, 2:3]
    P.emit("g", lambda eng: eng.dma_start(out=cbc[:], in_=W["L1f_cb"][0]),
           reads=[W["L1f_cb"][1]], writes=["CBC"], tag="cb")
    P.emit("g", lambda eng: eng.dma_start(out=cws[:], in_=W["L1f_cw"][0]),
           reads=[W["L1f_cw"][1]], writes=["CWS"], tag="cw")
    emit_bcast_row(P, M, "l1mw1", mw1, "SM1", 128, sclx[:, 0:1], "T32A", 1)
    P.emit("v", lambda eng: eng.tensor_scalar(
        cws[:], cws[:], sclx[:, 0:1], None, ALU.mult),
        reads=["CWS", "T32A"], writes=["CWS"], tag="cwm")
    sm2q = emit_weight_quant(P, M, "l1w2", W["L1f_w2T"][0], W["L1f_w2T"][1],
                             4096, D, w2q, "L1W2Q")
    mw2 = sm2q[0:1, 2:3]

    absx = M.sb("U5", [128, S], BF16)
    for i in range(NT):
        P.emit("s", lambda eng, i=i: eng.activation(
            absx[:], xt[i][:], AF.Abs),
            reads=[xk[i]], writes=["U5"], tag="aqxa")
        if i == 0:
            P.emit("v", lambda eng: eng.tensor_copy(mxt[:], absx[:]),
                   reads=["U5"], writes=["U3"], tag="aqx0")
        else:
            P.emit("v", lambda eng: eng.tensor_max(mxt[:], mxt[:], absx[:]),
                   reads=["U3", "U5"], writes=["U3"], tag="aqx")
    emit_act_quant(P, M, "l1x", S, 0, True, None, None)
    for i in range(NT):
        P.emit("v", lambda eng, i=i: eng.tensor_mul(ai[i][:], xt[i][:], mxt[:]),
               reads=[xk[i], "U3"], writes=[aik[i]], tag="aqxm")
        P.emit("v", lambda eng, i=i: eng.tensor_scalar(
            ai[i][:], ai[i][:], MAGIC, MAGIC, ALU.add, ALU.subtract),
            reads=[aik[i]], writes=[aik[i]], tag="aqxr")

    wb = [wpa, wpb]
    wbk = ["WPA", "WPB"]
    def loadw1(mt, par):
        def fn(eng, mt=mt):
            src = w1q[:, mt * 128:(mt + 1) * 128].rearrange(
                "(kc p) c -> p kc c", p=128)
            return eng.dma_start(out=wb[par][:, 0:1024], in_=src)
        P.emit("g", fn, reads=["L1W1Q"], writes=[wbk[par]], tag="w1l")
    loadw1(0, 0)
    for mt in range(32):
        par = mt % 2
        if mt + 1 < 32:
            loadw1(mt + 1, 1 - par)
        for sc in range(4):
            pn = ["P4", "P5"][sc % 2]
            psum = [p4, p5][sc % 2]
            for kc in range(NT):
                def fn(eng, kc=kc, sc=sc, par=par, psum=psum):
                    return eng.matmul(
                        psum[:], wb[par][:, kc * 128:(kc + 1) * 128],
                        ai[kc][:, sc * 512:(sc + 1) * 512],
                        start=(kc == 0), stop=(kc == NT - 1))
                P.emit("t", fn, reads=[wbk[par], aik[kc]], writes=[pn],
                       tag="l1fmm1")
            P.emit("v", lambda eng, sc=sc, psum=psum: eng.tensor_mul(
                hh[:, sc * 512:(sc + 1) * 512], psum[:],
                sclx[:, sc * 512:(sc + 1) * 512]),
                reads=[pn, "T32A"], writes=["U4"], tag="l1fev1")
        P.emit("v", lambda eng, mt=mt: eng.tensor_scalar(
            ut[0][:], hh[:, 0:2048], cws[:, 3 * mt + 1:3 * mt + 2], None,
            ALU.mult), reads=["U4", "CWS"], writes=["U0"], tag="cv1")
        P.emit("v", lambda eng, mt=mt: eng.scalar_tensor_tensor(
            ut[0][:, 1:2048], hh[:, 0:2047], cws[:, 3 * mt:3 * mt + 1],
            ut[0][:, 1:2048],
            ALU.mult, ALU.add), reads=["U4", "CWS", "U0"], writes=["U0"],
            tag="cv0")
        P.emit("v", lambda eng, mt=mt: eng.scalar_tensor_tensor(
            ut[0][:, 0:2047], hh[:, 1:2048], cws[:, 3 * mt + 2:3 * mt + 3],
            ut[0][:, 0:2047],
            ALU.mult, ALU.add), reads=["U4", "CWS", "U0"], writes=["U0"],
            tag="cv2")
        P.emit("s", lambda eng, mt=mt: eng.activation(
            ut[1][:], ut[0][:], AF.Sigmoid, bias=cbc[:, mt:mt + 1]),
            reads=["U0", "CBC"], writes=["U1"], tag="sil1")
        P.emit("v", lambda eng, mt=mt: eng.scalar_tensor_tensor(
            ut[2][:], ut[0][:], cbc[:, mt:mt + 1], ut[1][:],
            ALU.add, ALU.mult), reads=["U0", "CBC", "U1"], writes=["U2"],
            tag="sil2")
        P.emit("s", lambda eng: eng.activation(absx[:], ut[2][:], AF.Abs),
               reads=["U2"], writes=["U5"], tag="aqha")
        if mt == 0:
            P.emit("v", lambda eng: eng.tensor_copy(mxt[:], absx[:]),
                   reads=["U5"], writes=["U3"], tag="aqh0")
        else:
            P.emit("v", lambda eng: eng.tensor_max(mxt[:], mxt[:], absx[:]),
                   reads=["U3", "U5"], writes=["U3"], tag="aqh")
        P.emit("y", lambda eng, mt=mt: eng.dma_start(
            out=sild[mt * 128:(mt + 1) * 128, :], in_=ut[2][:]),
            reads=["U2"], writes=["L1SILD"], tag="silo")

    emit_act_quant(P, M, "l1h", S, 0, True, mw2, None)

    w2b = [wpa, wpb]
    w2bk = ["WPA", "WPB"]
    for sc in range(4):
        for i in range(NT):
            def fl(eng, i=i, sc=sc):
                src = sild[:, sc * 512:(sc + 1) * 512].rearrange(
                    "(j q p) c -> p j q c", p=128, q=NT)[:, :, i, :]
                return eng.dma_start(
                    out=ai[i].rearrange("p (j c) -> p j c", j=4), in_=src)
            P.emit("g", fl, reads=["L1SILD"], writes=[aik[i]], tag="sili")
            for j in range(4):
                P.emit("v", lambda eng, i=i, j=j, sc=sc: eng.tensor_mul(
                    ai[i][:, j * 512:(j + 1) * 512],
                    ai[i][:, j * 512:(j + 1) * 512],
                    mxt[:, sc * 512:(sc + 1) * 512]),
                    reads=[aik[i], "U3"], writes=[aik[i]], tag="hqm")
            P.emit("v", lambda eng, i=i: eng.tensor_scalar(
                ai[i][:], ai[i][:], MAGIC, MAGIC, ALU.add, ALU.subtract),
                reads=[aik[i]], writes=[aik[i]], tag="hqr")
        def loadw2(dt, par):
            def fn(eng, dt=dt):
                src = w2q[:, dt * 128:(dt + 1) * 128].rearrange(
                    "(kc p) c -> p kc c", p=128)
                return eng.dma_start(out=w2b[par][:, 0:4096], in_=src)
            P.emit("g", fn, reads=["L1W2Q"], writes=[w2bk[par]], tag="w2l")
        loadw2(0, 0)
        for dt in range(NT):
            par = dt % 2
            if dt + 1 < NT:
                loadw2(dt + 1, 1 - par)
            pn = ["P4", "P5"][dt % 2]
            psum = [p4, p5][dt % 2]
            for kc in range(32):
                def fn(eng, kc=kc, par=par, psum=psum):
                    return eng.matmul(
                        psum[:], w2b[par][:, kc * 128:(kc + 1) * 128],
                        ai[kc % NT][:, (kc // NT) * 512:(kc // NT) * 512 + 512],
                        start=(kc == 0), stop=(kc == 31))
                P.emit("t", fn, reads=[w2bk[par], aik[kc % NT]], writes=[pn],
                       tag="l1fmm2")
            P.emit("v", lambda eng, dt=dt, sc=sc, psum=psum: eng.tensor_mul(
                ut[0][:, 0:512], psum[:], sclx[:, sc * 512:(sc + 1) * 512]),
                reads=[pn, "T32A"], writes=["U0"], tag="l1fev2a")
            P.emit("v", lambda eng, dt=dt, sc=sc: eng.tensor_add(
                xt[dt][:, sc * 512:(sc + 1) * 512],
                xt[dt][:, sc * 512:(sc + 1) * 512], ut[0][:, 0:512]),
                reads=["U0", xk[dt]], writes=[xk[dt]], tag="l1fev2b")


STAGE_EMITTERS[3] = emit_stage_l1_ffn



L2EPS = 1e-6
L2TAP = None


def emit_stage_l2_attn(P, M, W):
    xt = M.tiles("X", NT)
    xk = keys("X", NT)
    ut = M.tiles("U", NT)
    bt = M.tiles("B", 3)
    a0, a1 = M.sb("A0", [128, S], BF16), M.sb("A1", [128, S], BF16)
    ct = c_area(M)
    sm2 = M.sb("SM2", [128, 64], F32)
    ksem = M.sb("KSEM", [128, 4], BF16)
    onesc = M.sb("ONESC", [128, 1], BF16)
    bca = M.sb("BCA", [128, S], BF16)
    bcb = M.sb("BCB", [128, S], BF16)
    r16a = M.sb("R16A", [16, S], F32)
    r16b = M.sb("R16B", [16, S], F32)
    bqc = M.sb("BQC", [128, 8], F32)
    bkc = M.sb("BKC", [128, 8], F32)
    boc = M.sb("BPC", [128, 8], F32)
    bvb = M.sb("BVB", [128, 1024], F32)
    kvs = M.sb("KVS", [128, 128], BF16)
    wpa = M.sb("WPA", [128, 4096], BF16)
    p4, p5 = M.ps("P4", [128, 512]), M.ps("P5", [128, 512])
    p6 = M.ps("P6", [128, 512])
    pb = M.ps("PB", [128, 2048])
    drl2 = M.dr("drl2", [2, S], F32)

    for nm, t_, k_ in [("L2_bq", bqc, "BQC"), ("L2_bk", bkc, "BKC"),
                       ("L2_bo", boc, "BPC")]:
        P.emit("g", lambda eng, t_=t_, nm=nm: eng.dma_start(
            out=t_[:], in_=W[nm][0]), reads=[W[nm][1]], writes=[k_], tag="bia")
    P.emit("g", lambda eng: eng.dma_start(
        out=bvb[:], in_=W["L2_bv"][0][0:1, :].partition_broadcast(128)),
        reads=[W["L2_bv"][1]], writes=["BVB"], tag="bvb")

    def rowexp(nm, src_rows, src_key, dst, dst_key):
        P.emit("y", lambda eng: eng.dma_start(out=drl2[:, :], in_=src_rows),
               reads=[src_key], writes=["drl2"], tag=nm + "xo")
        P.emit("g", lambda eng: eng.dma_start(
            out=dst[0:64, :], in_=drl2[0:1, :].partition_broadcast(64)),
            reads=["drl2"], writes=[dst_key], tag=nm + "xi0")
        P.emit("g", lambda eng: eng.dma_start(
            out=dst[64:128, :], in_=drl2[1:2, :].partition_broadcast(64)),
            reads=["drl2"], writes=[dst_key], tag=nm + "xi1")

    def rowmm(nm, msk_col_f32, rhs, rhs_key, eps_corr_col, out_rows, out_key):
        P.emit("v", lambda eng: eng.memset(ksem[:, 0:2], 0.0), (), ["KSEM"])
        P.emit("v", lambda eng: eng.tensor_copy(ksem[0:64, 0:1],
                                                msk_col_f32[0:64, :]),
               reads=["SM2"], writes=["KSEM"], tag=nm + "m0")
        P.emit("v", lambda eng: eng.tensor_copy(ksem[64:128, 1:2],
                                                msk_col_f32[64:128, :]),
               reads=["SM2"], writes=["KSEM"], tag=nm + "m1")
        for sc in range(4):
            P.emit("t", lambda eng, sc=sc: eng.matmul(
                p6[0:2, 0:512], ksem[:, 0:2],
                rhs[:, sc * 512:(sc + 1) * 512], start=True, stop=True),
                reads=["KSEM", rhs_key], writes=["P6"], tag=nm + "mm")
            P.emit("v", lambda eng, sc=sc: eng.tensor_copy(
                out_rows[0:2, sc * 512:(sc + 1) * 512], p6[0:2, 0:512]),
                reads=["P6"], writes=[out_key], tag=nm + "cp")
        P.emit("t", lambda eng: eng.matmul(
            p6[0:2, 0:1], ksem[:, 0:2], onesc[:, 0:1], start=True, stop=True),
            reads=["KSEM", "ONESC"], writes=["P6"], tag=nm + "cs")
        P.emit("v", lambda eng: eng.tensor_scalar(
            sm2[0:2, 20:21], p6[0:2, 0:1], L2EPS, None, ALU.mult),
            reads=["P6"], writes=["SM2"], tag=nm + "cse")
        P.emit("v", lambda eng: eng.tensor_scalar(
            out_rows[0:2, :], out_rows[0:2, :], sm2[0:2, 20:21], None,
            ALU.add), reads=[out_key, "SM2"], writes=[out_key], tag=nm + "ca")

    for pair in range(8):
        fc = pair * 128
        for wi, wnm in enumerate(["L2_WqT", "L2_WkT", "L2_WvT"]):
            def fl(eng, wi=wi, wnm=wnm, fc=fc):
                src = W[wnm][0][:, fc:fc + 128].rearrange(
                    "(kc p) c -> p kc c", p=128)
                return eng.dma_start(out=wpa[:, wi * 1024:(wi + 1) * 1024]
                                     .rearrange("p (kc c) -> p kc c", kc=8),
                                     in_=src)
            P.emit("g", fl, reads=[W[wnm][1]], writes=["WPA"], tag="l2wl")
        for wi, (dst, dkey, bcol, bckey) in enumerate(
                [(a0, "A0", bqc, "BQC"), (a1, "A1", bkc, "BKC")]):
            for sc in range(4):
                psum, pn = (p4, "P4") if wi == 0 else (p5, "P5")
                for kc in range(NT):
                    def fn(eng, wi=wi, kc=kc, sc=sc, psum=psum):
                        return eng.matmul(
                            psum[:],
                            wpa[:, wi * 1024 + kc * 128:wi * 1024 + (kc + 1) * 128],
                            xt[kc][:, sc * 512:(sc + 1) * 512],
                            start=(kc == 0), stop=(kc == NT - 1))
                    P.emit("t", fn, reads=["WPA", xk[kc]], writes=[pn],
                           tag="l2qk")
                P.emit("s", lambda eng, sc=sc, psum=psum, dst=dst, bcol=bcol,
                       pair=pair: eng.activation(
                           dst[:, sc * 512:(sc + 1) * 512], psum[:],
                           AF.Sigmoid, bias=bcol[:, pair:pair + 1]),
                       reads=[pn, bckey], writes=[dkey], tag="l2qk_ev")
        for st in range(STL):
            psum, pn = (p4, "P4") if st % 2 == 0 else (p5, "P5")
            for kc in range(NT):
                def fn(eng, st=st, kc=kc, psum=psum):
                    return eng.matmul(
                        psum[:, 0:128], xt[kc][:, st * 128:(st + 1) * 128],
                        wpa[:, 2048 + kc * 128:2048 + (kc + 1) * 128],
                        start=(kc == 0), stop=(kc == NT - 1))
                P.emit("t", fn, reads=[xk[kc], "WPA"], writes=[pn], tag="l2v")
            def fne(eng, st=st, psum=psum, fc=fc):
                return eng.tensor_add(
                    ct[2 + st // 8][:, (st % 8) * 128:(st % 8) * 128 + 128],
                    psum[:, 0:128], bvb[:, fc:fc + 128])
            P.emit("v", fne, reads=[pn, "BVB"],
                   writes=["C%d" % (2 + st // 8)], tag="l2v_ev")
        P.emit("v", lambda eng: eng.reduce_sum(sm2[:, 0:1], a1[:], axis=AX),
               reads=["A1"], writes=["SM2"], tag="l2ks")
        P.emit("v", lambda eng: eng.tensor_scalar(
            sm2[:, 1:2], sm2[:, 0:1], L2EPS, None, ALU.add),
            reads=["SM2"], writes=["SM2"], tag="l2kse")
        P.emit("v", lambda eng: eng.reduce_sum(sm2[:, 2:3], a0[:], axis=AX),
               reads=["A0"], writes=["SM2"], tag="l2qs")
        P.emit("v", lambda eng: eng.tensor_scalar(
            sm2[:, 3:4], sm2[:, 2:3], L2EPS, None, ALU.add),
            reads=["SM2"], writes=["SM2"], tag="l2qse")
        rowmm("l2si", sm2[:, 1:2], a0, "A0", None, r16a, "R16A")
        P.emit("v", lambda eng: eng.reciprocal(r16a[0:2, :], r16a[0:2, :]),
               reads=["R16A"], writes=["R16A"], tag="l2sir")
        rowmm("l2so", sm2[:, 3:4], a1, "A1", None, r16b, "R16B")
        P.emit("v", lambda eng: eng.reciprocal(r16b[0:2, :], r16b[0:2, :]),
               reads=["R16B"], writes=["R16B"], tag="l2sor")
        rowexp("l2so", r16b[0:2, :], "R16B", bca, "BCA")
        P.emit("v", lambda eng: eng.tensor_mul(bt[0][:], a1[:], bca[:]),
               reads=["A1", "BCA"], writes=["B0"], tag="l2kso")
        P.emit("v", lambda eng: eng.reduce_sum(sm2[:, 5:6], bt[0][:], axis=AX),
               reads=["B0"], writes=["SM2"], tag="l2ksos")
        P.emit("v", lambda eng: eng.tensor_scalar(
            sm2[:, 6:7], sm2[:, 5:6], L2EPS, None, ALU.add),
            reads=["SM2"], writes=["SM2"], tag="l2ksose")
        rowexp("l2si", r16a[0:2, :], "R16A", bcb, "BCB")
        P.emit("v", lambda eng: eng.tensor_mul(bt[1][:], a0[:], bcb[:]),
               reads=["A0", "BCB"], writes=["B1"], tag="l2qsi")
        P.emit("v", lambda eng: eng.reduce_sum(sm2[:, 7:8], bt[1][:], axis=AX),
               reads=["B1"], writes=["SM2"], tag="l2qsis")
        P.emit("v", lambda eng: eng.tensor_scalar(
            sm2[:, 8:9], sm2[:, 7:8], L2EPS, None, ALU.add),
            reads=["SM2"], writes=["SM2"], tag="l2qsise")
        rowmm("l2ck", sm2[:, 6:7], a0, "A0", None, r16b, "R16B")
        P.emit("s", lambda eng: eng.activation(r16b[0:2, :], r16b[0:2, :],
                                               AF.Sigmoid),
               reads=["R16B"], writes=["R16B"], tag="l2sa")
        P.emit("v", lambda eng: eng.tensor_mul(r16a[0:2, :], r16a[0:2, :],
                                               r16b[0:2, :]),
               reads=["R16A", "R16B"], writes=["R16A"], tag="l2sisa")
        rowmm("l2cs", sm2[:, 8:9], a1, "A1", None, r16b, "R16B")
        P.emit("v", lambda eng: eng.tensor_scalar(
            r16b[0:2, :], r16b[0:2, :], -1.0, 1.0, ALU.max, ALU.min),
            reads=["R16B"], writes=["R16B"], tag="l2clip")
        P.emit("s", lambda eng: eng.activation(r16b[0:2, :], r16b[0:2, :],
                                               AF.Exp),
               reads=["R16B"], writes=["R16B"], tag="l2exp")
        P.emit("v", lambda eng: eng.reduce_sum(sm2[0:2, 9:10], r16b[0:2, :],
                                               axis=AX),
               reads=["R16B"], writes=["SM2"], tag="l2ssum")
        P.emit("v", lambda eng: eng.reciprocal(sm2[0:2, 10:11],
                                               sm2[0:2, 9:10]),
               reads=["SM2"], writes=["SM2"], tag="l2srec")
        P.emit("v", lambda eng: eng.tensor_scalar(
            sm2[0:2, 10:11], sm2[0:2, 10:11], float(S), None, ALU.mult),
            reads=["SM2"], writes=["SM2"], tag="l2s2048")
        P.emit("v", lambda eng: eng.tensor_scalar(
            r16b[0:2, :], r16b[0:2, :], sm2[0:2, 10:11], None, ALU.mult),
            reads=["R16B", "SM2"], writes=["R16B"], tag="l2sc")
        if L2TAP is not None and pair == 0:
            P.emit("g", lambda eng: eng.dma_start(
                out=L2TAP["sisa"], in_=r16a[0:2, :]),
                reads=["R16A"], writes=["__t__"], tag="tap")
            P.emit("g", lambda eng: eng.dma_start(
                out=L2TAP["sc"], in_=r16b[0:2, :]),
                reads=["R16B"], writes=["__t__"], tag="tap")
            P.emit("g", lambda eng: eng.dma_start(
                out=L2TAP["q"], in_=a0[:]),
                reads=["A0"], writes=["__t__"], tag="tap")
            P.emit("g", lambda eng: eng.dma_start(
                out=L2TAP["k"], in_=a1[:]),
                reads=["A1"], writes=["__t__"], tag="tap")
            P.emit("g", lambda eng: eng.dma_start(
                out=L2TAP["v"], in_=ct[2][:, 0:1024]),
                reads=["C2"], writes=["__t__"], tag="tap")
        rowexp("l2sc", r16b[0:2, :], "R16B", bca, "BCA")
        P.emit("v", lambda eng: eng.tensor_mul(bt[0][:], a1[:], bca[:]),
               reads=["A1", "BCA"], writes=["B0"], tag="l2ksc")
        for st in range(STL):
            def ft(eng, st=st):
                return eng.dma_start(
                    out=ct[st // 8][:, (st % 8) * 128:(st % 8) * 128 + 128],
                    in_=bt[0][:, st * 128:(st + 1) * 128], transpose=True)
            P.emit("y", ft, reads=["B0"], writes=["C%d" % (st // 8)],
                   tag="l2tr")
        for hh in range(2):
            for st in range(STL):
                def fkv(eng, hh=hh, st=st):
                    lhsT = ct[st // 8][:, (st % 8) * 128 + hh * 64:
                                       (st % 8) * 128 + hh * 64 + 64]
                    rhs = ct[2 + st // 8][:, (st % 8) * 128 + hh * 64:
                                          (st % 8) * 128 + hh * 64 + 64]
                    if hh == 0:
                        return eng.matmul(p6[0:64, 0:64], lhsT, rhs,
                                          start=(st == 0), stop=(st == 15))
                    return eng.matmul(p6[64:128, 0:64], lhsT, rhs,
                                      start=(st == 0), stop=(st == 15),
                                      tile_position=(0, 64))
                P.emit("t", fkv, reads=["C%d" % (st // 8),
                                        "C%d" % (2 + st // 8)],
                       writes=["P6"], tag="l2kv")
        P.emit("v", lambda eng: eng.tensor_copy(kvs[0:64, 0:64],
                                                p6[0:64, 0:64]),
               reads=["P6"], writes=["KVS"], tag="l2kve0")
        P.emit("v", lambda eng: eng.tensor_copy(kvs[64:128, 64:128],
                                                p6[64:128, 0:64]),
               reads=["P6"], writes=["KVS"], tag="l2kve1")
        rowexp("l2sisa", r16a[0:2, :], "R16A", bcb, "BCB")
        P.emit("v", lambda eng: eng.tensor_mul(bt[1][:], a0[:], bcb[:]),
               reads=["A0", "BCB"], writes=["B1"], tag="l2qx")
        for hh in range(2):
            for sc in range(4):
                def fo(eng, hh=hh, sc=sc):
                    if hh == 0:
                        return eng.matmul(
                            pb[0:64, sc * 512:(sc + 1) * 512],
                            kvs[0:64, 0:64],
                            bt[1][0:64, sc * 512:(sc + 1) * 512],
                            start=True, stop=True)
                    return eng.matmul(
                        pb[64:128, sc * 512:(sc + 1) * 512],
                        kvs[64:128, 64:128],
                        bt[1][64:128, sc * 512:(sc + 1) * 512],
                        start=True, stop=True, tile_position=(64, 64))
                P.emit("t", fo, reads=["KVS", "B1"],
                       writes=["PB%d" % sc], tag="l2out")
        P.emit("v", lambda eng, pair=pair: eng.tensor_copy(
            ut[pair][:], pb[0:128, 0:2048]),
            reads=pbkeys(), writes=["U%d" % pair], tag="l2oev")
        if L2TAP is not None and pair == 0:
            P.emit("g", lambda eng: eng.dma_start(
                out=L2TAP["kv"], in_=kvs[:]),
                reads=["KVS"], writes=["__t__"], tag="tap")
            P.emit("g", lambda eng: eng.dma_start(
                out=L2TAP["out"], in_=ut[0][:]),
                reads=["U0"], writes=["__t__"], tag="tap")

    def evac(mt, sc, psum, pn):
        def fn(eng, mt=mt, sc=sc, psum=psum):
            return eng.scalar_tensor_tensor(
                xt[mt][:, sc * 512:(sc + 1) * 512], psum[:],
                boc[:, mt:mt + 1], xt[mt][:, sc * 512:(sc + 1) * 512],
                ALU.add, ALU.add)
        P.emit("v", fn, reads=[pn, "BPC", xk[mt]], writes=[xk[mt]],
               tag="l2wo_ev")
    emit_proj_fmajor(P, M, "l2wo", W["L2_WoT"][0], D, D,
                     ut, keys("U", NT), evac, wt_keys=[W["L2_WoT"][1]])


STAGE_EMITTERS[4] = emit_stage_l2_attn



def emit_stage_l2_ffn(P, M, W):
    xt = M.tiles("X", NT)
    xk = keys("X", NT)
    ut = M.tiles("U", NT)
    at = [M.sb("A%d" % i, [128, S], BF16) for i in range(3)]
    bt = M.tiles("B", NT)
    bk = keys("B", NT)
    ct = c_area(M)
    mur = M.sb("MUR", [128, 16], F32)
    p4, p5 = M.ps("P4", [128, 512]), M.ps("P5", [128, 512])

    P.emit("g", lambda eng: eng.dma_start(out=mur[:], in_=W["L2_mu"][0]),
           reads=[W["L2_mu"][1]], writes=["MUR"], tag="mu")
    sm2 = M.sb("SM2", [128, 64], F32)
    P.emit("v", lambda eng: eng.tensor_scalar(
        sm2[:, 0:16], mur[:], -1.0, 1.0, ALU.mult, ALU.add),
        reads=["MUR"], writes=["SM2"], tag="mu1")

    def mix(dst_tiles, dkeys, mucol0, nm):
        for i in range(NT):
            P.emit("v", lambda eng, i=i: eng.tensor_scalar(
                dst_tiles[i][:], xt[i][:], mur[:, mucol0 + i:mucol0 + i + 1],
                None, ALU.mult),
                reads=[xk[i], "MUR"], writes=[dkeys[i]], tag=nm + "a")
            P.emit("v", lambda eng, i=i: eng.scalar_tensor_tensor(
                dst_tiles[i][:, 1:2048], xt[i][:, 0:2047],
                sm2[:, mucol0 + i:mucol0 + i + 1],
                dst_tiles[i][:, 1:2048], ALU.mult, ALU.add),
                reads=[xk[i], "SM2", dkeys[i]], writes=[dkeys[i]],
                tag=nm + "b")

    mix(bt, bk, 0, "mxr")
    def evr(mt, sc, psum, pn):
        P.emit("s", lambda eng, mt=mt, sc=sc, psum=psum: eng.activation(
            ut[mt][:, sc * 512:(sc + 1) * 512], psum[:], AF.Sigmoid),
            reads=[pn], writes=["U%d" % mt], tag="l2r_ev")
    emit_proj_fmajor(P, M, "l2r", W["L2f_WrT"][0], D, D, bt, bk, evr,
                     wt_keys=[W["L2f_WrT"][1]])
    mix(bt, bk, 8, "mxk")
    def evk(mt, sc, psum, pn):
        P.emit("s", lambda eng, mt=mt, sc=sc, psum=psum: eng.activation(
            at[0][:, 0:512], psum[:], AF.Relu),
            reads=[pn], writes=["A0"], tag="l2k_ev1")
        P.emit("v", lambda eng, mt=mt, sc=sc: eng.tensor_mul(
            ct[mt][:, sc * 512:(sc + 1) * 512], at[0][:, 0:512],
            at[0][:, 0:512]),
            reads=["A0"], writes=["C%d" % mt], tag="l2k_ev2")
    emit_proj_fmajor(P, M, "l2k", W["L2f_WkT"][0], D, D, bt, bk, evk,
                     wt_keys=[W["L2f_WkT"][1]])
    def evv(mt, sc, psum, pn):
        P.emit("v", lambda eng, mt=mt, sc=sc, psum=psum: eng.tensor_mul(
            at[0][:, 0:512], psum[:], ut[mt][:, sc * 512:(sc + 1) * 512]),
            reads=[pn, "U%d" % mt], writes=["A0"], tag="l2v_ev1")
        P.emit("v", lambda eng, mt=mt, sc=sc: eng.tensor_add(
            xt[mt][:, sc * 512:(sc + 1) * 512],
            xt[mt][:, sc * 512:(sc + 1) * 512], at[0][:, 0:512]),
            reads=["A0", xk[mt]], writes=[xk[mt]], tag="l2v_ev2")
    emit_proj_fmajor(P, M, "l2v", W["L2f_WvT"][0], D, D,
                     [ct[i][:, 0:2048] for i in range(NT)], keys("C", NT),
                     evv, wt_keys=[W["L2f_WvT"][1]])


STAGE_EMITTERS[5] = emit_stage_l2_ffn



def emit_stage_l3_attn(P, M, W):
    xt = M.tiles("X", NT)
    xk = keys("X", NT)
    bt = M.tiles("B", 4)
    lk = keys("B", 4)
    a0, a1, a2 = M.sb("A0", [128, S], BF16), M.sb("A1", [128, S], BF16), \
        M.sb("A2", [128, S], BF16)
    ck = M.sb("BCA", [128, S], BF16)[:, 0:S]
    sk = M.sb("BCB", [128, S], BF16)[:, 0:S]
    ctiles = c_area(M)
    p4, p5 = M.ps("P4", [128, 512]), M.ps("P5", [128, 512])
    wpa = M.sb("WPA", [128, 4096], BF16)
    wpb = M.sb("WPB", [128, 4096], BF16)

    P.emit("g", lambda eng: eng.dma_start(out=ck, in_=W["ropeC"][0]),
           reads=[W["ropeC"][1]], writes=["BCA"], tag="ropec")
    P.emit("g", lambda eng: eng.dma_start(out=sk, in_=W["ropeS"][0]),
           reads=[W["ropeS"][1]], writes=["BCB"], tag="ropes")

    emit_load_wchunk(P, wpa, "WPA", W["L3_wd"][0], [W["L3_wd"][1]], NT, 0, 512)
    for mt in range(4):
        for sc in range(4):
            pn = ["P4", "P5"][(mt * 4 + sc) % 2]
            psum = [p4, p5][(mt * 4 + sc) % 2]
            for kc in range(NT):
                def fn(eng, kc=kc, mt=mt, sc=sc, psum=psum):
                    return eng.matmul(
                        psum[:], wpa[:, kc * 512 + mt * 128:
                                     kc * 512 + (mt + 1) * 128],
                        xt[kc][:, sc * 512:(sc + 1) * 512],
                        start=(kc == 0), stop=(kc == NT - 1))
                P.emit("t", fn, reads=["WPA", xk[kc]], writes=[pn],
                       tag="l3lat")
            P.emit("s", lambda eng, mt=mt, sc=sc, psum=psum: eng.activation(
                bt[mt][:, sc * 512:(sc + 1) * 512], psum[:], AF.Copy),
                reads=[pn], writes=[lk[mt]], tag="l3lat_ev")

    emit_load_wchunk(P, wpa, "WPA", W["L3_wvup"][0], [W["L3_wvup"][1]], 4, 0,
                     512)
    emit_load_wchunk(P, wpb, "WPB", W["L3_wvup"][0], [W["L3_wvup"][1]], 4, 512,
                     512)
    for kt in range(STL):
        P.emit("v", lambda eng, kt=kt: eng.memset(
            va_ap(ctiles, kt).rearrange("p (h c) -> p h c", c=65)[:, :, 64:65],
            1.0), reads=[], writes=["C%d" % (kt // 2)], tag="vones")
    for st in range(STL):
        for ec in range(2):
            pn = ["P4", "P5"][(st * 2 + ec) % 2]
            psum = [p4, p5][(st * 2 + ec) % 2]
            wsrc, wkey = (wpa, "WPA") if ec == 0 else (wpb, "WPB")
            for kc in range(4):
                def fn(eng, st=st, kc=kc, psum=psum, wsrc=wsrc):
                    return eng.matmul(
                        psum[:], bt[kc][:, st * 128:(st + 1) * 128],
                        wsrc[:, kc * 512:(kc + 1) * 512],
                        start=(kc == 0), stop=(kc == 3))
                P.emit("t", fn, reads=[lk[kc], wkey], writes=[pn], tag="l3v")
            def fne(eng, st=st, ec=ec, psum=psum):
                dst = va_ap(ctiles, st).rearrange("p (h c) -> p h c", c=65)[
                    :, 8 * ec:8 * ec + 8, 0:64]
                src = psum[:].rearrange("p (h c) -> p h c", c=64)
                return eng.activation(dst, src, AF.Copy)
            P.emit("s", fne, reads=[pn], writes=["C%d" % (st // 2)],
                   tag="l3v_ev")

    qwb = M.sb("WPA", [128, 4096], BF16)[:, 0:1024]
    kwb = M.sb("WPA", [128, 4096], BF16)[:, 1024:1536]

    def proj_qk(pair):
        emit_load_wchunk(P, qwb, "WPA", W["L3_wq"][0], [W["L3_wq"][1]], NT,
                         pair * 128, 128)
        emit_load_wchunk(P, kwb, "WPA", W["L3_wkup"][0], [W["L3_wkup"][1]], 4,
                         pair * 128, 128)
        for sc in range(4):
            psum = p4
            for kc in range(NT):
                def fn(eng, kc=kc, sc=sc, psum=psum):
                    return eng.matmul(
                        psum[:], qwb[:, kc * 128:(kc + 1) * 128],
                        xt[kc][:, sc * 512:(sc + 1) * 512],
                        start=(kc == 0), stop=(kc == NT - 1))
                P.emit("t", fn, reads=["WPA", xk[kc]], writes=["P4"],
                       tag="l3q")
            P.emit("s", lambda eng, sc=sc, psum=psum: eng.activation(
                a0[:, sc * 512:(sc + 1) * 512], psum[:], AF.Copy, scale=0.125),
                reads=["P4"], writes=["A0"], tag="l3q_ev")
        for sc in range(4):
            psum = p5
            for kc in range(4):
                def fn(eng, kc=kc, sc=sc, psum=psum):
                    return eng.matmul(
                        psum[:], kwb[:, kc * 128:(kc + 1) * 128],
                        bt[kc][:, sc * 512:(sc + 1) * 512],
                        start=(kc == 0), stop=(kc == 3))
                P.emit("t", fn, reads=["WPA", lk[kc]], writes=["P5"],
                       tag="l3k")
            P.emit("s", lambda eng, sc=sc, psum=psum: eng.activation(
                a1[:, sc * 512:(sc + 1) * 512], psum[:], AF.Copy),
                reads=["P5"], writes=["A1"], tag="l3k_ev")
        for (dst, dkey) in [(a0, "A0"), (a1, "A1")]:
            P.emit("v", lambda eng, dst=dst: eng.stream_shuffle(
                a2[:], dst[:], [i ^ 1 for i in range(32)]),
                reads=[dkey], writes=["A2"], tag="rope_sh")
            P.emit("v", lambda eng, dst=dst: eng.tensor_mul(dst[:], dst[:], ck),
                   reads=[dkey, "BCA"], writes=[dkey], tag="rope_c")
            P.emit("v", lambda eng: eng.tensor_mul(a2[:], a2[:], sk),
                   reads=["A2", "BCB"], writes=["A2"], tag="rope_s")
            P.emit("v", lambda eng, dst=dst: eng.tensor_add(dst[:], dst[:],
                                                            a2[:]),
                   reads=[dkey, "A2"], writes=[dkey], tag="rope_a")

    emit_attention(P, M, "l3", proj_qk, lambda kt: 256 * (kt // 2), False)

    def evac(mt, sc, psum, pn):
        def fn(eng, mt=mt, sc=sc, psum=psum):
            return eng.tensor_add(xt[mt][:, sc * 512:(sc + 1) * 512],
                                  xt[mt][:, sc * 512:(sc + 1) * 512],
                                  psum[:])
        P.emit("v", fn, reads=[pn, xk[mt]], writes=[xk[mt]], tag="l3wo_ev")
    emit_proj_fmajor(P, M, "l3wo", W["L3_wo"][0], D, D,
                     M.tiles("U", NT), keys("U", NT), evac,
                     wt_keys=[W["L3_wo"][1]])


STAGE_EMITTERS[6] = emit_stage_l3_attn



def emit_stage_l3_ffn(P, M, W):
    xt = M.tiles("X", NT)
    xk = keys("X", NT)
    ut = M.tiles("U", NT)
    ct = c_area(M)
    ci = [ct[i][:, 0:2048] for i in range(NT)]
    cik = keys("C", NT)
    mxt = M.sb("U3", [128, S], BF16)
    sclx = M.sb("T32A", [128, S], F32)
    sclx2 = M.sb("T32B", [128, S], F32)
    sm1 = M.sb("SM1", [128, 64], F32)
    p4, p5 = M.ps("P4", [128, 512]), M.ps("P5", [128, 512])
    wpa = M.sb("WPA", [128, 4096], BF16)
    wpb = M.sb("WPB", [128, 4096], BF16)
    wgq = M.dr("L3WGQ", [D, 4096], BF16)
    wuq = M.dr("L3WUQ", [D, 4096], BF16)
    wdq = M.dr("L3WDQ", [4096, D], BF16)

    emit_weight_quant(P, M, "l3wg", W["L3f_wgT"][0], W["L3f_wgT"][1], D, 4096,
                      wgq, "L3WGQ")
    P.emit("v", lambda eng: eng.tensor_copy(sm1[0:1, 8:9], sm1[0:1, 2:3]),
           reads=["SM1"], writes=["SM1"], tag="mwg")
    emit_weight_quant(P, M, "l3wu", W["L3f_wuT"][0], W["L3f_wuT"][1], D, 4096,
                      wuq, "L3WUQ")
    P.emit("v", lambda eng: eng.tensor_copy(sm1[0:1, 9:10], sm1[0:1, 2:3]),
           reads=["SM1"], writes=["SM1"], tag="mwu")
    emit_weight_quant(P, M, "l3wd", W["L3f_wdT"][0], W["L3f_wdT"][1], 4096, D,
                      wdq, "L3WDQ")
    P.emit("v", lambda eng: eng.tensor_copy(sm1[0:1, 10:11], sm1[0:1, 2:3]),
           reads=["SM1"], writes=["SM1"], tag="mwd")
    P.emit("v", lambda eng: eng.tensor_mul(sm1[0:1, 12:13], sm1[0:1, 8:9],
                                           sm1[0:1, 9:10]),
           reads=["SM1"], writes=["SM1"], tag="mwc1")
    P.emit("v", lambda eng: eng.tensor_mul(sm1[0:1, 12:13], sm1[0:1, 12:13],
                                           sm1[0:1, 10:11]),
           reads=["SM1"], writes=["SM1"], tag="mwc2")
    emit_bcast_row(P, M, "l3mwg", sm1[0:1, 8:9], "SM1", 128, sm1[:, 13:14],
                   "SM1", 1)

    absx = M.sb("U5", [128, S], BF16)
    for i in range(NT):
        P.emit("s", lambda eng, i=i: eng.activation(absx[:], xt[i][:], AF.Abs),
               reads=[xk[i]], writes=["U5"], tag="l3aqa")
        if i == 0:
            P.emit("v", lambda eng: eng.tensor_copy(mxt[:], absx[:]),
                   reads=["U5"], writes=["U3"], tag="l3aq0")
        else:
            P.emit("v", lambda eng: eng.tensor_max(mxt[:], mxt[:], absx[:]),
                   reads=["U3", "U5"], writes=["U3"], tag="l3aq")
    emit_act_quant(P, M, "l3x", S, 0, True, None, None)
    for i in range(NT):
        P.emit("v", lambda eng, i=i: eng.tensor_mul(ci[i], xt[i][:], mxt[:]),
               reads=[xk[i], "U3"], writes=[cik[i]], tag="l3xm")
        P.emit("v", lambda eng, i=i: eng.tensor_scalar(
            ci[i], ci[i], MAGIC, MAGIC, ALU.add, ALU.subtract),
            reads=[cik[i]], writes=[cik[i]], tag="l3xr")

    for sc in range(4):
        for mt in range(32):
            def flg(eng, mt=mt):
                src = wgq[:, mt * 128:(mt + 1) * 128].rearrange(
                    "(kc p) c -> p kc c", p=128)
                return eng.dma_start(out=wpa[:, 0:1024], in_=src)
            P.emit("g", flg, reads=["L3WGQ"], writes=["WPA"], tag="wgl")
            def flu(eng, mt=mt):
                src = wuq[:, mt * 128:(mt + 1) * 128].rearrange(
                    "(kc p) c -> p kc c", p=128)
                return eng.dma_start(out=wpb[:, 0:1024], in_=src)
            P.emit("g", flu, reads=["L3WUQ"], writes=["WPB"], tag="wul")
            for kc in range(NT):
                P.emit("t", lambda eng, kc=kc, sc=sc: eng.matmul(
                    p4[:], wpa[:, kc * 128:(kc + 1) * 128],
                    ci[kc][:, sc * 512:(sc + 1) * 512],
                    start=(kc == 0), stop=(kc == NT - 1)),
                    reads=["WPA", cik[kc]], writes=["P4"], tag="l3g")
            for kc in range(NT):
                P.emit("t", lambda eng, kc=kc, sc=sc: eng.matmul(
                    p5[:], wpb[:, kc * 128:(kc + 1) * 128],
                    ci[kc][:, sc * 512:(sc + 1) * 512],
                    start=(kc == 0), stop=(kc == NT - 1)),
                    reads=["WPB", cik[kc]], writes=["P5"], tag="l3u")
            P.emit("v", lambda eng, sc=sc: eng.tensor_mul(
                ut[0][:, 0:512], p4[:], sclx[:, sc * 512:(sc + 1) * 512]),
                reads=["P4", "T32A"], writes=["U0"], tag="l3gev")
            P.emit("v", lambda eng, sc=sc: eng.tensor_mul(
                ut[1][:, 0:512], p5[:], sclx[:, sc * 512:(sc + 1) * 512]),
                reads=["P5", "T32A"], writes=["U1"], tag="l3uev")
            P.emit("s", lambda eng: eng.activation(
                ut[2][:, 0:512], ut[0][:, 0:512], AF.Sigmoid,
                scale=sm1[:, 13:14]),
                reads=["U0", "SM1"], writes=["U2"], tag="l3sg")
            P.emit("v", lambda eng: eng.tensor_mul(
                ut[0][:, 0:512], ut[0][:, 0:512], ut[2][:, 0:512]),
                reads=["U0", "U2"], writes=["U0"], tag="l3gg")
            P.emit("v", lambda eng, mt=mt: eng.tensor_mul(
                ct[mt % 8][:, (mt // 8) * 512:(mt // 8) * 512 + 512]
                if False else ut[4][:, 0:512], ut[0][:, 0:512],
                ut[1][:, 0:512]),
                reads=["U0", "U1"], writes=["U4"], tag="l3gu")
            P.emit("s", lambda eng: eng.activation(
                absx[:, 0:512], ut[4][:, 0:512], AF.Abs),
                reads=["U4"], writes=["U5"], tag="l3ga")
            if mt == 0:
                P.emit("v", lambda eng, sc=sc: eng.tensor_copy(
                    mxt[:, sc * 512:(sc + 1) * 512], absx[:, 0:512]),
                    reads=["U5"], writes=["U3"], tag="l3gm0")
            else:
                P.emit("v", lambda eng, sc=sc: eng.tensor_max(
                    mxt[:, sc * 512:(sc + 1) * 512],
                    mxt[:, sc * 512:(sc + 1) * 512], absx[:, 0:512]),
                    reads=["U3", "U5"], writes=["U3"], tag="l3gm")
            bts = M.sb("B%d" % (mt % 8), [128, S], BF16)
            P.emit("v", lambda eng, mt=mt, bts=bts: eng.tensor_copy(
                bts[:, (mt // 8) * 512:(mt // 8) * 512 + 512],
                ut[4][:, 0:512]),
                reads=["U4"], writes=["B%d" % (mt % 8)], tag="l3gst")
        emit_act_quant(P, M, "l3gu%d" % sc, 512, sc * 512, True,
                       sm1[0:1, 12:13], None, sclx_name="T32B")
        for i in range(NT):
            bts = M.sb("B%d" % i, [128, S], BF16)
            for j in range(4):
                P.emit("v", lambda eng, i=i, j=j, sc=sc, bts=bts:
                       eng.tensor_mul(
                           bts[:, j * 512:(j + 1) * 512],
                           bts[:, j * 512:(j + 1) * 512],
                           mxt[:, sc * 512:(sc + 1) * 512]),
                       reads=["B%d" % i, "U3"], writes=["B%d" % i],
                       tag="l3gqm")
            P.emit("v", lambda eng, i=i, bts=bts: eng.tensor_scalar(
                bts[:], bts[:], MAGIC, MAGIC, ALU.add, ALU.subtract),
                reads=["B%d" % i], writes=["B%d" % i], tag="l3gqr")
        def loadwd(dt, par):
            wb = [wpa, wpb][par]
            def fn(eng, dt=dt, wb=wb):
                src = wdq[:, dt * 128:(dt + 1) * 128].rearrange(
                    "(kc p) c -> p kc c", p=128)
                return eng.dma_start(out=wb[:, 0:4096], in_=src)
            P.emit("g", fn, reads=["L3WDQ"], writes=[["WPA", "WPB"][par]],
                   tag="wdl")
        loadwd(0, 0)
        for dt in range(NT):
            par = dt % 2
            if dt + 1 < NT:
                loadwd(dt + 1, 1 - par)
            wb = [wpa, wpb][par]
            wk = ["WPA", "WPB"][par]
            pn = ["P4", "P5"][dt % 2]
            psum = [p4, p5][dt % 2]
            for kc in range(32):
                bts = M.sb("B%d" % (kc % 8), [128, S], BF16)
                def fn(eng, kc=kc, wb=wb, psum=psum, bts=bts):
                    return eng.matmul(
                        psum[:], wb[:, kc * 128:(kc + 1) * 128],
                        bts[:, (kc // 8) * 512:(kc // 8) * 512 + 512],
                        start=(kc == 0), stop=(kc == 31))
                P.emit("t", fn, reads=[wk, "B%d" % (kc % 8)], writes=[pn],
                       tag="l3dmm")
            P.emit("v", lambda eng, sc=sc, psum=psum: eng.tensor_mul(
                ut[0][:, 0:512], psum[:], sclx2[:, sc * 512:(sc + 1) * 512]),
                reads=[pn, "T32B"], writes=["U0"], tag="l3dev1")
            P.emit("v", lambda eng, dt=dt, sc=sc: eng.tensor_add(
                xt[dt][:, sc * 512:(sc + 1) * 512],
                xt[dt][:, sc * 512:(sc + 1) * 512], ut[0][:, 0:512]),
                reads=["U0", xk[dt]], writes=[xk[dt]], tag="l3dev2")


STAGE_EMITTERS[7] = emit_stage_l3_ffn


# version 05
# speedup vs baseline: 1.1180x; 1.0801x over previous
import math
from contextlib import ExitStack

import numpy as np
import ml_dtypes
import concourse.bass as bass
import concourse.mybir as mybir
from concourse.bass_utils import run_bass_kernel_spmd

F32 = mybir.dt.float32
BF16 = mybir.dt.bfloat16
FP16 = mybir.dt.float16
FP8 = mybir.dt.float8e4
AF = mybir.ActivationFunctionType
ALU = mybir.AluOpType
AX = mybir.AxisListType.X

NCORE = 8
D = 1024
S = 2048
NT = D // 128
STL = S // 128
NH = 16
DK = 64
MAGIC = 12582912.0
NEGBIG = -1.0e30
LN_EPS = 1e-5

_bf = ml_dtypes.bfloat16


class Prog:
    NQ = 8

    def __init__(self, nc):
        self.nc = nc
        self.stream = {e: [] for e in "tvsyg"}
        self.semnames = ["t", "v", "s"] + \
            ["y%d" % i for i in range(self.NQ)] + \
            ["g%d" % i for i in range(self.NQ)]
        self.cnt = {sn: 0 for sn in self.semnames}
        self.dmaidx = {"y": 0, "g": 0}
        self.lastw = {}
        self.readers = {}
        self.hw = {e: {} for e in "tvsyg"}

    def emit(self, eng, fn, reads=(), writes=(), tag=""):
        need = {}
        if eng in "yg":
            slot = self.dmaidx[eng] % self.NQ
            self.dmaidx[eng] += 1
            sn = eng + str(slot)
            if self.cnt[sn]:
                need[sn] = self.cnt[sn]
        else:
            sn = eng
        for b in reads:
            w = self.lastw.get(b)
            if w is not None and not (eng == "t" and w[0] == "t"):
                need[w[0]] = max(need.get(w[0], 0), w[1])
        for b in writes:
            w = self.lastw.get(b)
            if w is not None and not (eng == "t" and w[0] == "t"):
                need[w[0]] = max(need.get(w[0], 0), w[1])
            for s2, c2 in self.readers.get(b, {}).items():
                if not (eng == "t" and s2 == "t"):
                    need[s2] = max(need.get(s2, 0), c2)
        for s2 in sorted(need):
            c2 = need[s2]
            if self.hw[eng].get(s2, 0) >= c2:
                continue
            self.stream[eng].append(("wait", s2, c2))
            self.hw[eng][s2] = c2
        inc = 16 if eng in "yg" else 1
        self.cnt[sn] += inc
        c = self.cnt[sn]
        self.stream[eng].append(("op", fn, sn))
        for b in writes:
            self.lastw[b] = (sn, c)
            self.readers[b] = {}
        for b in reads:
            self.readers.setdefault(b, {})[sn] = c

    def finalize(self, ctx):
        nc = self.nc
        sems = {sn: ctx.enter_context(nc.semaphore("sem_" + sn))
                for sn in self.semnames}
        block = ctx.enter_context(nc.Block())
        streams, cnt = self.stream, self.cnt

        def replay(e):
            inc = 16 if e in "yg" else 1

            def body(eng):
                for item in streams[e]:
                    if item[0] == "wait":
                        eng.wait_ge(sems[item[1]], item[2])
                    else:
                        item[1](eng).then_inc(sems[item[2]], inc)
                if e == "g":
                    for sn in self.semnames:
                        if cnt[sn]:
                            eng.wait_ge(sems[sn], cnt[sn])
            return body

        block.tensor(replay("t"))
        block.vector(replay("v"))
        block.scalar(replay("s"))
        block.sync(replay("y"))
        block.gpsimd(replay("g"))


class Mem:
    def __init__(self, nc, ctx):
        self.nc = nc
        self.ctx = ctx
        self.t = {}
        self.dram = {}

    def sb(self, name, shape, dtype):
        if name not in self.t:
            self.t[name] = self.ctx.enter_context(
                self.nc.sbuf_tensor(name, shape, dtype))
        return self.t[name]

    def ps(self, name, shape, dtype=F32):
        if name not in self.t:
            self.t[name] = self.ctx.enter_context(
                self.nc.psum_tensor(name, shape, dtype))
        return self.t[name]

    def dr(self, name, shape, dtype=F32):
        if name not in self.dram:
            self.dram[name] = self.nc.dram_tensor(name, shape, dtype,
                                                  kind="Internal")
        return self.dram[name].ap()

    def tiles(self, base, n, cols=S, dtype=BF16):
        return [self.sb("%s%d" % (base, i), [128, cols], dtype)
                for i in range(n)]


def keys(base, n):
    return ["%s%d" % (base, i) for i in range(n)]



def emit_load_wchunk(P, dst, dst_key, src_ap, src_keys, kt, c0, cols):
    def fn(eng):
        src = src_ap[:, c0:c0 + cols]
        src3 = src.rearrange("(kc p) c -> p kc c", p=128)
        return eng.dma_start(out=dst[:, 0:kt * cols], in_=src3)
    P.emit("g", fn, reads=list(src_keys), writes=[dst_key], tag="wload")


def emit_proj_fmajor(P, M, name, wt_ap, Kdim, Mout, rhs_tiles, rhs_keys, evac,
                     wchunk=512, wdtype=BF16, wt_keys=(), pbanks=("P4", "P5")):
    kt = Kdim // 128
    assert kt * wchunk <= 4096 and wdtype == BF16
    wb = [M.sb("WPA", [128, 4096], BF16), M.sb("WPB", [128, 4096], BF16)]
    wk = ["WPA", "WPB"]
    nchunk = (Mout + wchunk - 1) // wchunk
    emit_load_wchunk(P, wb[0], wk[0], wt_ap, wt_keys, kt, 0, min(wchunk, Mout))
    pidx = 0
    for ci in range(nchunk):
        par = ci % 2
        if ci + 1 < nchunk:
            c0n = (ci + 1) * wchunk
            emit_load_wchunk(P, wb[1 - par], wk[1 - par], wt_ap, wt_keys, kt,
                             c0n, min(wchunk, Mout - c0n))
        ntile = min(wchunk, Mout - ci * wchunk) // 128
        for mt in range(ntile):
            for sc in range(4):
                pn = pbanks[pidx % len(pbanks)]
                pidx += 1
                psum = M.ps(pn, [128, 512])
                for kc in range(kt):
                    def fn(eng, kc=kc, mt=mt, sc=sc, par=par, psum=psum):
                        lhsT = wb[par][:, kc * wchunk + mt * 128:
                                       kc * wchunk + (mt + 1) * 128]
                        rhs = rhs_tiles[kc][:, sc * 512:(sc + 1) * 512]
                        return eng.matmul(psum[:], lhsT, rhs,
                                          start=(kc == 0), stop=(kc == kt - 1))
                    P.emit("t", fn, reads=[wk[par], rhs_keys[kc]], writes=[pn],
                           tag=name + "mm")
                evac(ci * (wchunk // 128) + mt, sc, psum, pn)


def emit_bcast_row(P, M, name, src_row_ap, src_key, npart, dst_ap, dst_key,
                   ncols):
    dr = M.dr("dsc_" + name, [1, ncols], F32)
    dk = "dsc_" + name
    P.emit("y", lambda eng: eng.dma_start(out=dr[:, 0:ncols], in_=src_row_ap),
           reads=[src_key], writes=[dk], tag="bco")
    P.emit("g", lambda eng: eng.dma_start(
        out=dst_ap, in_=dr[0:1, 0:ncols].partition_broadcast(npart)),
        reads=[dk], writes=[dst_key], tag="bci")


def sq_views(M):
    t32 = [M.sb("T32A", [128, S], F32), M.sb("T32B", [128, S], F32)]
    return ([t32[0].bitcast(BF16)[:, 0:S], t32[1].bitcast(BF16)[:, 0:S]],
            ["T32A", "T32B"])


def c_area(M):
    ct = [M.sb("C%d" % i, [128, 2080], BF16) for i in range(NT)]
    return ct


def va_ap(ct, kt):
    return ct[kt // 2][:, (kt % 2) * 1040:(kt % 2) * 1040 + 1040]


def fq8_views(M):
    ut = M.tiles("U", NT)
    aps = []
    ks = []
    for i in range(NT):
        host = ut[2 + i // 2]
        aps.append(host.bitcast(FP8)[:, (i % 2) * 2048:(i % 2) * 2048 + 2048])
        ks.append("U%d" % (2 + i // 2))
    return aps, ks


def h8_views(M):
    ut = M.tiles("U", NT)
    b0 = M.sb("B0", [128, S], BF16)
    b1 = M.sb("B1", [128, S], BF16)
    hosts = [(ut[6], "U6"), (ut[7], "U7"), (b0, "B0"), (b1, "B1")]
    aps, ks = [], []
    for i in range(32):
        h, k = hosts[i // 8]
        aps.append(h.bitcast(FP8)[:, (i % 8) * 512:(i % 8) * 512 + 512])
        ks.append(k)
    return aps, ks


def emit_ln(P, M, x_tiles, x_keys, eps=LN_EPS, out_tiles=None, out_keys=None,
            fq8=False, f32out=None):
    ones = M.sb("ONESC", [128, 1], BF16)
    p6 = M.ps("P6", [128, 512])
    pb = M.ps("PB", [128, 2048])
    r16a = M.sb("R16A", [16, S], F32)
    r16b = M.sb("R16B", [16, S], F32)
    bc1 = M.sb("BCA", [128, S], BF16)
    bc2 = M.sb("BCB", [128, S], BF16)
    t32 = [M.sb("T32A", [128, S], F32), M.sb("T32B", [128, S], F32)]
    sq, sqk = sq_views(M)
    if out_tiles is None:
        out_tiles, out_keys = x_tiles, x_keys
    if fq8:
        q8, q8k = fq8_views(M)
    for sc in range(4):
        for i in range(NT):
            def fn(eng, i=i, sc=sc):
                return eng.matmul(p6[0:1, 0:512], ones[:, 0:1],
                                  x_tiles[i][:, sc * 512:(sc + 1) * 512],
                                  start=(i == 0), stop=(i == NT - 1))
            P.emit("t", fn, reads=["ONESC", x_keys[i]], writes=["P6"],
                   tag="ln_sum")
        P.emit("v", lambda eng, sc=sc: eng.tensor_scalar(
            r16a[0:1, sc * 512:(sc + 1) * 512], p6[0:1, 0:512],
            1.0 / D, None, ALU.mult),
            reads=["P6"], writes=["R16A"], tag="ln_mean")
    for i in range(NT):
        P.emit("s", lambda eng, i=i: eng.activation(
            sq[i % 2], x_tiles[i][:], AF.Square),
            reads=[x_keys[i]], writes=[sqk[i % 2]], tag="ln_sq")
        for sc in range(4):
            def fn(eng, i=i, sc=sc):
                return eng.matmul(pb[0:1, sc * 512:(sc + 1) * 512], ones[:, 0:1],
                                  sq[i % 2][:, sc * 512:(sc + 1) * 512],
                                  start=(i == 0), stop=(i == NT - 1))
            P.emit("t", fn, reads=["ONESC", sqk[i % 2]],
                   writes=["PB%d" % sc], tag="ln_ssq")
    P.emit("v", lambda eng: eng.tensor_scalar(
        r16b[0:1, :], pb[0:1, 0:2048], 1.0 / D, None, ALU.mult),
        reads=pbkeys(), writes=["R16B"], tag="ln_msq")
    P.emit("v", lambda eng: eng.tensor_mul(
        t32[0][0:1, 0:2048], r16a[0:1, :], r16a[0:1, :]),
        reads=["R16A"], writes=["T32A"], tag="ln_m2")
    P.emit("v", lambda eng: eng.scalar_tensor_tensor(
        r16b[0:1, :], r16b[0:1, :], float(eps), t32[0][0:1, 0:2048],
        ALU.add, ALU.subtract),
        reads=["R16B", "T32A"], writes=["R16B"], tag="ln_var")
    P.emit("v", lambda eng: eng.reciprocal(r16b[0:1, :], r16b[0:1, :]),
           reads=["R16B"], writes=["R16B"], tag="ln_rec")
    P.emit("s", lambda eng: eng.activation(r16b[0:1, :], r16b[0:1, :], AF.Sqrt),
           reads=["R16B"], writes=["R16B"], tag="ln_rstd")
    emit_bcast_row(P, M, "lnm", r16a[0:1, :], "R16A", 128, bc1[:], "BCA", S)
    emit_bcast_row(P, M, "lnr", r16b[0:1, :], "R16B", 128, bc2[:], "BCB", S)
    for i in range(NT):
        tk = "T32%s" % ("A" if i % 2 == 0 else "B")
        tt = t32[i % 2]
        P.emit("v", lambda eng, i=i, tt=tt: eng.tensor_sub(
            tt[:], x_tiles[i][:], bc1[:]),
            reads=[x_keys[i], "BCA"], writes=[tk], tag="ln_a1")
        if fq8:
            P.emit("v", lambda eng, i=i, tt=tt: eng.tensor_mul(
                q8[i], tt[:], bc2[:]),
                reads=[tk, "BCB"], writes=[q8k[i]], tag="ln_a8")
        if f32out is not None:
            P.emit("v", lambda eng, i=i, tt=tt: eng.tensor_mul(
                tt[:], tt[:], bc2[:]),
                reads=[tk, "BCB"], writes=[tk], tag="ln_a32")
            P.emit("g", lambda eng, i=i, tt=tt: eng.dma_start(
                out=f32out[i * 128:(i + 1) * 128, :], in_=tt[:]),
                reads=[tk], writes=["__out__"], tag="ln_oy")
            P.emit("v", lambda eng, i=i, tt=tt: eng.tensor_copy(
                out_tiles[i][:], tt[:]),
                reads=[tk], writes=[out_keys[i]], tag="ln_a2c")
        else:
            P.emit("v", lambda eng, i=i, tt=tt: eng.tensor_mul(
                out_tiles[i][:], tt[:], bc2[:]),
                reads=[tk, "BCB"], writes=[out_keys[i]], tag="ln_a2")


def pbkeys(c0=0, c1=2048):
    return ["PB%d" % i for i in range(c0 // 512, (c1 + 511) // 512)]



def emit_attention(P, M, name, proj_qk, kstart_fn, masked_diag):
    a0, a1 = M.sb("A0", [128, S], BF16), M.sb("A1", [128, S], BF16)
    sq, sqk = sq_views(M)
    sqf0 = M.sb("T32A", [128, S], F32)
    idn = M.sb("IDN", [128, 128], BF16)
    mtri = M.sb("MTRI", [128, 512], BF16)
    negb = M.sb("NEGB", [128, 16], F32)
    r16a = M.sb("R16A", [16, S], F32)
    r16b = M.sb("R16B", [16, S], F32)
    bc1 = M.sb("A2", [128, S], BF16)
    bo = M.sb("BO16", [128, 16], BF16)
    pb = M.ps("PB", [128, 2048])
    p4, p5 = M.ps("P4", [128, 512]), M.ps("P5", [128, 512])
    pq = M.ps("P6", [128, 512])
    ctiles = c_area(M)
    utiles = M.tiles("U", NT)
    pbuf = [M.sb("PF0", [128, 512], BF16), M.sb("PF1", [128, 512], BF16)]

    for pair in range(NH // 2):
        proj_qk(pair)
        for t, (src, skey) in enumerate([(a0, "A0"), (a1, "A1")]):
            dstrow = r16a if t == 0 else r16b
            dkey = "R16A" if t == 0 else "R16B"
            P.emit("s", lambda eng, src=src: eng.activation(
                sq[0], src[:], AF.Square),
                reads=[skey], writes=[sqk[0]], tag="bn_sq")
            for sc in range(4):
                def fn(eng, sc=sc, pair=pair):
                    return eng.matmul(pq[0:2, 0:512], bo[:, 2 * pair:2 * pair + 2],
                                      sq[0][:, sc * 512:(sc + 1) * 512],
                                      start=True, stop=True)
                P.emit("t", fn, reads=["BO16", "T32A"], writes=["P6"], tag="bn_mm")
                P.emit("v", lambda eng, sc=sc, dstrow=dstrow: eng.tensor_copy(
                    dstrow[0:2, sc * 512:(sc + 1) * 512], pq[0:2, 0:512]),
                    reads=["P6"], writes=[dkey], tag="bn_cp")
        P.emit("v", lambda eng: eng.reduce_max(
            sqf0[0:2, 0:1], r16a[0:2, :], axis=AX),
            reads=["R16A"], writes=["T32A"], tag="bn_rmax")
        P.emit("v", lambda eng: eng.reduce_max(
            sqf0[0:2, 1:2], r16b[0:2, :], axis=AX),
            reads=["R16B"], writes=["T32A"], tag="bn_rmax2")
        P.emit("v", lambda eng: eng.tensor_mul(
            sqf0[0:2, 2:3], sqf0[0:2, 0:1],
            sqf0[0:2, 1:2]),
            reads=["T32A"], writes=["T32A"], tag="bn_mul")
        P.emit("s", lambda eng: eng.activation(
            sqf0[0:2, 2:3], sqf0[0:2, 2:3], AF.Sqrt),
            reads=["T32A"], writes=["T32A"], tag="bn_sqrt")
        P.emit("v", lambda eng: eng.tensor_scalar(
            sqf0[0:2, 2:3], sqf0[0:2, 2:3],
            -1.0, -1.0, ALU.add, ALU.mult),
            reads=["T32A"], writes=["T32A"], tag="bn_neg")
        dnb = M.dr("dnb", [1, 16], F32)
        P.emit("y", lambda eng, pair=pair: eng.dma_start(
            out=dnb[0:1, 2 * pair:2 * pair + 2],
            in_=sqf0[0:2, 2:3]),
            reads=["T32A"], writes=["dnb"], tag="bn_out")
        P.emit("g", lambda eng, pair=pair: eng.dma_start(
            out=negb[:, 2 * pair:2 * pair + 2],
            in_=dnb[0:1, 2 * pair:2 * pair + 2].partition_broadcast(128)),
            reads=["dnb"], writes=["NEGB"], tag="bn_in")
        for hh in range(2):
            h = 2 * pair + hh
            qrow = a0[64 * hh:64 * (hh + 1), :]
            krow = a1[64 * hh:64 * (hh + 1), :]
            sidx = 0
            for kt in range(STL):
                q0 = kstart_fn(kt)
                bounds = []
                c = q0
                while c < S:
                    e = min(S, (c // 512 + 1) * 512)
                    bounds.append((c, e))
                    c = e
                for (c0, c1) in bounds:
                    w = c1 - c0
                    pS = [p4, p5][sidx % 2]
                    pSk = ["P4", "P5"][sidx % 2]
                    sidx += 1
                    first = (c0 == q0 and masked_diag)
                    if first:
                        def fnm(eng, c0=c0, w=w, pS=pS):
                            return eng.matmul(pS[:, 0:w], idn[:],
                                              mtri[:, 0:w], start=True,
                                              stop=False)
                        P.emit("t", fnm, reads=["IDN", "MTRI"], writes=[pSk],
                               tag="attn_mask")
                    def fns(eng, kt=kt, c0=c0, w=w, pS=pS, krow=krow,
                            qrow=qrow, first=first):
                        return eng.matmul(
                            pS[:, 0:w], krow[:, kt * 128:(kt + 1) * 128],
                            qrow[:, c0:c0 + w],
                            start=not first, stop=True)
                    P.emit("t", fns, reads=["A1", "A0"], writes=[pSk],
                           tag="attn_sc")
                    pf = pbuf[sidx % 2]
                    pfk = "PF%d" % (sidx % 2)
                    def fne(eng, w=w, pS=pS, pf=pf, h=h):
                        return eng.activation(pf[:, 0:w], pS[:, 0:w], AF.Exp,
                                              bias=negb[:, h:h + 1])
                    P.emit("s", fne, reads=[pSk, "NEGB"], writes=[pfk],
                           tag="attn_exp")
                    kt_last = (c1 - 1) // 128
                    def fnv(eng, kt=kt, c0=c0, w=w, pf=pf, h=h,
                            kt_last=kt_last):
                        va = va_ap(ctiles, kt)
                        return eng.matmul(
                            pb[0:65, c0:c0 + w],
                            va[:, 65 * h:65 * h + 65], pf[:, 0:w],
                            start=(kt == 0), stop=(kt == kt_last))
                    P.emit("t", fnv, reads=["C%d" % (kt // 2), pfk],
                           writes=pbkeys(c0, c1), tag="attn_av")
            drs = M.dr(name + "_sums", [16, S], F32)
            P.emit("s", lambda eng: eng.activation(
                r16b[0:1, :], pb[64:65, 0:2048], AF.Copy),
                reads=pbkeys(), writes=["R16B"], tag="attn_sum")
            P.emit("y", lambda eng, h=h: eng.dma_start(
                out=drs[h:h + 1, :], in_=r16b[0:1, :]),
                reads=["R16B"], writes=[name + "_sums"], tag="attn_so")
            def fnev(eng, hh=hh, h=h):
                return eng.tensor_copy(
                    utiles[h // 2][64 * hh:64 * (hh + 1), :],
                    pb[0:64, 0:2048])
            P.emit("v", fnev, reads=pbkeys(),
                   writes=["U%d" % (h // 2)], tag="attn_ev")
    drs = M.dr(name + "_sums", [16, S], F32)
    P.emit("g", lambda eng: eng.dma_start(out=r16a[0:16, :], in_=drs[:, :]),
           reads=[name + "_sums"], writes=["R16A"], tag="attn_rl")
    P.emit("v", lambda eng: eng.reciprocal(r16a[0:16, :], r16a[0:16, :]),
           reads=["R16A"], writes=["R16A"], tag="attn_rec")
    P.emit("y", lambda eng: eng.dma_start(out=drs[:, :], in_=r16a[0:16, :]),
           reads=["R16A"], writes=[name + "_sums"], tag="attn_ro")
    for p in range(8):
        P.emit("g", lambda eng, p=p: eng.dma_start(
            out=bc1[0:64, :],
            in_=drs[2 * p:2 * p + 1, :].partition_broadcast(64)),
            reads=[name + "_sums"], writes=["A2"], tag="attn_x0")
        P.emit("g", lambda eng, p=p: eng.dma_start(
            out=bc1[64:128, :],
            in_=drs[2 * p + 1:2 * p + 2, :].partition_broadcast(64)),
            reads=[name + "_sums"], writes=["A2"], tag="attn_x1")
        P.emit("v", lambda eng, p=p: eng.tensor_mul(
            utiles[p][:], utiles[p][:], bc1[:]),
            reads=["U%d" % p, "A2"], writes=["U%d" % p], tag="attn_nm")



def emit_stage_l0_attn(P, M, W):
    nc = P.nc
    xt = M.tiles("X", NT)
    xk = keys("X", NT)
    ut = M.tiles("U", NT)
    a0, a1, a2 = M.sb("A0", [128, S], BF16), M.sb("A1", [128, S], BF16), \
        M.sb("A2", [128, S], BF16)
    ck = M.sb("BCA", [128, S], BF16)[:, 0:S]
    sk = M.sb("BCB", [128, S], BF16)[:, 0:S]
    ctiles = c_area(M)
    p4 = M.ps("P4", [128, 512])
    p5 = M.ps("P5", [128, 512])

    wvb = M.sb("WPA", [128, 4096], BF16)
    emit_load_wchunk(P, wvb, "WPA", W["L0_wvT"][0], [W["L0_wvT"][1]], NT, 0, 512)
    wvb2 = M.sb("WPB", [128, 4096], BF16)
    emit_load_wchunk(P, wvb2, "WPB", W["L0_wvT"][0], [W["L0_wvT"][1]], NT,
                     512, 512)
    for kt in range(STL):
        P.emit("v", lambda eng, kt=kt: eng.memset(
            va_ap(ctiles, kt).rearrange("p (h c) -> p h c", c=65)[:, :, 64:65],
            1.0), reads=[], writes=["C%d" % (kt // 2)], tag="vones")
    p6v = M.ps("P6", [128, 512])
    for st in range(STL):
        for ec in range(2):
            pn = ["P4", "P5", "P6"][(st * 2 + ec) % 3]
            psum = [p4, p5, p6v][(st * 2 + ec) % 3]
            wsrc, wkey = (wvb, "WPA") if ec == 0 else (wvb2, "WPB")
            for kc in range(NT):
                def fn(eng, st=st, ec=ec, kc=kc, psum=psum, wsrc=wsrc):
                    return eng.matmul(
                        psum[:], xt[kc][:, st * 128:(st + 1) * 128],
                        wsrc[:, kc * 512:(kc + 1) * 512],
                        start=(kc == 0), stop=(kc == NT - 1))
                P.emit("t", fn, reads=[xk[kc], wkey], writes=[pn], tag="l0v")
            def fne(eng, st=st, ec=ec, psum=psum):
                dst = va_ap(ctiles, st).rearrange("p (h c) -> p h c", c=65)[
                    :, 8 * ec:8 * ec + 8, 0:64]
                src = psum[:].rearrange("p (h c) -> p h c", c=64)
                return eng.activation(dst, src, AF.Copy)
            P.emit("s", fne, reads=[pn], writes=["C%d" % (st // 2)],
                   tag="l0v_ev")

    P.emit("g", lambda eng: eng.dma_start(out=ck, in_=W["ropeC"][0]),
           reads=[W["ropeC"][1]], writes=["BCA"], tag="ropec")
    P.emit("g", lambda eng: eng.dma_start(out=sk, in_=W["ropeS"][0]),
           reads=[W["ropeS"][1]], writes=["BCB"], tag="ropes")

    qwb = M.sb("WPA", [128, 4096], BF16)[:, 0:1024]
    kwb = M.sb("WPA", [128, 4096], BF16)[:, 1024:2048]

    def proj_qk(pair):
        emit_load_wchunk(P, qwb, "WPA", W["L0_wqT"][0], [W["L0_wqT"][1]], NT,
                         pair * 128, 128)
        emit_load_wchunk(P, kwb, "WPA", W["L0_wkT"][0], [W["L0_wkT"][1]], NT,
                         pair * 128, 128)
        for (wb, wkey, dst, dkey, scl) in [
                (qwb, "WPA", a0, "A0", 0.125), (kwb, "WPA", a1, "A1", 1.0)]:
            for sc in range(4):
                pn = "P4" if dkey == "A0" else "P5"
                psum = p4 if dkey == "A0" else p5
                for kc in range(NT):
                    def fn(eng, kc=kc, sc=sc, psum=psum, wb=wb):
                        return eng.matmul(
                            psum[:], wb[:, kc * 128:(kc + 1) * 128],
                            xt[kc][:, sc * 512:(sc + 1) * 512],
                            start=(kc == 0), stop=(kc == NT - 1))
                    P.emit("t", fn, reads=[wkey, xk[kc]], writes=[pn],
                           tag="l0qk")
                P.emit("s", lambda eng, sc=sc, psum=psum, dst=dst, scl=scl:
                       eng.activation(dst[:, sc * 512:(sc + 1) * 512],
                                      psum[:], AF.Copy, scale=scl),
                       reads=[pn], writes=[dkey], tag="l0qk_ev")
            P.emit("v", lambda eng, dst=dst: eng.stream_shuffle(
                a2[:], dst[:], [i ^ 1 for i in range(32)]),
                reads=[dkey], writes=["A2"], tag="rope_sh")
            P.emit("v", lambda eng, dst=dst: eng.tensor_mul(dst[:], dst[:], ck),
                   reads=[dkey, "BCA"], writes=[dkey], tag="rope_c")
            P.emit("v", lambda eng: eng.tensor_mul(a2[:], a2[:], sk),
                   reads=["A2", "BCB"], writes=["A2"], tag="rope_s")
            P.emit("v", lambda eng, dst=dst: eng.tensor_add(dst[:], dst[:], a2[:]),
                   reads=[dkey, "A2"], writes=[dkey], tag="rope_a")

    emit_attention(P, M, "l0", proj_qk, lambda kt: kt * 128, True)

    def evac(mt, sc, psum, pn):
        def fn(eng, mt=mt, sc=sc, psum=psum):
            return eng.tensor_add(xt[mt][:, sc * 512:(sc + 1) * 512],
                                  xt[mt][:, sc * 512:(sc + 1) * 512],
                                  psum[:])
        P.emit("v", fn, reads=[pn, xk[mt]], writes=[xk[mt]], tag="l0wo_ev")
    emit_proj_fmajor(P, M, "l0wo", W["L0_woT"][0], D, D,
                     M.tiles("U", NT), keys("U", NT), evac,
                     wt_keys=[W["L0_woT"][1]])


def emit_stage_l0_ffn(P, M, W):
    nc = P.nc
    xt = M.tiles("X", NT)
    xk = keys("X", NT)
    xq8, xq8k = fq8_views(M)
    h8, h8k = h8_views(M)
    ut = M.tiles("U", NT)
    h16 = ut[0].bitcast(FP16)[:, 0:512]
    p4, p5 = M.ps("P4", [128, 512]), M.ps("P5", [128, 512])
    b1q = M.sb("B1Q", [128, 32], FP8)
    b2q = M.sb("B2Q", [128, 8], FP8)
    P.emit("g", lambda eng: eng.dma_start(out=b1q[:], in_=W["L0_b1"][0]),
           reads=[W["L0_b1"][1]], writes=["B1Q"], tag="b1")
    P.emit("g", lambda eng: eng.dma_start(out=b2q[:], in_=W["L0_b2"][0]),
           reads=[W["L0_b2"][1]], writes=["B2Q"], tag="b2")
    b1f = M.sb("B1F", [128, 32], F32)
    b2f = M.sb("B2F", [128, 8], F32)
    P.emit("v", lambda eng: eng.tensor_copy(b1f[:], b1q[:]),
           reads=["B1Q"], writes=["B1F"], tag="b1f")
    P.emit("v", lambda eng: eng.tensor_copy(b2f[:], b2q[:]),
           reads=["B2Q"], writes=["B2F"], tag="b2f")

    wpa = M.sb("WPA", [128, 4096], BF16)
    wpb = M.sb("WPB", [128, 4096], BF16)
    w1b = [wpa.bitcast(FP8)[:, 0:4096], wpb.bitcast(FP8)[:, 0:4096]]
    w1k = ["WPA", "WPB"]
    w2b = w1b
    w2k = w1k
    dm = M.sb("DMHB", [128, 1024], FP16)

    for sc in range(4):
        emit_load_wchunk(P, w1b[0], w1k[0], W["L0_w1T"][0], [W["L0_w1T"][1]],
                         NT, 0, 512)
        for mc in range(8):
            par = mc % 2
            if mc + 1 < 8:
                emit_load_wchunk(P, w1b[1 - par], w1k[1 - par], W["L0_w1T"][0],
                                 [W["L0_w1T"][1]], NT, (mc + 1) * 512, 512)
            for mt in range(4):
                mg = mc * 4 + mt
                pn = ["P4", "P5", "P6"][mg % 3]
                psum = [p4, p5, M.ps("P6", [128, 512])][mg % 3]
                for kc in range(NT):
                    def fn(eng, kc=kc, mt=mt, sc=sc, par=par, psum=psum):
                        return eng.matmul(
                            psum[:], w1b[par][:, kc * 512 + mt * 128:
                                              kc * 512 + (mt + 1) * 128],
                            xq8[kc][:, sc * 512:(sc + 1) * 512],
                            start=(kc == 0), stop=(kc == NT - 1))
                    P.emit("t", fn, reads=[w1k[par], xq8k[kc]], writes=[pn],
                           tag="l0f_mm1")
                def fe1(eng, mg=mg, psum=psum):
                    return eng.activation(h16, psum[:], AF.Relu,
                                          bias=b1f[:, mg:mg + 1])
                P.emit("s", fe1, reads=[pn, "B1F"], writes=["U0"], tag="l0f_h16")
                def fe2(eng, mg=mg):
                    return eng.activation(h8[mg], h16, AF.Copy)
                P.emit("s", fe2, reads=["U0"], writes=[h8k[mg]], tag="l0f_h8")
        emit_load_wchunk(P, w2b[0], w2k[0], W["L0_w2T"][0], [W["L0_w2T"][1]],
                         32, 0, 128)
        for dt in range(NT):
            par = dt % 2
            if dt + 1 < NT:
                emit_load_wchunk(P, w2b[1 - par], w2k[1 - par], W["L0_w2T"][0],
                                 [W["L0_w2T"][1]], 32, (dt + 1) * 128, 128)
            pn = ["P4", "P5", "P6"][dt % 3]
            psum = [p4, p5, M.ps("P6", [128, 512])][dt % 3]
            for kc in range(32):
                def fn(eng, kc=kc, dt=dt, sc=sc, par=par, psum=psum):
                    return eng.matmul(
                        psum[:], w2b[par][:, kc * 128:(kc + 1) * 128],
                        h8[kc][:, 0:512],
                        start=(kc == 0), stop=(kc == 31))
                P.emit("t", fn, reads=[w2k[par], h8k[kc]], writes=[pn],
                       tag="l0f_mm2")
            def fm(eng, dt=dt, sc=sc):
                src = W["L0_dmask"][0][dt * 128:(dt + 1) * 128,
                                       sc * 512:(sc + 1) * 512]
                return eng.dma_start(out=dm[:, 0:512], in_=src)
            P.emit("g", fm, reads=[W["L0_dmask"][1]], writes=["DMHB"],
                   tag="l0f_dm")
            def fe(eng, dt=dt, psum=psum):
                return eng.tensor_scalar(h16, psum[:], b2f[:, dt:dt + 1],
                                         None, ALU.add)
            P.emit("v", fe, reads=[pn, "B2F"], writes=["U0"], tag="l0f_y16")
            def fmu(eng, dt=dt, sc=sc):
                return eng.tensor_mul(ut[1].bitcast(FP16)[:, 0:512], h16,
                                      dm[:, 0:512])
            P.emit("v", fmu, reads=["U0", "DMHB"], writes=["U1"], tag="l0f_msk")
            def fad(eng, dt=dt, sc=sc):
                return eng.tensor_add(
                    xt[dt][:, sc * 512:(sc + 1) * 512],
                    xt[dt][:, sc * 512:(sc + 1) * 512],
                    ut[1].bitcast(FP16)[:, 0:512])
            P.emit("v", fad, reads=["U1", xk[dt]], writes=[xk[dt]],
                   tag="l0f_res")



STAGE_EMITTERS = {}


_DECLARED = {}


def _declare_weight_inputs(nc, stage, host_w):
    W = {}
    decl = _DECLARED.setdefault(id(nc), {})
    for name, arr in host_w[stage].items():
        if name in decl:
            W[name] = decl[name]
            continue
        dt = {np.dtype(np.float32): F32, np.dtype(np.float16): FP16,
              np.dtype(_bf): BF16}[arr.dtype]
        h = nc.dram_tensor("i_" + name, list(arr.shape), dt,
                           kind="ExternalInput")
        W[name] = (h.ap(), "i_" + name)
        decl[name] = W[name]
    return W


def emit_consts(P, M, W):
    ones = M.sb("ONESC", [128, 1], BF16)
    P.emit("v", lambda eng: eng.memset(ones[:], 1.0), reads=(), writes=["ONESC"])
    for cname, sname, shape in [("cIDN", "IDN", [128, 128]),
                                ("cMTRI", "MTRI", [128, 512]),
                                ("cBO16", "BO16", [128, 16])]:
        if cname in W:
            t = M.sb(sname, shape, BF16)
            P.emit("g", lambda eng, t=t, cname=cname: eng.dma_start(
                out=t[:], in_=W[cname][0]),
                reads=[W[cname][1]], writes=[sname], tag="const")


def emit_x_in(P, M, W, fq8):
    xt = M.tiles("X", NT)
    for i in range(NT):
        P.emit("g", lambda eng, i=i: eng.dma_start(
            out=xt[i][:], in_=W["x_in"][0][i * 128:(i + 1) * 128, :]),
            reads=[W["x_in"][1]], writes=["X%d" % i], tag="xin")
    if fq8:
        q8, q8k = fq8_views(M)
        for i in range(NT):
            P.emit("g", lambda eng, i=i: eng.dma_start(
                out=q8[i], in_=W["x_in"][0][i * 128:(i + 1) * 128, :]),
                reads=[W["x_in"][1]], writes=[q8k[i]], tag="xin8")


def build_program(stages, host_w, first, last):
    nc = bass.Bass("TRN2", target_bir_lowering=False)
    ctx = ExitStack()
    M = Mem(nc, ctx)
    P = Prog(nc)
    Wall = {}
    for st in stages:
        Wall.update(_declare_weight_inputs(nc, st, host_w))
    Wall.update(_declare_weight_inputs(nc, "const", host_w))
    if 1 in stages:
        hm = nc.dram_tensor("i_L0_dmask", [D, S], FP16, kind="ExternalInput")
        Wall["L0_dmask"] = (hm.ap(), "i_L0_dmask")
    h = nc.dram_tensor("i_x_in", [D, S], F32, kind="ExternalInput")
    Wall["x_in"] = (h.ap(), "i_x_in")
    ho = nc.dram_tensor("o_y", [D, S], F32, kind="ExternalOutput")
    NEED_FQ8 = {1}
    eps_of = {i: (1e-10 if False else LN_EPS) for i in range(8)}
    with ctx:
        emit_consts(P, M, Wall)
        emit_x_in(P, M, Wall, stages[0] in NEED_FQ8)
        xt = M.tiles("X", NT)
        xk = keys("X", NT)
        for si, st in enumerate(stages):
            STAGE_EMITTERS[st](P, M, Wall)
            nxt = stages[si + 1] if si + 1 < len(stages) else None
            lastln = (si == len(stages) - 1)
            emit_ln(P, M, xt, xk, eps=eps_of[st],
                    fq8=(nxt in NEED_FQ8),
                    f32out=(ho.ap() if lastln else None))
        P.finalize(ctx)
    return nc


STAGE_EMITTERS[0] = emit_stage_l0_attn
STAGE_EMITTERS[1] = emit_stage_l0_ffn



def _ct(a):
    return np.ascontiguousarray(a)


def make_host_weights(inp):
    w = {}
    w[0] = {
        "L0_wqT": _ct(inp["L0_attn_wq"].T),
        "L0_wkT": _ct(inp["L0_attn_wk"].T),
        "L0_wvT": _ct(inp["L0_attn_wv"].T),
        "L0_woT": _ct(inp["L0_attn_wo"].T),
        "ropeC": _rope_table(cos=True),
        "ropeS": _rope_table(cos=False),
    }
    w[2] = {
        "L1_wattnT": _ct(inp["L1_attn_w_attn"].T),
        "L1_bq": _ct(inp["L1_attn_b_attn"][0:1024].reshape(8, 128).T),
        "L1_bk": _ct(inp["L1_attn_b_attn"][1024:2048].reshape(8, 128).T),
        "L1_bv": _ct(inp["L1_attn_b_attn"][2048:3072].reshape(1, 1024)),
        "L1_wprojT": _ct(inp["L1_attn_w_proj"].T),
        "L1_bp": _ct(inp["L1_attn_b_proj"].reshape(8, 128).T),
        "L1_wfcT": _ct(inp["L1_attn_w_fc"].T),
        "L1_bfc": _ct(inp["L1_attn_b_fc"].reshape(32, 128).T),
        "L1_wmlpT": _ct(inp["L1_attn_w_mlp_proj"].T),
        "L1_bm": _ct(inp["L1_attn_b_mlp_proj"].reshape(8, 128).T),
    }
    w[3] = {
        "L1f_w1T": _ct(inp["L1_ffn_w1"].T),
        "L1f_w2T": _ct(inp["L1_ffn_w2"].T),
        "L1f_cw": _ct(inp["L1_ffn_conv_w"].reshape(32, 128, 3)
                      .transpose(1, 0, 2).reshape(128, 96)),
        "L1f_cb": _ct(inp["L1_ffn_conv_b"].reshape(32, 128).T),
    }
    w[4] = {
        "L2_WqT": _ct(inp["L2_attn_Wq"].T),
        "L2_WkT": _ct(inp["L2_attn_Wk"].T),
        "L2_WvT": _ct(inp["L2_attn_Wv"].T),
        "L2_WoT": _ct(inp["L2_attn_Wo"].T),
        "L2_bq": _ct(inp["L2_attn_bq"].reshape(8, 128).T),
        "L2_bk": _ct(inp["L2_attn_bk"].reshape(8, 128).T),
        "L2_bv": _ct(inp["L2_attn_bv"].reshape(1, 1024)),
        "L2_bo": _ct(inp["L2_attn_bo"].reshape(8, 128).T),
    }
    w[5] = {
        "L2f_WrT": _ct(inp["L2_ffn_W_r"].T),
        "L2f_WkT": _ct(inp["L2_ffn_W_k"].T),
        "L2f_WvT": _ct(inp["L2_ffn_W_v"].T),
        "L2_mu": _ct(np.concatenate([
            inp["L2_ffn_mu_r"].reshape(8, 128).T,
            inp["L2_ffn_mu_k"].reshape(8, 128).T], axis=1)),
    }
    w[6] = {
        "L3_wq": _ct(inp["L3_attn_wq"]),
        "L3_wd": _ct(inp["L3_attn_w_down"]),
        "L3_wkup": _ct(inp["L3_attn_wk_up"]),
        "L3_wvup": _ct(inp["L3_attn_wv_up"]),
        "L3_wo": _ct(inp["L3_attn_wo"]),
        "ropeC": _rope_table(cos=True),
        "ropeS": _rope_table(cos=False),
    }
    w[7] = {
        "L3f_wgT": _ct(inp["L3_ffn_w_gate"].T),
        "L3f_wuT": _ct(inp["L3_ffn_w_up"].T),
        "L3f_wdT": _ct(inp["L3_ffn_w_down"].T),
    }
    w[1] = {
        "L0_w1T": _ct(inp["L0_ffn_w1"].T),
        "L0_w2T": _ct(inp["L0_ffn_w2"].T),
        "L0_b1": _ct(inp["L0_ffn_b1"].reshape(32, 128).T),
        "L0_b2": _ct(inp["L0_ffn_b2"].reshape(8, 128).T),
    }
    idn = np.eye(128, dtype=_bf)
    mtri = np.zeros((128, 512), dtype=np.float32)
    for r in range(128):
        mtri[r, :r] = NEGBIG
    bo = np.zeros((128, 16), dtype=np.float32)
    for c in range(16):
        bo[(c % 2) * 64:(c % 2) * 64 + 64, c] = 1.0
    w["const"] = {"cIDN": idn, "cMTRI": mtri.astype(_bf), "cBO16": bo.astype(_bf)}
    return w


def _rope_table(cos):
    pos = np.arange(S, dtype=np.float64)
    j = np.arange(32, dtype=np.float64)
    inv = 1.0 / (10000.0 ** (2.0 * j / 64.0))
    ang = np.outer(pos, inv)
    tab = np.zeros((128, S), dtype=np.float64)
    for r in range(128):
        jj = (r % 64) // 2
        if cos:
            tab[r] = np.cos(ang[:, jj])
        else:
            tab[r] = np.sin(ang[:, jj]) * (-1.0 if r % 2 == 0 else 1.0)
    return tab.astype(_bf)


def _threefry2x32(key, x0, x1):
    def rotl(x, d):
        return ((x << np.uint32(d)) | (x >> np.uint32(32 - d))).astype(np.uint32)
    ks0, ks1 = np.uint32(key[0]), np.uint32(key[1])
    ks2 = np.uint32(0x1BD11BDA) ^ ks0 ^ ks1
    rot = [(13, 15, 26, 6), (17, 29, 16, 24)]
    x0 = (x0 + ks0).astype(np.uint32)
    x1 = (x1 + ks1).astype(np.uint32)
    ks = [(ks1, ks2), (ks2, ks0), (ks0, ks1), (ks1, ks2), (ks2, ks0)]
    for r in range(5):
        for d in rot[r % 2]:
            x0 = (x0 + x1).astype(np.uint32)
            x1 = rotl(x1, d) ^ x0
        x0 = (x0 + ks[r][0]).astype(np.uint32)
        x1 = (x1 + ks[r][1] + np.uint32(r + 1)).astype(np.uint32)
    return x0, x1


def make_dropout_mask():
    n = NCORE * S * D
    i = np.arange(n, dtype=np.uint64)
    hi = (i >> np.uint64(32)).astype(np.uint32)
    lo = (i & np.uint64(0xFFFFFFFF)).astype(np.uint32)
    b0, b1 = _threefry2x32((np.uint32(0), np.uint32(42)), hi, lo)
    bits = b0 ^ b1
    fl = ((bits >> np.uint32(9)) | np.uint32(0x3F800000)).view(np.float32) - 1.0
    keep = (fl < np.float32(0.8)).reshape(NCORE, S, D)
    return np.where(keep, np.float16(1.25), np.float16(0.0))



_STAGE_GROUPS = [list(range(8))]


def _run(inputs, groups):
    host_w = make_host_weights(inputs)
    dmask = make_dropout_mask()
    x = np.asarray(inputs["L0_attn_x"])
    xT = [np.ascontiguousarray(x[b].T).astype(np.float32) for b in range(NCORE)]
    state = None
    for gi, stages in enumerate(groups):
        first = (gi == 0)
        last = (gi == len(groups) - 1)
        nc = build_program(stages, host_w, first, last)
        in_maps = []
        for b in range(NCORE):
            m = {}
            for st in stages + ["const"]:
                for name, arr in host_w[st].items():
                    m["i_" + name] = arr
            if 1 in stages:
                m["i_L0_dmask"] = np.ascontiguousarray(dmask[b].T)
            m["i_x_in"] = xT[b] if first else state[b]
            in_maps.append(m)
        res = run_bass_kernel_spmd(nc, in_maps, core_ids=list(range(NCORE)))
        if last:
            return np.stack([res.results[b]["o_y"].T for b in range(NCORE)])
        state = [res.results[b]["o_y"] for b in range(NCORE)]
    raise RuntimeError("no last group")


def kernel(**inputs):
    return _run(inputs, _STAGE_GROUPS)



GELU_K2 = math.sqrt(2.0 / math.pi) * 2.0
GELU_K1 = GELU_K2 * 0.044715


def emit_stage_l1_attn(P, M, W):
    xt = M.tiles("X", NT)
    xk = keys("X", NT)
    bt = M.tiles("B", NT)
    bk = keys("B", NT)
    ut = M.tiles("U", NT)
    at, ak = ut, keys("U", NT)
    a0, a1 = M.sb("A0", [128, S], BF16), M.sb("A1", [128, S], BF16)
    ctiles = c_area(M)
    p4, p5 = M.ps("P4", [128, 512]), M.ps("P5", [128, 512])
    bqc = M.sb("BQC", [128, 8], F32)
    bkc = M.sb("BKC", [128, 8], F32)
    bvb = M.sb("BVB", [128, 1024], F32)
    bpc = M.sb("BPC", [128, 8], F32)
    bfcc = M.sb("BFCC", [128, 32], F32)
    bmc = M.sb("BMC", [128, 8], F32)
    for nm, t_, k_ in [("L1_bq", bqc, "BQC"), ("L1_bk", bkc, "BKC"),
                       ("L1_bp", bpc, "BPC"), ("L1_bfc", bfcc, "BFCC"),
                       ("L1_bm", bmc, "BMC")]:
        P.emit("g", lambda eng, t_=t_, nm=nm: eng.dma_start(
            out=t_[:], in_=W[nm][0]), reads=[W[nm][1]], writes=[k_], tag="bia")
    P.emit("g", lambda eng: eng.dma_start(
        out=bvb[:], in_=W["L1_bv"][0][0:1, :].partition_broadcast(128)),
        reads=[W["L1_bv"][1]], writes=["BVB"], tag="bvb")

    emit_ln(P, M, xt, xk, eps=1e-10, out_tiles=bt, out_keys=bk)

    wvb = M.sb("WPA", [128, 4096], BF16)
    wvb2 = M.sb("WPB", [128, 4096], BF16)
    emit_load_wchunk(P, wvb, "WPA", W["L1_wattnT"][0], [W["L1_wattnT"][1]],
                     NT, 2048, 512)
    emit_load_wchunk(P, wvb2, "WPB", W["L1_wattnT"][0], [W["L1_wattnT"][1]],
                     NT, 2560, 512)
    for kt in range(STL):
        P.emit("v", lambda eng, kt=kt: eng.memset(
            va_ap(ctiles, kt).rearrange("p (h c) -> p h c", c=65)[:, :, 64:65],
            1.0), reads=[], writes=["C%d" % (kt // 2)], tag="vones")
    p6v = M.ps("P6", [128, 512])
    for st in range(STL):
        for ec in range(2):
            pn = ["P4", "P5", "P6"][(st * 2 + ec) % 3]
            psum = [p4, p5, p6v][(st * 2 + ec) % 3]
            wsrc, wkey = (wvb, "WPA") if ec == 0 else (wvb2, "WPB")
            for kc in range(NT):
                def fn(eng, st=st, kc=kc, psum=psum, wsrc=wsrc):
                    return eng.matmul(
                        psum[:], bt[kc][:, st * 128:(st + 1) * 128],
                        wsrc[:, kc * 512:(kc + 1) * 512],
                        start=(kc == 0), stop=(kc == NT - 1))
                P.emit("t", fn, reads=[bk[kc], wkey], writes=[pn], tag="l1v")
            def fne(eng, st=st, ec=ec, psum=psum, pn=pn):
                dst = va_ap(ctiles, st).rearrange("p (h c) -> p h c", c=65)[
                    :, 8 * ec:8 * ec + 8, 0:64]
                src = psum[:].rearrange("p (h c) -> p h c", c=64)
                bsl = bvb[:, ec * 512:(ec + 1) * 512].rearrange(
                    "p (h c) -> p h c", c=64)
                return eng.tensor_add(dst, src, bsl)
            P.emit("v", fne, reads=[pn, "BVB"], writes=["C%d" % (st // 2)],
                   tag="l1v_ev")

    qwb = M.sb("WPA", [128, 4096], BF16)[:, 0:1024]
    kwb = M.sb("WPA", [128, 4096], BF16)[:, 1024:2048]

    def proj_qk(pair):
        emit_load_wchunk(P, qwb, "WPA", W["L1_wattnT"][0],
                         [W["L1_wattnT"][1]], NT, pair * 128, 128)
        emit_load_wchunk(P, kwb, "WPA", W["L1_wattnT"][0],
                         [W["L1_wattnT"][1]], NT, 1024 + pair * 128, 128)
        for (wb, wkey, dst, dkey, scl, bcol, bckey) in [
                (qwb, "WPA", a0, "A0", 0.125, bqc, "BQC"),
                (kwb, "WPA", a1, "A1", 1.0, bkc, "BKC")]:
            for sc in range(4):
                pn = "P4" if dkey == "A0" else "P5"
                psum = p4 if dkey == "A0" else p5
                for kc in range(NT):
                    def fn(eng, kc=kc, sc=sc, psum=psum, wb=wb):
                        return eng.matmul(
                            psum[:], wb[:, kc * 128:(kc + 1) * 128],
                            bt[kc][:, sc * 512:(sc + 1) * 512],
                            start=(kc == 0), stop=(kc == NT - 1))
                    P.emit("t", fn, reads=[wkey, bk[kc]], writes=[pn],
                           tag="l1qk")
                def fev(eng, sc=sc, psum=psum, dst=dst, scl=scl, bcol=bcol,
                        pair=pair):
                    return eng.tensor_scalar(
                        dst[:, sc * 512:(sc + 1) * 512], psum[:],
                        bcol[:, pair:pair + 1], scl, ALU.add, ALU.mult)
                P.emit("v", fev, reads=[pn, bckey], writes=[dkey],
                       tag="l1qk_ev")

    emit_attention(P, M, "l1", proj_qk, lambda kt: kt * 128, True)

    def evac(mt, sc, psum, pn):
        def fn(eng, mt=mt, sc=sc, psum=psum):
            return eng.scalar_tensor_tensor(
                bt[mt][:, sc * 512:(sc + 1) * 512], psum[:],
                bpc[:, mt:mt + 1], xt[mt][:, sc * 512:(sc + 1) * 512],
                ALU.add, ALU.add)
        P.emit("v", fn, reads=[pn, "BPC", xk[mt]], writes=[bk[mt]],
               tag="l1pr_ev")
    emit_proj_fmajor(P, M, "l1pr", W["L1_wprojT"][0], D, D,
                     ut, keys("U", NT), evac, wt_keys=[W["L1_wprojT"][1]])

    emit_ln(P, M, bt, bk, eps=1e-10, out_tiles=at, out_keys=ak)

    gsc = M.sb("DMHB", [128, 1024], FP16).bitcast(BF16)
    g0 = gsc[:, 0:512]
    g1 = gsc[:, 512:1024]
    wfb = [M.sb("WPA", [128, 4096], BF16), M.sb("WPB", [128, 4096], BF16)]
    wfk = ["WPA", "WPB"]
    hml = [(ctiles[i // 4], "C%d" % (i // 4), (i % 4) * 512) for i in range(32)]
    p6v = M.ps("P6", [128, 512])
    for sc in range(4):
        emit_load_wchunk(P, wfb[0], wfk[0], W["L1_wfcT"][0],
                         [W["L1_wfcT"][1]], NT, 0, 512)
        for mc in range(8):
            par = mc % 2
            if mc + 1 < 8:
                emit_load_wchunk(P, wfb[1 - par], wfk[1 - par],
                                 W["L1_wfcT"][0], [W["L1_wfcT"][1]], NT,
                                 (mc + 1) * 512, 512)
            for mt in range(4):
                mg = mc * 4 + mt
                pn = ["P4", "P5", "P6"][mg % 3]
                psum = [p4, p5, p6v][mg % 3]
                for kc in range(NT):
                    def fn(eng, kc=kc, mt=mt, sc=sc, par=par, psum=psum):
                        return eng.matmul(
                            psum[:], wfb[par][:, kc * 512 + mt * 128:
                                              kc * 512 + (mt + 1) * 128],
                            at[kc][:, sc * 512:(sc + 1) * 512],
                            start=(kc == 0), stop=(kc == NT - 1))
                    P.emit("t", fn, reads=[wfk[par], ak[kc]], writes=[pn],
                           tag="l1fc")
                P.emit("v", lambda eng, mg=mg, psum=psum: eng.tensor_scalar(
                    g0, psum[:], bfcc[:, mg:mg + 1], None, ALU.add),
                    reads=[pn, "BFCC"], writes=["DMHB"], tag="gel0")
                P.emit("s", lambda eng: eng.activation(g1, g0, AF.Square),
                       reads=["DMHB"], writes=["DMHB"], tag="gel1")
                P.emit("v", lambda eng: eng.tensor_scalar(
                    g1, g1, GELU_K1, GELU_K2, ALU.mult, ALU.add),
                    reads=["DMHB"], writes=["DMHB"], tag="gel2")
                P.emit("v", lambda eng: eng.tensor_mul(g1, g0, g1),
                       reads=["DMHB"], writes=["DMHB"], tag="gel3")
                P.emit("s", lambda eng: eng.activation(g1, g1, AF.Sigmoid),
                       reads=["DMHB"], writes=["DMHB"], tag="gel4")
                hm, hk, hoff = hml[mg]
                P.emit("v", lambda eng, hm=hm, hoff=hoff: eng.tensor_mul(
                    hm[:, hoff:hoff + 512], g0, g1),
                    reads=["DMHB"], writes=[hk], tag="gel5")
        w2b = wfb
        w2k = wfk
        emit_load_wchunk(P, w2b[0], w2k[0], W["L1_wmlpT"][0],
                         [W["L1_wmlpT"][1]], 32, 0, 128)
        for dt in range(NT):
            par = dt % 2
            if dt + 1 < NT:
                emit_load_wchunk(P, w2b[1 - par], w2k[1 - par],
                                 W["L1_wmlpT"][0], [W["L1_wmlpT"][1]], 32,
                                 (dt + 1) * 128, 128)
            pn = ["P4", "P5", "P6"][dt % 3]
            psum = [p4, p5, p6v][dt % 3]
            for kc in range(32):
                hm, hk, hoff = hml[kc]
                def fn(eng, kc=kc, dt=dt, par=par, psum=psum, hm=hm,
                       hoff=hoff):
                    return eng.matmul(
                        psum[:], w2b[par][:, kc * 128:(kc + 1) * 128],
                        hm[:, hoff:hoff + 512],
                        start=(kc == 0), stop=(kc == 31))
                P.emit("t", fn, reads=[w2k[par], hk], writes=[pn],
                       tag="l1mlp")
            def fe(eng, dt=dt, sc=sc, psum=psum):
                return eng.scalar_tensor_tensor(
                    bt[dt][:, sc * 512:(sc + 1) * 512], psum[:],
                    bmc[:, dt:dt + 1], bt[dt][:, sc * 512:(sc + 1) * 512],
                    ALU.add, ALU.add)
            P.emit("v", fe, reads=[pn, "BMC", bk[dt]], writes=[bk[dt]],
                   tag="l1mlp_ev")
    for i in range(NT):
        P.emit("v", lambda eng, i=i: eng.tensor_add(xt[i][:], xt[i][:],
                                                    bt[i][:]),
               reads=[xk[i], bk[i]], writes=[xk[i]], tag="l1out")


STAGE_EMITTERS[2] = emit_stage_l1_attn



def emit_weight_quant(P, M, nm, wt_ap, wt_key, Kdim, Mout, stash_ap, stash_key):
    kt = Kdim // 128
    wf = [M.sb("T32A", [128, S], F32), M.sb("T32B", [128, S], F32)]
    wfk = ["T32A", "T32B"]
    sm = M.sb("SM1", [128, 64], F32)
    onesf = M.sb("ONESF", [128, 1], F32)
    bca = M.sb("BCA", [128, S], BF16)
    p6 = M.ps("P6", [128, 512])
    P.emit("v", lambda eng: eng.memset(onesf[:], 1.0), (), ["ONESF"])
    P.emit("v", lambda eng: eng.memset(sm[:, 0:2], 0.0), (), ["SM1"])
    cw = 2048 // kt
    nch = Mout // cw
    for ci in range(nch):
        par = ci % 2
        emit_load_wchunk(P, wf[par], wfk[par], wt_ap, [wt_key], kt, ci * cw, cw)
        for kc in range(kt):
            P.emit("v", lambda eng, kc=kc, par=par: eng.reduce_sum(
                sm[:, 1:2], wf[par][:, kc * cw:(kc + 1) * cw], axis=AX,
                apply_absolute_value=True),
                reads=[wfk[par]], writes=["SM1"], tag=nm + "wqr")
            P.emit("v", lambda eng: eng.tensor_add(sm[:, 0:1], sm[:, 0:1],
                                                   sm[:, 1:2]),
                   reads=["SM1"], writes=["SM1"], tag=nm + "wqa")
    P.emit("t", lambda eng: eng.matmul(p6[0:1, 0:1], onesf[:, 0:1],
                                       sm[:, 0:1], start=True, stop=True),
           reads=["ONESF", "SM1"], writes=["P6"], tag=nm + "wqs")
    P.emit("v", lambda eng: eng.tensor_scalar(
        sm[0:1, 2:3], p6[0:1, 0:1], 1.0 / (Kdim * Mout), 1e-5,
        ALU.mult, ALU.max), reads=["P6"], writes=["SM1"], tag=nm + "wqm")
    P.emit("v", lambda eng: eng.reciprocal(sm[0:1, 3:4], sm[0:1, 2:3]),
           reads=["SM1"], writes=["SM1"], tag=nm + "wqi")
    emit_bcast_row(P, M, nm + "sw", sm[0:1, 3:4], "SM1", 128, sm[:, 4:5],
                   "SM1", 1)
    for ci in range(nch):
        par = ci % 2
        emit_load_wchunk(P, wf[par], wfk[par], wt_ap, [wt_key], kt, ci * cw, cw)
        P.emit("v", lambda eng, par=par: eng.tensor_scalar(
            wf[par][:, 0:2048], wf[par][:, 0:2048], sm[:, 4:5], None, ALU.mult),
            reads=[wfk[par], "SM1"], writes=[wfk[par]], tag=nm + "wq1")
        P.emit("v", lambda eng, par=par: eng.tensor_scalar(
            wf[par][:, 0:2048], wf[par][:, 0:2048], MAGIC, MAGIC,
            ALU.add, ALU.subtract),
            reads=[wfk[par]], writes=[wfk[par]], tag=nm + "wq2")
        P.emit("v", lambda eng, par=par: eng.tensor_scalar(
            bca[:, 0:2048], wf[par][:, 0:2048], -1.0, 1.0, ALU.max, ALU.min),
            reads=[wfk[par]], writes=["BCA"], tag=nm + "wq3")
        def fo(eng, ci=ci):
            dst = stash_ap[:, ci * cw:(ci + 1) * cw].rearrange(
                "(kc p) c -> p kc c", p=128)
            return eng.dma_start(out=dst, in_=bca[:, 0:2048].rearrange(
                "p (kc c) -> p kc c", kc=kt))
        P.emit("y", fo, reads=["BCA"], writes=[stash_key], tag=nm + "wqo")
    return sm


def emit_act_quant(P, M, nm, width, col0, mx_ready, mw_col_ap, int_round_cb,
                   sclx_name="T32A"):
    mxt = M.sb("U3", [128, S], BF16)
    idn = M.sb("IDN", [128, 128], BF16)
    pt = M.ps("PT", [128, 1024], BF16)
    sm2 = M.sb("SM2", [128, 64], F32)
    spb = M.sb("SPB", [128, 64], BF16)
    sclx = M.sb(sclx_name, [128, S], F32)
    bca = M.sb("BCA", [128, S], BF16)
    nb = width // 128
    for b in range(nb):
        P.emit("t", lambda eng, b=b: eng.transpose(
            pt[:, 0:128], mxt[:, col0 + b * 128:col0 + (b + 1) * 128], idn[:]),
            reads=["U3", "IDN"], writes=["PT"], tag=nm + "aqt")
        P.emit("v", lambda eng, b=b: eng.reduce_max(
            sm2[:, b:b + 1], pt[:, 0:128], axis=AX),
            reads=["PT"], writes=["SM2"], tag=nm + "aqm")
    P.emit("v", lambda eng: eng.tensor_scalar(
        sm2[:, 16:16 + nb], sm2[:, 0:nb], 1e-5, None, ALU.max),
        reads=["SM2"], writes=["SM2"], tag=nm + "aqc")
    P.emit("v", lambda eng: eng.reciprocal(sm2[:, 32:32 + nb],
                                           sm2[:, 16:16 + nb]),
           reads=["SM2"], writes=["SM2"], tag=nm + "aqr")
    P.emit("v", lambda eng: eng.tensor_scalar(
        spb[:, 0:nb], sm2[:, 32:32 + nb], 127.0, None, ALU.mult),
        reads=["SM2"], writes=["SPB"], tag=nm + "aqs")
    P.emit("v", lambda eng: eng.reciprocal(sm2[:, 32:32 + nb], spb[:, 0:nb]),
           reads=["SPB"], writes=["SM2"], tag=nm + "aqi")
    if mw_col_ap is not None:
        emit_bcast_row(P, M, nm + "mw", mw_col_ap, "SM1", 128, sm2[:, 48:49],
                       "SM2", 1)
        P.emit("v", lambda eng: eng.tensor_scalar(
            sm2[:, 32:32 + nb], sm2[:, 32:32 + nb], sm2[:, 48:49], None,
            ALU.mult), reads=["SM2"], writes=["SM2"], tag=nm + "aqmw")
    P.emit("v", lambda eng: eng.tensor_copy(spb[:, 16:16 + nb],
                                            sm2[:, 32:32 + nb]),
           reads=["SM2"], writes=["SPB"], tag=nm + "aqh")
    P.emit("v", lambda eng: eng.tensor_sub(
        sm2[:, 48:48 + nb], sm2[:, 32:32 + nb], spb[:, 16:16 + nb]),
        reads=["SM2", "SPB"], writes=["SM2"], tag=nm + "aql0")
    P.emit("v", lambda eng: eng.tensor_copy(spb[:, 32:32 + nb],
                                            sm2[:, 48:48 + nb]),
           reads=["SM2"], writes=["SPB"], tag=nm + "aql")
    P.emit("t", lambda eng: eng.transpose(pt[0:48, 0:128], spb[:, 0:48],
                                          idn[:]),
           reads=["SPB", "IDN"], writes=["PT"], tag=nm + "aqt2")
    P.emit("v", lambda eng: eng.tensor_copy(bca[0:48, 0:128],
                                            pt[0:48, 0:128]),
           reads=["PT"], writes=["BCA"], tag=nm + "aqe")
    drq = M.dr("dq_" + nm, [3, S], BF16)
    def fo(eng):
        dst = drq.rearrange("t c -> (t c)")[0:48 * 128]
        return eng.dma_start(out=dst, in_=bca[0:48, 0:128])
    P.emit("y", fo, reads=["BCA"], writes=["dq_" + nm], tag=nm + "aqo")
    P.emit("g", lambda eng: eng.dma_start(
        out=mxt[:, col0:col0 + width],
        in_=drq[0:1, 0:width].partition_broadcast(128)),
        reads=["dq_" + nm], writes=["U3"], tag=nm + "aqb1")
    P.emit("g", lambda eng: eng.dma_start(
        out=sclx[:, col0:col0 + width],
        in_=drq[1:2, 0:width].partition_broadcast(128)),
        reads=["dq_" + nm], writes=[sclx_name], tag=nm + "aqb2")
    P.emit("g", lambda eng: eng.dma_start(
        out=bca[:, col0:col0 + width],
        in_=drq[2:3, 0:width].partition_broadcast(128)),
        reads=["dq_" + nm], writes=["BCA"], tag=nm + "aqb3")
    P.emit("v", lambda eng: eng.tensor_add(
        sclx[:, col0:col0 + width], sclx[:, col0:col0 + width],
        bca[:, col0:col0 + width]),
        reads=[sclx_name, "BCA"], writes=[sclx_name], tag=nm + "aqlo")



def emit_stage_l1_ffn(P, M, W):
    xt = M.tiles("X", NT)
    xk = keys("X", NT)
    ct = c_area(M)
    ai = [ct[i][:, 0:2048] for i in range(NT)]
    aik = keys("C", NT)
    ut = M.tiles("U", NT)
    mxt = M.sb("U3", [128, S], BF16)
    sclx = M.sb("T32A", [128, S], F32)
    hh = M.sb("U4", [128, S], BF16)
    cws = M.sb("CWS", [128, 96], F32)
    cbc = M.sb("CBC", [128, 32], F32)
    p4, p5 = M.ps("P4", [128, 512]), M.ps("P5", [128, 512])
    w1q = M.dr("L1W1Q", [D, 4096], BF16)
    w2q = M.dr("L1W2Q", [4096, D], BF16)
    sild = M.dr("L1SILD", [4096, S], BF16)
    wpa = M.sb("WPA", [128, 4096], BF16)
    wpb = M.sb("WPB", [128, 4096], BF16)

    sm = emit_weight_quant(P, M, "l1w1", W["L1f_w1T"][0], W["L1f_w1T"][1],
                           D, 4096, w1q, "L1W1Q")
    mw1 = sm[0:1, 2:3]
    P.emit("g", lambda eng: eng.dma_start(out=cbc[:], in_=W["L1f_cb"][0]),
           reads=[W["L1f_cb"][1]], writes=["CBC"], tag="cb")
    P.emit("g", lambda eng: eng.dma_start(out=cws[:], in_=W["L1f_cw"][0]),
           reads=[W["L1f_cw"][1]], writes=["CWS"], tag="cw")
    emit_bcast_row(P, M, "l1mw1", mw1, "SM1", 128, sclx[:, 0:1], "T32A", 1)
    P.emit("v", lambda eng: eng.tensor_scalar(
        cws[:], cws[:], sclx[:, 0:1], None, ALU.mult),
        reads=["CWS", "T32A"], writes=["CWS"], tag="cwm")
    sm2q = emit_weight_quant(P, M, "l1w2", W["L1f_w2T"][0], W["L1f_w2T"][1],
                             4096, D, w2q, "L1W2Q")
    mw2 = sm2q[0:1, 2:3]

    absx = M.sb("U5", [128, S], BF16)
    for i in range(NT):
        P.emit("s", lambda eng, i=i: eng.activation(
            absx[:], xt[i][:], AF.Abs),
            reads=[xk[i]], writes=["U5"], tag="aqxa")
        if i == 0:
            P.emit("v", lambda eng: eng.tensor_copy(mxt[:], absx[:]),
                   reads=["U5"], writes=["U3"], tag="aqx0")
        else:
            P.emit("v", lambda eng: eng.tensor_max(mxt[:], mxt[:], absx[:]),
                   reads=["U3", "U5"], writes=["U3"], tag="aqx")
    emit_act_quant(P, M, "l1x", S, 0, True, None, None)
    for i in range(NT):
        P.emit("v", lambda eng, i=i: eng.tensor_mul(ai[i][:], xt[i][:], mxt[:]),
               reads=[xk[i], "U3"], writes=[aik[i]], tag="aqxm")
        P.emit("v", lambda eng, i=i: eng.tensor_scalar(
            ai[i][:], ai[i][:], MAGIC, MAGIC, ALU.add, ALU.subtract),
            reads=[aik[i]], writes=[aik[i]], tag="aqxr")

    wb = [wpa, wpb]
    wbk = ["WPA", "WPB"]
    def loadw1(mt, par):
        def fn(eng, mt=mt):
            src = w1q[:, mt * 128:(mt + 1) * 128].rearrange(
                "(kc p) c -> p kc c", p=128)
            return eng.dma_start(out=wb[par][:, 0:1024], in_=src)
        P.emit("g", fn, reads=["L1W1Q"], writes=[wbk[par]], tag="w1l")
    loadw1(0, 0)
    for mt in range(32):
        par = mt % 2
        if mt + 1 < 32:
            loadw1(mt + 1, 1 - par)
        for sc in range(4):
            pn = ["P4", "P5", "P6"][sc % 3]
            psum = [p4, p5, M.ps("P6", [128, 512])][sc % 3]
            for kc in range(NT):
                def fn(eng, kc=kc, sc=sc, par=par, psum=psum):
                    return eng.matmul(
                        psum[:], wb[par][:, kc * 128:(kc + 1) * 128],
                        ai[kc][:, sc * 512:(sc + 1) * 512],
                        start=(kc == 0), stop=(kc == NT - 1))
                P.emit("t", fn, reads=[wbk[par], aik[kc]], writes=[pn],
                       tag="l1fmm1")
            P.emit("v", lambda eng, sc=sc, psum=psum: eng.tensor_mul(
                hh[:, sc * 512:(sc + 1) * 512], psum[:],
                sclx[:, sc * 512:(sc + 1) * 512]),
                reads=[pn, "T32A"], writes=["U4"], tag="l1fev1")
        P.emit("v", lambda eng, mt=mt: eng.tensor_scalar(
            ut[0][:], hh[:, 0:2048], cws[:, 3 * mt + 1:3 * mt + 2], None,
            ALU.mult), reads=["U4", "CWS"], writes=["U0"], tag="cv1")
        P.emit("v", lambda eng, mt=mt: eng.scalar_tensor_tensor(
            ut[0][:, 1:2048], hh[:, 0:2047], cws[:, 3 * mt:3 * mt + 1],
            ut[0][:, 1:2048],
            ALU.mult, ALU.add), reads=["U4", "CWS", "U0"], writes=["U0"],
            tag="cv0")
        P.emit("v", lambda eng, mt=mt: eng.scalar_tensor_tensor(
            ut[0][:, 0:2047], hh[:, 1:2048], cws[:, 3 * mt + 2:3 * mt + 3],
            ut[0][:, 0:2047],
            ALU.mult, ALU.add), reads=["U4", "CWS", "U0"], writes=["U0"],
            tag="cv2")
        P.emit("s", lambda eng, mt=mt: eng.activation(
            ut[1][:], ut[0][:], AF.Sigmoid, bias=cbc[:, mt:mt + 1]),
            reads=["U0", "CBC"], writes=["U1"], tag="sil1")
        P.emit("v", lambda eng, mt=mt: eng.scalar_tensor_tensor(
            ut[2][:], ut[0][:], cbc[:, mt:mt + 1], ut[1][:],
            ALU.add, ALU.mult), reads=["U0", "CBC", "U1"], writes=["U2"],
            tag="sil2")
        P.emit("s", lambda eng: eng.activation(absx[:], ut[2][:], AF.Abs),
               reads=["U2"], writes=["U5"], tag="aqha")
        if mt == 0:
            P.emit("v", lambda eng: eng.tensor_copy(mxt[:], absx[:]),
                   reads=["U5"], writes=["U3"], tag="aqh0")
        else:
            P.emit("v", lambda eng: eng.tensor_max(mxt[:], mxt[:], absx[:]),
                   reads=["U3", "U5"], writes=["U3"], tag="aqh")
        P.emit("y", lambda eng, mt=mt: eng.dma_start(
            out=sild[mt * 128:(mt + 1) * 128, :], in_=ut[2][:]),
            reads=["U2"], writes=["L1SILD"], tag="silo")

    emit_act_quant(P, M, "l1h", S, 0, True, mw2, None)

    w2b = [wpa, wpb]
    w2bk = ["WPA", "WPB"]
    for sc in range(4):
        for i in range(NT):
            def fl(eng, i=i, sc=sc):
                src = sild[:, sc * 512:(sc + 1) * 512].rearrange(
                    "(j q p) c -> p j q c", p=128, q=NT)[:, :, i, :]
                return eng.dma_start(
                    out=ai[i].rearrange("p (j c) -> p j c", j=4), in_=src)
            P.emit("g", fl, reads=["L1SILD"], writes=[aik[i]], tag="sili")
            for j in range(4):
                P.emit("v", lambda eng, i=i, j=j, sc=sc: eng.tensor_mul(
                    ai[i][:, j * 512:(j + 1) * 512],
                    ai[i][:, j * 512:(j + 1) * 512],
                    mxt[:, sc * 512:(sc + 1) * 512]),
                    reads=[aik[i], "U3"], writes=[aik[i]], tag="hqm")
            P.emit("v", lambda eng, i=i: eng.tensor_scalar(
                ai[i][:], ai[i][:], MAGIC, MAGIC, ALU.add, ALU.subtract),
                reads=[aik[i]], writes=[aik[i]], tag="hqr")
        def loadw2(dt, par):
            def fn(eng, dt=dt):
                src = w2q[:, dt * 128:(dt + 1) * 128].rearrange(
                    "(kc p) c -> p kc c", p=128)
                return eng.dma_start(out=w2b[par][:, 0:4096], in_=src)
            P.emit("g", fn, reads=["L1W2Q"], writes=[w2bk[par]], tag="w2l")
        loadw2(0, 0)
        for dt in range(NT):
            par = dt % 2
            if dt + 1 < NT:
                loadw2(dt + 1, 1 - par)
            pn = ["P4", "P5", "P6"][dt % 3]
            psum = [p4, p5, M.ps("P6", [128, 512])][dt % 3]
            for kc in range(32):
                def fn(eng, kc=kc, par=par, psum=psum):
                    return eng.matmul(
                        psum[:], w2b[par][:, kc * 128:(kc + 1) * 128],
                        ai[kc % NT][:, (kc // NT) * 512:(kc // NT) * 512 + 512],
                        start=(kc == 0), stop=(kc == 31))
                P.emit("t", fn, reads=[w2bk[par], aik[kc % NT]], writes=[pn],
                       tag="l1fmm2")
            P.emit("v", lambda eng, dt=dt, sc=sc, psum=psum: eng.tensor_mul(
                ut[0][:, 0:512], psum[:], sclx[:, sc * 512:(sc + 1) * 512]),
                reads=[pn, "T32A"], writes=["U0"], tag="l1fev2a")
            P.emit("v", lambda eng, dt=dt, sc=sc: eng.tensor_add(
                xt[dt][:, sc * 512:(sc + 1) * 512],
                xt[dt][:, sc * 512:(sc + 1) * 512], ut[0][:, 0:512]),
                reads=["U0", xk[dt]], writes=[xk[dt]], tag="l1fev2b")


STAGE_EMITTERS[3] = emit_stage_l1_ffn



L2EPS = 1e-6
L2TAP = None


def emit_stage_l2_attn(P, M, W):
    xt = M.tiles("X", NT)
    xk = keys("X", NT)
    ut = M.tiles("U", NT)
    bt = M.tiles("B", 3)
    a0, a1 = M.sb("A0", [128, S], BF16), M.sb("A1", [128, S], BF16)
    ct = c_area(M)
    sm2 = M.sb("SM2", [128, 64], F32)
    ksem = M.sb("KSEM", [128, 4], BF16)
    onesc = M.sb("ONESC", [128, 1], BF16)
    bca = M.sb("BCA", [128, S], BF16)
    bcb = M.sb("BCB", [128, S], BF16)
    r16a = M.sb("R16A", [16, S], F32)
    r16b = M.sb("R16B", [16, S], F32)
    bqc = M.sb("BQC", [128, 8], F32)
    bkc = M.sb("BKC", [128, 8], F32)
    boc = M.sb("BPC", [128, 8], F32)
    bvb = M.sb("BVB", [128, 1024], F32)
    kvs = M.sb("KVS", [128, 128], BF16)
    wpa = M.sb("WPA", [128, 4096], BF16)
    p4, p5 = M.ps("P4", [128, 512]), M.ps("P5", [128, 512])
    p6 = M.ps("P6", [128, 512])
    pb = M.ps("PB", [128, 2048])
    drl2 = M.dr("drl2", [2, S], F32)

    for nm, t_, k_ in [("L2_bq", bqc, "BQC"), ("L2_bk", bkc, "BKC"),
                       ("L2_bo", boc, "BPC")]:
        P.emit("g", lambda eng, t_=t_, nm=nm: eng.dma_start(
            out=t_[:], in_=W[nm][0]), reads=[W[nm][1]], writes=[k_], tag="bia")
    P.emit("g", lambda eng: eng.dma_start(
        out=bvb[:], in_=W["L2_bv"][0][0:1, :].partition_broadcast(128)),
        reads=[W["L2_bv"][1]], writes=["BVB"], tag="bvb")

    def rowexp(nm, src_rows, src_key, dst, dst_key):
        P.emit("y", lambda eng: eng.dma_start(out=drl2[:, :], in_=src_rows),
               reads=[src_key], writes=["drl2"], tag=nm + "xo")
        P.emit("g", lambda eng: eng.dma_start(
            out=dst[0:64, :], in_=drl2[0:1, :].partition_broadcast(64)),
            reads=["drl2"], writes=[dst_key], tag=nm + "xi0")
        P.emit("g", lambda eng: eng.dma_start(
            out=dst[64:128, :], in_=drl2[1:2, :].partition_broadcast(64)),
            reads=["drl2"], writes=[dst_key], tag=nm + "xi1")

    def rowmm(nm, msk_col_f32, rhs, rhs_key, eps_corr_col, out_rows, out_key):
        P.emit("v", lambda eng: eng.memset(ksem[:, 0:2], 0.0), (), ["KSEM"])
        P.emit("v", lambda eng: eng.tensor_copy(ksem[0:64, 0:1],
                                                msk_col_f32[0:64, :]),
               reads=["SM2"], writes=["KSEM"], tag=nm + "m0")
        P.emit("v", lambda eng: eng.tensor_copy(ksem[64:128, 1:2],
                                                msk_col_f32[64:128, :]),
               reads=["SM2"], writes=["KSEM"], tag=nm + "m1")
        for sc in range(4):
            P.emit("t", lambda eng, sc=sc: eng.matmul(
                p6[0:2, 0:512], ksem[:, 0:2],
                rhs[:, sc * 512:(sc + 1) * 512], start=True, stop=True),
                reads=["KSEM", rhs_key], writes=["P6"], tag=nm + "mm")
            P.emit("v", lambda eng, sc=sc: eng.tensor_copy(
                out_rows[0:2, sc * 512:(sc + 1) * 512], p6[0:2, 0:512]),
                reads=["P6"], writes=[out_key], tag=nm + "cp")
        P.emit("t", lambda eng: eng.matmul(
            p6[0:2, 0:1], ksem[:, 0:2], onesc[:, 0:1], start=True, stop=True),
            reads=["KSEM", "ONESC"], writes=["P6"], tag=nm + "cs")
        P.emit("v", lambda eng: eng.tensor_scalar(
            sm2[0:2, 20:21], p6[0:2, 0:1], L2EPS, None, ALU.mult),
            reads=["P6"], writes=["SM2"], tag=nm + "cse")
        P.emit("v", lambda eng: eng.tensor_scalar(
            out_rows[0:2, :], out_rows[0:2, :], sm2[0:2, 20:21], None,
            ALU.add), reads=[out_key, "SM2"], writes=[out_key], tag=nm + "ca")

    for pair in range(8):
        fc = pair * 128
        for wi, wnm in enumerate(["L2_WqT", "L2_WkT", "L2_WvT"]):
            def fl(eng, wi=wi, wnm=wnm, fc=fc):
                src = W[wnm][0][:, fc:fc + 128].rearrange(
                    "(kc p) c -> p kc c", p=128)
                return eng.dma_start(out=wpa[:, wi * 1024:(wi + 1) * 1024]
                                     .rearrange("p (kc c) -> p kc c", kc=8),
                                     in_=src)
            P.emit("g", fl, reads=[W[wnm][1]], writes=["WPA"], tag="l2wl")
        for wi, (dst, dkey, bcol, bckey) in enumerate(
                [(a0, "A0", bqc, "BQC"), (a1, "A1", bkc, "BKC")]):
            for sc in range(4):
                psum, pn = (p4, "P4") if wi == 0 else (p5, "P5")
                for kc in range(NT):
                    def fn(eng, wi=wi, kc=kc, sc=sc, psum=psum):
                        return eng.matmul(
                            psum[:],
                            wpa[:, wi * 1024 + kc * 128:wi * 1024 + (kc + 1) * 128],
                            xt[kc][:, sc * 512:(sc + 1) * 512],
                            start=(kc == 0), stop=(kc == NT - 1))
                    P.emit("t", fn, reads=["WPA", xk[kc]], writes=[pn],
                           tag="l2qk")
                P.emit("s", lambda eng, sc=sc, psum=psum, dst=dst, bcol=bcol,
                       pair=pair: eng.activation(
                           dst[:, sc * 512:(sc + 1) * 512], psum[:],
                           AF.Sigmoid, bias=bcol[:, pair:pair + 1]),
                       reads=[pn, bckey], writes=[dkey], tag="l2qk_ev")
        for st in range(STL):
            psum, pn = (p4, "P4") if st % 2 == 0 else (p5, "P5")
            for kc in range(NT):
                def fn(eng, st=st, kc=kc, psum=psum):
                    return eng.matmul(
                        psum[:, 0:128], xt[kc][:, st * 128:(st + 1) * 128],
                        wpa[:, 2048 + kc * 128:2048 + (kc + 1) * 128],
                        start=(kc == 0), stop=(kc == NT - 1))
                P.emit("t", fn, reads=[xk[kc], "WPA"], writes=[pn], tag="l2v")
            def fne(eng, st=st, psum=psum, fc=fc):
                return eng.tensor_add(
                    ct[2 + st // 8][:, (st % 8) * 128:(st % 8) * 128 + 128],
                    psum[:, 0:128], bvb[:, fc:fc + 128])
            P.emit("v", fne, reads=[pn, "BVB"],
                   writes=["C%d" % (2 + st // 8)], tag="l2v_ev")
        P.emit("v", lambda eng: eng.reduce_sum(sm2[:, 0:1], a1[:], axis=AX),
               reads=["A1"], writes=["SM2"], tag="l2ks")
        P.emit("v", lambda eng: eng.tensor_scalar(
            sm2[:, 1:2], sm2[:, 0:1], L2EPS, None, ALU.add),
            reads=["SM2"], writes=["SM2"], tag="l2kse")
        P.emit("v", lambda eng: eng.reduce_sum(sm2[:, 2:3], a0[:], axis=AX),
               reads=["A0"], writes=["SM2"], tag="l2qs")
        P.emit("v", lambda eng: eng.tensor_scalar(
            sm2[:, 3:4], sm2[:, 2:3], L2EPS, None, ALU.add),
            reads=["SM2"], writes=["SM2"], tag="l2qse")
        rowmm("l2si", sm2[:, 1:2], a0, "A0", None, r16a, "R16A")
        P.emit("v", lambda eng: eng.reciprocal(r16a[0:2, :], r16a[0:2, :]),
               reads=["R16A"], writes=["R16A"], tag="l2sir")
        rowmm("l2so", sm2[:, 3:4], a1, "A1", None, r16b, "R16B")
        P.emit("v", lambda eng: eng.reciprocal(r16b[0:2, :], r16b[0:2, :]),
               reads=["R16B"], writes=["R16B"], tag="l2sor")
        rowexp("l2so", r16b[0:2, :], "R16B", bca, "BCA")
        P.emit("v", lambda eng: eng.tensor_mul(bt[0][:], a1[:], bca[:]),
               reads=["A1", "BCA"], writes=["B0"], tag="l2kso")
        P.emit("v", lambda eng: eng.reduce_sum(sm2[:, 5:6], bt[0][:], axis=AX),
               reads=["B0"], writes=["SM2"], tag="l2ksos")
        P.emit("v", lambda eng: eng.tensor_scalar(
            sm2[:, 6:7], sm2[:, 5:6], L2EPS, None, ALU.add),
            reads=["SM2"], writes=["SM2"], tag="l2ksose")
        rowexp("l2si", r16a[0:2, :], "R16A", bcb, "BCB")
        P.emit("v", lambda eng: eng.tensor_mul(bt[1][:], a0[:], bcb[:]),
               reads=["A0", "BCB"], writes=["B1"], tag="l2qsi")
        P.emit("v", lambda eng: eng.reduce_sum(sm2[:, 7:8], bt[1][:], axis=AX),
               reads=["B1"], writes=["SM2"], tag="l2qsis")
        P.emit("v", lambda eng: eng.tensor_scalar(
            sm2[:, 8:9], sm2[:, 7:8], L2EPS, None, ALU.add),
            reads=["SM2"], writes=["SM2"], tag="l2qsise")
        rowmm("l2ck", sm2[:, 6:7], a0, "A0", None, r16b, "R16B")
        P.emit("s", lambda eng: eng.activation(r16b[0:2, :], r16b[0:2, :],
                                               AF.Sigmoid),
               reads=["R16B"], writes=["R16B"], tag="l2sa")
        P.emit("v", lambda eng: eng.tensor_mul(r16a[0:2, :], r16a[0:2, :],
                                               r16b[0:2, :]),
               reads=["R16A", "R16B"], writes=["R16A"], tag="l2sisa")
        rowmm("l2cs", sm2[:, 8:9], a1, "A1", None, r16b, "R16B")
        P.emit("v", lambda eng: eng.tensor_scalar(
            r16b[0:2, :], r16b[0:2, :], -1.0, 1.0, ALU.max, ALU.min),
            reads=["R16B"], writes=["R16B"], tag="l2clip")
        P.emit("s", lambda eng: eng.activation(r16b[0:2, :], r16b[0:2, :],
                                               AF.Exp),
               reads=["R16B"], writes=["R16B"], tag="l2exp")
        P.emit("v", lambda eng: eng.reduce_sum(sm2[0:2, 9:10], r16b[0:2, :],
                                               axis=AX),
               reads=["R16B"], writes=["SM2"], tag="l2ssum")
        P.emit("v", lambda eng: eng.reciprocal(sm2[0:2, 10:11],
                                               sm2[0:2, 9:10]),
               reads=["SM2"], writes=["SM2"], tag="l2srec")
        P.emit("v", lambda eng: eng.tensor_scalar(
            sm2[0:2, 10:11], sm2[0:2, 10:11], float(S), None, ALU.mult),
            reads=["SM2"], writes=["SM2"], tag="l2s2048")
        P.emit("v", lambda eng: eng.tensor_scalar(
            r16b[0:2, :], r16b[0:2, :], sm2[0:2, 10:11], None, ALU.mult),
            reads=["R16B", "SM2"], writes=["R16B"], tag="l2sc")
        if L2TAP is not None and pair == 0:
            P.emit("g", lambda eng: eng.dma_start(
                out=L2TAP["sisa"], in_=r16a[0:2, :]),
                reads=["R16A"], writes=["__t__"], tag="tap")
            P.emit("g", lambda eng: eng.dma_start(
                out=L2TAP["sc"], in_=r16b[0:2, :]),
                reads=["R16B"], writes=["__t__"], tag="tap")
            P.emit("g", lambda eng: eng.dma_start(
                out=L2TAP["q"], in_=a0[:]),
                reads=["A0"], writes=["__t__"], tag="tap")
            P.emit("g", lambda eng: eng.dma_start(
                out=L2TAP["k"], in_=a1[:]),
                reads=["A1"], writes=["__t__"], tag="tap")
            P.emit("g", lambda eng: eng.dma_start(
                out=L2TAP["v"], in_=ct[2][:, 0:1024]),
                reads=["C2"], writes=["__t__"], tag="tap")
        rowexp("l2sc", r16b[0:2, :], "R16B", bca, "BCA")
        P.emit("v", lambda eng: eng.tensor_mul(bt[0][:], a1[:], bca[:]),
               reads=["A1", "BCA"], writes=["B0"], tag="l2ksc")
        for st in range(STL):
            def ft(eng, st=st):
                return eng.dma_start(
                    out=ct[st // 8][:, (st % 8) * 128:(st % 8) * 128 + 128],
                    in_=bt[0][:, st * 128:(st + 1) * 128], transpose=True)
            P.emit("y", ft, reads=["B0"], writes=["C%d" % (st // 8)],
                   tag="l2tr")
        for hh in range(2):
            for st in range(STL):
                def fkv(eng, hh=hh, st=st):
                    lhsT = ct[st // 8][:, (st % 8) * 128 + hh * 64:
                                       (st % 8) * 128 + hh * 64 + 64]
                    rhs = ct[2 + st // 8][:, (st % 8) * 128 + hh * 64:
                                          (st % 8) * 128 + hh * 64 + 64]
                    if hh == 0:
                        return eng.matmul(p6[0:64, 0:64], lhsT, rhs,
                                          start=(st == 0), stop=(st == 15))
                    return eng.matmul(p6[64:128, 0:64], lhsT, rhs,
                                      start=(st == 0), stop=(st == 15),
                                      tile_position=(0, 64))
                P.emit("t", fkv, reads=["C%d" % (st // 8),
                                        "C%d" % (2 + st // 8)],
                       writes=["P6"], tag="l2kv")
        P.emit("v", lambda eng: eng.tensor_copy(kvs[0:64, 0:64],
                                                p6[0:64, 0:64]),
               reads=["P6"], writes=["KVS"], tag="l2kve0")
        P.emit("v", lambda eng: eng.tensor_copy(kvs[64:128, 64:128],
                                                p6[64:128, 0:64]),
               reads=["P6"], writes=["KVS"], tag="l2kve1")
        rowexp("l2sisa", r16a[0:2, :], "R16A", bcb, "BCB")
        P.emit("v", lambda eng: eng.tensor_mul(bt[1][:], a0[:], bcb[:]),
               reads=["A0", "BCB"], writes=["B1"], tag="l2qx")
        for hh in range(2):
            for sc in range(4):
                def fo(eng, hh=hh, sc=sc):
                    if hh == 0:
                        return eng.matmul(
                            pb[0:64, sc * 512:(sc + 1) * 512],
                            kvs[0:64, 0:64],
                            bt[1][0:64, sc * 512:(sc + 1) * 512],
                            start=True, stop=True)
                    return eng.matmul(
                        pb[64:128, sc * 512:(sc + 1) * 512],
                        kvs[64:128, 64:128],
                        bt[1][64:128, sc * 512:(sc + 1) * 512],
                        start=True, stop=True, tile_position=(64, 64))
                P.emit("t", fo, reads=["KVS", "B1"],
                       writes=["PB%d" % sc], tag="l2out")
        P.emit("v", lambda eng, pair=pair: eng.tensor_copy(
            ut[pair][:], pb[0:128, 0:2048]),
            reads=pbkeys(), writes=["U%d" % pair], tag="l2oev")
        if L2TAP is not None and pair == 0:
            P.emit("g", lambda eng: eng.dma_start(
                out=L2TAP["kv"], in_=kvs[:]),
                reads=["KVS"], writes=["__t__"], tag="tap")
            P.emit("g", lambda eng: eng.dma_start(
                out=L2TAP["out"], in_=ut[0][:]),
                reads=["U0"], writes=["__t__"], tag="tap")

    def evac(mt, sc, psum, pn):
        def fn(eng, mt=mt, sc=sc, psum=psum):
            return eng.scalar_tensor_tensor(
                xt[mt][:, sc * 512:(sc + 1) * 512], psum[:],
                boc[:, mt:mt + 1], xt[mt][:, sc * 512:(sc + 1) * 512],
                ALU.add, ALU.add)
        P.emit("v", fn, reads=[pn, "BPC", xk[mt]], writes=[xk[mt]],
               tag="l2wo_ev")
    emit_proj_fmajor(P, M, "l2wo", W["L2_WoT"][0], D, D,
                     ut, keys("U", NT), evac, wt_keys=[W["L2_WoT"][1]])


STAGE_EMITTERS[4] = emit_stage_l2_attn



def emit_stage_l2_ffn(P, M, W):
    xt = M.tiles("X", NT)
    xk = keys("X", NT)
    ut = M.tiles("U", NT)
    at = [M.sb("A%d" % i, [128, S], BF16) for i in range(3)]
    bt = M.tiles("B", NT)
    bk = keys("B", NT)
    ct = c_area(M)
    mur = M.sb("MUR", [128, 16], F32)
    p4, p5 = M.ps("P4", [128, 512]), M.ps("P5", [128, 512])

    P.emit("g", lambda eng: eng.dma_start(out=mur[:], in_=W["L2_mu"][0]),
           reads=[W["L2_mu"][1]], writes=["MUR"], tag="mu")
    sm2 = M.sb("SM2", [128, 64], F32)
    P.emit("v", lambda eng: eng.tensor_scalar(
        sm2[:, 0:16], mur[:], -1.0, 1.0, ALU.mult, ALU.add),
        reads=["MUR"], writes=["SM2"], tag="mu1")

    def mix(dst_tiles, dkeys, mucol0, nm):
        for i in range(NT):
            P.emit("v", lambda eng, i=i: eng.tensor_scalar(
                dst_tiles[i][:], xt[i][:], mur[:, mucol0 + i:mucol0 + i + 1],
                None, ALU.mult),
                reads=[xk[i], "MUR"], writes=[dkeys[i]], tag=nm + "a")
            P.emit("v", lambda eng, i=i: eng.scalar_tensor_tensor(
                dst_tiles[i][:, 1:2048], xt[i][:, 0:2047],
                sm2[:, mucol0 + i:mucol0 + i + 1],
                dst_tiles[i][:, 1:2048], ALU.mult, ALU.add),
                reads=[xk[i], "SM2", dkeys[i]], writes=[dkeys[i]],
                tag=nm + "b")

    mix(bt, bk, 0, "mxr")
    def evr(mt, sc, psum, pn):
        P.emit("s", lambda eng, mt=mt, sc=sc, psum=psum: eng.activation(
            ut[mt][:, sc * 512:(sc + 1) * 512], psum[:], AF.Sigmoid),
            reads=[pn], writes=["U%d" % mt], tag="l2r_ev")
    emit_proj_fmajor(P, M, "l2r", W["L2f_WrT"][0], D, D, bt, bk, evr,
                     wt_keys=[W["L2f_WrT"][1]])
    mix(bt, bk, 8, "mxk")
    def evk(mt, sc, psum, pn):
        P.emit("s", lambda eng, mt=mt, sc=sc, psum=psum: eng.activation(
            at[0][:, 0:512], psum[:], AF.Relu),
            reads=[pn], writes=["A0"], tag="l2k_ev1")
        P.emit("v", lambda eng, mt=mt, sc=sc: eng.tensor_mul(
            ct[mt][:, sc * 512:(sc + 1) * 512], at[0][:, 0:512],
            at[0][:, 0:512]),
            reads=["A0"], writes=["C%d" % mt], tag="l2k_ev2")
    emit_proj_fmajor(P, M, "l2k", W["L2f_WkT"][0], D, D, bt, bk, evk,
                     wt_keys=[W["L2f_WkT"][1]])
    def evv(mt, sc, psum, pn):
        P.emit("v", lambda eng, mt=mt, sc=sc, psum=psum: eng.tensor_mul(
            at[0][:, 0:512], psum[:], ut[mt][:, sc * 512:(sc + 1) * 512]),
            reads=[pn, "U%d" % mt], writes=["A0"], tag="l2v_ev1")
        P.emit("v", lambda eng, mt=mt, sc=sc: eng.tensor_add(
            xt[mt][:, sc * 512:(sc + 1) * 512],
            xt[mt][:, sc * 512:(sc + 1) * 512], at[0][:, 0:512]),
            reads=["A0", xk[mt]], writes=[xk[mt]], tag="l2v_ev2")
    emit_proj_fmajor(P, M, "l2v", W["L2f_WvT"][0], D, D,
                     [ct[i][:, 0:2048] for i in range(NT)], keys("C", NT),
                     evv, wt_keys=[W["L2f_WvT"][1]])


STAGE_EMITTERS[5] = emit_stage_l2_ffn



def emit_stage_l3_attn(P, M, W):
    xt = M.tiles("X", NT)
    xk = keys("X", NT)
    bt = M.tiles("B", 4)
    lk = keys("B", 4)
    a0, a1, a2 = M.sb("A0", [128, S], BF16), M.sb("A1", [128, S], BF16), \
        M.sb("A2", [128, S], BF16)
    ck = M.sb("BCA", [128, S], BF16)[:, 0:S]
    sk = M.sb("BCB", [128, S], BF16)[:, 0:S]
    ctiles = c_area(M)
    p4, p5 = M.ps("P4", [128, 512]), M.ps("P5", [128, 512])
    wpa = M.sb("WPA", [128, 4096], BF16)
    wpb = M.sb("WPB", [128, 4096], BF16)

    P.emit("g", lambda eng: eng.dma_start(out=ck, in_=W["ropeC"][0]),
           reads=[W["ropeC"][1]], writes=["BCA"], tag="ropec")
    P.emit("g", lambda eng: eng.dma_start(out=sk, in_=W["ropeS"][0]),
           reads=[W["ropeS"][1]], writes=["BCB"], tag="ropes")

    emit_load_wchunk(P, wpa, "WPA", W["L3_wd"][0], [W["L3_wd"][1]], NT, 0, 512)
    for mt in range(4):
        for sc in range(4):
            pn = ["P4", "P5"][(mt * 4 + sc) % 2]
            psum = [p4, p5][(mt * 4 + sc) % 2]
            for kc in range(NT):
                def fn(eng, kc=kc, mt=mt, sc=sc, psum=psum):
                    return eng.matmul(
                        psum[:], wpa[:, kc * 512 + mt * 128:
                                     kc * 512 + (mt + 1) * 128],
                        xt[kc][:, sc * 512:(sc + 1) * 512],
                        start=(kc == 0), stop=(kc == NT - 1))
                P.emit("t", fn, reads=["WPA", xk[kc]], writes=[pn],
                       tag="l3lat")
            P.emit("s", lambda eng, mt=mt, sc=sc, psum=psum: eng.activation(
                bt[mt][:, sc * 512:(sc + 1) * 512], psum[:], AF.Copy),
                reads=[pn], writes=[lk[mt]], tag="l3lat_ev")

    emit_load_wchunk(P, wpa, "WPA", W["L3_wvup"][0], [W["L3_wvup"][1]], 4, 0,
                     512)
    emit_load_wchunk(P, wpb, "WPB", W["L3_wvup"][0], [W["L3_wvup"][1]], 4, 512,
                     512)
    for kt in range(STL):
        P.emit("v", lambda eng, kt=kt: eng.memset(
            va_ap(ctiles, kt).rearrange("p (h c) -> p h c", c=65)[:, :, 64:65],
            1.0), reads=[], writes=["C%d" % (kt // 2)], tag="vones")
    for st in range(STL):
        for ec in range(2):
            pn = ["P4", "P5"][(st * 2 + ec) % 2]
            psum = [p4, p5][(st * 2 + ec) % 2]
            wsrc, wkey = (wpa, "WPA") if ec == 0 else (wpb, "WPB")
            for kc in range(4):
                def fn(eng, st=st, kc=kc, psum=psum, wsrc=wsrc):
                    return eng.matmul(
                        psum[:], bt[kc][:, st * 128:(st + 1) * 128],
                        wsrc[:, kc * 512:(kc + 1) * 512],
                        start=(kc == 0), stop=(kc == 3))
                P.emit("t", fn, reads=[lk[kc], wkey], writes=[pn], tag="l3v")
            def fne(eng, st=st, ec=ec, psum=psum):
                dst = va_ap(ctiles, st).rearrange("p (h c) -> p h c", c=65)[
                    :, 8 * ec:8 * ec + 8, 0:64]
                src = psum[:].rearrange("p (h c) -> p h c", c=64)
                return eng.activation(dst, src, AF.Copy)
            P.emit("s", fne, reads=[pn], writes=["C%d" % (st // 2)],
                   tag="l3v_ev")

    qwb = M.sb("WPA", [128, 4096], BF16)[:, 0:1024]
    kwb = M.sb("WPA", [128, 4096], BF16)[:, 1024:1536]

    def proj_qk(pair):
        emit_load_wchunk(P, qwb, "WPA", W["L3_wq"][0], [W["L3_wq"][1]], NT,
                         pair * 128, 128)
        emit_load_wchunk(P, kwb, "WPA", W["L3_wkup"][0], [W["L3_wkup"][1]], 4,
                         pair * 128, 128)
        for sc in range(4):
            psum = p4
            for kc in range(NT):
                def fn(eng, kc=kc, sc=sc, psum=psum):
                    return eng.matmul(
                        psum[:], qwb[:, kc * 128:(kc + 1) * 128],
                        xt[kc][:, sc * 512:(sc + 1) * 512],
                        start=(kc == 0), stop=(kc == NT - 1))
                P.emit("t", fn, reads=["WPA", xk[kc]], writes=["P4"],
                       tag="l3q")
            P.emit("s", lambda eng, sc=sc, psum=psum: eng.activation(
                a0[:, sc * 512:(sc + 1) * 512], psum[:], AF.Copy, scale=0.125),
                reads=["P4"], writes=["A0"], tag="l3q_ev")
        for sc in range(4):
            psum = p5
            for kc in range(4):
                def fn(eng, kc=kc, sc=sc, psum=psum):
                    return eng.matmul(
                        psum[:], kwb[:, kc * 128:(kc + 1) * 128],
                        bt[kc][:, sc * 512:(sc + 1) * 512],
                        start=(kc == 0), stop=(kc == 3))
                P.emit("t", fn, reads=["WPA", lk[kc]], writes=["P5"],
                       tag="l3k")
            P.emit("s", lambda eng, sc=sc, psum=psum: eng.activation(
                a1[:, sc * 512:(sc + 1) * 512], psum[:], AF.Copy),
                reads=["P5"], writes=["A1"], tag="l3k_ev")
        for (dst, dkey) in [(a0, "A0"), (a1, "A1")]:
            P.emit("v", lambda eng, dst=dst: eng.stream_shuffle(
                a2[:], dst[:], [i ^ 1 for i in range(32)]),
                reads=[dkey], writes=["A2"], tag="rope_sh")
            P.emit("v", lambda eng, dst=dst: eng.tensor_mul(dst[:], dst[:], ck),
                   reads=[dkey, "BCA"], writes=[dkey], tag="rope_c")
            P.emit("v", lambda eng: eng.tensor_mul(a2[:], a2[:], sk),
                   reads=["A2", "BCB"], writes=["A2"], tag="rope_s")
            P.emit("v", lambda eng, dst=dst: eng.tensor_add(dst[:], dst[:],
                                                            a2[:]),
                   reads=[dkey, "A2"], writes=[dkey], tag="rope_a")

    emit_attention(P, M, "l3", proj_qk, lambda kt: 256 * (kt // 2), False)

    def evac(mt, sc, psum, pn):
        def fn(eng, mt=mt, sc=sc, psum=psum):
            return eng.tensor_add(xt[mt][:, sc * 512:(sc + 1) * 512],
                                  xt[mt][:, sc * 512:(sc + 1) * 512],
                                  psum[:])
        P.emit("v", fn, reads=[pn, xk[mt]], writes=[xk[mt]], tag="l3wo_ev")
    emit_proj_fmajor(P, M, "l3wo", W["L3_wo"][0], D, D,
                     M.tiles("U", NT), keys("U", NT), evac,
                     wt_keys=[W["L3_wo"][1]])


STAGE_EMITTERS[6] = emit_stage_l3_attn



def emit_stage_l3_ffn(P, M, W):
    xt = M.tiles("X", NT)
    xk = keys("X", NT)
    ut = M.tiles("U", NT)
    ct = c_area(M)
    ci = [ct[i][:, 0:2048] for i in range(NT)]
    cik = keys("C", NT)
    mxt = M.sb("U3", [128, S], BF16)
    sclx = M.sb("T32A", [128, S], F32)
    sclx2 = M.sb("T32B", [128, S], F32)
    sm1 = M.sb("SM1", [128, 64], F32)
    p4, p5 = M.ps("P4", [128, 512]), M.ps("P5", [128, 512])
    wpa = M.sb("WPA", [128, 4096], BF16)
    wpb = M.sb("WPB", [128, 4096], BF16)
    wgq = M.dr("L3WGQ", [D, 4096], BF16)
    wuq = M.dr("L3WUQ", [D, 4096], BF16)
    wdq = M.dr("L3WDQ", [4096, D], BF16)

    emit_weight_quant(P, M, "l3wg", W["L3f_wgT"][0], W["L3f_wgT"][1], D, 4096,
                      wgq, "L3WGQ")
    P.emit("v", lambda eng: eng.tensor_copy(sm1[0:1, 8:9], sm1[0:1, 2:3]),
           reads=["SM1"], writes=["SM1"], tag="mwg")
    emit_weight_quant(P, M, "l3wu", W["L3f_wuT"][0], W["L3f_wuT"][1], D, 4096,
                      wuq, "L3WUQ")
    P.emit("v", lambda eng: eng.tensor_copy(sm1[0:1, 9:10], sm1[0:1, 2:3]),
           reads=["SM1"], writes=["SM1"], tag="mwu")
    emit_weight_quant(P, M, "l3wd", W["L3f_wdT"][0], W["L3f_wdT"][1], 4096, D,
                      wdq, "L3WDQ")
    P.emit("v", lambda eng: eng.tensor_copy(sm1[0:1, 10:11], sm1[0:1, 2:3]),
           reads=["SM1"], writes=["SM1"], tag="mwd")
    P.emit("v", lambda eng: eng.tensor_mul(sm1[0:1, 12:13], sm1[0:1, 8:9],
                                           sm1[0:1, 9:10]),
           reads=["SM1"], writes=["SM1"], tag="mwc1")
    P.emit("v", lambda eng: eng.tensor_mul(sm1[0:1, 12:13], sm1[0:1, 12:13],
                                           sm1[0:1, 10:11]),
           reads=["SM1"], writes=["SM1"], tag="mwc2")
    emit_bcast_row(P, M, "l3mwg", sm1[0:1, 8:9], "SM1", 128, sm1[:, 13:14],
                   "SM1", 1)

    absx = M.sb("U5", [128, S], BF16)
    for i in range(NT):
        P.emit("s", lambda eng, i=i: eng.activation(absx[:], xt[i][:], AF.Abs),
               reads=[xk[i]], writes=["U5"], tag="l3aqa")
        if i == 0:
            P.emit("v", lambda eng: eng.tensor_copy(mxt[:], absx[:]),
                   reads=["U5"], writes=["U3"], tag="l3aq0")
        else:
            P.emit("v", lambda eng: eng.tensor_max(mxt[:], mxt[:], absx[:]),
                   reads=["U3", "U5"], writes=["U3"], tag="l3aq")
    emit_act_quant(P, M, "l3x", S, 0, True, None, None)
    for i in range(NT):
        P.emit("v", lambda eng, i=i: eng.tensor_mul(ci[i], xt[i][:], mxt[:]),
               reads=[xk[i], "U3"], writes=[cik[i]], tag="l3xm")
        P.emit("v", lambda eng, i=i: eng.tensor_scalar(
            ci[i], ci[i], MAGIC, MAGIC, ALU.add, ALU.subtract),
            reads=[cik[i]], writes=[cik[i]], tag="l3xr")

    gud = M.dr("L3GUD", [4096, S], BF16)
    pb0 = M.ps("PB", [128, 2048])
    p6 = M.ps("P6", [128, 512])
    banksets = [[(p4, "P4"), (p5, "P5")], [(p6, "P6"), (pb0, "PB0")]]

    def loadgu(mt, par):
        wb = [wpa, wpb][par]
        wk = ["WPA", "WPB"][par]
        def fg(eng, mt=mt, wb=wb):
            src = wgq[:, mt * 128:(mt + 1) * 128].rearrange(
                "(kc p) c -> p kc c", p=128)
            return eng.dma_start(out=wb[:, 0:1024], in_=src)
        P.emit("g", fg, reads=["L3WGQ"], writes=[wk], tag="wgl")
        def fu(eng, mt=mt, wb=wb):
            src = wuq[:, mt * 128:(mt + 1) * 128].rearrange(
                "(kc p) c -> p kc c", p=128)
            return eng.dma_start(out=wb[:, 1024:2048].rearrange(
                "p c -> p c"), in_=src)
        P.emit("g", fu, reads=["L3WUQ"], writes=[wk], tag="wul")

    loadgu(0, 0)
    for mt in range(32):
        par = mt % 2
        if mt + 1 < 32:
            loadgu(mt + 1, 1 - par)
        wb = [wpa, wpb][par]
        wk = ["WPA", "WPB"][par]
        bs = banksets[mt % 2]
        for sc in range(4):
            (pg, pgn), (pu, pun) = bs
            for kc in range(NT):
                P.emit("t", lambda eng, kc=kc, sc=sc, wb=wb, pg=pg: eng.matmul(
                    pg[:, 0:512], wb[:, kc * 128:(kc + 1) * 128],
                    ci[kc][:, sc * 512:(sc + 1) * 512],
                    start=(kc == 0), stop=(kc == NT - 1)),
                    reads=[wk, cik[kc]], writes=[pgn], tag="l3g")
            for kc in range(NT):
                P.emit("t", lambda eng, kc=kc, sc=sc, wb=wb, pu=pu: eng.matmul(
                    pu[:, 0:512], wb[:, 1024 + kc * 128:1024 + (kc + 1) * 128],
                    ci[kc][:, sc * 512:(sc + 1) * 512],
                    start=(kc == 0), stop=(kc == NT - 1)),
                    reads=[wk, cik[kc]], writes=[pun], tag="l3u")
            P.emit("v", lambda eng, sc=sc, pg=pg, pgn=pgn: eng.tensor_mul(
                ut[0][:, sc * 512:(sc + 1) * 512], pg[:, 0:512],
                sclx[:, sc * 512:(sc + 1) * 512]),
                reads=[pgn, "T32A"], writes=["U0"], tag="l3gev")
            P.emit("v", lambda eng, sc=sc, pu=pu, pun=pun: eng.tensor_mul(
                ut[1][:, sc * 512:(sc + 1) * 512], pu[:, 0:512],
                sclx[:, sc * 512:(sc + 1) * 512]),
                reads=[pun, "T32A"], writes=["U1"], tag="l3uev")
        P.emit("s", lambda eng: eng.activation(
            ut[2][:], ut[0][:], AF.Sigmoid, scale=sm1[:, 13:14]),
            reads=["U0", "SM1"], writes=["U2"], tag="l3sg")
        P.emit("v", lambda eng: eng.tensor_mul(ut[0][:], ut[0][:], ut[2][:]),
               reads=["U0", "U2"], writes=["U0"], tag="l3gg")
        P.emit("v", lambda eng: eng.tensor_mul(ut[4][:], ut[0][:], ut[1][:]),
               reads=["U0", "U1"], writes=["U4"], tag="l3gu")
        P.emit("s", lambda eng: eng.activation(absx[:], ut[4][:], AF.Abs),
               reads=["U4"], writes=["U5"], tag="l3ga")
        if mt == 0:
            P.emit("v", lambda eng: eng.tensor_copy(mxt[:], absx[:]),
                   reads=["U5"], writes=["U3"], tag="l3gm0")
        else:
            P.emit("v", lambda eng: eng.tensor_max(mxt[:], mxt[:], absx[:]),
                   reads=["U3", "U5"], writes=["U3"], tag="l3gm")
        P.emit("y", lambda eng, mt=mt: eng.dma_start(
            out=gud[mt * 128:(mt + 1) * 128, :], in_=ut[4][:]),
            reads=["U4"], writes=["L3GUD"], tag="l3guo")

    emit_act_quant(P, M, "l3gu", S, 0, True, sm1[0:1, 12:13], None,
                   sclx_name="T32B")

    w2b = [wpa, wpb]
    w2bk = ["WPA", "WPB"]
    for sc in range(4):
        for i in range(NT):
            def fl(eng, i=i, sc=sc):
                src = gud[:, sc * 512:(sc + 1) * 512].rearrange(
                    "(j q p) c -> p j q c", p=128, q=NT)[:, :, i, :]
                return eng.dma_start(
                    out=ci[i].rearrange("p (j c) -> p j c", j=4), in_=src)
            P.emit("g", fl, reads=["L3GUD"], writes=[cik[i]], tag="l3gi")
            for j in range(4):
                P.emit("v", lambda eng, i=i, j=j, sc=sc: eng.tensor_mul(
                    ci[i][:, j * 512:(j + 1) * 512],
                    ci[i][:, j * 512:(j + 1) * 512],
                    mxt[:, sc * 512:(sc + 1) * 512]),
                    reads=[cik[i], "U3"], writes=[cik[i]], tag="l3gqm")
            P.emit("v", lambda eng, i=i: eng.tensor_scalar(
                ci[i], ci[i], MAGIC, MAGIC, ALU.add, ALU.subtract),
                reads=[cik[i]], writes=[cik[i]], tag="l3gqr")
        def loadwd(dt, par):
            wb = [wpa, wpb][par]
            def fn(eng, dt=dt, wb=wb):
                src = wdq[:, dt * 128:(dt + 1) * 128].rearrange(
                    "(kc p) c -> p kc c", p=128)
                return eng.dma_start(out=wb[:, 0:4096], in_=src)
            P.emit("g", fn, reads=["L3WDQ"], writes=[["WPA", "WPB"][par]],
                   tag="wdl")
        loadwd(0, 0)
        for dt in range(NT):
            par = dt % 2
            if dt + 1 < NT:
                loadwd(dt + 1, 1 - par)
            wb = [wpa, wpb][par]
            wk = ["WPA", "WPB"][par]
            pn = ["P4", "P5", "P6"][dt % 3]
            psum = [p4, p5, p6][dt % 3]
            for kc in range(32):
                def fn(eng, kc=kc, wb=wb, psum=psum):
                    return eng.matmul(
                        psum[:], wb[:, kc * 128:(kc + 1) * 128],
                        ci[kc % NT][:, (kc // NT) * 512:(kc // NT) * 512 + 512],
                        start=(kc == 0), stop=(kc == 31))
                P.emit("t", fn, reads=[wk, cik[kc % NT]], writes=[pn],
                       tag="l3dmm")
            P.emit("v", lambda eng, sc=sc, psum=psum, pn=pn: eng.tensor_mul(
                ut[0][:, 0:512], psum[:], sclx2[:, sc * 512:(sc + 1) * 512]),
                reads=[pn, "T32B"], writes=["U0"], tag="l3dev1")
            P.emit("v", lambda eng, dt=dt, sc=sc: eng.tensor_add(
                xt[dt][:, sc * 512:(sc + 1) * 512],
                xt[dt][:, sc * 512:(sc + 1) * 512], ut[0][:, 0:512]),
                reads=["U0", xk[dt]], writes=[xk[dt]], tag="l3dev2")


STAGE_EMITTERS[7] = emit_stage_l3_ffn


# version 06
# speedup vs baseline: 1.2068x; 1.0794x over previous
import math
from contextlib import ExitStack

import numpy as np
import ml_dtypes
import concourse.bass as bass
import concourse.mybir as mybir
from concourse.bass_utils import run_bass_kernel_spmd

F32 = mybir.dt.float32
BF16 = mybir.dt.bfloat16
FP16 = mybir.dt.float16
FP8 = mybir.dt.float8e4
AF = mybir.ActivationFunctionType
ALU = mybir.AluOpType
AX = mybir.AxisListType.X

NCORE = 8
D = 1024
S = 2048
NT = D // 128
STL = S // 128
NH = 16
DK = 64
MAGIC = 12582912.0
NEGBIG = -1.0e30
LN_EPS = 1e-5

_bf = ml_dtypes.bfloat16


class Prog:
    NQ = 8

    def __init__(self, nc):
        self.nc = nc
        self.stream = {e: [] for e in "tvsyg"}
        self.semnames = ["t", "v", "s"] + \
            ["y%d" % i for i in range(self.NQ)] + \
            ["g%d" % i for i in range(self.NQ)]
        self.cnt = {sn: 0 for sn in self.semnames}
        self.dmaidx = {"y": 0, "g": 0}
        self.lastw = {}
        self.readers = {}
        self.hw = {e: {} for e in "tvsyg"}

    def emit(self, eng, fn, reads=(), writes=(), tag=""):
        need = {}
        if eng in "yg":
            slot = self.dmaidx[eng] % self.NQ
            self.dmaidx[eng] += 1
            sn = eng + str(slot)
            if self.cnt[sn]:
                need[sn] = self.cnt[sn]
        else:
            sn = eng
        for b in reads:
            w = self.lastw.get(b)
            if w is not None and not (eng == "t" and w[0] == "t"):
                need[w[0]] = max(need.get(w[0], 0), w[1])
        for b in writes:
            w = self.lastw.get(b)
            if w is not None and not (eng == "t" and w[0] == "t"):
                need[w[0]] = max(need.get(w[0], 0), w[1])
            for s2, c2 in self.readers.get(b, {}).items():
                if not (eng == "t" and s2 == "t"):
                    need[s2] = max(need.get(s2, 0), c2)
        for s2 in sorted(need):
            c2 = need[s2]
            if self.hw[eng].get(s2, 0) >= c2:
                continue
            self.stream[eng].append(("wait", s2, c2))
            self.hw[eng][s2] = c2
        inc = 16 if eng in "yg" else 1
        self.cnt[sn] += inc
        c = self.cnt[sn]
        self.stream[eng].append(("op", fn, sn))
        for b in writes:
            self.lastw[b] = (sn, c)
            self.readers[b] = {}
        for b in reads:
            self.readers.setdefault(b, {})[sn] = c

    def finalize(self, ctx):
        nc = self.nc
        sems = {sn: ctx.enter_context(nc.semaphore("sem_" + sn))
                for sn in self.semnames}
        block = ctx.enter_context(nc.Block())
        streams, cnt = self.stream, self.cnt

        def replay(e):
            inc = 16 if e in "yg" else 1

            def body(eng):
                for item in streams[e]:
                    if item[0] == "wait":
                        eng.wait_ge(sems[item[1]], item[2])
                    else:
                        item[1](eng).then_inc(sems[item[2]], inc)
                if e == "g":
                    for sn in self.semnames:
                        if cnt[sn]:
                            eng.wait_ge(sems[sn], cnt[sn])
            return body

        block.tensor(replay("t"))
        block.vector(replay("v"))
        block.scalar(replay("s"))
        block.sync(replay("y"))
        block.gpsimd(replay("g"))


class Mem:
    def __init__(self, nc, ctx):
        self.nc = nc
        self.ctx = ctx
        self.t = {}
        self.dram = {}

    def sb(self, name, shape, dtype):
        if name not in self.t:
            self.t[name] = self.ctx.enter_context(
                self.nc.sbuf_tensor(name, shape, dtype))
        return self.t[name]

    def ps(self, name, shape, dtype=F32):
        if name not in self.t:
            self.t[name] = self.ctx.enter_context(
                self.nc.psum_tensor(name, shape, dtype))
        return self.t[name]

    def dr(self, name, shape, dtype=F32):
        if name not in self.dram:
            self.dram[name] = self.nc.dram_tensor(name, shape, dtype,
                                                  kind="Internal")
        return self.dram[name].ap()

    def tiles(self, base, n, cols=S, dtype=BF16):
        return [self.sb("%s%d" % (base, i), [128, cols], dtype)
                for i in range(n)]


def keys(base, n):
    return ["%s%d" % (base, i) for i in range(n)]



def emit_load_wchunk(P, dst, dst_key, src_ap, src_keys, kt, c0, cols):
    def fn(eng):
        src = src_ap[:, c0:c0 + cols]
        src3 = src.rearrange("(kc p) c -> p kc c", p=128)
        return eng.dma_start(out=dst[:, 0:kt * cols], in_=src3)
    P.emit("g", fn, reads=list(src_keys), writes=[dst_key], tag="wload")


def emit_proj_fmajor(P, M, name, wt_ap, Kdim, Mout, rhs_tiles, rhs_keys, evac,
                     wchunk=512, wdtype=BF16, wt_keys=(), pbanks=("P4", "P5")):
    kt = Kdim // 128
    assert kt * wchunk <= 4096 and wdtype == BF16
    wb = [M.sb("WPA", [128, 4096], BF16), M.sb("WPB", [128, 4096], BF16)]
    wk = ["WPA", "WPB"]
    nchunk = (Mout + wchunk - 1) // wchunk
    emit_load_wchunk(P, wb[0], wk[0], wt_ap, wt_keys, kt, 0, min(wchunk, Mout))
    pidx = 0
    for ci in range(nchunk):
        par = ci % 2
        if ci + 1 < nchunk:
            c0n = (ci + 1) * wchunk
            emit_load_wchunk(P, wb[1 - par], wk[1 - par], wt_ap, wt_keys, kt,
                             c0n, min(wchunk, Mout - c0n))
        ntile = min(wchunk, Mout - ci * wchunk) // 128
        for mt in range(ntile):
            for sc in range(4):
                pn = pbanks[pidx % len(pbanks)]
                pidx += 1
                psum = M.ps(pn, [128, 512])
                for kc in range(kt):
                    def fn(eng, kc=kc, mt=mt, sc=sc, par=par, psum=psum):
                        lhsT = wb[par][:, kc * wchunk + mt * 128:
                                       kc * wchunk + (mt + 1) * 128]
                        rhs = rhs_tiles[kc][:, sc * 512:(sc + 1) * 512]
                        return eng.matmul(psum[:], lhsT, rhs,
                                          start=(kc == 0), stop=(kc == kt - 1))
                    P.emit("t", fn, reads=[wk[par], rhs_keys[kc]], writes=[pn],
                           tag=name + "mm")
                evac(ci * (wchunk // 128) + mt, sc, psum, pn)


def emit_bcast_row(P, M, name, src_row_ap, src_key, npart, dst_ap, dst_key,
                   ncols):
    dr = M.dr("dsc_" + name, [1, ncols], F32)
    dk = "dsc_" + name
    P.emit("y", lambda eng: eng.dma_start(out=dr[:, 0:ncols], in_=src_row_ap),
           reads=[src_key], writes=[dk], tag="bco")
    P.emit("g", lambda eng: eng.dma_start(
        out=dst_ap, in_=dr[0:1, 0:ncols].partition_broadcast(npart)),
        reads=[dk], writes=[dst_key], tag="bci")


def sq_views(M):
    t32 = [M.sb("T32A", [128, S], F32), M.sb("T32B", [128, S], F32)]
    return ([t32[0].bitcast(BF16)[:, 0:S], t32[1].bitcast(BF16)[:, 0:S]],
            ["T32A", "T32B"])


def c_area(M):
    ct = [M.sb("C%d" % i, [128, 2080], BF16) for i in range(NT)]
    return ct


def va_ap(ct, kt):
    return ct[kt // 2][:, (kt % 2) * 1040:(kt % 2) * 1040 + 1040]


def fq8_views(M):
    ut = M.tiles("U", NT)
    aps = []
    ks = []
    for i in range(NT):
        host = ut[2 + i // 2]
        aps.append(host.bitcast(FP8)[:, (i % 2) * 2048:(i % 2) * 2048 + 2048])
        ks.append("U%d" % (2 + i // 2))
    return aps, ks


def h8_views(M):
    ut = M.tiles("U", NT)
    b0 = M.sb("B0", [128, S], BF16)
    b1 = M.sb("B1", [128, S], BF16)
    hosts = [(ut[6], "U6"), (ut[7], "U7"), (b0, "B0"), (b1, "B1")]
    aps, ks = [], []
    for i in range(32):
        h, k = hosts[i // 8]
        aps.append(h.bitcast(FP8)[:, (i % 8) * 512:(i % 8) * 512 + 512])
        ks.append(k)
    return aps, ks


def emit_ln(P, M, x_tiles, x_keys, eps=LN_EPS, out_tiles=None, out_keys=None,
            fq8=False, f32out=None):
    ones = M.sb("ONESC", [128, 1], BF16)
    p6 = M.ps("P6", [128, 512])
    pb = M.ps("PB", [128, 2048])
    r16a = M.sb("R16A", [16, S], F32)
    r16b = M.sb("R16B", [16, S], F32)
    bc1 = M.sb("BCA", [128, S], BF16)
    bc2 = M.sb("BCB", [128, S], BF16)
    t32 = [M.sb("T32A", [128, S], F32), M.sb("T32B", [128, S], F32)]
    sq, sqk = sq_views(M)
    if out_tiles is None:
        out_tiles, out_keys = x_tiles, x_keys
    if fq8:
        q8, q8k = fq8_views(M)
    for sc in range(4):
        for i in range(NT):
            def fn(eng, i=i, sc=sc):
                return eng.matmul(p6[0:1, 0:512], ones[:, 0:1],
                                  x_tiles[i][:, sc * 512:(sc + 1) * 512],
                                  start=(i == 0), stop=(i == NT - 1))
            P.emit("t", fn, reads=["ONESC", x_keys[i]], writes=["P6"],
                   tag="ln_sum")
        P.emit("v", lambda eng, sc=sc: eng.tensor_scalar(
            r16a[0:1, sc * 512:(sc + 1) * 512], p6[0:1, 0:512],
            1.0 / D, None, ALU.mult),
            reads=["P6"], writes=["R16A"], tag="ln_mean")
    for i in range(NT):
        P.emit("s", lambda eng, i=i: eng.activation(
            sq[i % 2], x_tiles[i][:], AF.Square),
            reads=[x_keys[i]], writes=[sqk[i % 2]], tag="ln_sq")
        for sc in range(4):
            def fn(eng, i=i, sc=sc):
                return eng.matmul(pb[0:1, sc * 512:(sc + 1) * 512], ones[:, 0:1],
                                  sq[i % 2][:, sc * 512:(sc + 1) * 512],
                                  start=(i == 0), stop=(i == NT - 1))
            P.emit("t", fn, reads=["ONESC", sqk[i % 2]],
                   writes=["PB%d" % sc], tag="ln_ssq")
    P.emit("v", lambda eng: eng.tensor_scalar(
        r16b[0:1, :], pb[0:1, 0:2048], 1.0 / D, None, ALU.mult),
        reads=pbkeys(), writes=["R16B"], tag="ln_msq")
    P.emit("v", lambda eng: eng.tensor_mul(
        t32[0][0:1, 0:2048], r16a[0:1, :], r16a[0:1, :]),
        reads=["R16A"], writes=["T32A"], tag="ln_m2")
    P.emit("v", lambda eng: eng.scalar_tensor_tensor(
        r16b[0:1, :], r16b[0:1, :], float(eps), t32[0][0:1, 0:2048],
        ALU.add, ALU.subtract),
        reads=["R16B", "T32A"], writes=["R16B"], tag="ln_var")
    P.emit("v", lambda eng: eng.reciprocal(r16b[0:1, :], r16b[0:1, :]),
           reads=["R16B"], writes=["R16B"], tag="ln_rec")
    P.emit("s", lambda eng: eng.activation(r16b[0:1, :], r16b[0:1, :], AF.Sqrt),
           reads=["R16B"], writes=["R16B"], tag="ln_rstd")
    emit_bcast_row(P, M, "lnm", r16a[0:1, :], "R16A", 128, bc1[:], "BCA", S)
    emit_bcast_row(P, M, "lnr", r16b[0:1, :], "R16B", 128, bc2[:], "BCB", S)
    for i in range(NT):
        tk = "T32%s" % ("A" if i % 2 == 0 else "B")
        tt = t32[i % 2]
        P.emit("v", lambda eng, i=i, tt=tt: eng.tensor_sub(
            tt[:], x_tiles[i][:], bc1[:]),
            reads=[x_keys[i], "BCA"], writes=[tk], tag="ln_a1")
        if fq8:
            P.emit("v", lambda eng, i=i, tt=tt: eng.tensor_mul(
                q8[i], tt[:], bc2[:]),
                reads=[tk, "BCB"], writes=[q8k[i]], tag="ln_a8")
        if f32out is not None:
            P.emit("v", lambda eng, i=i, tt=tt: eng.tensor_mul(
                tt[:], tt[:], bc2[:]),
                reads=[tk, "BCB"], writes=[tk], tag="ln_a32")
            P.emit("g", lambda eng, i=i, tt=tt: eng.dma_start(
                out=f32out[i * 128:(i + 1) * 128, :], in_=tt[:]),
                reads=[tk], writes=["__out__"], tag="ln_oy")
            P.emit("v", lambda eng, i=i, tt=tt: eng.tensor_copy(
                out_tiles[i][:], tt[:]),
                reads=[tk], writes=[out_keys[i]], tag="ln_a2c")
        else:
            P.emit("v", lambda eng, i=i, tt=tt: eng.tensor_mul(
                out_tiles[i][:], tt[:], bc2[:]),
                reads=[tk, "BCB"], writes=[out_keys[i]], tag="ln_a2")


def pbkeys(c0=0, c1=2048):
    return ["PB%d" % i for i in range(c0 // 512, (c1 + 511) // 512)]



def emit_attention(P, M, name, proj_qk, kstart_fn, masked_diag):
    a0, a1 = M.sb("A0", [128, S], BF16), M.sb("A1", [128, S], BF16)
    a3, a4 = M.sb("A3", [128, S], BF16), M.sb("A4", [128, S], BF16)
    qk_bufs = [((a0, "A0"), (a1, "A1")), ((a3, "A3"), (a4, "A4"))]
    sq, sqk = sq_views(M)
    sqf0 = M.sb("T32A", [128, S], F32)
    idn = M.sb("IDN", [128, 128], BF16)
    mtri = M.sb("MTRI", [128, 128], BF16)
    negb = M.sb("NEGB", [128, 16], F32)
    r16a = M.sb("R16A", [16, S], F32)
    r16b = M.sb("R16B", [16, S], F32)
    bc1 = M.sb("B4", [128, S], BF16)
    bo = M.sb("BO16", [128, 16], BF16)
    pb = M.ps("PB", [128, 2048])
    p4, p5 = M.ps("P4", [128, 512]), M.ps("P5", [128, 512])
    pq = M.ps("P6", [128, 512])
    ctiles = c_area(M)
    utiles = M.tiles("U", NT)
    dmh = M.sb("DMHB", [128, 1024], FP16)
    pbuf = [dmh.bitcast(BF16)[:, 0:512], dmh.bitcast(BF16)[:, 512:1024]]
    drs = M.dr(name + "_sums", [16, S], F32)

    proj_qk(0, *qk_bufs[0][0], *qk_bufs[0][1])
    for pair in range(NH // 2):
        (qa, qak), (ka, kak) = qk_bufs[pair % 2]
        if pair + 1 < NH // 2:
            proj_qk(pair + 1, *qk_bufs[(pair + 1) % 2][0],
                    *qk_bufs[(pair + 1) % 2][1])
        for t, (src, skey) in enumerate([(qa, qak), (ka, kak)]):
            dstrow = r16a if t == 0 else r16b
            dkey = "R16A" if t == 0 else "R16B"
            P.emit("s", lambda eng, src=src: eng.activation(
                sq[0], src[:], AF.Square),
                reads=[skey], writes=[sqk[0]], tag="bn_sq")
            for sc in range(4):
                def fn(eng, sc=sc, pair=pair):
                    return eng.matmul(pq[0:2, 0:512], bo[:, 0:2],
                                      sq[0][:, sc * 512:(sc + 1) * 512],
                                      start=True, stop=True)
                P.emit("t", fn, reads=["BO16", "T32A"], writes=["P6"],
                       tag="bn_mm")
                P.emit("v", lambda eng, sc=sc, dstrow=dstrow, dkey=dkey:
                       eng.tensor_copy(
                           dstrow[0:2, sc * 512:(sc + 1) * 512],
                           pq[0:2, 0:512]),
                       reads=["P6"], writes=[dkey], tag="bn_cp")
        P.emit("v", lambda eng: eng.reduce_max(
            sqf0[0:2, 0:1], r16a[0:2, :], axis=AX),
            reads=["R16A"], writes=["T32A"], tag="bn_rmax")
        P.emit("v", lambda eng: eng.reduce_max(
            sqf0[0:2, 1:2], r16b[0:2, :], axis=AX),
            reads=["R16B"], writes=["T32A"], tag="bn_rmax2")
        P.emit("v", lambda eng: eng.tensor_mul(
            sqf0[0:2, 2:3], sqf0[0:2, 0:1], sqf0[0:2, 1:2]),
            reads=["T32A"], writes=["T32A"], tag="bn_mul")
        P.emit("s", lambda eng: eng.activation(
            sqf0[0:2, 2:3], sqf0[0:2, 2:3], AF.Sqrt),
            reads=["T32A"], writes=["T32A"], tag="bn_sqrt")
        P.emit("v", lambda eng: eng.tensor_scalar(
            sqf0[0:2, 2:3], sqf0[0:2, 2:3], -1.0, -1.0, ALU.add, ALU.mult),
            reads=["T32A"], writes=["T32A"], tag="bn_neg")
        dnb = M.dr("dnb", [1, 16], F32)
        P.emit("y", lambda eng, pair=pair: eng.dma_start(
            out=dnb[0:1, 2 * pair:2 * pair + 2],
            in_=sqf0[0:2, 2:3]),
            reads=["T32A"], writes=["dnb"], tag="bn_out")
        P.emit("g", lambda eng, pair=pair: eng.dma_start(
            out=negb[:, 2 * pair:2 * pair + 2],
            in_=dnb[0:1, 2 * pair:2 * pair + 2].partition_broadcast(128)),
            reads=["dnb"], writes=["NEGB"], tag="bn_in")
        for hh in range(2):
            h = 2 * pair + hh
            qrow = qa[64 * hh:64 * (hh + 1), :]
            krow = ka[64 * hh:64 * (hh + 1), :]
            sidx = 0
            pend = None
            for kt in range(STL):
                q0 = kstart_fn(kt)
                bounds = []
                c = q0
                while c < S:
                    e = min(S, (c // 512 + 1) * 512)
                    bounds.append((c, e))
                    c = e
                for (c0, c1) in bounds:
                    w = c1 - c0
                    pS = [p4, p5][sidx % 2]
                    pSk = ["P4", "P5"][sidx % 2]
                    first = (c0 == q0 and masked_diag)
                    if first:
                        def fnm(eng, c0=c0, w=w, pS=pS):
                            return eng.matmul(pS[:, 0:min(w, 128)], idn[:],
                                              mtri[:, 0:min(w, 128)],
                                              start=True, stop=False)
                        P.emit("t", fnm, reads=["IDN", "MTRI"], writes=[pSk],
                               tag="attn_mask")
                    def fns(eng, kt=kt, c0=c0, w=w, pS=pS, krow=krow,
                            qrow=qrow, first=first):
                        return eng.matmul(
                            pS[:, 0:w], krow[:, kt * 128:(kt + 1) * 128],
                            qrow[:, c0:c0 + w],
                            start=not first, stop=True)
                    P.emit("t", fns, reads=[kak, qak], writes=[pSk],
                           tag="attn_sc")
                    if pend is not None:
                        P.emit("t", pend[0], reads=pend[1], writes=pend[2],
                               tag="attn_av")
                        pend = None
                    pf = pbuf[sidx % 2]
                    pfk = "DMHB"
                    def fne(eng, w=w, pS=pS, pf=pf, h=h):
                        return eng.activation(pf[:, 0:w], pS[:, 0:w], AF.Exp,
                                              bias=negb[:, h:h + 1])
                    P.emit("s", fne, reads=[pSk, "NEGB"], writes=[pfk],
                           tag="attn_exp")
                    kt_last = (c1 - 1) // 128
                    def fnv(eng, kt=kt, c0=c0, w=w, pf=pf, h=h,
                            kt_last=kt_last):
                        va = va_ap(ctiles, kt)
                        return eng.matmul(
                            pb[0:65, c0:c0 + w],
                            va[:, 65 * h:65 * h + 65], pf[:, 0:w],
                            start=(kt == 0), stop=(kt == kt_last))
                    pend = (fnv, ["C%d" % (kt // 2), pfk],
                            pbkeys(c0, c1))
                    sidx += 1
            if pend is not None:
                P.emit("t", pend[0], reads=pend[1], writes=pend[2],
                       tag="attn_av")
                pend = None
            P.emit("s", lambda eng: eng.activation(
                r16b[0:1, :], pb[64:65, 0:2048], AF.Copy),
                reads=pbkeys(), writes=["R16B"], tag="attn_sum")
            P.emit("y", lambda eng, h=h: eng.dma_start(
                out=drs[h:h + 1, :], in_=r16b[0:1, :]),
                reads=["R16B"], writes=[name + "_sums"], tag="attn_so")
            def fnev(eng, hh=hh, h=h):
                return eng.tensor_copy(
                    utiles[h // 2][64 * hh:64 * (hh + 1), :],
                    pb[0:64, 0:2048])
            P.emit("v", fnev, reads=pbkeys(),
                   writes=["U%d" % (h // 2)], tag="attn_ev")
    P.emit("g", lambda eng: eng.dma_start(out=r16a[0:16, :], in_=drs[:, :]),
           reads=[name + "_sums"], writes=["R16A"], tag="attn_rl")
    P.emit("v", lambda eng: eng.reciprocal(r16a[0:16, :], r16a[0:16, :]),
           reads=["R16A"], writes=["R16A"], tag="attn_rec")
    P.emit("y", lambda eng: eng.dma_start(out=drs[:, :], in_=r16a[0:16, :]),
           reads=["R16A"], writes=[name + "_sums"], tag="attn_ro")
    for p in range(8):
        P.emit("g", lambda eng, p=p: eng.dma_start(
            out=bc1[0:64, :],
            in_=drs[2 * p:2 * p + 1, :].partition_broadcast(64)),
            reads=[name + "_sums"], writes=["B4"], tag="attn_x0")
        P.emit("g", lambda eng, p=p: eng.dma_start(
            out=bc1[64:128, :],
            in_=drs[2 * p + 1:2 * p + 2, :].partition_broadcast(64)),
            reads=[name + "_sums"], writes=["B4"], tag="attn_x1")
        P.emit("v", lambda eng, p=p: eng.tensor_mul(
            utiles[p][:], utiles[p][:], bc1[:]),
            reads=["U%d" % p, "B4"], writes=["U%d" % p], tag="attn_nm")



def emit_stage_l0_attn(P, M, W):
    nc = P.nc
    xt = M.tiles("X", NT)
    xk = keys("X", NT)
    ut = M.tiles("U", NT)
    a2 = M.sb("B4", [128, S], BF16)
    ck = M.sb("BCA", [128, S], BF16)[:, 0:S]
    sk = M.sb("BCB", [128, S], BF16)[:, 0:S]
    ctiles = c_area(M)
    p4 = M.ps("P4", [128, 512])
    p5 = M.ps("P5", [128, 512])

    wvb = M.sb("WPA", [128, 4096], BF16)
    emit_load_wchunk(P, wvb, "WPA", W["L0_wvT"][0], [W["L0_wvT"][1]], NT, 0, 512)
    wvb2 = M.sb("WPB", [128, 4096], BF16)
    emit_load_wchunk(P, wvb2, "WPB", W["L0_wvT"][0], [W["L0_wvT"][1]], NT,
                     512, 512)
    for kt in range(STL):
        P.emit("v", lambda eng, kt=kt: eng.memset(
            va_ap(ctiles, kt).rearrange("p (h c) -> p h c", c=65)[:, :, 64:65],
            1.0), reads=[], writes=["C%d" % (kt // 2)], tag="vones")
    p6v = M.ps("P6", [128, 512])
    for st in range(STL):
        for ec in range(2):
            pn = ["P4", "P5", "P6"][(st * 2 + ec) % 3]
            psum = [p4, p5, p6v][(st * 2 + ec) % 3]
            wsrc, wkey = (wvb, "WPA") if ec == 0 else (wvb2, "WPB")
            for kc in range(NT):
                def fn(eng, st=st, ec=ec, kc=kc, psum=psum, wsrc=wsrc):
                    return eng.matmul(
                        psum[:], xt[kc][:, st * 128:(st + 1) * 128],
                        wsrc[:, kc * 512:(kc + 1) * 512],
                        start=(kc == 0), stop=(kc == NT - 1))
                P.emit("t", fn, reads=[xk[kc], wkey], writes=[pn], tag="l0v")
            def fne(eng, st=st, ec=ec, psum=psum):
                dst = va_ap(ctiles, st).rearrange("p (h c) -> p h c", c=65)[
                    :, 8 * ec:8 * ec + 8, 0:64]
                src = psum[:].rearrange("p (h c) -> p h c", c=64)
                return eng.activation(dst, src, AF.Copy)
            P.emit("s", fne, reads=[pn], writes=["C%d" % (st // 2)],
                   tag="l0v_ev")

    P.emit("g", lambda eng: eng.dma_start(out=ck, in_=W["ropeC"][0]),
           reads=[W["ropeC"][1]], writes=["BCA"], tag="ropec")
    P.emit("g", lambda eng: eng.dma_start(out=sk, in_=W["ropeS"][0]),
           reads=[W["ropeS"][1]], writes=["BCB"], tag="ropes")

    qwb = M.sb("WPA", [128, 4096], BF16)[:, 0:1024]
    kwb = M.sb("WPA", [128, 4096], BF16)[:, 1024:2048]

    def proj_qk(pair, qa, qak, ka, kak):
        emit_load_wchunk(P, qwb, "WPA", W["L0_wqT"][0], [W["L0_wqT"][1]], NT,
                         pair * 128, 128)
        emit_load_wchunk(P, kwb, "WPA", W["L0_wkT"][0], [W["L0_wkT"][1]], NT,
                         pair * 128, 128)
        for (wb, wkey, dst, dkey, scl) in [
                (qwb, "WPA", qa, qak, 0.125), (kwb, "WPA", ka, kak, 1.0)]:
            for sc in range(4):
                pn = "P4" if dkey == "A0" else "P5"
                psum = p4 if dkey == "A0" else p5
                for kc in range(NT):
                    def fn(eng, kc=kc, sc=sc, psum=psum, wb=wb):
                        return eng.matmul(
                            psum[:], wb[:, kc * 128:(kc + 1) * 128],
                            xt[kc][:, sc * 512:(sc + 1) * 512],
                            start=(kc == 0), stop=(kc == NT - 1))
                    P.emit("t", fn, reads=[wkey, xk[kc]], writes=[pn],
                           tag="l0qk")
                P.emit("s", lambda eng, sc=sc, psum=psum, dst=dst, scl=scl:
                       eng.activation(dst[:, sc * 512:(sc + 1) * 512],
                                      psum[:], AF.Copy, scale=scl),
                       reads=[pn], writes=[dkey], tag="l0qk_ev")
            P.emit("v", lambda eng, dst=dst: eng.stream_shuffle(
                a2[:], dst[:], [i ^ 1 for i in range(32)]),
                reads=[dkey], writes=["B4"], tag="rope_sh")
            P.emit("v", lambda eng, dst=dst, dkey=dkey: eng.tensor_mul(
                dst[:], dst[:], ck),
                reads=[dkey, "BCA"], writes=[dkey], tag="rope_c")
            P.emit("v", lambda eng: eng.tensor_mul(a2[:], a2[:], sk),
                   reads=["B4", "BCB"], writes=["B4"], tag="rope_s")
            P.emit("v", lambda eng, dst=dst, dkey=dkey: eng.tensor_add(
                dst[:], dst[:], a2[:]),
                reads=[dkey, "B4"], writes=[dkey], tag="rope_a")

    emit_attention(P, M, "l0", proj_qk, lambda kt: kt * 128, True)

    def evac(mt, sc, psum, pn):
        def fn(eng, mt=mt, sc=sc, psum=psum):
            return eng.tensor_add(xt[mt][:, sc * 512:(sc + 1) * 512],
                                  xt[mt][:, sc * 512:(sc + 1) * 512],
                                  psum[:])
        P.emit("v", fn, reads=[pn, xk[mt]], writes=[xk[mt]], tag="l0wo_ev")
    emit_proj_fmajor(P, M, "l0wo", W["L0_woT"][0], D, D,
                     M.tiles("U", NT), keys("U", NT), evac,
                     wt_keys=[W["L0_woT"][1]])


def emit_stage_l0_ffn(P, M, W):
    nc = P.nc
    xt = M.tiles("X", NT)
    xk = keys("X", NT)
    xq8, xq8k = fq8_views(M)
    h8, h8k = h8_views(M)
    ut = M.tiles("U", NT)
    h16 = ut[0].bitcast(FP16)[:, 0:512]
    p4, p5 = M.ps("P4", [128, 512]), M.ps("P5", [128, 512])
    b1q = M.sb("B1Q", [128, 32], FP8)
    b2q = M.sb("B2Q", [128, 8], FP8)
    P.emit("g", lambda eng: eng.dma_start(out=b1q[:], in_=W["L0_b1"][0]),
           reads=[W["L0_b1"][1]], writes=["B1Q"], tag="b1")
    P.emit("g", lambda eng: eng.dma_start(out=b2q[:], in_=W["L0_b2"][0]),
           reads=[W["L0_b2"][1]], writes=["B2Q"], tag="b2")
    b1f = M.sb("B1F", [128, 32], F32)
    b2f = M.sb("B2F", [128, 8], F32)
    P.emit("v", lambda eng: eng.tensor_copy(b1f[:], b1q[:]),
           reads=["B1Q"], writes=["B1F"], tag="b1f")
    P.emit("v", lambda eng: eng.tensor_copy(b2f[:], b2q[:]),
           reads=["B2Q"], writes=["B2F"], tag="b2f")

    wpa = M.sb("WPA", [128, 4096], BF16)
    wpb = M.sb("WPB", [128, 4096], BF16)
    w1b = [wpa.bitcast(FP8)[:, 0:4096], wpb.bitcast(FP8)[:, 0:4096]]
    w1k = ["WPA", "WPB"]
    w2b = w1b
    w2k = w1k
    dm = M.sb("DMHB", [128, 1024], FP16)

    for sc in range(4):
        emit_load_wchunk(P, w1b[0], w1k[0], W["L0_w1T"][0], [W["L0_w1T"][1]],
                         NT, 0, 512)
        for mc in range(8):
            par = mc % 2
            if mc + 1 < 8:
                emit_load_wchunk(P, w1b[1 - par], w1k[1 - par], W["L0_w1T"][0],
                                 [W["L0_w1T"][1]], NT, (mc + 1) * 512, 512)
            for mt in range(4):
                mg = mc * 4 + mt
                pn = ["P4", "P5", "P6"][mg % 3]
                psum = [p4, p5, M.ps("P6", [128, 512])][mg % 3]
                for kc in range(NT):
                    def fn(eng, kc=kc, mt=mt, sc=sc, par=par, psum=psum):
                        return eng.matmul(
                            psum[:], w1b[par][:, kc * 512 + mt * 128:
                                              kc * 512 + (mt + 1) * 128],
                            xq8[kc][:, sc * 512:(sc + 1) * 512],
                            start=(kc == 0), stop=(kc == NT - 1))
                    P.emit("t", fn, reads=[w1k[par], xq8k[kc]], writes=[pn],
                           tag="l0f_mm1")
                def fe1(eng, mg=mg, psum=psum):
                    return eng.activation(h16, psum[:], AF.Relu,
                                          bias=b1f[:, mg:mg + 1])
                P.emit("s", fe1, reads=[pn, "B1F"], writes=["U0"], tag="l0f_h16")
                def fe2(eng, mg=mg):
                    return eng.activation(h8[mg], h16, AF.Copy)
                P.emit("s", fe2, reads=["U0"], writes=[h8k[mg]], tag="l0f_h8")
        emit_load_wchunk(P, w2b[0], w2k[0], W["L0_w2T"][0], [W["L0_w2T"][1]],
                         32, 0, 128)
        for dt in range(NT):
            par = dt % 2
            if dt + 1 < NT:
                emit_load_wchunk(P, w2b[1 - par], w2k[1 - par], W["L0_w2T"][0],
                                 [W["L0_w2T"][1]], 32, (dt + 1) * 128, 128)
            pn = ["P4", "P5", "P6"][dt % 3]
            psum = [p4, p5, M.ps("P6", [128, 512])][dt % 3]
            for kc in range(32):
                def fn(eng, kc=kc, dt=dt, sc=sc, par=par, psum=psum):
                    return eng.matmul(
                        psum[:], w2b[par][:, kc * 128:(kc + 1) * 128],
                        h8[kc][:, 0:512],
                        start=(kc == 0), stop=(kc == 31))
                P.emit("t", fn, reads=[w2k[par], h8k[kc]], writes=[pn],
                       tag="l0f_mm2")
            def fm(eng, dt=dt, sc=sc):
                src = W["L0_dmask"][0][dt * 128:(dt + 1) * 128,
                                       sc * 512:(sc + 1) * 512]
                return eng.dma_start(out=dm[:, 0:512], in_=src)
            P.emit("g", fm, reads=[W["L0_dmask"][1]], writes=["DMHB"],
                   tag="l0f_dm")
            def fe(eng, dt=dt, psum=psum):
                return eng.tensor_scalar(h16, psum[:], b2f[:, dt:dt + 1],
                                         None, ALU.add)
            P.emit("v", fe, reads=[pn, "B2F"], writes=["U0"], tag="l0f_y16")
            def fmu(eng, dt=dt, sc=sc):
                return eng.tensor_mul(ut[1].bitcast(FP16)[:, 0:512], h16,
                                      dm[:, 0:512])
            P.emit("v", fmu, reads=["U0", "DMHB"], writes=["U1"], tag="l0f_msk")
            def fad(eng, dt=dt, sc=sc):
                return eng.tensor_add(
                    xt[dt][:, sc * 512:(sc + 1) * 512],
                    xt[dt][:, sc * 512:(sc + 1) * 512],
                    ut[1].bitcast(FP16)[:, 0:512])
            P.emit("v", fad, reads=["U1", xk[dt]], writes=[xk[dt]],
                   tag="l0f_res")



STAGE_EMITTERS = {}


_DECLARED = {}


def _declare_weight_inputs(nc, stage, host_w):
    W = {}
    decl = _DECLARED.setdefault(id(nc), {})
    for name, arr in host_w[stage].items():
        if name in decl:
            W[name] = decl[name]
            continue
        dt = {np.dtype(np.float32): F32, np.dtype(np.float16): FP16,
              np.dtype(_bf): BF16}[arr.dtype]
        h = nc.dram_tensor("i_" + name, list(arr.shape), dt,
                           kind="ExternalInput")
        W[name] = (h.ap(), "i_" + name)
        decl[name] = W[name]
    return W


def emit_consts(P, M, W):
    ones = M.sb("ONESC", [128, 1], BF16)
    P.emit("v", lambda eng: eng.memset(ones[:], 1.0), reads=(), writes=["ONESC"])
    for cname, sname, shape in [("cIDN", "IDN", [128, 128]),
                                ("cMTRI", "MTRI", [128, 128]),
                                ("cBO16", "BO16", [128, 16])]:
        if cname in W:
            t = M.sb(sname, shape, BF16)
            P.emit("g", lambda eng, t=t, cname=cname: eng.dma_start(
                out=t[:], in_=W[cname][0]),
                reads=[W[cname][1]], writes=[sname], tag="const")


def emit_x_in(P, M, W, fq8):
    xt = M.tiles("X", NT)
    for i in range(NT):
        P.emit("g", lambda eng, i=i: eng.dma_start(
            out=xt[i][:], in_=W["x_in"][0][i * 128:(i + 1) * 128, :]),
            reads=[W["x_in"][1]], writes=["X%d" % i], tag="xin")
    if fq8:
        q8, q8k = fq8_views(M)
        for i in range(NT):
            P.emit("g", lambda eng, i=i: eng.dma_start(
                out=q8[i], in_=W["x_in"][0][i * 128:(i + 1) * 128, :]),
                reads=[W["x_in"][1]], writes=[q8k[i]], tag="xin8")


def build_program(stages, host_w, first, last):
    nc = bass.Bass("TRN2", target_bir_lowering=False)
    ctx = ExitStack()
    M = Mem(nc, ctx)
    P = Prog(nc)
    Wall = {}
    for st in stages:
        Wall.update(_declare_weight_inputs(nc, st, host_w))
    Wall.update(_declare_weight_inputs(nc, "const", host_w))
    if 1 in stages:
        hm = nc.dram_tensor("i_L0_dmask", [D, S], FP16, kind="ExternalInput")
        Wall["L0_dmask"] = (hm.ap(), "i_L0_dmask")
    h = nc.dram_tensor("i_x_in", [D, S], F32, kind="ExternalInput")
    Wall["x_in"] = (h.ap(), "i_x_in")
    ho = nc.dram_tensor("o_y", [D, S], F32, kind="ExternalOutput")
    NEED_FQ8 = {1}
    eps_of = {i: (1e-10 if False else LN_EPS) for i in range(8)}
    with ctx:
        emit_consts(P, M, Wall)
        emit_x_in(P, M, Wall, stages[0] in NEED_FQ8)
        xt = M.tiles("X", NT)
        xk = keys("X", NT)
        for si, st in enumerate(stages):
            STAGE_EMITTERS[st](P, M, Wall)
            nxt = stages[si + 1] if si + 1 < len(stages) else None
            lastln = (si == len(stages) - 1)
            emit_ln(P, M, xt, xk, eps=eps_of[st],
                    fq8=(nxt in NEED_FQ8),
                    f32out=(ho.ap() if lastln else None))
        P.finalize(ctx)
    return nc


STAGE_EMITTERS[0] = emit_stage_l0_attn
STAGE_EMITTERS[1] = emit_stage_l0_ffn



def _ct(a):
    return np.ascontiguousarray(a)


def make_host_weights(inp):
    w = {}
    w[0] = {
        "L0_wqT": _ct(inp["L0_attn_wq"].T),
        "L0_wkT": _ct(inp["L0_attn_wk"].T),
        "L0_wvT": _ct(inp["L0_attn_wv"].T),
        "L0_woT": _ct(inp["L0_attn_wo"].T),
        "ropeC": _rope_table(cos=True),
        "ropeS": _rope_table(cos=False),
    }
    w[2] = {
        "L1_wattnT": _ct(inp["L1_attn_w_attn"].T),
        "L1_bq": _ct(inp["L1_attn_b_attn"][0:1024].reshape(8, 128).T),
        "L1_bk": _ct(inp["L1_attn_b_attn"][1024:2048].reshape(8, 128).T),
        "L1_bv": _ct(inp["L1_attn_b_attn"][2048:3072].reshape(1, 1024)),
        "L1_wprojT": _ct(inp["L1_attn_w_proj"].T),
        "L1_bp": _ct(inp["L1_attn_b_proj"].reshape(8, 128).T),
        "L1_wfcT": _ct(inp["L1_attn_w_fc"].T),
        "L1_bfc": _ct(inp["L1_attn_b_fc"].reshape(32, 128).T),
        "L1_wmlpT": _ct(inp["L1_attn_w_mlp_proj"].T),
        "L1_bm": _ct(inp["L1_attn_b_mlp_proj"].reshape(8, 128).T),
    }
    w[3] = {
        "L1f_w1T": _ct(inp["L1_ffn_w1"].T),
        "L1f_w2T": _ct(inp["L1_ffn_w2"].T),
        "L1f_cw": _ct(inp["L1_ffn_conv_w"].reshape(32, 128, 3)
                      .transpose(1, 0, 2).reshape(128, 96)),
        "L1f_cb": _ct(inp["L1_ffn_conv_b"].reshape(32, 128).T),
    }
    w[4] = {
        "L2_WqT": _ct(inp["L2_attn_Wq"].T),
        "L2_WkT": _ct(inp["L2_attn_Wk"].T),
        "L2_WvT": _ct(inp["L2_attn_Wv"].T),
        "L2_WoT": _ct(inp["L2_attn_Wo"].T),
        "L2_bq": _ct(inp["L2_attn_bq"].reshape(8, 128).T),
        "L2_bk": _ct(inp["L2_attn_bk"].reshape(8, 128).T),
        "L2_bv": _ct(inp["L2_attn_bv"].reshape(1, 1024)),
        "L2_bo": _ct(inp["L2_attn_bo"].reshape(8, 128).T),
    }
    w[5] = {
        "L2f_WrT": _ct(inp["L2_ffn_W_r"].T),
        "L2f_WkT": _ct(inp["L2_ffn_W_k"].T),
        "L2f_WvT": _ct(inp["L2_ffn_W_v"].T),
        "L2_mu": _ct(np.concatenate([
            inp["L2_ffn_mu_r"].reshape(8, 128).T,
            inp["L2_ffn_mu_k"].reshape(8, 128).T], axis=1)),
    }
    w[6] = {
        "L3_wq": _ct(inp["L3_attn_wq"]),
        "L3_wd": _ct(inp["L3_attn_w_down"]),
        "L3_wkup": _ct(inp["L3_attn_wk_up"]),
        "L3_wvup": _ct(inp["L3_attn_wv_up"]),
        "L3_wo": _ct(inp["L3_attn_wo"]),
        "ropeC": _rope_table(cos=True),
        "ropeS": _rope_table(cos=False),
    }
    w[7] = {
        "L3f_wgT": _ct(inp["L3_ffn_w_gate"].T),
        "L3f_wuT": _ct(inp["L3_ffn_w_up"].T),
        "L3f_wdT": _ct(inp["L3_ffn_w_down"].T),
    }
    w[1] = {
        "L0_w1T": _ct(inp["L0_ffn_w1"].T),
        "L0_w2T": _ct(inp["L0_ffn_w2"].T),
        "L0_b1": _ct(inp["L0_ffn_b1"].reshape(32, 128).T),
        "L0_b2": _ct(inp["L0_ffn_b2"].reshape(8, 128).T),
    }
    idn = np.eye(128, dtype=_bf)
    mtri = np.zeros((128, 128), dtype=np.float32)
    for r in range(128):
        mtri[r, :r] = NEGBIG
    bo = np.zeros((128, 16), dtype=np.float32)
    for c in range(16):
        bo[(c % 2) * 64:(c % 2) * 64 + 64, c] = 1.0
    w["const"] = {"cIDN": idn, "cMTRI": mtri.astype(_bf), "cBO16": bo.astype(_bf)}
    return w


def _rope_table(cos):
    pos = np.arange(S, dtype=np.float64)
    j = np.arange(32, dtype=np.float64)
    inv = 1.0 / (10000.0 ** (2.0 * j / 64.0))
    ang = np.outer(pos, inv)
    tab = np.zeros((128, S), dtype=np.float64)
    for r in range(128):
        jj = (r % 64) // 2
        if cos:
            tab[r] = np.cos(ang[:, jj])
        else:
            tab[r] = np.sin(ang[:, jj]) * (-1.0 if r % 2 == 0 else 1.0)
    return tab.astype(_bf)


def _threefry2x32(key, x0, x1):
    def rotl(x, d):
        return ((x << np.uint32(d)) | (x >> np.uint32(32 - d))).astype(np.uint32)
    ks0, ks1 = np.uint32(key[0]), np.uint32(key[1])
    ks2 = np.uint32(0x1BD11BDA) ^ ks0 ^ ks1
    rot = [(13, 15, 26, 6), (17, 29, 16, 24)]
    x0 = (x0 + ks0).astype(np.uint32)
    x1 = (x1 + ks1).astype(np.uint32)
    ks = [(ks1, ks2), (ks2, ks0), (ks0, ks1), (ks1, ks2), (ks2, ks0)]
    for r in range(5):
        for d in rot[r % 2]:
            x0 = (x0 + x1).astype(np.uint32)
            x1 = rotl(x1, d) ^ x0
        x0 = (x0 + ks[r][0]).astype(np.uint32)
        x1 = (x1 + ks[r][1] + np.uint32(r + 1)).astype(np.uint32)
    return x0, x1


def make_dropout_mask():
    n = NCORE * S * D
    i = np.arange(n, dtype=np.uint64)
    hi = (i >> np.uint64(32)).astype(np.uint32)
    lo = (i & np.uint64(0xFFFFFFFF)).astype(np.uint32)
    b0, b1 = _threefry2x32((np.uint32(0), np.uint32(42)), hi, lo)
    bits = b0 ^ b1
    fl = ((bits >> np.uint32(9)) | np.uint32(0x3F800000)).view(np.float32) - 1.0
    keep = (fl < np.float32(0.8)).reshape(NCORE, S, D)
    return np.where(keep, np.float16(1.25), np.float16(0.0))



_STAGE_GROUPS = [list(range(8))]


def _run(inputs, groups):
    host_w = make_host_weights(inputs)
    dmask = make_dropout_mask()
    x = np.asarray(inputs["L0_attn_x"])
    xT = [np.ascontiguousarray(x[b].T).astype(np.float32) for b in range(NCORE)]
    state = None
    for gi, stages in enumerate(groups):
        first = (gi == 0)
        last = (gi == len(groups) - 1)
        nc = build_program(stages, host_w, first, last)
        in_maps = []
        for b in range(NCORE):
            m = {}
            for st in stages + ["const"]:
                for name, arr in host_w[st].items():
                    m["i_" + name] = arr
            if 1 in stages:
                m["i_L0_dmask"] = np.ascontiguousarray(dmask[b].T)
            m["i_x_in"] = xT[b] if first else state[b]
            in_maps.append(m)
        res = run_bass_kernel_spmd(nc, in_maps, core_ids=list(range(NCORE)))
        if last:
            return np.stack([res.results[b]["o_y"].T for b in range(NCORE)])
        state = [res.results[b]["o_y"] for b in range(NCORE)]
    raise RuntimeError("no last group")


def kernel(**inputs):
    return _run(inputs, _STAGE_GROUPS)



GELU_K2 = math.sqrt(2.0 / math.pi) * 2.0
GELU_K1 = GELU_K2 * 0.044715


def emit_stage_l1_attn(P, M, W):
    xt = M.tiles("X", NT)
    xk = keys("X", NT)
    bt = M.tiles("B", NT)
    bk = keys("B", NT)
    ut = M.tiles("U", NT)
    at, ak = ut, keys("U", NT)
    ctiles = c_area(M)
    p4, p5 = M.ps("P4", [128, 512]), M.ps("P5", [128, 512])
    bqc = M.sb("BQC", [128, 8], F32)
    bkc = M.sb("BKC", [128, 8], F32)
    bvb = M.sb("BVB", [128, 1024], BF16)
    bpc = M.sb("BPC", [128, 8], F32)
    bfcc = M.sb("BFCC", [128, 32], F32)
    bmc = M.sb("BMC", [128, 8], F32)
    for nm, t_, k_ in [("L1_bq", bqc, "BQC"), ("L1_bk", bkc, "BKC"),
                       ("L1_bp", bpc, "BPC"), ("L1_bfc", bfcc, "BFCC"),
                       ("L1_bm", bmc, "BMC")]:
        P.emit("g", lambda eng, t_=t_, nm=nm: eng.dma_start(
            out=t_[:], in_=W[nm][0]), reads=[W[nm][1]], writes=[k_], tag="bia")
    P.emit("g", lambda eng: eng.dma_start(
        out=bvb[:], in_=W["L1_bv"][0][0:1, :].partition_broadcast(128)),
        reads=[W["L1_bv"][1]], writes=["BVB"], tag="bvb")

    emit_ln(P, M, xt, xk, eps=1e-10, out_tiles=bt, out_keys=bk)

    wvb = M.sb("WPA", [128, 4096], BF16)
    wvb2 = M.sb("WPB", [128, 4096], BF16)
    emit_load_wchunk(P, wvb, "WPA", W["L1_wattnT"][0], [W["L1_wattnT"][1]],
                     NT, 2048, 512)
    emit_load_wchunk(P, wvb2, "WPB", W["L1_wattnT"][0], [W["L1_wattnT"][1]],
                     NT, 2560, 512)
    for kt in range(STL):
        P.emit("v", lambda eng, kt=kt: eng.memset(
            va_ap(ctiles, kt).rearrange("p (h c) -> p h c", c=65)[:, :, 64:65],
            1.0), reads=[], writes=["C%d" % (kt // 2)], tag="vones")
    p6v = M.ps("P6", [128, 512])
    for st in range(STL):
        for ec in range(2):
            pn = ["P4", "P5", "P6"][(st * 2 + ec) % 3]
            psum = [p4, p5, p6v][(st * 2 + ec) % 3]
            wsrc, wkey = (wvb, "WPA") if ec == 0 else (wvb2, "WPB")
            for kc in range(NT):
                def fn(eng, st=st, kc=kc, psum=psum, wsrc=wsrc):
                    return eng.matmul(
                        psum[:], bt[kc][:, st * 128:(st + 1) * 128],
                        wsrc[:, kc * 512:(kc + 1) * 512],
                        start=(kc == 0), stop=(kc == NT - 1))
                P.emit("t", fn, reads=[bk[kc], wkey], writes=[pn], tag="l1v")
            def fne(eng, st=st, ec=ec, psum=psum, pn=pn):
                dst = va_ap(ctiles, st).rearrange("p (h c) -> p h c", c=65)[
                    :, 8 * ec:8 * ec + 8, 0:64]
                src = psum[:].rearrange("p (h c) -> p h c", c=64)
                bsl = bvb[:, ec * 512:(ec + 1) * 512].rearrange(
                    "p (h c) -> p h c", c=64)
                return eng.tensor_add(dst, src, bsl)
            P.emit("v", fne, reads=[pn, "BVB"], writes=["C%d" % (st // 2)],
                   tag="l1v_ev")

    qwb = M.sb("WPA", [128, 4096], BF16)[:, 0:1024]
    kwb = M.sb("WPA", [128, 4096], BF16)[:, 1024:2048]

    def proj_qk(pair, qa, qak, ka, kak):
        emit_load_wchunk(P, qwb, "WPA", W["L1_wattnT"][0],
                         [W["L1_wattnT"][1]], NT, pair * 128, 128)
        emit_load_wchunk(P, kwb, "WPA", W["L1_wattnT"][0],
                         [W["L1_wattnT"][1]], NT, 1024 + pair * 128, 128)
        for (wb, wkey, dst, dkey, scl, bcol, bckey) in [
                (qwb, "WPA", qa, qak, 0.125, bqc, "BQC"),
                (kwb, "WPA", ka, kak, 1.0, bkc, "BKC")]:
            for sc in range(4):
                pn = "P4" if dkey == "A0" else "P5"
                psum = p4 if dkey == "A0" else p5
                for kc in range(NT):
                    def fn(eng, kc=kc, sc=sc, psum=psum, wb=wb):
                        return eng.matmul(
                            psum[:], wb[:, kc * 128:(kc + 1) * 128],
                            bt[kc][:, sc * 512:(sc + 1) * 512],
                            start=(kc == 0), stop=(kc == NT - 1))
                    P.emit("t", fn, reads=[wkey, bk[kc]], writes=[pn],
                           tag="l1qk")
                def fev(eng, sc=sc, psum=psum, dst=dst, scl=scl, bcol=bcol,
                        pair=pair):
                    return eng.tensor_scalar(
                        dst[:, sc * 512:(sc + 1) * 512], psum[:],
                        bcol[:, pair:pair + 1], scl, ALU.add, ALU.mult)
                P.emit("v", fev, reads=[pn, bckey], writes=[dkey],
                       tag="l1qk_ev")

    emit_attention(P, M, "l1", proj_qk, lambda kt: kt * 128, True)

    def evac(mt, sc, psum, pn):
        def fn(eng, mt=mt, sc=sc, psum=psum):
            return eng.scalar_tensor_tensor(
                bt[mt][:, sc * 512:(sc + 1) * 512], psum[:],
                bpc[:, mt:mt + 1], xt[mt][:, sc * 512:(sc + 1) * 512],
                ALU.add, ALU.add)
        P.emit("v", fn, reads=[pn, "BPC", xk[mt]], writes=[bk[mt]],
               tag="l1pr_ev")
    emit_proj_fmajor(P, M, "l1pr", W["L1_wprojT"][0], D, D,
                     ut, keys("U", NT), evac, wt_keys=[W["L1_wprojT"][1]])

    emit_ln(P, M, bt, bk, eps=1e-10, out_tiles=at, out_keys=ak)

    gsc = M.sb("DMHB", [128, 1024], FP16).bitcast(BF16)
    g0 = gsc[:, 0:512]
    g1 = gsc[:, 512:1024]
    wfb = [M.sb("WPA", [128, 4096], BF16), M.sb("WPB", [128, 4096], BF16)]
    wfk = ["WPA", "WPB"]
    hml = [(ctiles[i // 4], "C%d" % (i // 4), (i % 4) * 512) for i in range(32)]
    p6v = M.ps("P6", [128, 512])
    for sc in range(4):
        emit_load_wchunk(P, wfb[0], wfk[0], W["L1_wfcT"][0],
                         [W["L1_wfcT"][1]], NT, 0, 512)
        for mc in range(8):
            par = mc % 2
            if mc + 1 < 8:
                emit_load_wchunk(P, wfb[1 - par], wfk[1 - par],
                                 W["L1_wfcT"][0], [W["L1_wfcT"][1]], NT,
                                 (mc + 1) * 512, 512)
            for mt in range(4):
                mg = mc * 4 + mt
                pn = ["P4", "P5", "P6"][mg % 3]
                psum = [p4, p5, p6v][mg % 3]
                for kc in range(NT):
                    def fn(eng, kc=kc, mt=mt, sc=sc, par=par, psum=psum):
                        return eng.matmul(
                            psum[:], wfb[par][:, kc * 512 + mt * 128:
                                              kc * 512 + (mt + 1) * 128],
                            at[kc][:, sc * 512:(sc + 1) * 512],
                            start=(kc == 0), stop=(kc == NT - 1))
                    P.emit("t", fn, reads=[wfk[par], ak[kc]], writes=[pn],
                           tag="l1fc")
                P.emit("v", lambda eng, mg=mg, psum=psum: eng.tensor_scalar(
                    g0, psum[:], bfcc[:, mg:mg + 1], None, ALU.add),
                    reads=[pn, "BFCC"], writes=["DMHB"], tag="gel0")
                P.emit("s", lambda eng: eng.activation(g1, g0, AF.Square),
                       reads=["DMHB"], writes=["DMHB"], tag="gel1")
                P.emit("v", lambda eng: eng.tensor_scalar(
                    g1, g1, GELU_K1, GELU_K2, ALU.mult, ALU.add),
                    reads=["DMHB"], writes=["DMHB"], tag="gel2")
                P.emit("v", lambda eng: eng.tensor_mul(g1, g0, g1),
                       reads=["DMHB"], writes=["DMHB"], tag="gel3")
                P.emit("s", lambda eng: eng.activation(g1, g1, AF.Sigmoid),
                       reads=["DMHB"], writes=["DMHB"], tag="gel4")
                hm, hk, hoff = hml[mg]
                P.emit("v", lambda eng, hm=hm, hoff=hoff: eng.tensor_mul(
                    hm[:, hoff:hoff + 512], g0, g1),
                    reads=["DMHB"], writes=[hk], tag="gel5")
        w2b = wfb
        w2k = wfk
        emit_load_wchunk(P, w2b[0], w2k[0], W["L1_wmlpT"][0],
                         [W["L1_wmlpT"][1]], 32, 0, 128)
        for dt in range(NT):
            par = dt % 2
            if dt + 1 < NT:
                emit_load_wchunk(P, w2b[1 - par], w2k[1 - par],
                                 W["L1_wmlpT"][0], [W["L1_wmlpT"][1]], 32,
                                 (dt + 1) * 128, 128)
            pn = ["P4", "P5", "P6"][dt % 3]
            psum = [p4, p5, p6v][dt % 3]
            for kc in range(32):
                hm, hk, hoff = hml[kc]
                def fn(eng, kc=kc, dt=dt, par=par, psum=psum, hm=hm,
                       hoff=hoff):
                    return eng.matmul(
                        psum[:], w2b[par][:, kc * 128:(kc + 1) * 128],
                        hm[:, hoff:hoff + 512],
                        start=(kc == 0), stop=(kc == 31))
                P.emit("t", fn, reads=[w2k[par], hk], writes=[pn],
                       tag="l1mlp")
            def fe(eng, dt=dt, sc=sc, psum=psum):
                return eng.scalar_tensor_tensor(
                    bt[dt][:, sc * 512:(sc + 1) * 512], psum[:],
                    bmc[:, dt:dt + 1], bt[dt][:, sc * 512:(sc + 1) * 512],
                    ALU.add, ALU.add)
            P.emit("v", fe, reads=[pn, "BMC", bk[dt]], writes=[bk[dt]],
                   tag="l1mlp_ev")
    for i in range(NT):
        P.emit("v", lambda eng, i=i: eng.tensor_add(xt[i][:], xt[i][:],
                                                    bt[i][:]),
               reads=[xk[i], bk[i]], writes=[xk[i]], tag="l1out")


STAGE_EMITTERS[2] = emit_stage_l1_attn



def emit_weight_quant(P, M, nm, wt_ap, wt_key, Kdim, Mout, stash_ap, stash_key):
    kt = Kdim // 128
    wf = [M.sb("T32A", [128, S], F32), M.sb("T32B", [128, S], F32)]
    wfk = ["T32A", "T32B"]
    sm = M.sb("SM1", [128, 32], F32)
    onesf = M.sb("ONESF", [128, 1], F32)
    bca = M.sb("BCA", [128, S], BF16)
    p6 = M.ps("P6", [128, 512])
    P.emit("v", lambda eng: eng.memset(onesf[:], 1.0), (), ["ONESF"])
    P.emit("v", lambda eng: eng.memset(sm[:, 0:2], 0.0), (), ["SM1"])
    cw = 2048 // kt
    nch = Mout // cw
    for ci in range(nch):
        par = ci % 2
        emit_load_wchunk(P, wf[par], wfk[par], wt_ap, [wt_key], kt, ci * cw, cw)
        for kc in range(kt):
            P.emit("v", lambda eng, kc=kc, par=par: eng.reduce_sum(
                sm[:, 1:2], wf[par][:, kc * cw:(kc + 1) * cw], axis=AX,
                apply_absolute_value=True),
                reads=[wfk[par]], writes=["SM1"], tag=nm + "wqr")
            P.emit("v", lambda eng: eng.tensor_add(sm[:, 0:1], sm[:, 0:1],
                                                   sm[:, 1:2]),
                   reads=["SM1"], writes=["SM1"], tag=nm + "wqa")
    P.emit("t", lambda eng: eng.matmul(p6[0:1, 0:1], onesf[:, 0:1],
                                       sm[:, 0:1], start=True, stop=True),
           reads=["ONESF", "SM1"], writes=["P6"], tag=nm + "wqs")
    P.emit("v", lambda eng: eng.tensor_scalar(
        sm[0:1, 2:3], p6[0:1, 0:1], 1.0 / (Kdim * Mout), 1e-5,
        ALU.mult, ALU.max), reads=["P6"], writes=["SM1"], tag=nm + "wqm")
    P.emit("v", lambda eng: eng.reciprocal(sm[0:1, 3:4], sm[0:1, 2:3]),
           reads=["SM1"], writes=["SM1"], tag=nm + "wqi")
    emit_bcast_row(P, M, nm + "sw", sm[0:1, 3:4], "SM1", 128, sm[:, 4:5],
                   "SM1", 1)
    for ci in range(nch):
        par = ci % 2
        emit_load_wchunk(P, wf[par], wfk[par], wt_ap, [wt_key], kt, ci * cw, cw)
        P.emit("v", lambda eng, par=par: eng.tensor_scalar(
            wf[par][:, 0:2048], wf[par][:, 0:2048], sm[:, 4:5], None, ALU.mult),
            reads=[wfk[par], "SM1"], writes=[wfk[par]], tag=nm + "wq1")
        P.emit("v", lambda eng, par=par: eng.tensor_scalar(
            wf[par][:, 0:2048], wf[par][:, 0:2048], MAGIC, MAGIC,
            ALU.add, ALU.subtract),
            reads=[wfk[par]], writes=[wfk[par]], tag=nm + "wq2")
        P.emit("v", lambda eng, par=par: eng.tensor_scalar(
            bca[:, 0:2048], wf[par][:, 0:2048], -1.0, 1.0, ALU.max, ALU.min),
            reads=[wfk[par]], writes=["BCA"], tag=nm + "wq3")
        def fo(eng, ci=ci):
            dst = stash_ap[:, ci * cw:(ci + 1) * cw].rearrange(
                "(kc p) c -> p kc c", p=128)
            return eng.dma_start(out=dst, in_=bca[:, 0:2048].rearrange(
                "p (kc c) -> p kc c", kc=kt))
        P.emit("y", fo, reads=["BCA"], writes=[stash_key], tag=nm + "wqo")
    return sm


def emit_act_quant(P, M, nm, width, col0, mx_ready, mw_col_ap, int_round_cb,
                   sclx_name="T32A"):
    mxt = M.sb("U3", [128, S], BF16)
    idn = M.sb("IDN", [128, 128], BF16)
    pt = M.ps("PT", [128, 1024], BF16)
    sm2 = M.sb("SM2", [128, 64], F32)
    spb = M.sb("SPB", [128, 64], BF16)
    sclx = M.sb(sclx_name, [128, S], F32)
    bca = M.sb("BCA", [128, S], BF16)
    nb = width // 128
    for b in range(nb):
        P.emit("t", lambda eng, b=b: eng.transpose(
            pt[:, 0:128], mxt[:, col0 + b * 128:col0 + (b + 1) * 128], idn[:]),
            reads=["U3", "IDN"], writes=["PT"], tag=nm + "aqt")
        P.emit("v", lambda eng, b=b: eng.reduce_max(
            sm2[:, b:b + 1], pt[:, 0:128], axis=AX),
            reads=["PT"], writes=["SM2"], tag=nm + "aqm")
    P.emit("v", lambda eng: eng.tensor_scalar(
        sm2[:, 16:16 + nb], sm2[:, 0:nb], 1e-5, None, ALU.max),
        reads=["SM2"], writes=["SM2"], tag=nm + "aqc")
    P.emit("v", lambda eng: eng.reciprocal(sm2[:, 32:32 + nb],
                                           sm2[:, 16:16 + nb]),
           reads=["SM2"], writes=["SM2"], tag=nm + "aqr")
    P.emit("v", lambda eng: eng.tensor_scalar(
        spb[:, 0:nb], sm2[:, 32:32 + nb], 127.0, None, ALU.mult),
        reads=["SM2"], writes=["SPB"], tag=nm + "aqs")
    P.emit("v", lambda eng: eng.reciprocal(sm2[:, 32:32 + nb], spb[:, 0:nb]),
           reads=["SPB"], writes=["SM2"], tag=nm + "aqi")
    if mw_col_ap is not None:
        emit_bcast_row(P, M, nm + "mw", mw_col_ap, "SM1", 128, sm2[:, 48:49],
                       "SM2", 1)
        P.emit("v", lambda eng: eng.tensor_scalar(
            sm2[:, 32:32 + nb], sm2[:, 32:32 + nb], sm2[:, 48:49], None,
            ALU.mult), reads=["SM2"], writes=["SM2"], tag=nm + "aqmw")
    P.emit("v", lambda eng: eng.tensor_copy(spb[:, 16:16 + nb],
                                            sm2[:, 32:32 + nb]),
           reads=["SM2"], writes=["SPB"], tag=nm + "aqh")
    P.emit("v", lambda eng: eng.tensor_sub(
        sm2[:, 48:48 + nb], sm2[:, 32:32 + nb], spb[:, 16:16 + nb]),
        reads=["SM2", "SPB"], writes=["SM2"], tag=nm + "aql0")
    P.emit("v", lambda eng: eng.tensor_copy(spb[:, 32:32 + nb],
                                            sm2[:, 48:48 + nb]),
           reads=["SM2"], writes=["SPB"], tag=nm + "aql")
    P.emit("t", lambda eng: eng.transpose(pt[0:48, 0:128], spb[:, 0:48],
                                          idn[:]),
           reads=["SPB", "IDN"], writes=["PT"], tag=nm + "aqt2")
    P.emit("v", lambda eng: eng.tensor_copy(bca[0:48, 0:128],
                                            pt[0:48, 0:128]),
           reads=["PT"], writes=["BCA"], tag=nm + "aqe")
    drq = M.dr("dq_" + nm, [3, S], BF16)
    def fo(eng):
        dst = drq.rearrange("t c -> (t c)")[0:48 * 128]
        return eng.dma_start(out=dst, in_=bca[0:48, 0:128])
    P.emit("y", fo, reads=["BCA"], writes=["dq_" + nm], tag=nm + "aqo")
    P.emit("g", lambda eng: eng.dma_start(
        out=mxt[:, col0:col0 + width],
        in_=drq[0:1, 0:width].partition_broadcast(128)),
        reads=["dq_" + nm], writes=["U3"], tag=nm + "aqb1")
    P.emit("g", lambda eng: eng.dma_start(
        out=sclx[:, col0:col0 + width],
        in_=drq[1:2, 0:width].partition_broadcast(128)),
        reads=["dq_" + nm], writes=[sclx_name], tag=nm + "aqb2")
    P.emit("g", lambda eng: eng.dma_start(
        out=bca[:, col0:col0 + width],
        in_=drq[2:3, 0:width].partition_broadcast(128)),
        reads=["dq_" + nm], writes=["BCA"], tag=nm + "aqb3")
    P.emit("v", lambda eng: eng.tensor_add(
        sclx[:, col0:col0 + width], sclx[:, col0:col0 + width],
        bca[:, col0:col0 + width]),
        reads=[sclx_name, "BCA"], writes=[sclx_name], tag=nm + "aqlo")



def emit_stage_l1_ffn(P, M, W):
    xt = M.tiles("X", NT)
    xk = keys("X", NT)
    ct = c_area(M)
    ai = [ct[i][:, 0:2048] for i in range(NT)]
    aik = keys("C", NT)
    ut = M.tiles("U", NT)
    mxt = M.sb("U3", [128, S], BF16)
    sclx = M.sb("T32A", [128, S], F32)
    hh = M.sb("U4", [128, S], BF16)
    cws = M.sb("CWS", [128, 96], F32)
    cbc = M.sb("CBC", [128, 32], F32)
    p4, p5 = M.ps("P4", [128, 512]), M.ps("P5", [128, 512])
    w1q = M.dr("L1W1Q", [D, 4096], BF16)
    w2q = M.dr("L1W2Q", [4096, D], BF16)
    sild = M.dr("L1SILD", [4096, S], BF16)
    wpa = M.sb("WPA", [128, 4096], BF16)
    wpb = M.sb("WPB", [128, 4096], BF16)

    sm = emit_weight_quant(P, M, "l1w1", W["L1f_w1T"][0], W["L1f_w1T"][1],
                           D, 4096, w1q, "L1W1Q")
    mw1 = sm[0:1, 2:3]
    P.emit("g", lambda eng: eng.dma_start(out=cbc[:], in_=W["L1f_cb"][0]),
           reads=[W["L1f_cb"][1]], writes=["CBC"], tag="cb")
    P.emit("g", lambda eng: eng.dma_start(out=cws[:], in_=W["L1f_cw"][0]),
           reads=[W["L1f_cw"][1]], writes=["CWS"], tag="cw")
    emit_bcast_row(P, M, "l1mw1", mw1, "SM1", 128, sclx[:, 0:1], "T32A", 1)
    P.emit("v", lambda eng: eng.tensor_scalar(
        cws[:], cws[:], sclx[:, 0:1], None, ALU.mult),
        reads=["CWS", "T32A"], writes=["CWS"], tag="cwm")
    sm2q = emit_weight_quant(P, M, "l1w2", W["L1f_w2T"][0], W["L1f_w2T"][1],
                             4096, D, w2q, "L1W2Q")
    mw2 = sm2q[0:1, 2:3]

    absx = M.sb("U5", [128, S], BF16)
    for i in range(NT):
        P.emit("s", lambda eng, i=i: eng.activation(
            absx[:], xt[i][:], AF.Abs),
            reads=[xk[i]], writes=["U5"], tag="aqxa")
        if i == 0:
            P.emit("v", lambda eng: eng.tensor_copy(mxt[:], absx[:]),
                   reads=["U5"], writes=["U3"], tag="aqx0")
        else:
            P.emit("v", lambda eng: eng.tensor_max(mxt[:], mxt[:], absx[:]),
                   reads=["U3", "U5"], writes=["U3"], tag="aqx")
    emit_act_quant(P, M, "l1x", S, 0, True, None, None)
    for i in range(NT):
        P.emit("v", lambda eng, i=i: eng.tensor_mul(ai[i][:], xt[i][:], mxt[:]),
               reads=[xk[i], "U3"], writes=[aik[i]], tag="aqxm")
        P.emit("v", lambda eng, i=i: eng.tensor_scalar(
            ai[i][:], ai[i][:], MAGIC, MAGIC, ALU.add, ALU.subtract),
            reads=[aik[i]], writes=[aik[i]], tag="aqxr")

    wb = [wpa, wpb]
    wbk = ["WPA", "WPB"]
    def loadw1(mt, par):
        def fn(eng, mt=mt):
            src = w1q[:, mt * 128:(mt + 1) * 128].rearrange(
                "(kc p) c -> p kc c", p=128)
            return eng.dma_start(out=wb[par][:, 0:1024], in_=src)
        P.emit("g", fn, reads=["L1W1Q"], writes=[wbk[par]], tag="w1l")
    loadw1(0, 0)
    for mt in range(32):
        par = mt % 2
        if mt + 1 < 32:
            loadw1(mt + 1, 1 - par)
        for sc in range(4):
            pn = ["P4", "P5", "P6"][sc % 3]
            psum = [p4, p5, M.ps("P6", [128, 512])][sc % 3]
            for kc in range(NT):
                def fn(eng, kc=kc, sc=sc, par=par, psum=psum):
                    return eng.matmul(
                        psum[:], wb[par][:, kc * 128:(kc + 1) * 128],
                        ai[kc][:, sc * 512:(sc + 1) * 512],
                        start=(kc == 0), stop=(kc == NT - 1))
                P.emit("t", fn, reads=[wbk[par], aik[kc]], writes=[pn],
                       tag="l1fmm1")
            P.emit("v", lambda eng, sc=sc, psum=psum: eng.tensor_mul(
                hh[:, sc * 512:(sc + 1) * 512], psum[:],
                sclx[:, sc * 512:(sc + 1) * 512]),
                reads=[pn, "T32A"], writes=["U4"], tag="l1fev1")
        P.emit("v", lambda eng, mt=mt: eng.tensor_scalar(
            ut[0][:], hh[:, 0:2048], cws[:, 3 * mt + 1:3 * mt + 2], None,
            ALU.mult), reads=["U4", "CWS"], writes=["U0"], tag="cv1")
        P.emit("v", lambda eng, mt=mt: eng.scalar_tensor_tensor(
            ut[0][:, 1:2048], hh[:, 0:2047], cws[:, 3 * mt:3 * mt + 1],
            ut[0][:, 1:2048],
            ALU.mult, ALU.add), reads=["U4", "CWS", "U0"], writes=["U0"],
            tag="cv0")
        P.emit("v", lambda eng, mt=mt: eng.scalar_tensor_tensor(
            ut[0][:, 0:2047], hh[:, 1:2048], cws[:, 3 * mt + 2:3 * mt + 3],
            ut[0][:, 0:2047],
            ALU.mult, ALU.add), reads=["U4", "CWS", "U0"], writes=["U0"],
            tag="cv2")
        P.emit("s", lambda eng, mt=mt: eng.activation(
            ut[1][:], ut[0][:], AF.Sigmoid, bias=cbc[:, mt:mt + 1]),
            reads=["U0", "CBC"], writes=["U1"], tag="sil1")
        P.emit("v", lambda eng, mt=mt: eng.scalar_tensor_tensor(
            ut[2][:], ut[0][:], cbc[:, mt:mt + 1], ut[1][:],
            ALU.add, ALU.mult), reads=["U0", "CBC", "U1"], writes=["U2"],
            tag="sil2")
        P.emit("s", lambda eng: eng.activation(absx[:], ut[2][:], AF.Abs),
               reads=["U2"], writes=["U5"], tag="aqha")
        if mt == 0:
            P.emit("v", lambda eng: eng.tensor_copy(mxt[:], absx[:]),
                   reads=["U5"], writes=["U3"], tag="aqh0")
        else:
            P.emit("v", lambda eng: eng.tensor_max(mxt[:], mxt[:], absx[:]),
                   reads=["U3", "U5"], writes=["U3"], tag="aqh")
        P.emit("y", lambda eng, mt=mt: eng.dma_start(
            out=sild[mt * 128:(mt + 1) * 128, :], in_=ut[2][:]),
            reads=["U2"], writes=["L1SILD"], tag="silo")

    emit_act_quant(P, M, "l1h", S, 0, True, mw2, None)

    w2b = [wpa, wpb]
    w2bk = ["WPA", "WPB"]
    for sc in range(4):
        for i in range(NT):
            def fl(eng, i=i, sc=sc):
                src = sild[:, sc * 512:(sc + 1) * 512].rearrange(
                    "(j q p) c -> p j q c", p=128, q=NT)[:, :, i, :]
                return eng.dma_start(
                    out=ai[i].rearrange("p (j c) -> p j c", j=4), in_=src)
            P.emit("g", fl, reads=["L1SILD"], writes=[aik[i]], tag="sili")
            for j in range(4):
                P.emit("v", lambda eng, i=i, j=j, sc=sc: eng.tensor_mul(
                    ai[i][:, j * 512:(j + 1) * 512],
                    ai[i][:, j * 512:(j + 1) * 512],
                    mxt[:, sc * 512:(sc + 1) * 512]),
                    reads=[aik[i], "U3"], writes=[aik[i]], tag="hqm")
            P.emit("v", lambda eng, i=i: eng.tensor_scalar(
                ai[i][:], ai[i][:], MAGIC, MAGIC, ALU.add, ALU.subtract),
                reads=[aik[i]], writes=[aik[i]], tag="hqr")
        def loadw2(dt, par):
            def fn(eng, dt=dt):
                src = w2q[:, dt * 128:(dt + 1) * 128].rearrange(
                    "(kc p) c -> p kc c", p=128)
                return eng.dma_start(out=w2b[par][:, 0:4096], in_=src)
            P.emit("g", fn, reads=["L1W2Q"], writes=[w2bk[par]], tag="w2l")
        loadw2(0, 0)
        for dt in range(NT):
            par = dt % 2
            if dt + 1 < NT:
                loadw2(dt + 1, 1 - par)
            pn = ["P4", "P5", "P6"][dt % 3]
            psum = [p4, p5, M.ps("P6", [128, 512])][dt % 3]
            for kc in range(32):
                def fn(eng, kc=kc, par=par, psum=psum):
                    return eng.matmul(
                        psum[:], w2b[par][:, kc * 128:(kc + 1) * 128],
                        ai[kc % NT][:, (kc // NT) * 512:(kc // NT) * 512 + 512],
                        start=(kc == 0), stop=(kc == 31))
                P.emit("t", fn, reads=[w2bk[par], aik[kc % NT]], writes=[pn],
                       tag="l1fmm2")
            P.emit("v", lambda eng, dt=dt, sc=sc, psum=psum: eng.tensor_mul(
                ut[0][:, 0:512], psum[:], sclx[:, sc * 512:(sc + 1) * 512]),
                reads=[pn, "T32A"], writes=["U0"], tag="l1fev2a")
            P.emit("v", lambda eng, dt=dt, sc=sc: eng.tensor_add(
                xt[dt][:, sc * 512:(sc + 1) * 512],
                xt[dt][:, sc * 512:(sc + 1) * 512], ut[0][:, 0:512]),
                reads=["U0", xk[dt]], writes=[xk[dt]], tag="l1fev2b")


STAGE_EMITTERS[3] = emit_stage_l1_ffn



L2EPS = 1e-6
L2TAP = None


def emit_stage_l2_attn(P, M, W):
    xt = M.tiles("X", NT)
    xk = keys("X", NT)
    ut = M.tiles("U", NT)
    bt = M.tiles("B", 3)
    a0, a1 = M.sb("A0", [128, S], BF16), M.sb("A1", [128, S], BF16)
    ct = c_area(M)
    sm2 = M.sb("SM2", [128, 64], F32)
    ksem = M.sb("KSEM", [128, 4], BF16)
    onesc = M.sb("ONESC", [128, 1], BF16)
    bca = M.sb("BCA", [128, S], BF16)
    bcb = M.sb("BCB", [128, S], BF16)
    r16a = M.sb("R16A", [16, S], F32)
    r16b = M.sb("R16B", [16, S], F32)
    bqc = M.sb("BQC", [128, 8], F32)
    bkc = M.sb("BKC", [128, 8], F32)
    boc = M.sb("BPC", [128, 8], F32)
    bvb = M.sb("BVB", [128, 1024], BF16)
    kvs = M.sb("KVS", [128, 128], BF16)
    wpa = M.sb("WPA", [128, 4096], BF16)
    p4, p5 = M.ps("P4", [128, 512]), M.ps("P5", [128, 512])
    p6 = M.ps("P6", [128, 512])
    pb = M.ps("PB", [128, 2048])
    drl2 = M.dr("drl2", [2, S], F32)

    for nm, t_, k_ in [("L2_bq", bqc, "BQC"), ("L2_bk", bkc, "BKC"),
                       ("L2_bo", boc, "BPC")]:
        P.emit("g", lambda eng, t_=t_, nm=nm: eng.dma_start(
            out=t_[:], in_=W[nm][0]), reads=[W[nm][1]], writes=[k_], tag="bia")
    P.emit("g", lambda eng: eng.dma_start(
        out=bvb[:], in_=W["L2_bv"][0][0:1, :].partition_broadcast(128)),
        reads=[W["L2_bv"][1]], writes=["BVB"], tag="bvb")

    def rowexp(nm, src_rows, src_key, dst, dst_key):
        P.emit("y", lambda eng: eng.dma_start(out=drl2[:, :], in_=src_rows),
               reads=[src_key], writes=["drl2"], tag=nm + "xo")
        P.emit("g", lambda eng: eng.dma_start(
            out=dst[0:64, :], in_=drl2[0:1, :].partition_broadcast(64)),
            reads=["drl2"], writes=[dst_key], tag=nm + "xi0")
        P.emit("g", lambda eng: eng.dma_start(
            out=dst[64:128, :], in_=drl2[1:2, :].partition_broadcast(64)),
            reads=["drl2"], writes=[dst_key], tag=nm + "xi1")

    def rowmm(nm, msk_col_f32, rhs, rhs_key, eps_corr_col, out_rows, out_key):
        P.emit("v", lambda eng: eng.memset(ksem[:, 0:2], 0.0), (), ["KSEM"])
        P.emit("v", lambda eng: eng.tensor_copy(ksem[0:64, 0:1],
                                                msk_col_f32[0:64, :]),
               reads=["SM2"], writes=["KSEM"], tag=nm + "m0")
        P.emit("v", lambda eng: eng.tensor_copy(ksem[64:128, 1:2],
                                                msk_col_f32[64:128, :]),
               reads=["SM2"], writes=["KSEM"], tag=nm + "m1")
        for sc in range(4):
            P.emit("t", lambda eng, sc=sc: eng.matmul(
                p6[0:2, 0:512], ksem[:, 0:2],
                rhs[:, sc * 512:(sc + 1) * 512], start=True, stop=True),
                reads=["KSEM", rhs_key], writes=["P6"], tag=nm + "mm")
            P.emit("v", lambda eng, sc=sc: eng.tensor_copy(
                out_rows[0:2, sc * 512:(sc + 1) * 512], p6[0:2, 0:512]),
                reads=["P6"], writes=[out_key], tag=nm + "cp")
        P.emit("t", lambda eng: eng.matmul(
            p6[0:2, 0:1], ksem[:, 0:2], onesc[:, 0:1], start=True, stop=True),
            reads=["KSEM", "ONESC"], writes=["P6"], tag=nm + "cs")
        P.emit("v", lambda eng: eng.tensor_scalar(
            sm2[0:2, 20:21], p6[0:2, 0:1], L2EPS, None, ALU.mult),
            reads=["P6"], writes=["SM2"], tag=nm + "cse")
        P.emit("v", lambda eng: eng.tensor_scalar(
            out_rows[0:2, :], out_rows[0:2, :], sm2[0:2, 20:21], None,
            ALU.add), reads=[out_key, "SM2"], writes=[out_key], tag=nm + "ca")

    for pair in range(8):
        fc = pair * 128
        for wi, wnm in enumerate(["L2_WqT", "L2_WkT", "L2_WvT"]):
            def fl(eng, wi=wi, wnm=wnm, fc=fc):
                src = W[wnm][0][:, fc:fc + 128].rearrange(
                    "(kc p) c -> p kc c", p=128)
                return eng.dma_start(out=wpa[:, wi * 1024:(wi + 1) * 1024]
                                     .rearrange("p (kc c) -> p kc c", kc=8),
                                     in_=src)
            P.emit("g", fl, reads=[W[wnm][1]], writes=["WPA"], tag="l2wl")
        for wi, (dst, dkey, bcol, bckey) in enumerate(
                [(a0, "A0", bqc, "BQC"), (a1, "A1", bkc, "BKC")]):
            for sc in range(4):
                psum, pn = (p4, "P4") if wi == 0 else (p5, "P5")
                for kc in range(NT):
                    def fn(eng, wi=wi, kc=kc, sc=sc, psum=psum):
                        return eng.matmul(
                            psum[:],
                            wpa[:, wi * 1024 + kc * 128:wi * 1024 + (kc + 1) * 128],
                            xt[kc][:, sc * 512:(sc + 1) * 512],
                            start=(kc == 0), stop=(kc == NT - 1))
                    P.emit("t", fn, reads=["WPA", xk[kc]], writes=[pn],
                           tag="l2qk")
                P.emit("s", lambda eng, sc=sc, psum=psum, dst=dst, bcol=bcol,
                       pair=pair: eng.activation(
                           dst[:, sc * 512:(sc + 1) * 512], psum[:],
                           AF.Sigmoid, bias=bcol[:, pair:pair + 1]),
                       reads=[pn, bckey], writes=[dkey], tag="l2qk_ev")
        for st in range(STL):
            psum, pn = (p4, "P4") if st % 2 == 0 else (p5, "P5")
            for kc in range(NT):
                def fn(eng, st=st, kc=kc, psum=psum):
                    return eng.matmul(
                        psum[:, 0:128], xt[kc][:, st * 128:(st + 1) * 128],
                        wpa[:, 2048 + kc * 128:2048 + (kc + 1) * 128],
                        start=(kc == 0), stop=(kc == NT - 1))
                P.emit("t", fn, reads=[xk[kc], "WPA"], writes=[pn], tag="l2v")
            def fne(eng, st=st, psum=psum, fc=fc):
                return eng.tensor_add(
                    ct[2 + st // 8][:, (st % 8) * 128:(st % 8) * 128 + 128],
                    psum[:, 0:128], bvb[:, fc:fc + 128])
            P.emit("v", fne, reads=[pn, "BVB"],
                   writes=["C%d" % (2 + st // 8)], tag="l2v_ev")
        P.emit("v", lambda eng: eng.reduce_sum(sm2[:, 0:1], a1[:], axis=AX),
               reads=["A1"], writes=["SM2"], tag="l2ks")
        P.emit("v", lambda eng: eng.tensor_scalar(
            sm2[:, 1:2], sm2[:, 0:1], L2EPS, None, ALU.add),
            reads=["SM2"], writes=["SM2"], tag="l2kse")
        P.emit("v", lambda eng: eng.reduce_sum(sm2[:, 2:3], a0[:], axis=AX),
               reads=["A0"], writes=["SM2"], tag="l2qs")
        P.emit("v", lambda eng: eng.tensor_scalar(
            sm2[:, 3:4], sm2[:, 2:3], L2EPS, None, ALU.add),
            reads=["SM2"], writes=["SM2"], tag="l2qse")
        rowmm("l2si", sm2[:, 1:2], a0, "A0", None, r16a, "R16A")
        P.emit("v", lambda eng: eng.reciprocal(r16a[0:2, :], r16a[0:2, :]),
               reads=["R16A"], writes=["R16A"], tag="l2sir")
        rowmm("l2so", sm2[:, 3:4], a1, "A1", None, r16b, "R16B")
        P.emit("v", lambda eng: eng.reciprocal(r16b[0:2, :], r16b[0:2, :]),
               reads=["R16B"], writes=["R16B"], tag="l2sor")
        rowexp("l2so", r16b[0:2, :], "R16B", bca, "BCA")
        P.emit("v", lambda eng: eng.tensor_mul(bt[0][:], a1[:], bca[:]),
               reads=["A1", "BCA"], writes=["B0"], tag="l2kso")
        P.emit("v", lambda eng: eng.reduce_sum(sm2[:, 5:6], bt[0][:], axis=AX),
               reads=["B0"], writes=["SM2"], tag="l2ksos")
        P.emit("v", lambda eng: eng.tensor_scalar(
            sm2[:, 6:7], sm2[:, 5:6], L2EPS, None, ALU.add),
            reads=["SM2"], writes=["SM2"], tag="l2ksose")
        rowexp("l2si", r16a[0:2, :], "R16A", bcb, "BCB")
        P.emit("v", lambda eng: eng.tensor_mul(bt[1][:], a0[:], bcb[:]),
               reads=["A0", "BCB"], writes=["B1"], tag="l2qsi")
        P.emit("v", lambda eng: eng.reduce_sum(sm2[:, 7:8], bt[1][:], axis=AX),
               reads=["B1"], writes=["SM2"], tag="l2qsis")
        P.emit("v", lambda eng: eng.tensor_scalar(
            sm2[:, 8:9], sm2[:, 7:8], L2EPS, None, ALU.add),
            reads=["SM2"], writes=["SM2"], tag="l2qsise")
        rowmm("l2ck", sm2[:, 6:7], a0, "A0", None, r16b, "R16B")
        P.emit("s", lambda eng: eng.activation(r16b[0:2, :], r16b[0:2, :],
                                               AF.Sigmoid),
               reads=["R16B"], writes=["R16B"], tag="l2sa")
        P.emit("v", lambda eng: eng.tensor_mul(r16a[0:2, :], r16a[0:2, :],
                                               r16b[0:2, :]),
               reads=["R16A", "R16B"], writes=["R16A"], tag="l2sisa")
        rowmm("l2cs", sm2[:, 8:9], a1, "A1", None, r16b, "R16B")
        P.emit("v", lambda eng: eng.tensor_scalar(
            r16b[0:2, :], r16b[0:2, :], -1.0, 1.0, ALU.max, ALU.min),
            reads=["R16B"], writes=["R16B"], tag="l2clip")
        P.emit("s", lambda eng: eng.activation(r16b[0:2, :], r16b[0:2, :],
                                               AF.Exp),
               reads=["R16B"], writes=["R16B"], tag="l2exp")
        P.emit("v", lambda eng: eng.reduce_sum(sm2[0:2, 9:10], r16b[0:2, :],
                                               axis=AX),
               reads=["R16B"], writes=["SM2"], tag="l2ssum")
        P.emit("v", lambda eng: eng.reciprocal(sm2[0:2, 10:11],
                                               sm2[0:2, 9:10]),
               reads=["SM2"], writes=["SM2"], tag="l2srec")
        P.emit("v", lambda eng: eng.tensor_scalar(
            sm2[0:2, 10:11], sm2[0:2, 10:11], float(S), None, ALU.mult),
            reads=["SM2"], writes=["SM2"], tag="l2s2048")
        P.emit("v", lambda eng: eng.tensor_scalar(
            r16b[0:2, :], r16b[0:2, :], sm2[0:2, 10:11], None, ALU.mult),
            reads=["R16B", "SM2"], writes=["R16B"], tag="l2sc")
        if L2TAP is not None and pair == 0:
            P.emit("g", lambda eng: eng.dma_start(
                out=L2TAP["sisa"], in_=r16a[0:2, :]),
                reads=["R16A"], writes=["__t__"], tag="tap")
            P.emit("g", lambda eng: eng.dma_start(
                out=L2TAP["sc"], in_=r16b[0:2, :]),
                reads=["R16B"], writes=["__t__"], tag="tap")
            P.emit("g", lambda eng: eng.dma_start(
                out=L2TAP["q"], in_=a0[:]),
                reads=["A0"], writes=["__t__"], tag="tap")
            P.emit("g", lambda eng: eng.dma_start(
                out=L2TAP["k"], in_=a1[:]),
                reads=["A1"], writes=["__t__"], tag="tap")
            P.emit("g", lambda eng: eng.dma_start(
                out=L2TAP["v"], in_=ct[2][:, 0:1024]),
                reads=["C2"], writes=["__t__"], tag="tap")
        rowexp("l2sc", r16b[0:2, :], "R16B", bca, "BCA")
        P.emit("v", lambda eng: eng.tensor_mul(bt[0][:], a1[:], bca[:]),
               reads=["A1", "BCA"], writes=["B0"], tag="l2ksc")
        for st in range(STL):
            def ft(eng, st=st):
                return eng.dma_start(
                    out=ct[st // 8][:, (st % 8) * 128:(st % 8) * 128 + 128],
                    in_=bt[0][:, st * 128:(st + 1) * 128], transpose=True)
            P.emit("y", ft, reads=["B0"], writes=["C%d" % (st // 8)],
                   tag="l2tr")
        for hh in range(2):
            for st in range(STL):
                def fkv(eng, hh=hh, st=st):
                    lhsT = ct[st // 8][:, (st % 8) * 128 + hh * 64:
                                       (st % 8) * 128 + hh * 64 + 64]
                    rhs = ct[2 + st // 8][:, (st % 8) * 128 + hh * 64:
                                          (st % 8) * 128 + hh * 64 + 64]
                    if hh == 0:
                        return eng.matmul(p6[0:64, 0:64], lhsT, rhs,
                                          start=(st == 0), stop=(st == 15))
                    return eng.matmul(p6[64:128, 0:64], lhsT, rhs,
                                      start=(st == 0), stop=(st == 15),
                                      tile_position=(0, 64))
                P.emit("t", fkv, reads=["C%d" % (st // 8),
                                        "C%d" % (2 + st // 8)],
                       writes=["P6"], tag="l2kv")
        P.emit("v", lambda eng: eng.tensor_copy(kvs[0:64, 0:64],
                                                p6[0:64, 0:64]),
               reads=["P6"], writes=["KVS"], tag="l2kve0")
        P.emit("v", lambda eng: eng.tensor_copy(kvs[64:128, 64:128],
                                                p6[64:128, 0:64]),
               reads=["P6"], writes=["KVS"], tag="l2kve1")
        rowexp("l2sisa", r16a[0:2, :], "R16A", bcb, "BCB")
        P.emit("v", lambda eng: eng.tensor_mul(bt[1][:], a0[:], bcb[:]),
               reads=["A0", "BCB"], writes=["B1"], tag="l2qx")
        for hh in range(2):
            for sc in range(4):
                def fo(eng, hh=hh, sc=sc):
                    if hh == 0:
                        return eng.matmul(
                            pb[0:64, sc * 512:(sc + 1) * 512],
                            kvs[0:64, 0:64],
                            bt[1][0:64, sc * 512:(sc + 1) * 512],
                            start=True, stop=True)
                    return eng.matmul(
                        pb[64:128, sc * 512:(sc + 1) * 512],
                        kvs[64:128, 64:128],
                        bt[1][64:128, sc * 512:(sc + 1) * 512],
                        start=True, stop=True, tile_position=(64, 64))
                P.emit("t", fo, reads=["KVS", "B1"],
                       writes=["PB%d" % sc], tag="l2out")
        P.emit("v", lambda eng, pair=pair: eng.tensor_copy(
            ut[pair][:], pb[0:128, 0:2048]),
            reads=pbkeys(), writes=["U%d" % pair], tag="l2oev")
        if L2TAP is not None and pair == 0:
            P.emit("g", lambda eng: eng.dma_start(
                out=L2TAP["kv"], in_=kvs[:]),
                reads=["KVS"], writes=["__t__"], tag="tap")
            P.emit("g", lambda eng: eng.dma_start(
                out=L2TAP["out"], in_=ut[0][:]),
                reads=["U0"], writes=["__t__"], tag="tap")

    def evac(mt, sc, psum, pn):
        def fn(eng, mt=mt, sc=sc, psum=psum):
            return eng.scalar_tensor_tensor(
                xt[mt][:, sc * 512:(sc + 1) * 512], psum[:],
                boc[:, mt:mt + 1], xt[mt][:, sc * 512:(sc + 1) * 512],
                ALU.add, ALU.add)
        P.emit("v", fn, reads=[pn, "BPC", xk[mt]], writes=[xk[mt]],
               tag="l2wo_ev")
    emit_proj_fmajor(P, M, "l2wo", W["L2_WoT"][0], D, D,
                     ut, keys("U", NT), evac, wt_keys=[W["L2_WoT"][1]])


STAGE_EMITTERS[4] = emit_stage_l2_attn



def emit_stage_l2_ffn(P, M, W):
    xt = M.tiles("X", NT)
    xk = keys("X", NT)
    ut = M.tiles("U", NT)
    at = [M.sb("A0", [128, S], BF16)]
    bt = M.tiles("B", NT)
    bk = keys("B", NT)
    ct = c_area(M)
    mur = M.sb("MUR", [128, 16], F32)
    p4, p5 = M.ps("P4", [128, 512]), M.ps("P5", [128, 512])

    P.emit("g", lambda eng: eng.dma_start(out=mur[:], in_=W["L2_mu"][0]),
           reads=[W["L2_mu"][1]], writes=["MUR"], tag="mu")
    sm2 = M.sb("SM2", [128, 64], F32)
    P.emit("v", lambda eng: eng.tensor_scalar(
        sm2[:, 0:16], mur[:], -1.0, 1.0, ALU.mult, ALU.add),
        reads=["MUR"], writes=["SM2"], tag="mu1")

    def mix(dst_tiles, dkeys, mucol0, nm):
        for i in range(NT):
            P.emit("v", lambda eng, i=i: eng.tensor_scalar(
                dst_tiles[i][:], xt[i][:], mur[:, mucol0 + i:mucol0 + i + 1],
                None, ALU.mult),
                reads=[xk[i], "MUR"], writes=[dkeys[i]], tag=nm + "a")
            P.emit("v", lambda eng, i=i: eng.scalar_tensor_tensor(
                dst_tiles[i][:, 1:2048], xt[i][:, 0:2047],
                sm2[:, mucol0 + i:mucol0 + i + 1],
                dst_tiles[i][:, 1:2048], ALU.mult, ALU.add),
                reads=[xk[i], "SM2", dkeys[i]], writes=[dkeys[i]],
                tag=nm + "b")

    mix(bt, bk, 0, "mxr")
    def evr(mt, sc, psum, pn):
        P.emit("s", lambda eng, mt=mt, sc=sc, psum=psum: eng.activation(
            ut[mt][:, sc * 512:(sc + 1) * 512], psum[:], AF.Sigmoid),
            reads=[pn], writes=["U%d" % mt], tag="l2r_ev")
    emit_proj_fmajor(P, M, "l2r", W["L2f_WrT"][0], D, D, bt, bk, evr,
                     wt_keys=[W["L2f_WrT"][1]])
    mix(bt, bk, 8, "mxk")
    def evk(mt, sc, psum, pn):
        P.emit("s", lambda eng, mt=mt, sc=sc, psum=psum: eng.activation(
            at[0][:, 0:512], psum[:], AF.Relu),
            reads=[pn], writes=["A0"], tag="l2k_ev1")
        P.emit("v", lambda eng, mt=mt, sc=sc: eng.tensor_mul(
            ct[mt][:, sc * 512:(sc + 1) * 512], at[0][:, 0:512],
            at[0][:, 0:512]),
            reads=["A0"], writes=["C%d" % mt], tag="l2k_ev2")
    emit_proj_fmajor(P, M, "l2k", W["L2f_WkT"][0], D, D, bt, bk, evk,
                     wt_keys=[W["L2f_WkT"][1]])
    def evv(mt, sc, psum, pn):
        P.emit("v", lambda eng, mt=mt, sc=sc, psum=psum: eng.tensor_mul(
            at[0][:, 0:512], psum[:], ut[mt][:, sc * 512:(sc + 1) * 512]),
            reads=[pn, "U%d" % mt], writes=["A0"], tag="l2v_ev1")
        P.emit("v", lambda eng, mt=mt, sc=sc: eng.tensor_add(
            xt[mt][:, sc * 512:(sc + 1) * 512],
            xt[mt][:, sc * 512:(sc + 1) * 512], at[0][:, 0:512]),
            reads=["A0", xk[mt]], writes=[xk[mt]], tag="l2v_ev2")
    emit_proj_fmajor(P, M, "l2v", W["L2f_WvT"][0], D, D,
                     [ct[i][:, 0:2048] for i in range(NT)], keys("C", NT),
                     evv, wt_keys=[W["L2f_WvT"][1]])


STAGE_EMITTERS[5] = emit_stage_l2_ffn



def emit_stage_l3_attn(P, M, W):
    xt = M.tiles("X", NT)
    xk = keys("X", NT)
    bt = M.tiles("B", 4)
    lk = keys("B", 4)
    a2 = M.sb("B5", [128, S], BF16)
    ck = M.sb("BCA", [128, S], BF16)[:, 0:S]
    sk = M.sb("BCB", [128, S], BF16)[:, 0:S]
    ctiles = c_area(M)
    p4, p5 = M.ps("P4", [128, 512]), M.ps("P5", [128, 512])
    wpa = M.sb("WPA", [128, 4096], BF16)
    wpb = M.sb("WPB", [128, 4096], BF16)

    P.emit("g", lambda eng: eng.dma_start(out=ck, in_=W["ropeC"][0]),
           reads=[W["ropeC"][1]], writes=["BCA"], tag="ropec")
    P.emit("g", lambda eng: eng.dma_start(out=sk, in_=W["ropeS"][0]),
           reads=[W["ropeS"][1]], writes=["BCB"], tag="ropes")

    emit_load_wchunk(P, wpa, "WPA", W["L3_wd"][0], [W["L3_wd"][1]], NT, 0, 512)
    for mt in range(4):
        for sc in range(4):
            pn = ["P4", "P5"][(mt * 4 + sc) % 2]
            psum = [p4, p5][(mt * 4 + sc) % 2]
            for kc in range(NT):
                def fn(eng, kc=kc, mt=mt, sc=sc, psum=psum):
                    return eng.matmul(
                        psum[:], wpa[:, kc * 512 + mt * 128:
                                     kc * 512 + (mt + 1) * 128],
                        xt[kc][:, sc * 512:(sc + 1) * 512],
                        start=(kc == 0), stop=(kc == NT - 1))
                P.emit("t", fn, reads=["WPA", xk[kc]], writes=[pn],
                       tag="l3lat")
            P.emit("s", lambda eng, mt=mt, sc=sc, psum=psum: eng.activation(
                bt[mt][:, sc * 512:(sc + 1) * 512], psum[:], AF.Copy),
                reads=[pn], writes=[lk[mt]], tag="l3lat_ev")

    emit_load_wchunk(P, wpa, "WPA", W["L3_wvup"][0], [W["L3_wvup"][1]], 4, 0,
                     512)
    emit_load_wchunk(P, wpb, "WPB", W["L3_wvup"][0], [W["L3_wvup"][1]], 4, 512,
                     512)
    for kt in range(STL):
        P.emit("v", lambda eng, kt=kt: eng.memset(
            va_ap(ctiles, kt).rearrange("p (h c) -> p h c", c=65)[:, :, 64:65],
            1.0), reads=[], writes=["C%d" % (kt // 2)], tag="vones")
    for st in range(STL):
        for ec in range(2):
            pn = ["P4", "P5"][(st * 2 + ec) % 2]
            psum = [p4, p5][(st * 2 + ec) % 2]
            wsrc, wkey = (wpa, "WPA") if ec == 0 else (wpb, "WPB")
            for kc in range(4):
                def fn(eng, st=st, kc=kc, psum=psum, wsrc=wsrc):
                    return eng.matmul(
                        psum[:], bt[kc][:, st * 128:(st + 1) * 128],
                        wsrc[:, kc * 512:(kc + 1) * 512],
                        start=(kc == 0), stop=(kc == 3))
                P.emit("t", fn, reads=[lk[kc], wkey], writes=[pn], tag="l3v")
            def fne(eng, st=st, ec=ec, psum=psum):
                dst = va_ap(ctiles, st).rearrange("p (h c) -> p h c", c=65)[
                    :, 8 * ec:8 * ec + 8, 0:64]
                src = psum[:].rearrange("p (h c) -> p h c", c=64)
                return eng.activation(dst, src, AF.Copy)
            P.emit("s", fne, reads=[pn], writes=["C%d" % (st // 2)],
                   tag="l3v_ev")

    qwb = M.sb("WPA", [128, 4096], BF16)[:, 0:1024]
    kwb = M.sb("WPA", [128, 4096], BF16)[:, 1024:1536]

    def proj_qk(pair, qa, qak, ka, kak):
        emit_load_wchunk(P, qwb, "WPA", W["L3_wq"][0], [W["L3_wq"][1]], NT,
                         pair * 128, 128)
        emit_load_wchunk(P, kwb, "WPA", W["L3_wkup"][0], [W["L3_wkup"][1]], 4,
                         pair * 128, 128)
        for sc in range(4):
            psum = p4
            for kc in range(NT):
                def fn(eng, kc=kc, sc=sc, psum=psum):
                    return eng.matmul(
                        psum[:], qwb[:, kc * 128:(kc + 1) * 128],
                        xt[kc][:, sc * 512:(sc + 1) * 512],
                        start=(kc == 0), stop=(kc == NT - 1))
                P.emit("t", fn, reads=["WPA", xk[kc]], writes=["P4"],
                       tag="l3q")
            P.emit("s", lambda eng, sc=sc, psum=psum, qa=qa: eng.activation(
                qa[:, sc * 512:(sc + 1) * 512], psum[:], AF.Copy, scale=0.125),
                reads=["P4"], writes=[qak], tag="l3q_ev")
        for sc in range(4):
            psum = p5
            for kc in range(4):
                def fn(eng, kc=kc, sc=sc, psum=psum):
                    return eng.matmul(
                        psum[:], kwb[:, kc * 128:(kc + 1) * 128],
                        bt[kc][:, sc * 512:(sc + 1) * 512],
                        start=(kc == 0), stop=(kc == 3))
                P.emit("t", fn, reads=["WPA", lk[kc]], writes=["P5"],
                       tag="l3k")
            P.emit("s", lambda eng, sc=sc, psum=psum, ka=ka: eng.activation(
                ka[:, sc * 512:(sc + 1) * 512], psum[:], AF.Copy),
                reads=["P5"], writes=[kak], tag="l3k_ev")
        for (dst, dkey) in [(qa, qak), (ka, kak)]:
            P.emit("v", lambda eng, dst=dst: eng.stream_shuffle(
                a2[:], dst[:], [i ^ 1 for i in range(32)]),
                reads=[dkey], writes=["B5"], tag="rope_sh")
            P.emit("v", lambda eng, dst=dst, dkey=dkey: eng.tensor_mul(
                dst[:], dst[:], ck),
                reads=[dkey, "BCA"], writes=[dkey], tag="rope_c")
            P.emit("v", lambda eng: eng.tensor_mul(a2[:], a2[:], sk),
                   reads=["B5", "BCB"], writes=["B5"], tag="rope_s")
            P.emit("v", lambda eng, dst=dst, dkey=dkey: eng.tensor_add(
                dst[:], dst[:], a2[:]),
                reads=[dkey, "B5"], writes=[dkey], tag="rope_a")

    emit_attention(P, M, "l3", proj_qk, lambda kt: 256 * (kt // 2), False)

    def evac(mt, sc, psum, pn):
        def fn(eng, mt=mt, sc=sc, psum=psum):
            return eng.tensor_add(xt[mt][:, sc * 512:(sc + 1) * 512],
                                  xt[mt][:, sc * 512:(sc + 1) * 512],
                                  psum[:])
        P.emit("v", fn, reads=[pn, xk[mt]], writes=[xk[mt]], tag="l3wo_ev")
    emit_proj_fmajor(P, M, "l3wo", W["L3_wo"][0], D, D,
                     M.tiles("U", NT), keys("U", NT), evac,
                     wt_keys=[W["L3_wo"][1]])


STAGE_EMITTERS[6] = emit_stage_l3_attn



def emit_stage_l3_ffn(P, M, W):
    xt = M.tiles("X", NT)
    xk = keys("X", NT)
    ut = M.tiles("U", NT)
    ct = c_area(M)
    ci = [ct[i][:, 0:2048] for i in range(NT)]
    cik = keys("C", NT)
    mxt = M.sb("U3", [128, S], BF16)
    sclx = M.sb("T32A", [128, S], F32)
    sclx2 = M.sb("T32B", [128, S], F32)
    sm1 = M.sb("SM1", [128, 32], F32)
    p4, p5 = M.ps("P4", [128, 512]), M.ps("P5", [128, 512])
    wpa = M.sb("WPA", [128, 4096], BF16)
    wpb = M.sb("WPB", [128, 4096], BF16)
    wgq = M.dr("L3WGQ", [D, 4096], BF16)
    wuq = M.dr("L3WUQ", [D, 4096], BF16)
    wdq = M.dr("L3WDQ", [4096, D], BF16)

    emit_weight_quant(P, M, "l3wg", W["L3f_wgT"][0], W["L3f_wgT"][1], D, 4096,
                      wgq, "L3WGQ")
    P.emit("v", lambda eng: eng.tensor_copy(sm1[0:1, 8:9], sm1[0:1, 2:3]),
           reads=["SM1"], writes=["SM1"], tag="mwg")
    emit_weight_quant(P, M, "l3wu", W["L3f_wuT"][0], W["L3f_wuT"][1], D, 4096,
                      wuq, "L3WUQ")
    P.emit("v", lambda eng: eng.tensor_copy(sm1[0:1, 9:10], sm1[0:1, 2:3]),
           reads=["SM1"], writes=["SM1"], tag="mwu")
    emit_weight_quant(P, M, "l3wd", W["L3f_wdT"][0], W["L3f_wdT"][1], 4096, D,
                      wdq, "L3WDQ")
    P.emit("v", lambda eng: eng.tensor_copy(sm1[0:1, 10:11], sm1[0:1, 2:3]),
           reads=["SM1"], writes=["SM1"], tag="mwd")
    P.emit("v", lambda eng: eng.tensor_mul(sm1[0:1, 12:13], sm1[0:1, 8:9],
                                           sm1[0:1, 9:10]),
           reads=["SM1"], writes=["SM1"], tag="mwc1")
    P.emit("v", lambda eng: eng.tensor_mul(sm1[0:1, 12:13], sm1[0:1, 12:13],
                                           sm1[0:1, 10:11]),
           reads=["SM1"], writes=["SM1"], tag="mwc2")
    emit_bcast_row(P, M, "l3mwg", sm1[0:1, 8:9], "SM1", 128, sm1[:, 13:14],
                   "SM1", 1)

    absx = M.sb("U5", [128, S], BF16)
    for i in range(NT):
        P.emit("s", lambda eng, i=i: eng.activation(absx[:], xt[i][:], AF.Abs),
               reads=[xk[i]], writes=["U5"], tag="l3aqa")
        if i == 0:
            P.emit("v", lambda eng: eng.tensor_copy(mxt[:], absx[:]),
                   reads=["U5"], writes=["U3"], tag="l3aq0")
        else:
            P.emit("v", lambda eng: eng.tensor_max(mxt[:], mxt[:], absx[:]),
                   reads=["U3", "U5"], writes=["U3"], tag="l3aq")
    emit_act_quant(P, M, "l3x", S, 0, True, None, None)
    for i in range(NT):
        P.emit("v", lambda eng, i=i: eng.tensor_mul(ci[i], xt[i][:], mxt[:]),
               reads=[xk[i], "U3"], writes=[cik[i]], tag="l3xm")
        P.emit("v", lambda eng, i=i: eng.tensor_scalar(
            ci[i], ci[i], MAGIC, MAGIC, ALU.add, ALU.subtract),
            reads=[cik[i]], writes=[cik[i]], tag="l3xr")

    gud = M.dr("L3GUD", [4096, S], BF16)
    pb0 = M.ps("PB", [128, 2048])
    p6 = M.ps("P6", [128, 512])
    banksets = [[(p4, "P4"), (p5, "P5")], [(p6, "P6"), (pb0, "PB0")]]

    def loadgu(mt, par):
        wb = [wpa, wpb][par]
        wk = ["WPA", "WPB"][par]
        def fg(eng, mt=mt, wb=wb):
            src = wgq[:, mt * 128:(mt + 1) * 128].rearrange(
                "(kc p) c -> p kc c", p=128)
            return eng.dma_start(out=wb[:, 0:1024], in_=src)
        P.emit("g", fg, reads=["L3WGQ"], writes=[wk], tag="wgl")
        def fu(eng, mt=mt, wb=wb):
            src = wuq[:, mt * 128:(mt + 1) * 128].rearrange(
                "(kc p) c -> p kc c", p=128)
            return eng.dma_start(out=wb[:, 1024:2048].rearrange(
                "p c -> p c"), in_=src)
        P.emit("g", fu, reads=["L3WUQ"], writes=[wk], tag="wul")

    loadgu(0, 0)
    for mt in range(32):
        par = mt % 2
        if mt + 1 < 32:
            loadgu(mt + 1, 1 - par)
        wb = [wpa, wpb][par]
        wk = ["WPA", "WPB"][par]
        bs = banksets[mt % 2]
        for sc in range(4):
            (pg, pgn), (pu, pun) = bs
            for kc in range(NT):
                P.emit("t", lambda eng, kc=kc, sc=sc, wb=wb, pg=pg: eng.matmul(
                    pg[:, 0:512], wb[:, kc * 128:(kc + 1) * 128],
                    ci[kc][:, sc * 512:(sc + 1) * 512],
                    start=(kc == 0), stop=(kc == NT - 1)),
                    reads=[wk, cik[kc]], writes=[pgn], tag="l3g")
            for kc in range(NT):
                P.emit("t", lambda eng, kc=kc, sc=sc, wb=wb, pu=pu: eng.matmul(
                    pu[:, 0:512], wb[:, 1024 + kc * 128:1024 + (kc + 1) * 128],
                    ci[kc][:, sc * 512:(sc + 1) * 512],
                    start=(kc == 0), stop=(kc == NT - 1)),
                    reads=[wk, cik[kc]], writes=[pun], tag="l3u")
            P.emit("v", lambda eng, sc=sc, pg=pg, pgn=pgn: eng.tensor_mul(
                ut[0][:, sc * 512:(sc + 1) * 512], pg[:, 0:512],
                sclx[:, sc * 512:(sc + 1) * 512]),
                reads=[pgn, "T32A"], writes=["U0"], tag="l3gev")
            P.emit("v", lambda eng, sc=sc, pu=pu, pun=pun: eng.tensor_mul(
                ut[1][:, sc * 512:(sc + 1) * 512], pu[:, 0:512],
                sclx[:, sc * 512:(sc + 1) * 512]),
                reads=[pun, "T32A"], writes=["U1"], tag="l3uev")
        P.emit("s", lambda eng: eng.activation(
            ut[2][:], ut[0][:], AF.Sigmoid, scale=sm1[:, 13:14]),
            reads=["U0", "SM1"], writes=["U2"], tag="l3sg")
        P.emit("v", lambda eng: eng.tensor_mul(ut[0][:], ut[0][:], ut[2][:]),
               reads=["U0", "U2"], writes=["U0"], tag="l3gg")
        P.emit("v", lambda eng: eng.tensor_mul(ut[4][:], ut[0][:], ut[1][:]),
               reads=["U0", "U1"], writes=["U4"], tag="l3gu")
        P.emit("s", lambda eng: eng.activation(absx[:], ut[4][:], AF.Abs),
               reads=["U4"], writes=["U5"], tag="l3ga")
        if mt == 0:
            P.emit("v", lambda eng: eng.tensor_copy(mxt[:], absx[:]),
                   reads=["U5"], writes=["U3"], tag="l3gm0")
        else:
            P.emit("v", lambda eng: eng.tensor_max(mxt[:], mxt[:], absx[:]),
                   reads=["U3", "U5"], writes=["U3"], tag="l3gm")
        P.emit("y", lambda eng, mt=mt: eng.dma_start(
            out=gud[mt * 128:(mt + 1) * 128, :], in_=ut[4][:]),
            reads=["U4"], writes=["L3GUD"], tag="l3guo")

    emit_act_quant(P, M, "l3gu", S, 0, True, sm1[0:1, 12:13], None,
                   sclx_name="T32B")

    w2b = [wpa, wpb]
    w2bk = ["WPA", "WPB"]
    for sc in range(4):
        for i in range(NT):
            def fl(eng, i=i, sc=sc):
                src = gud[:, sc * 512:(sc + 1) * 512].rearrange(
                    "(j q p) c -> p j q c", p=128, q=NT)[:, :, i, :]
                return eng.dma_start(
                    out=ci[i].rearrange("p (j c) -> p j c", j=4), in_=src)
            P.emit("g", fl, reads=["L3GUD"], writes=[cik[i]], tag="l3gi")
            for j in range(4):
                P.emit("v", lambda eng, i=i, j=j, sc=sc: eng.tensor_mul(
                    ci[i][:, j * 512:(j + 1) * 512],
                    ci[i][:, j * 512:(j + 1) * 512],
                    mxt[:, sc * 512:(sc + 1) * 512]),
                    reads=[cik[i], "U3"], writes=[cik[i]], tag="l3gqm")
            P.emit("v", lambda eng, i=i: eng.tensor_scalar(
                ci[i], ci[i], MAGIC, MAGIC, ALU.add, ALU.subtract),
                reads=[cik[i]], writes=[cik[i]], tag="l3gqr")
        def loadwd(dt, par):
            wb = [wpa, wpb][par]
            def fn(eng, dt=dt, wb=wb):
                src = wdq[:, dt * 128:(dt + 1) * 128].rearrange(
                    "(kc p) c -> p kc c", p=128)
                return eng.dma_start(out=wb[:, 0:4096], in_=src)
            P.emit("g", fn, reads=["L3WDQ"], writes=[["WPA", "WPB"][par]],
                   tag="wdl")
        loadwd(0, 0)
        for dt in range(NT):
            par = dt % 2
            if dt + 1 < NT:
                loadwd(dt + 1, 1 - par)
            wb = [wpa, wpb][par]
            wk = ["WPA", "WPB"][par]
            pn = ["P4", "P5", "P6"][dt % 3]
            psum = [p4, p5, p6][dt % 3]
            for kc in range(32):
                def fn(eng, kc=kc, wb=wb, psum=psum):
                    return eng.matmul(
                        psum[:], wb[:, kc * 128:(kc + 1) * 128],
                        ci[kc % NT][:, (kc // NT) * 512:(kc // NT) * 512 + 512],
                        start=(kc == 0), stop=(kc == 31))
                P.emit("t", fn, reads=[wk, cik[kc % NT]], writes=[pn],
                       tag="l3dmm")
            P.emit("v", lambda eng, sc=sc, psum=psum, pn=pn: eng.tensor_mul(
                ut[0][:, 0:512], psum[:], sclx2[:, sc * 512:(sc + 1) * 512]),
                reads=[pn, "T32B"], writes=["U0"], tag="l3dev1")
            P.emit("v", lambda eng, dt=dt, sc=sc: eng.tensor_add(
                xt[dt][:, sc * 512:(sc + 1) * 512],
                xt[dt][:, sc * 512:(sc + 1) * 512], ut[0][:, 0:512]),
                reads=["U0", xk[dt]], writes=[xk[dt]], tag="l3dev2")


STAGE_EMITTERS[7] = emit_stage_l3_ffn
